# version 03
# baseline (speedup 1.0000x reference)
import math
from contextlib import ExitStack

import numpy as np
import ml_dtypes

import concourse.bass as bass
import concourse.mybir as mybir
from concourse.bass_utils import run_bass_kernel_spmd

F32 = mybir.dt.float32
BF16 = mybir.dt.bfloat16
AF = mybir.ActivationFunctionType
ALU = mybir.AluOpType
AX = mybir.AxisListType

NCORES = 8
SEQS = 2
T = 2048
D = 1024
NC = 8
TH = 1024
HID = 2730
HIDP = 2816
NJ = HIDP // 128
NEG = -1.0e30

UNFUSED = False
DEV = {}
LAST_PROG = None

COMPUTE_STREAMS = ("pe", "act", "dve", "pool", "sp")
QUEUE_ISSUER = {"qs": "sp", "qa": "act", "qg": "pool"}
QUEUE_LANES = {"qs": 8, "qa": 6, "qg": 12}
LANE_STREAMS = [(q, r) for q, R in QUEUE_LANES.items() for r in range(R)]
STREAM_ENGINE = {"pe": "tensor", "act": "scalar", "dve": "vector",
                 "pool": "gpsimd", "sp": "sync"}


def _is_lane(s):
    return isinstance(s, tuple)


class _Op:
    __slots__ = ("stream", "issuer", "emit", "waits", "count", "inc", "tag")

    def __init__(self, stream, issuer, emit, waits, count, tag=None):
        self.stream = stream
        self.issuer = issuer
        self.emit = emit
        self.waits = waits
        self.count = count
        self.inc = False
        self.tag = tag


class Prog:
    def __init__(self, nc):
        self.nc = nc
        self.es = ExitStack()
        self.counts = {s: 0 for s in COMPUTE_STREAMS}
        self.counts.update({ls: 0 for ls in LANE_STREAMS})
        self.qidx = {q: 0 for q in QUEUE_ISSUER}
        self.ops = {s: [] for s in COMPUTE_STREAMS}
        self.vc = {s: {} for s in COMPUTE_STREAMS}
        self.res = {}
        self.sems = {}
        self.hazards = {s: set() for s in COMPUTE_STREAMS}
        self.self_hw = {s: 0 for s in COMPUTE_STREAMS}

    def sb(self, name, shape, dtype):
        return self.es.enter_context(
            self.nc.sbuf_tensor(_uname(name), shape, dtype))

    def ps(self, name, shape, dtype=F32):
        return self.es.enter_context(
            self.nc.psum_tensor(_uname(name), shape, dtype))

    @staticmethod
    def _merge(dst, src):
        for k, v in src.items():
            if dst.get(k, 0) < v:
                dst[k] = v

    def op(self, stream, emit, reads=(), writes=(), queue=None, tag=None,
           self_wait=False, hazard=False):
        lane_prewait = None
        if queue is not None:
            issuer = QUEUE_ISSUER[queue]
            r = self.qidx[queue] % QUEUE_LANES[queue]
            self.qidx[queue] += 1
            cstream = (queue, r)
            step = 16
            if self.counts[cstream] > 0:
                lane_prewait = (cstream, self.counts[cstream])
        else:
            issuer = stream
            cstream = stream
            step = 1
        deps = []
        bar = self.res.pop(("__bar__", issuer), None)
        if bar is not None and bar[0] is not None:
            deps.append(bar[0])
        for r in reads:
            st = self.res.get(r)
            if st is not None and st[0] is not None:
                deps.append(st[0])
        for w in writes:
            st = self.res.get(w)
            if st is not None:
                if st[0] is not None:
                    deps.append(st[0])
                deps.extend(st[1])
        need = {}
        for st in deps:
            self._merge(need, st[2])
        self_tgt = 0
        if queue is None:
            for st in deps:
                if st[0] == issuer:
                    if issuer == "pe" and not self_wait:
                        continue
                    self_tgt = max(self_tgt, st[1])
        ivc = self.vc[issuer]
        waits = {}
        if lane_prewait is not None:
            s, n = lane_prewait
            if ivc.get(s, 0) < n:
                waits[s] = n
                self._merge(ivc, {s: n})
        for s, n in need.items():
            if s == issuer and queue is None:
                if self_tgt > 0 and self.self_hw[issuer] < self_tgt:
                    waits[s] = self_tgt
                    self.self_hw[issuer] = self_tgt
                continue
            if ivc.get(s, 0) < n:
                waits[s] = n
        self._merge(ivc, need)
        self.counts[cstream] += step
        c = self.counts[cstream]
        if queue is None:
            ivc[cstream] = c
        rec = _Op(cstream, issuer, emit, waits, c,
                  tag or (tuple(writes)[:1] or ("?",))[0])
        self.ops[issuer].append(rec)
        if hazard and queue is None:
            self.hazards[cstream].add(c)
        if queue is None:
            svc = dict(ivc)
        else:
            svc = {s: n for s, n in ivc.items() if s != issuer}
            svc[cstream] = c
        stamp = (cstream, c, svc)
        for w in writes:
            self.res[w] = (stamp, [])
        for r in reads:
            st = self.res.get(r)
            if st is None:
                self.res[r] = (None, [stamp])
            else:
                st[1].append(stamp)
        return stamp

    def dma(self, queue, out_ap, in_ap, reads=(), writes=(), **kw):
        return self.op(
            None,
            (lambda eng, o=out_ap, i=in_ap, kw=kw: eng.dma_start(out=o, in_=i, **kw)),
            reads=reads, writes=writes, queue=queue)

    def mm(self, out, lhsT, rhs, start=True, stop=True, reads=(), writes=()):
        return self.op(
            "pe",
            (lambda t, o=out, l=lhsT, r=rhs, st=start, sp=stop:
             t.matmul(o, l, r, start=st, stop=sp)),
            reads=reads, writes=writes)

    def barrier(self):
        cur = {s: c for s, c in self.counts.items() if c > 0 and s != "sp"}
        need = dict(cur)
        ivc = self.vc["sp"]
        waits = {s: n for s, n in need.items() if ivc.get(s, 0) < n}
        self._merge(ivc, need)
        self.counts["sp"] += 1
        c = self.counts["sp"]
        ivc["sp"] = c
        rec = _Op("sp", "sp", (lambda eng: eng.nop()), waits, c)
        self.ops["sp"].append(rec)
        bstamp = ("sp", c, dict(ivc))
        for s in COMPUTE_STREAMS:
            if s != "sp":
                self.res[("__bar__", s)] = (bstamp, [])
        return bstamp

    def finalize_and_emit(self):
        nc = self.nc
        finals = {s: c for s, c in self.counts.items() if c > 0 and s != "sp"}
        targets = {s: set() for s in self.counts}
        for lst in self.ops.values():
            for o in lst:
                for s, n in o.waits.items():
                    targets[s].add(n)
        for s, c in finals.items():
            targets[s].add(c)
        for ls in LANE_STREAMS:
            targets[ls] = set(range(16, self.counts[ls] + 1, 16))
        miles = {s: sorted(ts) for s, ts in targets.items()}
        import bisect

        def remap(s, n):
            return bisect.bisect_right(miles[s], n) * (16 if _is_lane(s) else 1)

        mileset = {s: set(ts) for s, ts in miles.items()}
        for lst in self.ops.values():
            for o in lst:
                if o.count in mileset[o.stream]:
                    o.inc = True
        for s, ts in miles.items():
            if ts:
                nm = "sem_" + ("_".join(map(str, s)) if _is_lane(s) else str(s))
                self.sems[s] = self.es.enter_context(nc.semaphore(nm))
        block = self.es.enter_context(nc.Block())

        def emit_stream(eng_stream):
            ops = self.ops[eng_stream]
            if not ops and eng_stream != "sp":
                return
            dec = getattr(block, STREAM_ENGINE[eng_stream])

            def body(eng):
                for o in ops:
                    for s, n in sorted(o.waits.items(), key=lambda x: str(x[0])):
                        eng.wait_ge(self.sems[s], remap(s, n))
                    instr = o.emit(eng)
                    if o.inc:
                        instr.then_inc(self.sems[o.stream],
                                       16 if _is_lane(o.stream) else 1)
                if eng_stream == "sp":
                    for s, c in sorted(finals.items(), key=lambda x: str(x[0])):
                        if c and self.vc["sp"].get(s, 0) < c:
                            eng.wait_ge(self.sems[s], remap(s, c))
            dec(body)

        for s in ("sp", "pe", "act", "dve", "pool"):
            emit_stream(s)


def _bf(x):
    return np.asarray(x, np.float32).astype(ml_dtypes.bfloat16)


def host_constants():
    C = {}
    C["ones_bf"] = _bf(np.ones((128, 1)))
    on64a = np.zeros((128, 1), np.float32); on64a[:64] = 1.0
    on64b = np.zeros((128, 1), np.float32); on64b[64:] = 1.0
    C["ones64a_bf"] = _bf(on64a)
    C["ones64b_bf"] = _bf(on64b)
    C["ones1_f"] = np.ones((1, 128), np.float32)
    sa = np.zeros((1, 128), np.float32); sa[0, :64] = 1.0
    sb_ = np.zeros((1, 128), np.float32); sb_[0, 64:] = 1.0
    C["selA_f"] = sa
    C["selB_f"] = sb_
    C["ident_bf"] = _bf(np.eye(128))
    C["ident_f"] = np.eye(128, dtype=np.float32)
    R = np.zeros((128, 128), np.float32)
    for base in (0, 64):
        for i in range(32):
            R[base + 32 + i, base + i] = -1.0
            R[base + i, base + 32 + i] = 1.0
    C["rot_bf"] = _bf(R)
    DH = 64
    WINDOW = 512
    n = T
    half_idx = np.arange(0, DH, 2, dtype=np.float64)
    inv_freq = 1.0 / 10000.0 ** (half_idx / DH)
    t = np.arange(n, dtype=np.float64)
    freqs = t[:, None] * inv_freq[None, :]
    freqs = np.concatenate([freqs, freqs], axis=-1)
    base = (half_idx + 0.4 * DH) / (1.4 * DH)
    power = (t - n // 2) / float(WINDOW // 2)
    scale = base[None, :] ** power[:, None]
    scale = np.concatenate([scale, scale], axis=-1)
    cos, sin = np.cos(freqs), np.sin(freqs)
    rep = lambda a: np.tile(np.ascontiguousarray(a.T), (2, 1))
    C["c1q_bf"] = _bf(rep(cos * scale * 8.0))
    C["c2q_bf"] = _bf(rep(sin * scale * 8.0))
    C["c1k_bf"] = _bf(rep(cos / scale))
    C["c2k_bf"] = _bf(rep(sin / scale))
    W = 1408
    jj = np.arange(128)[:, None]
    y = np.arange(W)[None, :] - 384
    band = (y - jj >= 0) & (y - jj <= 512)
    C["maskband_bf"] = _bf(np.where(band, 0.0, NEG))
    PHI = (1.0 + math.sqrt(5.0)) / 2.0
    pos = (np.arange(T, dtype=np.float32) * np.float32(PHI)) % np.float32(1.0)
    dist = np.abs(pos[:, None] - pos[None, :])
    phi = np.log(np.exp(-dist) + 1e-8).astype(np.float32) * 8.0
    phiT = np.ascontiguousarray(phi.T)
    C["phi8_bf"] = _bf(
        phiT.reshape(16, 128, 4, 512).transpose(2, 0, 1, 3))
    Yb = np.zeros((32, 512), np.float32)
    Ye = np.zeros((16, 512), np.float32)
    for idx in range(512):
        Yb[idx // 16, idx] = 1.0
        Ye[idx % 16, idx] = 1.0
    Zd = np.zeros((32, 4, 128), np.float32)
    Zg = np.zeros((16, 4, 128), np.float32)
    for pc in range(4):
        for col in range(128):
            idx = pc * 128 + col
            Zd[idx // 16, pc, col] = 1.0
            Zg[idx % 16, pc, col] = 1.0
    C["Yb_f"] = Yb
    C["Ye_f"] = Ye
    C["Zd_f"] = Zd
    C["Zg_f"] = Zg
    return C


def _pp(vec, nch):
    return np.ascontiguousarray(np.asarray(vec, np.float32).reshape(nch, 128).T)


def host_prep(inputs):
    ix = {k: np.asarray(v) for k, v in inputs.items()}
    shared = dict(host_constants())
    for nm in ("Wq", "Wv", "Wg", "Wo"):
        shared["L0a_" + nm + "T"] = np.ascontiguousarray(ix["L0_attn_" + nm].T)
    shared["L0a_lng"] = _pp(ix["L0_attn_ln_gamma"], 8)
    shared["L0a_lnb"] = _pp(ix["L0_attn_ln_beta"], 8)
    shared["L0a_lam"] = np.ascontiguousarray(
        np.repeat(ix["L0_attn_decay"].astype(np.float32), 64)[None, :])
    shared["L0f_wqkv"] = np.ascontiguousarray(ix["L0_ffn_wqkv"])
    w1 = ix["L0_ffn_w1"]
    w1a = np.zeros((2, 1024, HIDP), np.float32)
    w1g = np.zeros((2, 1024, HIDP), np.float32)
    w1a[:, :, :HID] = w1[:, :, :HID]
    w1g[:, :, :HID] = w1[:, :, HID:]
    shared["L0f_w1a"] = w1a
    shared["L0f_w1g"] = w1g
    w2 = ix["L0_ffn_w2"]
    w2p = np.zeros((2, HIDP, 1024), np.float32)
    w2p[:, :HID, :] = w2
    shared["L0f_w2"] = w2p
    shared["L0f_wo"] = np.ascontiguousarray(ix["L0_ffn_wo"])
    b1 = ix["L0_ffn_b1"]
    b1a = np.zeros((2, HIDP), np.float32); b1a[:, :HID] = b1[:, :HID]
    b1g = np.zeros((2, HIDP), np.float32); b1g[:, :HID] = b1[:, HID:]
    shared["L0f_b1a"] = np.stack([_pp(b1a[l], NJ) for l in range(2)], 1)
    shared["L0f_b1g"] = np.stack([_pp(b1g[l], NJ) for l in range(2)], 1)
    shared["L0f_b2"] = np.stack([_pp(ix["L0_ffn_b2"][l], 8) for l in range(2)], 1)
    shared["L0f_lng"] = np.stack([_pp(ix["L0_ffn_ln_g"][l], 8) for l in range(2)], 1)
    shared["L0f_lnb"] = np.stack([_pp(ix["L0_ffn_ln_b"][l], 8) for l in range(2)], 1)
    qs = ix["L0_ffn_q_scale"].astype(np.float32)
    ks = ix["L0_ffn_k_scale"].astype(np.float32)
    tile2 = lambda v: np.ascontiguousarray(np.tile(v, 2)[:, None])
    shared["L0f_qs"] = np.concatenate([tile2(qs[l]) for l in range(2)], 1)
    shared["L0f_ks"] = np.concatenate([tile2(ks[l]) for l in range(2)], 1)
    swap = lambda v: np.concatenate([v[32:], v[:32]])
    shared["L0f_qs_sw"] = np.concatenate([tile2(swap(qs[l])) for l in range(2)], 1)
    shared["L0f_ks_sw"] = np.concatenate([tile2(swap(ks[l])) for l in range(2)], 1)
    shared["L1_wqkv"] = np.ascontiguousarray(ix["L1_attn_w_qkv"])
    shared["L1_wproj"] = np.ascontiguousarray(ix["L1_attn_w_proj"])
    shared["L1_bproj"] = _pp(ix["L1_attn_b_proj"], 8)
    shared["L1f_G1R"] = np.ascontiguousarray(
        ix["L1_ffn_G1_R"].reshape(32, 4, 32).astype(np.float32))
    shared["L1f_G1I"] = np.ascontiguousarray(
        ix["L1_ffn_G1_I"].reshape(32, 4, 32).astype(np.float32))
    shared["L1f_G2R"] = np.ascontiguousarray(
        ix["L1_ffn_G2_R"].reshape(4, 16, 16).transpose(1, 0, 2))
    shared["L1f_G2I"] = np.ascontiguousarray(
        ix["L1_ffn_G2_I"].reshape(4, 16, 16).transpose(1, 0, 2))
    shared["L1f_bri"] = _pp(
        np.concatenate([ix["L1_ffn_bias_real"], ix["L1_ffn_bias_imag"]]), 8)
    x = np.asarray(ix["L0_attn_x"], np.float32)
    per_core = []
    for core in range(NCORES):
        xs = x[core * SEQS:(core + 1) * SEQS]
        xf = xs.transpose(0, 2, 1).reshape(SEQS, NC, 128, T)
        per_core.append({"x": np.ascontiguousarray(xf)})
    return shared, per_core


class H:
    pass


def declare_consts(p, nc, hd):
    g = H()
    ld = []

    def load(name, shape, dtype, src_name=None, cast=False):
        tsb = p.sb("c_" + name, shape, dtype)
        src = hd[src_name or name]
        q = "qg" if cast else "qs"
        p.dma(q, tsb[:], src, writes=[("c", name)])
        setattr(g, name, tsb)
        return tsb

    load("ones_bf", [128, 1], BF16)
    load("ones64a_bf", [128, 1], BF16)
    load("ones64b_bf", [128, 1], BF16)
    load("ones1_f", [1, 128], F32)
    load("selA_f", [1, 128], F32)
    load("selB_f", [1, 128], F32)
    load("ident_bf", [128, 128], BF16)
    load("ident_f", [128, 128], F32)
    load("rot_bf", [128, 128], BF16)
    load("maskband_bf", [128, 1408], BF16)
    load("L0a_lng", [128, 8], F32)
    load("L0a_lnb", [128, 8], F32)
    load("L0a_lam", [1, 1024], F32)
    load("L0f_b1a", [128, 2, NJ], F32)
    load("L0f_b1g", [128, 2, NJ], F32)
    load("L0f_b2", [128, 2, 8], F32)
    load("L0f_lng", [128, 2, 8], F32)
    load("L0f_lnb", [128, 2, 8], F32)
    load("L0f_qs", [128, 2], F32)
    load("L0f_ks", [128, 2], F32)
    load("L0f_qs_sw", [128, 2], F32)
    load("L0f_ks_sw", [128, 2], F32)
    load("L1_bproj", [128, 8], F32)
    load("L1f_bri", [128, 8], F32)
    load("L1f_G1R", [32, 4, 32], F32)
    load("L1f_G1I", [32, 4, 32], F32)
    load("L1f_G2R", [16, 4, 16], F32)
    load("L1f_G2I", [16, 4, 16], F32)
    load("Yb_f", [32, 512], F32)
    load("Ye_f", [16, 512], F32)
    load("Zd_f", [32, 4, 128], F32)
    load("Zg_f", [16, 4, 128], F32)
    onesrow = p.sb("c_onesrow", [1, 512], F32)
    p.op("dve", lambda v: v.memset(onesrow[:], 1.0), writes=[("c", "onesrow")])
    g.onesrow = onesrow
    return g


def emit_ln(p, g, u, key, get_src, n, out_bf16, src_reads, out_keys,
            out_fp32=None, affine=None, eps=1e-5, get_src_f=None):
    r0 = lambda sl: u.rows[0:1, 0 * 512 + 0:0 * 512 + n]
    r1 = lambda sl: u.rows[0:1, 1 * 512 + 0:1 * 512 + n]
    r2 = lambda sl: u.rows[0:1, 2 * 512 + 0:2 * 512 + n]
    K = lambda *a: (key,) + a
    for c in range(NC):
        src = get_src(c)
        p.op("act", (lambda s, src=src: s.activation(u.sq[:, :n], src, AF.Square)),
             reads=list(src_reads) + [("psQ",)], writes=[("u", "sq")])
        p.mm(u.psS[0:1, :n], g.ones_bf[:], src, start=(c == 0), stop=(c == NC - 1),
             reads=list(src_reads), writes=[("psS",)])
        p.mm(u.psQ[0:1, :n], g.ones_bf[:], u.sq[:, :n], start=(c == 0),
             stop=(c == NC - 1), reads=[("u", "sq")], writes=[("psQ",)])
    p.op("dve", lambda v: v.tensor_scalar_mul(r0(0), u.psS[0:1, :n], 1.0 / D),
         reads=[("psS",)], writes=[("u", "rows", 0)], hazard=True)
    p.op("dve", lambda v: v.tensor_scalar_mul(r1(0), u.psQ[0:1, :n], 1.0 / D),
         reads=[("psQ",)], writes=[("u", "rows", 1)], hazard=True)
    p.op("dve", lambda v: v.tensor_mul(r2(0), r0(0), r0(0)),
         reads=[("u", "rows", 0)], writes=[("u", "rows", 2)])
    p.op("dve", lambda v: v.tensor_sub(r1(0), r1(0), r2(0)),
         reads=[("u", "rows", 1), ("u", "rows", 2)], writes=[("u", "rows", 1)])
    p.op("act", lambda s: s.activation(r2(0), r1(0), AF.Sqrt,
                                       bias=u.eps[0:1, 0:1]),
         reads=[("u", "rows", 1), ("u", "eps")], writes=[("u", "rows", 2)])
    p.op("dve", lambda v: v.reciprocal(r1(0), r2(0)),
         reads=[("u", "rows", 2)], writes=[("u", "rows", 1)], hazard=True)
    p.op("dve", lambda v: v.tensor_mul(r2(0), r0(0), r1(0)),
         reads=[("u", "rows", 0), ("u", "rows", 1)], writes=[("u", "rows", 2)])
    p.mm(u.psC[:, 0:n], g.ones1_f[:], r1(0), reads=[("u", "rows", 1)], writes=[("psC", 0)])
    p.mm(u.psC[:, 512:512 + n], g.ones1_f[:], r2(0), reads=[("u", "rows", 2)], writes=[("psC", 1)])
    for c in range(NC):
        src = (get_src_f or get_src)(c)
        p.op("dve",
             (lambda v, src=src: v.tensor_mul(u.tmp[:, :n], src, u.psC[:, 0:n])),
             reads=list(src_reads) + [("psC", 0)], writes=[("u", "tmp")])
        if affine is not None:
            p.op("dve", lambda v: v.tensor_sub(u.tmp[:, :n], u.tmp[:, :n],
                                               u.psC[:, 512:512 + n]),
                 reads=[("u", "tmp"), ("psC", 1)], writes=[("u", "tmp")])
            gam, bet = affine
            dstb = out_bf16(c)
            p.op("act",
                 (lambda s, c=c, dstb=dstb: s.activation(
                     dstb, u.tmp[:, :n], AF.Identity,
                     scale=gam(c), bias=bet(c))),
                 reads=[("u", "tmp")], writes=[out_keys(c)])
        elif out_fp32 is not None:
            dst = out_fp32(c)
            p.op("dve", (lambda v, dst=dst: v.tensor_sub(dst, u.tmp[:, :n],
                                                         u.psC[:, 512:512 + n])),
                 reads=[("u", "tmp"), ("psC", 1)], writes=[out_keys(c)])
            if out_bf16 is not None:
                dstb = out_bf16(c)
                p.op("act", (lambda s, dst=dst, dstb=dstb: s.copy(dstb, dst)),
                     reads=[out_keys(c)], writes=[out_keys(c)])
        else:
            dstb = out_bf16(c)
            p.op("dve", (lambda v, dstb=dstb: v.tensor_sub(dstb, u.tmp[:, :n],
                                                           u.psC[:, 512:512 + n])),
                 reads=[("u", "tmp"), ("psC", 1)], writes=[out_keys(c)])


def _wsl(w2d, oc, ow=128):
    r = w2d.rearrange("(kc p) o -> p kc o", p=128)
    return r[:, :, oc * ow:(oc + 1) * ow]


_NAMECTR = [0]


def _uname(name):
    _NAMECTR[0] += 1
    return f"{name}_{_NAMECTR[0]}"


class _Scope:
    def __init__(self, p):
        self.p = p
        self.es = ExitStack()

    def __enter__(self):
        return self

    def sb(self, name, shape, dtype):
        return self.es.enter_context(
            self.p.nc.sbuf_tensor(_uname(name), shape, dtype))

    def __exit__(self, *a):
        self.p.barrier()
        self.es.close()
        return False


def _shiftoff(fc):
    return 0 if fc >= 4 else 1


def emit_stage_a(p, g, io, u, base, s):
    nc = p.nc
    a = base
    X = lambda *k: ("A", s) + k
    p.barrier()
    p.op("dve", lambda v: v.memset(a.carry[:], float(DEV.get('carry0', 0.0))), writes=[("carry",)])

    for h in range(2):
        h0 = h * TH
        with _Scope(p) as sc:
            xt = sc.sb("xt", [128, NC, 512], F32)
            xb = sc.sb("xb", [128, NC, 512], BF16)
            for tb in range(2):
                tsl = slice(h0 + tb * 512, h0 + tb * 512 + 512)
                src = io.x[s].rearrange("c p t -> p c t")[:, :, tsl]
                p.dma("qs", xt[:], src, writes=[X("xt")])
                p.dma("qa", io.x1a[s].rearrange("c p t -> p c t")[:, :, tsl],
                      xt[:], reads=[X("xt")], writes=[X("x1a", "init", h, tb)])
                p.op("act", (lambda sx, _o=xb[:], _i=xt[:]: sx.copy(_o, _i)),
                     reads=[X("xt")], writes=[X("xb")])
                if "0" in DEV.get("bar", ""):
                    p.barrier()
                emit_ln(
                    p, g, u, X("ln0", h, tb),
                    get_src=lambda c: xb[:, c, :],
                    get_src_f=lambda c: xt[:, c, :],
                    n=512,
                    out_fp32=lambda c, tb=tb: a.XF[:, c, 1 + tb * 512:1 + tb * 512 + 512],
                    out_bf16=lambda c, tb=tb: a.XFb[:, c, 1 + tb * 512:1 + tb * 512 + 512],
                    src_reads=[X("xb"), X("xt")],
                    out_keys=lambda c: ("XF", c))
                if "0" in DEV.get("bar", ""):
                    p.barrier()

        if DEV.get("skip_mixer"):
            pass
        else:
         with _Scope(p) as sc:
            wq_sl = [sc.sb(f"wq{i}", [128, NC, 128], BF16) for i in range(2)]
            wv_sl = [sc.sb(f"wv{i}", [128, NC, 128], BF16) for i in range(2)]
            lam_sb = sc.sb("lam", [128, TH], F32)
            v_hc = [sc.sb(f"vhc{i}", [128, TH], F32) for i in range(2)]
            st_hc = [sc.sb(f"sthc{i}", [128, TH], F32) for i in range(2)]
            y_bf = sc.sb("y", [128, NC, TH], BF16)
            for hc in range(NC):
                b = hc % 2
                p.dma("qg", wq_sl[b][:], _wsl(io.wqT, hc),
                      reads=[X("wq", b, "free")], writes=[X("wq", b)])
                p.dma("qg", wv_sl[b][:], _wsl(io.wvT, hc),
                      reads=[X("wv", b, "free")], writes=[X("wv", b)])
                for nh in range(2):
                    p.mm(u.psC[:, nh * 512:nh * 512 + 512],
                         g.L0a_lam[0:1, hc * 128:hc * 128 + 128],
                         g.onesrow[:], reads=[("c", "L0a_lam"), ("c", "onesrow")],
                         writes=[("psC", nh)])
                p.op("dve", (lambda v, _o=lam_sb[:]: v.tensor_copy(_o, u.psC[:])),
                     reads=[("psC", 0), ("psC", 1)], writes=[X("lam")])
                for nh in range(2):
                    for fc in range(NC):
                        p.mm(u.psA[:, nh * 512:nh * 512 + 512],
                             wv_sl[b][:, fc, :],
                             a.XFb[:, fc, 1 + nh * 512:1 + nh * 512 + 512],
                             start=(fc == 0), stop=(fc == NC - 1),
                             reads=[X("wv", b), ("XF", fc)],
                             writes=[("psA", nh)])
                p.op("dve", (lambda v, _o=v_hc[b][:]: v.tensor_copy(_o, u.psA[:])),
                     reads=[("psA", 0), ("psA", 1)],
                     writes=[X("v", b), ("psA", 0), ("psA", 1)])
                for nh in range(2):
                    for fc in range(NC):
                        p.mm(u.psB[:, nh * 512:nh * 512 + 512],
                             wq_sl[b][:, fc, :],
                             a.XFb[:, fc, 1 + nh * 512:1 + nh * 512 + 512],
                             start=(fc == 0), stop=(fc == NC - 1),
                             reads=[X("wq", b), ("XF", fc)],
                             writes=[("psB", nh)])
                p.op("dve",
                     (lambda v, _o=st_hc[b][:], _l=lam_sb[:], _v=v_hc[b][:],
                      _c=a.carry[:, hc:hc + 1]: v.tensor_tensor_scan(
                         _o, _l, _v, _c, op0=ALU.mult, op1=ALU.add)),
                     reads=[X("lam"), X("v", b), ("carry",)],
                     writes=[X("st", b)], self_wait=True, hazard=True)
                p.op("dve",
                     (lambda v, _o=a.carry[:, hc:hc + 1], _i=st_hc[b][:, TH - 1:TH]:
                      v.tensor_copy(_o, _i)),
                     reads=[X("st", b)], writes=[("carry",)])
                p.op("dve",
                     (lambda v, _o=y_bf[:, hc, :], _s=st_hc[b][:]: v.tensor_mul(
                         _o, u.psB[:], _s)),
                     reads=[("psB", 0), ("psB", 1), X("st", b)],
                     writes=[X("y", hc), X("v", b, "free"), X("wq", b, "free"),
                             X("wv", b, "free"), ("psB", 0), ("psB", 1)])
            wg_sl = [sc.sb(f"wg{i}", [128, NC, 128], BF16) for i in range(2)]
            wo_sl = [sc.sb(f"wo{i}", [128, NC, 128], BF16) for i in range(2)]
            yln_bf = sc.sb("yln", [128, NC, 512], BF16)
            gate_bf = [sc.sb(f"gt{i}", [128, 512], BF16) for i in range(2)]
            yg_bf = sc.sb("yg", [128, NC, 512], BF16)
            atile = [sc.sb(f"atl{i}", [128, 512], F32) for i in range(2)]
            for tb in range(2):
                tsl = slice(tb * 512, tb * 512 + 512)
                emit_ln(
                    p, g, u, X("lny", h, tb),
                    get_src=lambda c, tsl=tsl: y_bf[:, c, tsl],
                    n=512,
                    out_bf16=lambda c: yln_bf[:, c, :],
                    src_reads=[X("y", c2) for c2 in range(NC)],
                    out_keys=lambda c: X("yln", c),
                    affine=(lambda c: g.L0a_lng[:, c:c + 1],
                            lambda c: g.L0a_lnb[:, c:c + 1]))
                for oc in range(NC):
                    b = oc % 2
                    p.dma("qg", wg_sl[b][:], _wsl(io.wgT, oc),
                          reads=[X("wg", b, "free")], writes=[X("wg", b)])
                    for fc in range(NC):
                        p.mm(u.psA[:, 0:512], wg_sl[b][:, fc, :],
                             a.XFb[:, fc, 1 + tb * 512:1 + tb * 512 + 512],
                             start=(fc == 0), stop=(fc == NC - 1),
                             reads=[X("wg", b), ("XF", fc)],
                             writes=[("psA", 0)])
                    p.op("act", (lambda sx, _o=gate_bf[b][:]: sx.activation(
                        _o, u.psA[:, 0:512], AF.Silu)),
                        reads=[("psA", 0)],
                        writes=[X("gate", b), ("psA", 0), X("wg", b, "free")])
                    p.op("dve", (lambda v, _o=yg_bf[:, oc, :], _a=yln_bf[:, oc, :],
                                 _g=gate_bf[b][:]: v.tensor_mul(_o, _a, _g)),
                        reads=[X("gate", b), X("yln", oc)],
                        writes=[X("yg", oc)])
                for mc in range(NC):
                    b = mc % 2
                    p.dma("qg", wo_sl[b][:], _wsl(io.woT, mc),
                          reads=[X("wo", b, "free")], writes=[X("wo", b)])
                    for kc in range(NC):
                        p.mm(u.psB[:, b * 512:b * 512 + 512],
                             wo_sl[b][:, kc, :], yg_bf[:, kc, :],
                             start=(kc == 0), stop=(kc == NC - 1),
                             reads=[X("wo", b), X("yg", kc)],
                             writes=[("psB", b)])
                    p.op("dve", (lambda v, _o=atile[b][:],
                                 _i=u.psB[:, b * 512:b * 512 + 512]: v.tensor_copy(
                        _o, _i)),
                        reads=[("psB", b)],
                        writes=[X("atile", b), ("psB", b), X("wo", b, "free")])
                    if not DEV.get("skip_amix"):
                        p.dma("qg",
                              io.x1a[s, mc][:, h0 + tb * 512:h0 + tb * 512 + 512],
                              atile[b][:],
                              reads=[X("atile", b), X("x1a", "init", h, tb)],
                              writes=[X("x1a", "acc"), X("atile", b)],
                              accum_op=ALU.add)

        for l in range(DEV.get("nlayers", 2)):
            emit_a_ffn_layer(p, g, io, u, a, s, h, l)

        if not DEV.get("skip_f"):
            p.dma("qg", io.x1a[s].rearrange("c p t -> p c t")[:, :, h0:h0 + TH],
                  a.XF[:, :, 1:TH + 1],
                  reads=[("XF", c) for c in range(NC)] + [X("x1a", "acc"),
                         X("x1a", "init", h, 0), X("x1a", "init", h, 1)],
                  writes=[X("x1a", "acc")], accum_op=ALU.add)
        if DEV.get("dump_carry") == (s, h):
            p.dma("qa", io.dbg[0][:, 0:NC], a.carry[:],
                  reads=[("carry",)], writes=[("dbgc",)])
        if DEV.get("dump_xf") == (s, h):
            p.dma("qa", io.dbg.rearrange("c p t -> p c t"), a.XF[:, :, 1:TH + 1],
                  reads=[("XF", c) for c in range(NC)], writes=[("dbg",)])


def emit_a_ffn_layer(p, g, io, u, a, s, h, l):
    nc = p.nc
    X = lambda *k: ("A", s, "f", l) + k
    XA = lambda *k: ("A", s) + k
    if h == 0:
        p.op("dve", lambda v: v.memset(a.XFb[:, :, 0:1], 0.0),
             reads=[("XF", c) for c in range(NC)],
             writes=[("XFb0",)])
    else:
        p.op("dve", lambda v: v.tensor_copy(a.XFb[:, :, 0:1],
                                            a.save_qkv[:, :, l:l + 1]),
             reads=[("save_qkv",)] + [("XF", c) for c in range(NC)],
             writes=[("XFb0",)])
    p.op("dve", lambda v: v.tensor_copy(a.save_qkv[:, :, l:l + 1],
                                        a.XFb[:, :, TH:TH + 1]),
         reads=[("XF", NC - 1)], writes=[("save_qkv",)])

    with _Scope(p) as lsc:
        c1q = lsc.sb("c1q", [128, T], BF16)
        c2q = lsc.sb("c2q", [128, T], BF16)
        c1k = lsc.sb("c1k", [128, T], BF16)
        c2k = lsc.sb("c2k", [128, T], BF16)
        wo_sb = lsc.sb("wo_sb", [128, 4, 1024], BF16)
        attn_bf = lsc.sb("attn", [128, 4, 512], BF16)
        negsh = lsc.sb("negsh", [128, 1], F32)
        rrow = lsc.sb("rrow", [1, 512], F32)
        bc_sb = lsc.sb("bc_sb", [64, 512], F32)
        for nm, dst, sc_col in (("c1q", c1q, g.L0f_qs), ("c2q", c2q, g.L0f_qs_sw),
                                ("c1k", c1k, g.L0f_ks), ("c2k", c2k, g.L0f_ks_sw)):
            p.dma("qs", dst[:], io.consts[nm + "_bf"], writes=[X("tbl", nm)])
            p.op("dve",
                 (lambda v, _d=dst[:], _s=sc_col[:, l:l + 1]: v.tensor_scalar_mul(
                     _d, _d, _s)),
                 reads=[X("tbl", nm), ("c", "L0f_qs")], writes=[X("tbl", nm)],
                 hazard=True)
        dbgrow = lsc.sb("dbgrow", [1, 512], F32)
        p.op("pe", (lambda t, _i=g.L0f_qs[:, l:l + 1]: t.transpose(
            u.psS[0:1, 0:128], _i, g.ident_f[:])),
             reads=[("c", "L0f_qs"), ("c", "ident_f")], writes=[("psS",)])
        if DEV.get("dump_sh") == (s, h, l):
            p.op("dve", (lambda v, _o=dbgrow[0:1, 0:128]: v.tensor_copy(
                _o, u.psS[0:1, 0:128])),
                 reads=[("psS",)], writes=[X("dbgrow")])
        p.op("dve", (lambda v, _o=rrow[0:1, 0:1]: v.reduce_max(
            _o, u.psS[0:1, 0:128], axis=AX.X, apply_absolute_value=True)),
             reads=[("psS",)], writes=[X("rrow"), ("psS",)], hazard=True)
        p.op("pe", (lambda t, _i=g.L0f_ks[:, l:l + 1]: t.transpose(
            u.psS[0:1, 0:128], _i, g.ident_f[:])),
             reads=[("c", "L0f_ks"), ("c", "ident_f"), ("psS",)], writes=[("psS",)])
        p.op("dve", (lambda v, _o=rrow[0:1, 1:2]: v.reduce_max(
            _o, u.psS[0:1, 0:128], axis=AX.X, apply_absolute_value=True)),
             reads=[("psS",)], writes=[X("rrow"), ("psS",)], hazard=True)
        p.op("dve", (lambda v, _a=rrow[0:1, 0:1], _b=rrow[0:1, 1:2]:
                     v.tensor_mul(_a, _a, _b)),
             reads=[X("rrow")], writes=[X("rrow")])
        p.mm(u.psS[0:128, 0:1], g.ones1_f[:], rrow[0:1, 0:1],
             reads=[X("rrow")], writes=[("psS",)])
        p.op("dve", (lambda v, _o=negsh[:]: v.tensor_scalar_mul(_o, u.psS[0:128, 0:1], -8.0)),
             reads=[("psS",)], writes=[X("negsh"), ("psS",)], hazard=True)
        if DEV.get("dump_sh") == (s, h, l):
            p.op("dve", (lambda v, _o=dbgrow[0:1, 256:260], _i=rrow[0:1, 0:4]:
                         v.tensor_copy(_o, _i)),
                 reads=[X("rrow")], writes=[X("dbgrow")])
            p.dma("qa", io.dbg[6][0:1, 0:512], dbgrow[:],
                  reads=[X("dbgrow")], writes=[("dbgsh",)])
        p.dma("qg", wo_sb[:], io.wo[l].rearrange("(kc p) o -> p kc o", p=128),
              writes=[X("wo_sb")])

        with _Scope(p) as qsc:
            qf = qsc.sb("qf", [128, 4, TH], BF16)
            kf = qsc.sb("kf", [128, 4, TH], BF16)
            vf = qsc.sb("vf", [128, 8, 8, 65], BF16)
            wqkv_sl = [qsc.sb(f"wqk{i}", [128, NC, 128], BF16) for i in range(2)]
            wqkv_v = qsc.sb("wqkvv", [128, NC, 512], BF16)
            for oc in range(8):
                b = oc % 2
                p.dma("qg", wqkv_sl[b][:], _wsl(io.wqkv[l], oc),
                      reads=[X("wsl", b, "free")], writes=[X("wsl", b)])
                for nh in range(2):
                    for fc in range(NC):
                        p.mm(u.psA[:, nh * 512:nh * 512 + 512],
                             wqkv_sl[b][:, fc, :],
                             a.XFb[:, fc, _shiftoff(fc) + nh * 512:
                                   _shiftoff(fc) + nh * 512 + 512],
                             start=(fc == 0), stop=(fc == NC - 1),
                             reads=[X("wsl", b), ("XF", fc), ("XFb0",)],
                             writes=[("psA", nh)])
                dst = qf[:, oc, :] if oc < 4 else kf[:, oc - 4, :]
                key = X("qf", oc) if oc < 4 else X("kf", oc - 4)
                p.op("act", (lambda sx, dst=dst: sx.copy(dst, u.psA[:])),
                     reads=[("psA", 0), ("psA", 1)],
                     writes=[key, ("psA", 0), ("psA", 1), X("wsl", b, "free")])
            p.dma("qg", wqkv_v[:],
                  io.wqkv[l].rearrange("(kc p) o -> p kc o", p=128)[:, :, 1024:1536],
                  writes=[X("wqv")])
            p.op("dve", (lambda v, _o=vf[:, :, :, 64:65]: v.memset(_o, 1.0)),
                 writes=[X("vf_ones")])
            for tt in range(8):
                for fc in range(NC):
                    p.mm(u.psB[:, 0:512],
                         a.XFb[:, fc, _shiftoff(fc) + tt * 128:
                               _shiftoff(fc) + tt * 128 + 128],
                         wqkv_v[:, fc, :],
                         start=(fc == 0), stop=(fc == NC - 1),
                         reads=[X("wqv"), ("XF", fc), ("XFb0",)],
                         writes=[("psB", 0)])
                p.op("dve", (lambda v, _o=vf[:, tt, :, 0:64],
                             _i=u.psB[:, 0:512].rearrange("p (hh d) -> p hh d", hh=8):
                             v.tensor_copy(_o, _i)),
                    reads=[("psB", 0)],
                    writes=[X("vf", tt), ("psB", 0)])
            if DEV.get("dump_q") == (s, h, l, "qkv"):
                for qc in range(4):
                    p.dma("qg", io.dbg[qc], qf[:, qc, :], reads=[X("qf", qc)],
                          writes=[("dbgq", qc)])
                    p.dma("qg", io.dbg[4 + qc], kf[:, qc, :], reads=[X("kf", qc)],
                          writes=[("dbgk", qc)])
            if "a" in DEV.get("bar", ""):
                p.barrier()
            for side, buf, c1, c2, scol in (
                    ("q", qf, c1q, c2q, "q"), ("k", kf, c1k, c2k, "k")):
                for qc in range(4):
                    bkey = X("rot", side, qc)
                    sq = u.sq1024
                    p.op("act", (lambda sx, buf=buf, qc=qc: sx.activation(
                        sq[:], buf[:, qc, :], AF.Square)),
                        reads=[X(side + "f", qc)], writes=[("u", "sq1024")])
                    for nh in range(2):
                        p.mm(u.psS[0:1, 0:512], g.ones64a_bf[:],
                             sq[:, nh * 512:nh * 512 + 512],
                             reads=[("u", "sq1024")], writes=[("psS",)])
                        p.mm(u.psQ[0:1, 0:512], g.ones64b_bf[:],
                             sq[:, nh * 512:nh * 512 + 512],
                             reads=[("u", "sq1024")], writes=[("psQ",)])
                        for nm, pssrc, seg in (("nA", u.psS, 0), ("nB", u.psQ, 1)):
                            pskey = ("psS",) if nm == "nA" else ("psQ",)
                            rr = u.nrows[0:1, seg * 512:seg * 512 + 512]
                            p.op("act", (lambda sx, pssrc=pssrc, rr=rr:
                                         sx.activation(rr, pssrc[0:1, 0:512],
                                                       AF.Sqrt)),
                                 reads=[pskey],
                                 writes=[("u", "nrows", nm)])
                            p.op("dve", (lambda v, rr=rr: v.tensor_scalar_max(
                                rr, rr, 1e-12)),
                                reads=[("u", "nrows", nm)], writes=[("u", "nrows", nm)],
                                hazard=True)
                            p.op("dve", (lambda v, rr=rr: v.reciprocal(rr, rr)),
                                reads=[("u", "nrows", nm)],
                                writes=[("u", "nrows", nm), pskey], hazard=True)
                        p.mm(u.psC[:, nh * 512:nh * 512 + 512], g.selA_f[:],
                             u.nrows[0:1, 0:512],
                             start=True, stop=False,
                             reads=[("u", "nrows", "nA")], writes=[("psC", nh)])
                        p.mm(u.psC[:, nh * 512:nh * 512 + 512], g.selB_f[:],
                             u.nrows[0:1, 512:1024],
                             start=False, stop=True,
                             reads=[("u", "nrows", "nB")], writes=[("psC", nh)])
                    for nh in range(2):
                        p.mm(u.psB[:, nh * 512:nh * 512 + 512], g.rot_bf[:],
                             buf[:, qc, nh * 512:nh * 512 + 512],
                             reads=[X(side + "f", qc), ("c", "rot_bf")],
                             writes=[("psB", nh)])
                    tsl = slice(h * TH, h * TH + TH)
                    p.op("dve", (lambda v, buf=buf, qc=qc, c1=c1, tsl=tsl:
                                 v.tensor_mul(u.tmp_bf[:], buf[:, qc, :],
                                              c1[:, tsl])),
                         reads=[X(side + "f", qc), X("tbl", "c1" + scol)],
                         writes=[("u", "tmp_bf")])
                    p.op("dve", (lambda v, c2=c2, tsl=tsl: v.tensor_mul(
                        u.tmp_f[:], u.psB[:], c2[:, tsl])),
                        reads=[("psB", 0), ("psB", 1),
                               X("tbl", "c2" + scol)],
                        writes=[("u", "tmp_f"), ("psB", 0), ("psB", 1)])
                    p.op("dve", lambda v: v.tensor_add(u.tmp_f[:], u.tmp_f[:],
                                                       u.tmp_bf[:]),
                         reads=[("u", "tmp_bf"), ("u", "tmp_f")],
                         writes=[("u", "tmp_f")])
                    p.op("dve", (lambda v, buf=buf, qc=qc: v.tensor_mul(
                        buf[:, qc, :], u.tmp_f[:], u.psC[:])),
                        reads=[("u", "tmp_f"), ("psC", 0), ("psC", 1)],
                        writes=[X(side + "f", qc), ("psC", 0),
                                ("psC", 1)])
            if DEV.get("dump_rn") == (s, h, l):
                p.dma("qa", io.dbg[0][0:1, 0:1024], u.nrows[:],
                      reads=[("u", "nrows", "nA"), ("u", "nrows", "nB")],
                      writes=[("dbgr",)])
                p.dma("qg", io.dbg[1], c1q[:, h * TH:h * TH + TH],
                      reads=[X("tbl", "c1q")], writes=[("dbgt1",)])
                p.dma("qg", io.dbg[2], c2q[:, h * TH:h * TH + TH],
                      reads=[X("tbl", "c2q")], writes=[("dbgt2",)])
            if DEV.get("dump_q") == (s, h, l, "rot"):
                for qc in range(4):
                    p.dma("qg", io.dbg[qc], qf[:, qc, :], reads=[X("qf", qc)],
                          writes=[("dbgq", qc)])
                    p.dma("qg", io.dbg[4 + qc], kf[:, qc, :], reads=[X("kf", qc)],
                          writes=[("dbgk", qc)])
            if DEV.get("dump_v") == (s, h, l):
                for tt in range(8):
                    p.dma("qg", io.dbg[tt % 8][:, tt // 8 * 1040: tt // 8 * 1040 + 520]
                          if False else io.dbg[tt][:, 0:520],
                          vf[:, tt, :, :], reads=[X("vf", tt), X("vf_ones")],
                          writes=[("dbgv", tt)])
            if "b" in DEV.get("bar", ""):
                p.barrier()
            PT = [qsc.sb(f"pt{i}", [128, 512], BF16) for i in range(4)]
            for wb in range(2):
                W = 2 * h + wb
                kts = [kt for kt in range((W - 1) * 4, (W - 1) * 4 + 8) if kt >= 0]
                for hh in range(8):
                    pb = (hh % 2) * 64
                    ch = hh // 2
                    qrhs = qf[pb:pb + 64, ch, wb * 512:wb * 512 + 512]
                    nkt = len(kts)
                    for idx, kt in enumerate(kts):
                        bk = idx % 4
                        loc = kt - 8 * h
                        if loc >= 0:
                            klhs = kf[pb:pb + 64, ch, loc * 128:loc * 128 + 128]
                            vtile = vf[:, loc, hh, :]
                            kread = [X("kf", ch), X("vf", loc)]
                        else:
                            sloc = loc + 4
                            klhs = a.save_k[pb:pb + 64, ch, sloc * 128:
                                            sloc * 128 + 128, l]
                            vtile = a.save_v[:, sloc, hh * 65:hh * 65 + 65, l]
                            kread = [("save_k",), ("save_v",)]
                        o = kt * 128 - W * 512
                        msl = g.maskband_bf[:, 384 - o:384 - o + 512]
                        pst = u.psA if bk < 2 else u.psC
                        pk = ("psA", bk) if bk < 2 else ("psC", bk - 2)
                        hf = (bk % 2) * 512
                        p.mm(pst[:, hf:hf + 512], klhs, qrhs,
                             start=True, stop=False,
                             reads=kread + [X("qf", ch)],
                             writes=[pk])
                        p.mm(pst[:, hf:hf + 512], g.ident_bf[:],
                             msl, start=False, stop=True,
                             reads=[("c", "maskband_bf"), ("c", "ident_bf")],
                             writes=[pk])
                        p.op("act", (lambda sx, _o=PT[bk][:],
                            _i=pst[:, hf:hf + 512],
                            _b=negsh[:, 0:1]: sx.activation(
                            _o, _i, AF.Exp, bias=_b)),
                            reads=[pk, X("negsh")],
                            writes=[X("pt", bk), pk])
                        p.mm(u.psB[0:65, 0:512], vtile, PT[bk][:],
                             start=(idx == 0), stop=(idx == nkt - 1),
                             reads=kread + [X("pt", bk)],
                             writes=[("psB", 0), X("pt", bk)])
                    p.op("dve", (lambda v, _o=rrow[:]: v.tensor_scalar(
                        _o, u.psB[64:65, 0:512], 1e-30, None, op0=ALU.add)),
                        reads=[("psB", 0)], writes=[X("rrow")], hazard=True)
                    p.op("dve", (lambda v, _o=rrow[:]: v.reciprocal(_o, _o)),
                         reads=[X("rrow")], writes=[X("rrow")], hazard=True)
                    p.mm(u.psB[0:64, 512:1024], g.ones1_f[0:1, 0:64], rrow[:],
                         reads=[X("rrow")], writes=[("psB", 1)])
                    p.op("dve", (lambda v, _o=bc_sb[:]: v.tensor_copy(
                        _o, u.psB[0:64, 512:1024])),
                         reads=[("psB", 1)], writes=[X("bcs"), ("psB", 1)])
                    p.op("dve", (lambda v, _o=attn_bf[pb:pb + 64, ch, :],
                        _b=bc_sb[:]: v.tensor_mul(_o, u.psB[0:64, 0:512], _b)),
                        reads=[("psB", 0), X("bcs")],
                        writes=[X("at", hh), ("psB", 0)])
                for mc in range(NC):
                    bk = mc % 2
                    for kc in range(4):
                        p.mm(u.psB[:, bk * 512:bk * 512 + 512],
                             wo_sb[:, kc, mc * 128:mc * 128 + 128],
                             attn_bf[:, kc, :],
                             start=(kc == 0), stop=(kc == 3),
                             reads=[X("wo_sb"), X("at", 2 * kc), X("at", 2 * kc + 1)],
                             writes=[("psB", bk)])
                    xsl = a.XF[:, mc, 1 + wb * 512:1 + wb * 512 + 512]
                    p.op("dve", (lambda v, xsl=xsl, bk=bk: v.tensor_add(
                        xsl, u.psB[:, bk * 512:bk * 512 + 512], xsl)),
                        reads=[("psB", bk), ("XF", mc)],
                        writes=[("XF", mc), ("psB", bk)])
                    p.op("act", (lambda sx, mc=mc, wb=wb: sx.copy(
                        a.XFb[:, mc, 1 + wb * 512:1 + wb * 512 + 512],
                        a.XF[:, mc, 1 + wb * 512:1 + wb * 512 + 512])),
                        reads=[("XF", mc)], writes=[("XF", mc)])
            if "c" in DEV.get("bar", ""):
                p.barrier()
            if DEV.get("dump_attn") == (s, h, l):
                for kc in range(4):
                    p.dma("qg", io.dbg[kc][:, 0:512], attn_bf[:, kc, :],
                          reads=[X("at", 2 * kc), X("at", 2 * kc + 1)],
                          writes=[("dbga", kc)])
                p.dma("qa", io.dbg[7][:, 0:1], negsh[:],
                      reads=[X("negsh")], writes=[("dbgn",)],
                      allow_slow_non_contiguous=True)
            p.op("dve", (lambda v, _o=a.save_k[:, :, :, l],
                         _i=kf[:, :, 512:1024]: v.tensor_copy(_o, _i)),
                 reads=[X("kf", 0), X("kf", 1), X("kf", 2), X("kf", 3)],
                 writes=[("save_k",)])
            p.op("dve", (lambda v, _o=a.save_v[:, :, :, l],
                         _i=vf[:, 4:8, :, :]: v.tensor_copy(_o, _i)),
                 reads=[X("vf", tt) for tt in range(4, 8)] + [X("vf_ones")],
                 writes=[("save_v",)])

        if "d" in DEV.get("bar", ""):
            p.barrier()
        with _Scope(p) as fsc:
            hln = fsc.sb("hln", [128, NC, TH], BF16)
            if h == 0:
                p.op("dve", lambda v: v.memset(a.XF[:, :, 0:1], 0.0),
                     reads=[("XF", c) for c in range(NC)], writes=[("XFp0",)])
            else:
                p.op("dve", lambda v: v.tensor_copy(a.XF[:, :, 0:1],
                                                    a.save_ln_f[:, :, l:l + 1]),
                     reads=[("save_ln",)] + [("XF", c) for c in range(NC)],
                     writes=[("XFp0",)])
            p.op("act", lambda sx: sx.copy(a.XFb[:, :, 0:1], a.XF[:, :, 0:1]),
                 reads=[("XFp0",)], writes=[("XFb0",)])
            p.op("dve", lambda v: v.tensor_copy(a.save_ln_f[:, :, l:l + 1],
                                                a.XF[:, :, TH:TH + 1]),
                 reads=[("XF", NC - 1)], writes=[("save_ln",)])
            for tb in range(2):
                emit_ln(
                    p, g, u, X("lnf", h, tb),
                    get_src=lambda c, tb=tb: a.XFb[:, c, _shiftoff(c) + tb * 512:
                                                   _shiftoff(c) + tb * 512 + 512],
                    get_src_f=lambda c, tb=tb: a.XF[:, c, _shiftoff(c) + tb * 512:
                                                    _shiftoff(c) + tb * 512 + 512],
                    n=512,
                    out_bf16=lambda c, tb=tb: hln[:, c, tb * 512:tb * 512 + 512],
                    src_reads=[("XF", c2) for c2 in range(NC)] + [("XFb0",),
                               ("XFp0",)],
                    out_keys=lambda c: X("hln", c),
                    affine=(lambda c: g.L0f_lng[:, l, c:c + 1],
                            lambda c: g.L0f_lnb[:, l, c:c + 1]))
            if "e" in DEV.get("bar", ""):
                p.barrier()
            w1a_sl = [fsc.sb(f"w1a{i}", [128, NC, 128], BF16) for i in range(2)]
            w1g_sl = [fsc.sb(f"w1g{i}", [128, NC, 128], BF16) for i in range(2)]
            w2_sl = [fsc.sb(f"w2{i}", [128, NJ, 128], BF16) for i in range(2)]
            t_all = fsc.sb("tall", [128, NJ, 512], BF16)
            ag_bf = [fsc.sb(f"ag{i}", [128, 1024], BF16) for i in range(2)]
            for tb in range(2):
                tsl = slice(tb * 512, tb * 512 + 512)
                for j in range(NJ):
                    b = j % 2
                    pw = u.psA if b == 0 else u.psC
                    pk0 = ("psA", 0) if b == 0 else ("psC", 0)
                    pk1 = ("psA", 1) if b == 0 else ("psC", 1)
                    p.dma("qg", w1a_sl[b][:], _wsl(io.w1a[l], j),
                          reads=[X("w1a", b, "free")], writes=[X("w1a", b)])
                    p.dma("qg", w1g_sl[b][:], _wsl(io.w1g[l], j),
                          reads=[X("w1g", b, "free")], writes=[X("w1g", b)])
                    for fc in range(NC):
                        p.mm(pw[:, 0:512], w1a_sl[b][:, fc, :],
                             hln[:, fc, tsl], start=(fc == 0), stop=(fc == NC - 1),
                             reads=[X("w1a", b), X("hln", fc)],
                             writes=[pk0])
                    for fc in range(NC):
                        p.mm(pw[:, 512:1024], w1g_sl[b][:, fc, :],
                             hln[:, fc, tsl], start=(fc == 0), stop=(fc == NC - 1),
                             reads=[X("w1g", b), X("hln", fc)],
                             writes=[pk1])
                    p.op("act", (lambda sx, _o=ag_bf[b][:, 0:512],
                        _i=pw[:, 0:512],
                        _b=g.L0f_b1a[:, l, j:j + 1]: sx.activation(
                        _o, _i, AF.Identity, bias=_b)),
                        reads=[pk0, ("c", "L0f_b1a")],
                        writes=[X("ag", b, 0), pk0, X("w1a", b, "free")])
                    p.op("act", (lambda sx, _o=ag_bf[b][:, 512:1024],
                        _i=pw[:, 512:1024],
                        _b=g.L0f_b1g[:, l, j:j + 1]: sx.activation(
                        _o, _i, AF.Gelu, bias=_b)),
                        reads=[pk1, ("c", "L0f_b1g")],
                        writes=[X("ag", b, 1), pk1, X("w1g", b, "free")])
                    p.op("dve", (lambda v, _o=t_all[:, j, :],
                        _a=ag_bf[b][:, 0:512], _g=ag_bf[b][:, 512:1024]:
                        v.tensor_mul(_o, _a, _g)),
                        reads=[X("ag", b, 0), X("ag", b, 1)],
                        writes=[X("tall", j), X("ag", b, 0), X("ag", b, 1)])
                    if "j" in DEV.get("bar", ""):
                        p.barrier()
                if "f" in DEV.get("bar", ""):
                    p.barrier()
                for mc in range(NC):
                    b = mc % 2
                    p.dma("qg", w2_sl[b][:],
                          io.w2[l].rearrange("(jc p) o -> p jc o", p=128)
                          [:, :, mc * 128:mc * 128 + 128],
                          reads=[X("w2", b, "free")], writes=[X("w2", b)])
                    for j in range(NJ):
                        p.mm(u.psB[:, b * 512:b * 512 + 512],
                             w2_sl[b][:, j, :], t_all[:, j, :],
                             start=(j == 0), stop=(j == NJ - 1),
                             reads=[X("w2", b), X("tall", j)],
                             writes=[("psB", b)])
                    xsl = a.XF[:, mc, 1 + tb * 512:1 + tb * 512 + 512]
                    p.op("dve", (lambda v, xsl=xsl, b=b, mc=mc, l=l:
                                 v.scalar_tensor_tensor(
                                     xsl, u.psB[:, b * 512:b * 512 + 512],
                                     g.L0f_b2[:, l, mc:mc + 1], xsl,
                                     op0=ALU.add, op1=ALU.add)),
                         reads=[("psB", b), ("XF", mc), ("c", "L0f_b2")],
                         writes=[("XF", mc), ("psB", b), X("w2", b, "free")], hazard=True)
                    p.op("act", (lambda sx, mc=mc, tb=tb: sx.copy(
                        a.XFb[:, mc, 1 + tb * 512:1 + tb * 512 + 512],
                        a.XF[:, mc, 1 + tb * 512:1 + tb * 512 + 512])),
                        reads=[("XF", mc)], writes=[("XF", mc)])


def emit_stage_b(p, g, io, u, bb):
    nc = p.nc
    with _Scope(p) as sc:
        WW = {"R": sc.sb("WR", [128, 4, 512], F32),
              "I": sc.sb("WI", [128, 4, 512], F32)}
        t1sb = sc.sb("t1sb", [32, 512], F32)
        tgsb = sc.sb("tgsb", [16, 512], F32)
        Ap = sc.sb("Ap", [128, 512], F32)
        Bp = sc.sb("Bp", [128, 512], F32)
        K = lambda *k: ("B", "w") + k
        for part in ("R", "I"):
            G1 = getattr(g, "L1f_G1" + part)
            G2 = getattr(g, "L1f_G2" + part)
            W = WW[part]
            p.op("dve", (lambda v, W=W: v.memset(W[:], 0.0)), writes=[K(part, "W")])
            for c in range(4):
                p.mm(u.psS[0:32, 0:512], G1[:, c, :], g.Yb_f[:],
                     reads=[("c", "L1f_G1" + part), ("c", "Yb_f")],
                     writes=[("psS",)])
                p.op("dve", lambda v: v.tensor_copy(t1sb[:], u.psS[0:32, 0:512]),
                     reads=[("psS",)], writes=[K("t1"), ("psS",)])
                p.mm(u.psQ[0:16, 0:512], G2[:, c, :], g.Ye_f[:],
                     reads=[("c", "L1f_G2" + part), ("c", "Ye_f")],
                     writes=[("psQ",)])
                p.op("dve", lambda v: v.tensor_copy(tgsb[:], u.psQ[0:16, 0:512]),
                     reads=[("psQ",)], writes=[K("tg"), ("psQ",)])
                for pc in range(4):
                    p.mm(u.psA[:, 0:512], g.Zd_f[:, pc, :], t1sb[:],
                         reads=[K("t1"), ("c", "Zd_f")], writes=[("psA", 0)])
                    p.op("dve", lambda v: v.tensor_copy(Ap[:], u.psA[:, 0:512]),
                         reads=[("psA", 0)], writes=[K("Ap"), ("psA", 0)])
                    p.mm(u.psB[:, 0:512], g.Zg_f[:, pc, :], tgsb[:],
                         reads=[K("tg"), ("c", "Zg_f")], writes=[("psB", 0)])
                    p.op("dve", lambda v: v.tensor_copy(Bp[:], u.psB[:, 0:512]),
                         reads=[("psB", 0)], writes=[K("Bp"), ("psB", 0)])
                    p.op("dve", lambda v: v.tensor_mul(Bp[:], Ap[:], Bp[:]),
                         reads=[K("Ap"), K("Bp")], writes=[K("Bp")])
                    p.op("dve", (lambda v, W=W, pc=pc: v.tensor_add(
                        W[:, pc, :], W[:, pc, :], Bp[:])),
                        reads=[K("Bp"), K(part, "W")], writes=[K(part, "W")])
        for ic in range(4):
            p.op("act", (lambda sx, ic=ic: sx.copy(bb.Wc[:, ic, 0:512],
                                                   WW["R"][:, ic, :])),
                 reads=[K("R", "W")], writes=[("Wc",)])
            p.op("act", (lambda sx, ic=ic: sx.copy(bb.Wc[:, ic, 512:1024],
                                                   WW["I"][:, ic, :])),
                 reads=[K("I", "W")], writes=[("Wc",)])
            p.op("act", (lambda sx, ic=ic: sx.activation(
                bb.Wc[:, 4 + ic, 0:512], WW["I"][:, ic, :], AF.Copy, scale=-1.0)),
                reads=[K("I", "W")], writes=[("Wc",)])
            p.op("act", (lambda sx, ic=ic: sx.copy(bb.Wc[:, 4 + ic, 512:1024],
                                                   WW["R"][:, ic, :])),
                 reads=[K("R", "W")], writes=[("Wc",)])

    for s in range(SEQS):
        X = lambda *k: ("B", s) + k
        with _Scope(p) as sc:
            xt = sc.sb("xt", [128, NC, 512], F32)
            xb = sc.sb("xb", [128, NC, 512], BF16)
            for tb in range(4):
                tsl = slice(tb * 512, tb * 512 + 512)
                p.dma("qs", xt[:], io.x1a[s].rearrange("c p t -> p c t")[:, :, tsl],
                      reads=[("A", s, "x1a", "acc")], writes=[X("xt")])
                p.dma("qa", io.x1b[s].rearrange("c p t -> p c t")[:, :, tsl],
                      xt[:], reads=[X("xt")], writes=[X("x1b", "init", tb)])
                p.op("act", (lambda sx, _o=xb[:], _i=xt[:]: sx.copy(_o, _i)),
                     reads=[X("xt")], writes=[X("xb")])
                emit_ln(
                    p, g, u, X("ln1", tb),
                    get_src=lambda c: xb[:, c, :],
                    get_src_f=lambda c: xt[:, c, :],
                    n=512,
                    out_bf16=lambda c, tb=tb: bb.XN1b[:, c, tsl],
                    src_reads=[X("xb"), X("xt")],
                    out_keys=lambda c: X("xn", c))
        with _Scope(p) as sc:
            f1 = [sc.sb(f"f1{i}", [128, 512], F32) for i in range(2)]
            for tb in range(4):
                tsl = slice(tb * 512, tb * 512 + 512)
                for mc in range(NC):
                    b = mc % 2
                    for ic in range(NC):
                        p.mm(u.psA[:, b * 512:b * 512 + 512],
                             bb.Wc[:, ic, mc * 128:mc * 128 + 128],
                             bb.XN1b[:, ic, tsl],
                             start=(ic == 0), stop=(ic == NC - 1),
                             reads=[("Wc",), X("xn", ic)], writes=[("psA", b)])
                    p.op("act", (lambda sx, b=b, mc=mc: sx.activation(
                        f1[b][:], u.psA[:, b * 512:b * 512 + 512], AF.Identity,
                        bias=g.L1f_bri[:, mc:mc + 1])),
                        reads=[("psA", b), ("c", "L1f_bri")],
                        writes=[X("f1", b), ("psA", b)])
                    p.dma("qg", io.x1b[s, mc][:, tsl], f1[b][:],
                          reads=[X("f1", b), X("x1b", "init", tb)],
                          writes=[X("x1b", "acc"), X("f1", b)],
                          accum_op=ALU.add)
        with _Scope(p) as sc:
            wsl = [sc.sb(f"wsl{i}", [128, NC, 128], BF16) for i in range(2)]
            stage = sc.sb("stage", [128, T], BF16)
            for oc in range(16):
                b = oc % 2
                p.dma("qg", wsl[b][:], _wsl(io.l1wqkv, oc),
                      reads=[X("wsl", b, "free")], writes=[X("wsl", b)])
                for tq in range(4):
                    bk = tq % 2
                    for fc in range(NC):
                        p.mm(u.psA[:, bk * 512:bk * 512 + 512],
                             wsl[b][:, fc, :],
                             bb.XN1b[:, fc, tq * 512:tq * 512 + 512],
                             start=(fc == 0), stop=(fc == NC - 1),
                             reads=[X("wsl", b), X("xn", fc)],
                             writes=[("psA", bk)])
                    p.op("act", (lambda sx, tq=tq, bk=bk: sx.copy(
                        stage[:, tq * 512:tq * 512 + 512],
                        u.psA[:, bk * 512:bk * 512 + 512])),
                        reads=[("psA", bk)],
                        writes=[X("stage", tq), ("psA", bk)])
                dst = io.qf[s, oc] if oc < 8 else io.kf[s, oc - 8]
                p.dma("qa", dst, stage[:],
                      reads=[X("stage", 0), X("stage", 1), X("stage", 2),
                             X("stage", 3)],
                      writes=[X("qk_out", oc), X("stage", 0), X("stage", 1),
                              X("stage", 2), X("stage", 3),
                              X("wsl", b, "free")])
            wqv = sc.sb("wqv", [128, NC, 1024], BF16)
            v_sb = sc.sb("v_sb", [128, 16, 65], BF16)
            p.dma("qg", wqv[:],
                  io.l1wqkv.rearrange("(kc p) o -> p kc o", p=128)[:, :, 2048:3072],
                  writes=[X("wqv")])
            p.op("dve", lambda v: v.memset(v_sb[:, :, 64:65], 1.0),
                 writes=[X("vones")])
            for tt in range(16):
                for vh in range(2):
                    for fc in range(NC):
                        p.mm(u.psB[:, vh * 512:vh * 512 + 512],
                             bb.XN1b[:, fc, tt * 128:tt * 128 + 128],
                             wqv[:, fc, vh * 512:vh * 512 + 512],
                             start=(fc == 0), stop=(fc == NC - 1),
                             reads=[X("wqv"), X("xn", fc)],
                             writes=[("psB", vh)])
                    p.op("dve", (lambda v, tt=tt, vh=vh: v.tensor_copy(
                        v_sb[:, vh * 8:vh * 8 + 8, 0:64],
                        u.psB[:, vh * 512:vh * 512 + 512].rearrange(
                            "p (hh d) -> p hh d", hh=8))),
                        reads=[("psB", vh)],
                        writes=[X("vsb", vh), ("psB", vh)])
                p.dma("qa", io.vf[s, tt], v_sb[:],
                      reads=[X("vsb", 0), X("vsb", 1), X("vones")],
                      writes=[X("v_out", tt), X("vsb", 0), X("vsb", 1)])


def emit_stage_c(p, g, io, u, s):
    nc = p.nc
    X = lambda *k: ("C", s) + k
    with _Scope(p) as sc:
        q_sb = sc.sb("q_sb", [128, NC, T], BF16)
        k_sb = sc.sb("k_sb", [128, NC, T], BF16)
        v_sb = sc.sb("v_sb", [128, 16, 1040], BF16)
        wproj = sc.sb("wproj", [128, NC, 1024], BF16)
        negsh = sc.sb("negsh", [128, 1], F32)
        rrow = sc.sb("rrow", [1, 512], F32)
        bc_sb = sc.sb("bc_sb", [64, 512], F32)
        phi_sb = [sc.sb("phi0", [128, 16, 512], BF16)]
        attn_bf = sc.sb("attn", [128, NC, 512], BF16)
        xt = [sc.sb("xt0", [128, NC, 512], F32)]
        out_sb = xt
        PT = [sc.sb(f"pt{i}", [128, 512], BF16) for i in range(4)]
        mrow = sc.sb("mrow", [1, 4], F32)

        bq = [("B", s, "qk_out", oc) for oc in range(16)]
        bv = [("B", s, "v_out", tt) for tt in range(16)]
        p.dma("qs", q_sb[:], io.qf[s].rearrange("c p t -> p c t"),
              reads=bq, writes=[X("q")])
        p.dma("qs", k_sb[:], io.kf[s].rearrange("c p t -> p c t"),
              reads=bq, writes=[X("k")])
        p.dma("qs", v_sb[:], io.vf[s].rearrange("tt p x -> p tt x"),
              reads=bv, writes=[X("v")])
        p.dma("qg", wproj[:], io.l1wproj.rearrange("(kc p) o -> p kc o", p=128),
              writes=[X("wp")])
        p.op("dve", lambda v: v.memset(mrow[:], 0.0), writes=[X("m0"), X("m1")])
        for side, buf, bkey, mi in ((0, q_sb, X("q"), 0), (1, k_sb, X("k"), 1)):
            for c in range(NC):
                for hf in range(2):
                    p.op("act", (lambda sx, buf=buf, c=c, hf=hf: sx.activation(
                        u.sq1024[:], buf[:, c, hf * 1024:hf * 1024 + 1024],
                        AF.Square)),
                        reads=[bkey], writes=[("u", "sq1024")])
                    for sub in range(2):
                        p.mm(u.psS[0:1, 0:512], g.ones_bf[:],
                             u.sq1024[:, sub * 512:sub * 512 + 512],
                             reads=[("u", "sq1024")], writes=[("psS",)])
                        p.op("dve", lambda v: v.reduce_max(
                            rrow[0:1, 0:1], u.psS[0:1, 0:512], axis=AX.X),
                            reads=[("psS",)], writes=[X("rr"), ("psS",)], hazard=True)
                        p.op("dve", (lambda v, mi=mi: v.tensor_max(
                            mrow[0:1, mi:mi + 1], mrow[0:1, mi:mi + 1],
                            rrow[0:1, 0:1])),
                            reads=[X("rr"), X("m%d" % mi)],
                            writes=[X("m%d" % mi)])
        p.op("act", lambda sx: sx.activation(mrow[0:1, 2:3], mrow[0:1, 0:1],
                                             AF.Sqrt),
             reads=[X("m0")], writes=[X("m2")])
        p.op("act", lambda sx: sx.activation(mrow[0:1, 3:4], mrow[0:1, 1:2],
                                             AF.Sqrt),
             reads=[X("m1")], writes=[X("m3")])
        p.op("dve", lambda v: v.tensor_mul(mrow[0:1, 2:3], mrow[0:1, 2:3],
                                           mrow[0:1, 3:4]),
             reads=[X("m2"), X("m3")], writes=[X("m2")])
        p.mm(u.psS[0:128, 0:1], g.ones1_f[:], mrow[0:1, 2:3],
             reads=[X("m2")], writes=[("psS",)])
        p.op("dve", lambda v: v.tensor_scalar_mul(negsh[:], u.psS[0:128, 0:1],
                                                  -0.125),
             reads=[("psS",)], writes=[X("negsh"), ("psS",)], hazard=True)
        for qt in range(4):
            bq_ = 0
            qsl = slice(qt * 512, qt * 512 + 512)
            p.dma("qs", phi_sb[bq_][:],
                  io.phi[qt].rearrange("kt p x -> p kt x"),
                  writes=[X("phi", bq_)])
            p.dma("qs", xt[bq_][:],
                  io.x1b[s].rearrange("c p t -> p c t")[:, :, qsl],
                  reads=[("B", s, "x1b", "acc")] +
                        [("B", s, "x1b", "init", tb) for tb in range(4)],
                  writes=[X("xt", bq_)] +
                         [X("osb", bq_, mc) for mc in range(NC)])
            for hh in range(16):
                pb = (hh % 2) * 64
                ch = hh // 2
                qrhs = q_sb[pb:pb + 64, ch, qsl]
                for kt in range(16):
                    bk = kt % 4
                    pst = u.psA if bk < 2 else u.psC
                    pk = ("psA", bk) if bk < 2 else ("psC", bk - 2)
                    hf = (bk % 2) * 512
                    klhs = k_sb[pb:pb + 64, ch, kt * 128:kt * 128 + 128]
                    p.mm(pst[:, hf:hf + 512], klhs, qrhs,
                         start=True, stop=False,
                         reads=[X("q"), X("k")], writes=[pk])
                    p.mm(pst[:, hf:hf + 512], g.ident_bf[:],
                         phi_sb[bq_][:, kt, :], start=False, stop=True,
                         reads=[X("phi", bq_), ("c", "ident_bf")],
                         writes=[pk])
                    p.op("act", (lambda sx, _o=PT[bk][:], _i=pst[:, hf:hf + 512],
                        _b=negsh[:, 0:1]: sx.activation(
                        _o, _i, AF.Exp, bias=_b, scale=0.125)),
                        reads=[pk, X("negsh")],
                        writes=[X("pt", bk), pk])
                    p.mm(u.psB[0:65, 0:512],
                         v_sb[:, kt, hh * 65:hh * 65 + 65], PT[bk][:],
                         start=(kt == 0), stop=(kt == 15),
                         reads=[X("v"), X("pt", bk)],
                         writes=[("psB", 0), X("pt", bk)])
                p.op("dve", lambda v: v.tensor_scalar(
                    rrow[:], u.psB[64:65, 0:512], 1e-30, None, op0=ALU.add),
                    reads=[("psB", 0)], writes=[X("rr")], hazard=True)
                p.op("dve", lambda v: v.reciprocal(rrow[:], rrow[:]),
                     reads=[X("rr")], writes=[X("rr")], hazard=True)
                p.mm(u.psB[0:64, 512:1024], g.ones1_f[0:1, 0:64], rrow[:],
                     reads=[X("rr")], writes=[("psB", 1)])
                p.op("dve", lambda v: v.tensor_copy(bc_sb[:],
                                                    u.psB[0:64, 512:1024]),
                     reads=[("psB", 1)], writes=[X("bcs"), ("psB", 1)])
                p.op("dve", (lambda v, pb=pb, ch=ch: v.tensor_mul(
                    attn_bf[pb:pb + 64, ch, :], u.psB[0:64, 0:512], bc_sb[:])),
                    reads=[("psB", 0), X("bcs")],
                    writes=[X("at", hh), ("psB", 0)])
            for mc in range(NC):
                bk = mc % 2
                for kc in range(NC):
                    p.mm(u.psC[:, bk * 512:bk * 512 + 512],
                         wproj[:, kc, mc * 128:mc * 128 + 128],
                         attn_bf[:, kc, :],
                         start=(kc == 0), stop=(kc == NC - 1),
                         reads=[X("wp"), X("at", 2 * kc), X("at", 2 * kc + 1)],
                         writes=[("psC", bk)])
                p.op("dve", (lambda v, bk=bk, mc=mc, bq_=bq_: v.scalar_tensor_tensor(
                    out_sb[bq_][:, mc, :], u.psC[:, bk * 512:bk * 512 + 512],
                    g.L1_bproj[:, mc:mc + 1], xt[bq_][:, mc, :],
                    op0=ALU.add, op1=ALU.add)),
                    reads=[("psC", bk), X("xt", bq_), ("c", "L1_bproj")],
                    writes=[X("osb", bq_, mc), ("psC", bk)], hazard=True)
            p.dma("qa", io.out[s].rearrange("c p t -> p c t")[:, :, qsl],
                  out_sb[bq_][:],
                  reads=[X("osb", bq_, mc) for mc in range(NC)],
                  writes=[X("out", qt)] + [X("osb", bq_, mc) for mc in range(NC)]
                  + [X("xt", bq_)])


def build_nc(stages, shared):
    nc = bass.Bass("TRN2", target_bir_lowering=False, debug=False)
    p = Prog(nc)
    io = H()
    declared = {}

    def din(name, shape, dtype):
        t = nc.dram_tensor(name, list(shape), dtype, kind="ExternalInput").ap()
        declared[name] = "in"
        return t

    def dout(name, shape, dtype):
        t = nc.dram_tensor(name, list(shape), dtype, kind="ExternalOutput").ap()
        declared[name] = "out"
        return t

    def dscratch(name, shape, dtype, produced, consumed):
        if produced and consumed:
            t = nc.dram_tensor(name, list(shape), dtype, kind="Internal").ap()
            declared[name] = "internal"
            return t
        if produced:
            return dout(name, shape, dtype)
        return din(name, shape, dtype)

    A = "A" in stages
    B = "B" in stages
    C = "C" in stages
    io.consts = {}
    cshapes = {}
    for k, v in shared.items():
        cshapes[k] = v
    const_names = ["ones_bf", "ones64a_bf", "ones64b_bf", "ones1_f", "selA_f",
                   "selB_f", "ident_bf", "ident_f", "rot_bf", "maskband_bf",
                   "L0a_lng", "L0a_lnb", "L0a_lam", "L0f_b1a", "L0f_b1g",
                   "L0f_b2", "L0f_lng", "L0f_lnb", "L0f_qs", "L0f_ks",
                   "L0f_qs_sw", "L0f_ks_sw", "L1_bproj", "L1f_bri",
                   "L1f_G1R", "L1f_G1I", "L1f_G2R", "L1f_G2I",
                   "Yb_f", "Ye_f", "Zd_f", "Zg_f",
                   "c1q_bf", "c2q_bf", "c1k_bf", "c2k_bf"]
    for nm in const_names:
        arr = shared[nm]
        dt = BF16 if arr.dtype == ml_dtypes.bfloat16 else F32
        io.consts[nm] = din("K_" + nm, arr.shape, dt)
    hd = io.consts

    if A and any(k in DEV for k in ("dump_xf", "dump_carry", "dump_q", "dump_v", "dump_attn", "dump_rn", "dump_sh")):
        io.dbg = dout("dbg", [NC, 128, TH], F32)
    if A:
        io.x = din("x", [SEQS, NC, 128, T], F32)
        io.wqT = din("L0a_WqT", [D, D], F32)
        io.wvT = din("L0a_WvT", [D, D], F32)
        io.wgT = din("L0a_WgT", [D, D], F32)
        io.woT = din("L0a_WoT", [D, D], F32)
        io.wqkv = din("L0f_wqkv", [2, D, 1536], F32)
        io.w1a = din("L0f_w1a", [2, D, HIDP], F32)
        io.w1g = din("L0f_w1g", [2, D, HIDP], F32)
        io.w2 = din("L0f_w2", [2, HIDP, D], F32)
        io.wo = din("L0f_wo", [2, 512, D], F32)
    if A or B:
        io.x1a = dscratch("x1a", [SEQS, NC, 128, T], F32, produced=A, consumed=B)
    if B:
        io.l1wqkv = din("L1_wqkv", [D, 3 * D], F32)
    if B or C:
        io.x1b = dscratch("x1b", [SEQS, NC, 128, T], F32, produced=B, consumed=C)
        io.qf = dscratch("qf", [SEQS, NC, 128, T], BF16, produced=B, consumed=C)
        io.kf = dscratch("kf", [SEQS, NC, 128, T], BF16, produced=B, consumed=C)
        io.vf = dscratch("vf", [SEQS, 16, 128, 16 * 65], BF16,
                         produced=B, consumed=C)
    if C:
        io.l1wproj = din("L1_wproj", [D, D], F32)
        io.phi = din("K_phi8_bf", [4, 16, 128, 512], BF16)
        io.out = dout("out", [SEQS, NC, 128, T], F32)
    u = H()
    u.psA = p.ps("psA", [128, 1024])
    u.psB = p.ps("psB", [128, 1024])
    u.psC = p.ps("psC", [128, 1024])
    u.psS = p.ps("psS", [128, 512])
    u.psQ = p.ps("psQ", [128, 512])
    u.sq = p.sb("u_sq", [128, 512], BF16)
    u.sq1024 = p.sb("u_sq1024", [128, 1024], BF16)
    u.tmp = p.sb("u_tmp", [128, 512], F32)
    u.tmp_f = p.sb("u_tmp_f", [128, 1024], F32)
    u.tmp_bf = p.sb("u_tmp_bf", [128, 1024], BF16)
    u.rows = p.sb("u_rows", [1, 2048], F32)
    u.nrows = p.sb("u_nrows", [1, 1024], F32)
    u.eps = p.sb("u_eps", [1, 1], F32)
    p.op("dve", lambda v: v.memset(u.eps[:], 1e-5), writes=[("u", "eps")])
    g = declare_consts(p, nc, io.consts)
    if A:
        with _Scope(p) as basesc:
            base = H()
            base.XF = basesc.sb("XF", [128, NC, TH + 1], F32)
            base.save_ln_f = basesc.sb("save_ln_f", [128, NC, 2], F32)
            base.XFb = basesc.sb("XFb", [128, NC, TH + 1], BF16)
            base.carry = basesc.sb("carry", [128, NC], F32)
            base.save_qkv = basesc.sb("save_qkv", [128, NC, 2], BF16)
            base.save_ln = basesc.sb("save_ln", [128, NC, 2], BF16)
            base.save_k = basesc.sb("save_k", [128, 4, 512, 2], BF16)
            base.save_v = basesc.sb("save_v", [128, 4, 520, 2], BF16)
            for s in range(DEV.get("nseq", SEQS)):
                emit_stage_a(p, g, io, u, base, s)
    if B:
        with _Scope(p) as bsc:
            bb = H()
            bb.Wc = bsc.sb("Wc", [128, NC, 1024], BF16)
            bb.XN1b = bsc.sb("XN1b", [128, NC, T], BF16)
            emit_stage_b(p, g, io, u, bb)
    if C:
        for s in range(SEQS):
            emit_stage_c(p, g, io, u, s)
    global LAST_PROG
    LAST_PROG = p
    p.finalize_and_emit()
    in_names = [k for k, v in declared.items() if v == "in"]
    out_names = [k for k, v in declared.items() if v == "out"]
    return nc, in_names, out_names


def _run(nc, in_names, maps):
    in_maps = [{k: m[k] for k in in_names} for m in maps]
    res = run_bass_kernel_spmd(nc, in_maps, core_ids=list(range(NCORES)))
    return res.results


def kernel(**inputs):
    shared, per_core = host_prep(inputs)
    smap = {}
    for k, v in shared.items():
        smap["K_" + k] = v
    smap["K_phi8_bf"] = shared["phi8_bf"]
    smap.update({
        "L0a_WqT": shared["L0a_WqT"], "L0a_WvT": shared["L0a_WvT"],
        "L0a_WgT": shared["L0a_WgT"], "L0a_WoT": shared["L0a_WoT"],
        "L0f_wqkv": shared["L0f_wqkv"], "L0f_w1a": shared["L0f_w1a"],
        "L0f_w1g": shared["L0f_w1g"], "L0f_w2": shared["L0f_w2"],
        "L0f_wo": shared["L0f_wo"], "L1_wqkv": shared["L1_wqkv"],
        "L1_wproj": shared["L1_wproj"],
    })
    if UNFUSED:
        stage_sets = [("A",), ("B",), ("C",)]
    else:
        stage_sets = [("A", "B", "C")]
    core_maps = [dict(smap, **pc) for pc in per_core]
    outs = None
    for stages in stage_sets:
        nc, in_names, out_names = build_nc(stages, shared)
        results = _run(nc, in_names, core_maps)
        for core in range(NCORES):
            for onm in out_names:
                core_maps[core][onm] = np.asarray(results[core][onm])
        outs = results
    out = np.empty((16, T, D), np.float32)
    for core in range(NCORES):
        o = np.asarray(outs[core]["out"], np.float32).reshape(SEQS, D, T)
        for s in range(SEQS):
            out[core * SEQS + s] = o[s].T
    return out


# version 04
# speedup vs baseline: 1.1776x; 1.1776x over previous
import math
from contextlib import ExitStack

import numpy as np
import ml_dtypes

import concourse.bass as bass
import concourse.mybir as mybir
from concourse.bass_utils import run_bass_kernel_spmd

F32 = mybir.dt.float32
BF16 = mybir.dt.bfloat16
AF = mybir.ActivationFunctionType
ALU = mybir.AluOpType
AX = mybir.AxisListType

NCORES = 8
SEQS = 2
T = 2048
D = 1024
NC = 8
TH = 1024
HID = 2730
HIDP = 2816
NJ = HIDP // 128
NEG = -1.0e30

UNFUSED = False
DEV = {}
LAST_PROG = None

COMPUTE_STREAMS = ("pe", "act", "dve", "pool", "sp")
QUEUE_ISSUER = {"qs": "sp", "qa": "act", "qg": "pool"}
QUEUE_LANES = {"qs": 8, "qa": 6, "qg": 12}
LANE_STREAMS = [(q, r) for q, R in QUEUE_LANES.items() for r in range(R)]
STREAM_ENGINE = {"pe": "tensor", "act": "scalar", "dve": "vector",
                 "pool": "gpsimd", "sp": "sync"}


def _is_lane(s):
    return isinstance(s, tuple)


class _Op:
    __slots__ = ("stream", "issuer", "emit", "waits", "count", "inc", "tag")

    def __init__(self, stream, issuer, emit, waits, count, tag=None):
        self.stream = stream
        self.issuer = issuer
        self.emit = emit
        self.waits = waits
        self.count = count
        self.inc = False
        self.tag = tag


class Prog:
    def __init__(self, nc):
        self.nc = nc
        self.es = ExitStack()
        self.counts = {s: 0 for s in COMPUTE_STREAMS}
        self.counts.update({ls: 0 for ls in LANE_STREAMS})
        self.qidx = {q: 0 for q in QUEUE_ISSUER}
        self.ops = {s: [] for s in COMPUTE_STREAMS}
        self.vc = {s: {} for s in COMPUTE_STREAMS}
        self.res = {}
        self.sems = {}
        self.hazards = {s: set() for s in COMPUTE_STREAMS}
        self.self_hw = {s: 0 for s in COMPUTE_STREAMS}

    def sb(self, name, shape, dtype):
        return self.es.enter_context(
            self.nc.sbuf_tensor(_uname(name), shape, dtype))

    def ps(self, name, shape, dtype=F32):
        return self.es.enter_context(
            self.nc.psum_tensor(_uname(name), shape, dtype))

    @staticmethod
    def _merge(dst, src):
        for k, v in src.items():
            if dst.get(k, 0) < v:
                dst[k] = v

    def op(self, stream, emit, reads=(), writes=(), queue=None, tag=None,
           self_wait=False, hazard=False):
        lane_prewait = None
        if queue is not None:
            issuer = QUEUE_ISSUER[queue]
            r = self.qidx[queue] % QUEUE_LANES[queue]
            self.qidx[queue] += 1
            cstream = (queue, r)
            step = 16
            if self.counts[cstream] > 0:
                lane_prewait = (cstream, self.counts[cstream])
        else:
            issuer = stream
            cstream = stream
            step = 1
        deps = []
        bar = self.res.pop(("__bar__", issuer), None)
        if bar is not None and bar[0] is not None:
            deps.append(bar[0])
        for r in reads:
            st = self.res.get(r)
            if st is not None and st[0] is not None:
                deps.append(st[0])
        for w in writes:
            st = self.res.get(w)
            if st is not None:
                if st[0] is not None:
                    deps.append(st[0])
                deps.extend(st[1])
        need = {}
        for st in deps:
            self._merge(need, st[2])
        self_tgt = 0
        if queue is None:
            for st in deps:
                if st[0] == issuer:
                    if issuer == "pe" and not self_wait:
                        continue
                    self_tgt = max(self_tgt, st[1])
        ivc = self.vc[issuer]
        waits = {}
        if lane_prewait is not None:
            s, n = lane_prewait
            if ivc.get(s, 0) < n:
                waits[s] = n
                self._merge(ivc, {s: n})
        for s, n in need.items():
            if s == issuer and queue is None:
                if self_tgt > 0 and self.self_hw[issuer] < self_tgt:
                    waits[s] = self_tgt
                    self.self_hw[issuer] = self_tgt
                continue
            if ivc.get(s, 0) < n:
                waits[s] = n
        self._merge(ivc, need)
        self.counts[cstream] += step
        c = self.counts[cstream]
        if queue is None:
            ivc[cstream] = c
        rec = _Op(cstream, issuer, emit, waits, c,
                  tag or (tuple(writes)[:1] or ("?",))[0])
        self.ops[issuer].append(rec)
        if hazard and queue is None:
            self.hazards[cstream].add(c)
        if queue is None:
            svc = dict(ivc)
        else:
            svc = {s: n for s, n in ivc.items() if s != issuer}
            svc[cstream] = c
        stamp = (cstream, c, svc)
        for w in writes:
            self.res[w] = (stamp, [])
        for r in reads:
            st = self.res.get(r)
            if st is None:
                self.res[r] = (None, [stamp])
            else:
                st[1].append(stamp)
        return stamp

    def dma(self, queue, out_ap, in_ap, reads=(), writes=(), **kw):
        return self.op(
            None,
            (lambda eng, o=out_ap, i=in_ap, kw=kw: eng.dma_start(out=o, in_=i, **kw)),
            reads=reads, writes=writes, queue=queue)

    def mm(self, out, lhsT, rhs, start=True, stop=True, reads=(), writes=()):
        return self.op(
            "pe",
            (lambda t, o=out, l=lhsT, r=rhs, st=start, sp=stop:
             t.matmul(o, l, r, start=st, stop=sp)),
            reads=reads, writes=writes)

    def barrier(self):
        cur = {s: c for s, c in self.counts.items() if c > 0 and s != "sp"}
        need = dict(cur)
        ivc = self.vc["sp"]
        waits = {s: n for s, n in need.items() if ivc.get(s, 0) < n}
        self._merge(ivc, need)
        self.counts["sp"] += 1
        c = self.counts["sp"]
        ivc["sp"] = c
        rec = _Op("sp", "sp", (lambda eng: eng.nop()), waits, c)
        self.ops["sp"].append(rec)
        bstamp = ("sp", c, dict(ivc))
        for s in COMPUTE_STREAMS:
            if s != "sp":
                self.res[("__bar__", s)] = (bstamp, [])
        return bstamp

    def finalize_and_emit(self):
        nc = self.nc
        finals = {s: c for s, c in self.counts.items() if c > 0 and s != "sp"}
        targets = {s: set() for s in self.counts}
        for lst in self.ops.values():
            for o in lst:
                for s, n in o.waits.items():
                    targets[s].add(n)
        for s, c in finals.items():
            targets[s].add(c)
        for ls in LANE_STREAMS:
            targets[ls] = set(range(16, self.counts[ls] + 1, 16))
        miles = {s: sorted(ts) for s, ts in targets.items()}
        import bisect

        def remap(s, n):
            return bisect.bisect_right(miles[s], n) * (16 if _is_lane(s) else 1)

        mileset = {s: set(ts) for s, ts in miles.items()}
        for lst in self.ops.values():
            for o in lst:
                if o.count in mileset[o.stream]:
                    o.inc = True
        for s, ts in miles.items():
            if ts:
                nm = "sem_" + ("_".join(map(str, s)) if _is_lane(s) else str(s))
                self.sems[s] = self.es.enter_context(nc.semaphore(nm))
        block = self.es.enter_context(nc.Block())

        def emit_stream(eng_stream):
            ops = self.ops[eng_stream]
            if not ops and eng_stream != "sp":
                return
            dec = getattr(block, STREAM_ENGINE[eng_stream])

            def body(eng):
                for o in ops:
                    for s, n in sorted(o.waits.items(), key=lambda x: str(x[0])):
                        eng.wait_ge(self.sems[s], remap(s, n))
                    instr = o.emit(eng)
                    if o.inc:
                        instr.then_inc(self.sems[o.stream],
                                       16 if _is_lane(o.stream) else 1)
                if eng_stream == "sp":
                    for s, c in sorted(finals.items(), key=lambda x: str(x[0])):
                        if c and self.vc["sp"].get(s, 0) < c:
                            eng.wait_ge(self.sems[s], remap(s, c))
            dec(body)

        for s in ("sp", "pe", "act", "dve", "pool"):
            emit_stream(s)


def _bf(x):
    return np.asarray(x, np.float32).astype(ml_dtypes.bfloat16)


def host_constants():
    C = {}
    C["ones_bf"] = _bf(np.ones((128, 1)))
    on64a = np.zeros((128, 1), np.float32); on64a[:64] = 1.0
    on64b = np.zeros((128, 1), np.float32); on64b[64:] = 1.0
    C["ones64a_bf"] = _bf(on64a)
    C["ones64b_bf"] = _bf(on64b)
    C["ones1_f"] = np.ones((1, 128), np.float32)
    sa = np.zeros((1, 128), np.float32); sa[0, :64] = 1.0
    sb_ = np.zeros((1, 128), np.float32); sb_[0, 64:] = 1.0
    C["selA_f"] = sa
    C["selB_f"] = sb_
    C["ident_bf"] = _bf(np.eye(128))
    C["ident_f"] = np.eye(128, dtype=np.float32)
    R = np.zeros((128, 128), np.float32)
    for base in (0, 64):
        for i in range(32):
            R[base + 32 + i, base + i] = -1.0
            R[base + i, base + 32 + i] = 1.0
    C["rot_bf"] = _bf(R)
    DH = 64
    WINDOW = 512
    n = T
    half_idx = np.arange(0, DH, 2, dtype=np.float64)
    inv_freq = 1.0 / 10000.0 ** (half_idx / DH)
    t = np.arange(n, dtype=np.float64)
    freqs = t[:, None] * inv_freq[None, :]
    freqs = np.concatenate([freqs, freqs], axis=-1)
    base = (half_idx + 0.4 * DH) / (1.4 * DH)
    power = (t - n // 2) / float(WINDOW // 2)
    scale = base[None, :] ** power[:, None]
    scale = np.concatenate([scale, scale], axis=-1)
    cos, sin = np.cos(freqs), np.sin(freqs)
    rep = lambda a: np.tile(np.ascontiguousarray(a.T), (2, 1))
    C["c1q_bf"] = _bf(rep(cos * scale * 8.0))
    C["c2q_bf"] = _bf(rep(sin * scale * 8.0))
    C["c1k_bf"] = _bf(rep(cos / scale))
    C["c2k_bf"] = _bf(rep(sin / scale))
    W = 1408
    jj = np.arange(128)[:, None]
    y = np.arange(W)[None, :] - 384
    band = (y - jj >= 0) & (y - jj <= 512)
    C["maskband_bf"] = _bf(np.where(band, 0.0, NEG))
    PHI = (1.0 + math.sqrt(5.0)) / 2.0
    pos = (np.arange(T, dtype=np.float32) * np.float32(PHI)) % np.float32(1.0)
    dist = np.abs(pos[:, None] - pos[None, :])
    phi = np.log(np.exp(-dist) + 1e-8).astype(np.float32) * 8.0
    phiT = np.ascontiguousarray(phi.T)
    C["phi8_bf"] = _bf(
        phiT.reshape(16, 128, 4, 512).transpose(2, 0, 1, 3))
    Yb = np.zeros((32, 512), np.float32)
    Ye = np.zeros((16, 512), np.float32)
    for idx in range(512):
        Yb[idx // 16, idx] = 1.0
        Ye[idx % 16, idx] = 1.0
    Zd = np.zeros((32, 4, 128), np.float32)
    Zg = np.zeros((16, 4, 128), np.float32)
    for pc in range(4):
        for col in range(128):
            idx = pc * 128 + col
            Zd[idx // 16, pc, col] = 1.0
            Zg[idx % 16, pc, col] = 1.0
    C["Yb_f"] = Yb
    C["Ye_f"] = Ye
    C["Zd_f"] = Zd
    C["Zg_f"] = Zg
    return C


def _pp(vec, nch):
    return np.ascontiguousarray(np.asarray(vec, np.float32).reshape(nch, 128).T)


def host_prep(inputs):
    ix = {k: np.asarray(v) for k, v in inputs.items()}
    shared = dict(host_constants())
    for nm in ("Wq", "Wv", "Wg", "Wo"):
        shared["L0a_" + nm + "T"] = np.ascontiguousarray(ix["L0_attn_" + nm].T)
    shared["L0a_lng"] = _pp(ix["L0_attn_ln_gamma"], 8)
    shared["L0a_lnb"] = _pp(ix["L0_attn_ln_beta"], 8)
    shared["L0a_lam"] = np.ascontiguousarray(
        np.repeat(ix["L0_attn_decay"].astype(np.float32), 64)[None, :])
    shared["L0f_wqkv"] = np.ascontiguousarray(ix["L0_ffn_wqkv"])
    w1 = ix["L0_ffn_w1"]
    w1a = np.zeros((2, 1024, HIDP), np.float32)
    w1g = np.zeros((2, 1024, HIDP), np.float32)
    w1a[:, :, :HID] = w1[:, :, :HID]
    w1g[:, :, :HID] = w1[:, :, HID:]
    shared["L0f_w1a"] = w1a
    shared["L0f_w1g"] = w1g
    w2 = ix["L0_ffn_w2"]
    w2p = np.zeros((2, HIDP, 1024), np.float32)
    w2p[:, :HID, :] = w2
    shared["L0f_w2"] = w2p
    shared["L0f_wo"] = np.ascontiguousarray(ix["L0_ffn_wo"])
    b1 = ix["L0_ffn_b1"]
    b1a = np.zeros((2, HIDP), np.float32); b1a[:, :HID] = b1[:, :HID]
    b1g = np.zeros((2, HIDP), np.float32); b1g[:, :HID] = b1[:, HID:]
    shared["L0f_b1a"] = np.stack([_pp(b1a[l], NJ) for l in range(2)], 1)
    shared["L0f_b1g"] = np.stack([_pp(b1g[l], NJ) for l in range(2)], 1)
    shared["L0f_b2"] = np.stack([_pp(ix["L0_ffn_b2"][l], 8) for l in range(2)], 1)
    shared["L0f_lng"] = np.stack([_pp(ix["L0_ffn_ln_g"][l], 8) for l in range(2)], 1)
    shared["L0f_lnb"] = np.stack([_pp(ix["L0_ffn_ln_b"][l], 8) for l in range(2)], 1)
    qs = ix["L0_ffn_q_scale"].astype(np.float32)
    ks = ix["L0_ffn_k_scale"].astype(np.float32)
    tile2 = lambda v: np.ascontiguousarray(np.tile(v, 2)[:, None])
    shared["L0f_qs"] = np.concatenate([tile2(qs[l]) for l in range(2)], 1)
    shared["L0f_ks"] = np.concatenate([tile2(ks[l]) for l in range(2)], 1)
    swap = lambda v: np.concatenate([v[32:], v[:32]])
    shared["L0f_qs_sw"] = np.concatenate([tile2(swap(qs[l])) for l in range(2)], 1)
    shared["L0f_ks_sw"] = np.concatenate([tile2(swap(ks[l])) for l in range(2)], 1)
    shared["L1_wqkv"] = np.ascontiguousarray(ix["L1_attn_w_qkv"])
    shared["L1_wproj"] = np.ascontiguousarray(ix["L1_attn_w_proj"])
    shared["L1_bproj"] = _pp(ix["L1_attn_b_proj"], 8)
    shared["L1f_G1R"] = np.ascontiguousarray(
        ix["L1_ffn_G1_R"].reshape(32, 4, 32).astype(np.float32))
    shared["L1f_G1I"] = np.ascontiguousarray(
        ix["L1_ffn_G1_I"].reshape(32, 4, 32).astype(np.float32))
    shared["L1f_G2R"] = np.ascontiguousarray(
        ix["L1_ffn_G2_R"].reshape(4, 16, 16).transpose(1, 0, 2))
    shared["L1f_G2I"] = np.ascontiguousarray(
        ix["L1_ffn_G2_I"].reshape(4, 16, 16).transpose(1, 0, 2))
    shared["L1f_bri"] = _pp(
        np.concatenate([ix["L1_ffn_bias_real"], ix["L1_ffn_bias_imag"]]), 8)
    x = np.asarray(ix["L0_attn_x"], np.float32)
    per_core = []
    for core in range(NCORES):
        xs = x[core * SEQS:(core + 1) * SEQS]
        xf = xs.transpose(0, 2, 1).reshape(SEQS, NC, 128, T)
        per_core.append({"x": np.ascontiguousarray(xf)})
    return shared, per_core


class H:
    pass


def declare_consts(p, nc, hd):
    g = H()
    ld = []

    def load(name, shape, dtype, src_name=None, cast=False):
        tsb = p.sb("c_" + name, shape, dtype)
        src = hd[src_name or name]
        q = "qg" if cast else "qs"
        p.dma(q, tsb[:], src, writes=[("c", name)])
        setattr(g, name, tsb)
        return tsb

    load("ones_bf", [128, 1], BF16)
    load("ones64a_bf", [128, 1], BF16)
    load("ones64b_bf", [128, 1], BF16)
    load("ones1_f", [1, 128], F32)
    load("selA_f", [1, 128], F32)
    load("selB_f", [1, 128], F32)
    load("ident_bf", [128, 128], BF16)
    load("ident_f", [128, 128], F32)
    load("rot_bf", [128, 128], BF16)
    load("maskband_bf", [128, 1408], BF16)
    load("L0a_lng", [128, 8], F32)
    load("L0a_lnb", [128, 8], F32)
    load("L0a_lam", [1, 1024], F32)
    load("L0f_b1a", [128, 2, NJ], F32)
    load("L0f_b1g", [128, 2, NJ], F32)
    load("L0f_b2", [128, 2, 8], F32)
    load("L0f_lng", [128, 2, 8], F32)
    load("L0f_lnb", [128, 2, 8], F32)
    load("L0f_qs", [128, 2], F32)
    load("L0f_ks", [128, 2], F32)
    load("L0f_qs_sw", [128, 2], F32)
    load("L0f_ks_sw", [128, 2], F32)
    load("L1_bproj", [128, 8], F32)
    load("L1f_bri", [128, 8], F32)
    load("L1f_G1R", [32, 4, 32], F32)
    load("L1f_G1I", [32, 4, 32], F32)
    load("L1f_G2R", [16, 4, 16], F32)
    load("L1f_G2I", [16, 4, 16], F32)
    load("Yb_f", [32, 512], F32)
    load("Ye_f", [16, 512], F32)
    load("Zd_f", [32, 4, 128], F32)
    load("Zg_f", [16, 4, 128], F32)
    onesrow = p.sb("c_onesrow", [1, 512], F32)
    p.op("dve", lambda v: v.memset(onesrow[:], 1.0), writes=[("c", "onesrow")])
    g.onesrow = onesrow
    return g


def emit_ln(p, g, u, key, get_src, n, out_bf16, src_reads, out_keys,
            out_fp32=None, affine=None, eps=1e-5, get_src_f=None):
    r0 = lambda sl: u.rows[0:1, 0 * 512 + 0:0 * 512 + n]
    r1 = lambda sl: u.rows[0:1, 1 * 512 + 0:1 * 512 + n]
    r2 = lambda sl: u.rows[0:1, 2 * 512 + 0:2 * 512 + n]
    K = lambda *a: (key,) + a
    for c in range(NC):
        src = get_src(c)
        p.op("act", (lambda s, src=src: s.activation(u.sq[:, :n], src, AF.Square)),
             reads=list(src_reads) + [("psQ",)], writes=[("u", "sq")])
        p.mm(u.psS[0:1, :n], g.ones_bf[:], src, start=(c == 0), stop=(c == NC - 1),
             reads=list(src_reads), writes=[("psS",)])
        p.mm(u.psQ[0:1, :n], g.ones_bf[:], u.sq[:, :n], start=(c == 0),
             stop=(c == NC - 1), reads=[("u", "sq")], writes=[("psQ",)])
    p.op("dve", lambda v: v.tensor_scalar_mul(r0(0), u.psS[0:1, :n], 1.0 / D),
         reads=[("psS",)], writes=[("u", "rows", 0)], hazard=True)
    p.op("dve", lambda v: v.tensor_scalar_mul(r1(0), u.psQ[0:1, :n], 1.0 / D),
         reads=[("psQ",)], writes=[("u", "rows", 1)], hazard=True)
    p.op("dve", lambda v: v.tensor_mul(r2(0), r0(0), r0(0)),
         reads=[("u", "rows", 0)], writes=[("u", "rows", 2)])
    p.op("dve", lambda v: v.tensor_sub(r1(0), r1(0), r2(0)),
         reads=[("u", "rows", 1), ("u", "rows", 2)], writes=[("u", "rows", 1)])
    p.op("act", lambda s: s.activation(r2(0), r1(0), AF.Sqrt,
                                       bias=u.eps[0:1, 0:1]),
         reads=[("u", "rows", 1), ("u", "eps")], writes=[("u", "rows", 2)])
    p.op("dve", lambda v: v.reciprocal(r1(0), r2(0)),
         reads=[("u", "rows", 2)], writes=[("u", "rows", 1)], hazard=True)
    p.op("dve", lambda v: v.tensor_mul(r2(0), r0(0), r1(0)),
         reads=[("u", "rows", 0), ("u", "rows", 1)], writes=[("u", "rows", 2)])
    p.mm(u.psC[:, 0:n], g.ones1_f[:], r1(0), reads=[("u", "rows", 1)], writes=[("psC", 0)])
    p.mm(u.psC[:, 512:512 + n], g.ones1_f[:], r2(0), reads=[("u", "rows", 2)], writes=[("psC", 1)])
    for c in range(NC):
        src = (get_src_f or get_src)(c)
        p.op("dve",
             (lambda v, src=src: v.tensor_mul(u.tmp[:, :n], src, u.psC[:, 0:n])),
             reads=list(src_reads) + [("psC", 0)], writes=[("u", "tmp")])
        if affine is not None:
            p.op("dve", lambda v: v.tensor_sub(u.tmp[:, :n], u.tmp[:, :n],
                                               u.psC[:, 512:512 + n]),
                 reads=[("u", "tmp"), ("psC", 1)], writes=[("u", "tmp")])
            gam, bet = affine
            dstb = out_bf16(c)
            p.op("act",
                 (lambda s, c=c, dstb=dstb: s.activation(
                     dstb, u.tmp[:, :n], AF.Identity,
                     scale=gam(c), bias=bet(c))),
                 reads=[("u", "tmp")], writes=[out_keys(c)])
        elif out_fp32 is not None:
            dst = out_fp32(c)
            p.op("dve", (lambda v, dst=dst: v.tensor_sub(dst, u.tmp[:, :n],
                                                         u.psC[:, 512:512 + n])),
                 reads=[("u", "tmp"), ("psC", 1)], writes=[out_keys(c)])
            if out_bf16 is not None:
                dstb = out_bf16(c)
                p.op("act", (lambda s, dst=dst, dstb=dstb: s.copy(dstb, dst)),
                     reads=[out_keys(c)], writes=[out_keys(c)])
        else:
            dstb = out_bf16(c)
            p.op("dve", (lambda v, dstb=dstb: v.tensor_sub(dstb, u.tmp[:, :n],
                                                           u.psC[:, 512:512 + n])),
                 reads=[("u", "tmp"), ("psC", 1)], writes=[out_keys(c)])


def _wsl(w2d, oc, ow=128):
    r = w2d.rearrange("(kc p) o -> p kc o", p=128)
    return r[:, :, oc * ow:(oc + 1) * ow]


_NAMECTR = [0]


def _uname(name):
    _NAMECTR[0] += 1
    return f"{name}_{_NAMECTR[0]}"


class _Scope:
    def __init__(self, p):
        self.p = p
        self.es = ExitStack()

    def __enter__(self):
        return self

    def sb(self, name, shape, dtype):
        return self.es.enter_context(
            self.p.nc.sbuf_tensor(_uname(name), shape, dtype))

    def __exit__(self, *a):
        self.p.barrier()
        self.es.close()
        return False


def _shiftoff(fc):
    return 0 if fc >= 4 else 1


def emit_stage_a(p, g, io, u, base, s):
    nc = p.nc
    a = base
    X = lambda *k: ("A", s) + k
    p.barrier()
    p.op("dve", lambda v: v.memset(a.carry[:], float(DEV.get('carry0', 0.0))), writes=[("carry",)])

    for h in range(2):
        h0 = h * TH
        with _Scope(p) as sc:
            xt = sc.sb("xt", [128, NC, 512], F32)
            xb = sc.sb("xb", [128, NC, 512], BF16)
            for tb in range(2):
                tsl = slice(h0 + tb * 512, h0 + tb * 512 + 512)
                src = io.x[s].rearrange("c p t -> p c t")[:, :, tsl]
                p.dma("qs", xt[:], src, writes=[X("xt")])
                p.dma("qa", io.x1a[s].rearrange("c p t -> p c t")[:, :, tsl],
                      xt[:], reads=[X("xt")], writes=[X("x1a", "init", h, tb)])
                p.op("act", (lambda sx, _o=xb[:], _i=xt[:]: sx.copy(_o, _i)),
                     reads=[X("xt")], writes=[X("xb")])
                if "0" in DEV.get("bar", ""):
                    p.barrier()
                emit_ln(
                    p, g, u, X("ln0", h, tb),
                    get_src=lambda c: xb[:, c, :],
                    get_src_f=lambda c: xt[:, c, :],
                    n=512,
                    out_fp32=lambda c, tb=tb: a.XF[:, c, 1 + tb * 512:1 + tb * 512 + 512],
                    out_bf16=lambda c, tb=tb: a.XFb[:, c, 1 + tb * 512:1 + tb * 512 + 512],
                    src_reads=[X("xb"), X("xt")],
                    out_keys=lambda c: ("XF", c))
                if "0" in DEV.get("bar", ""):
                    p.barrier()

        if DEV.get("skip_mixer"):
            pass
        else:
         with _Scope(p) as sc:
            wq_sl = [sc.sb(f"wq{i}", [128, NC, 128], BF16) for i in range(2)]
            wv_sl = [sc.sb(f"wv{i}", [128, NC, 128], BF16) for i in range(2)]
            lam_sb = sc.sb("lam", [128, TH], F32)
            v_hc = [sc.sb(f"vhc{i}", [128, TH], F32) for i in range(2)]
            st_hc = [sc.sb(f"sthc{i}", [128, TH], F32) for i in range(2)]
            y_bf = sc.sb("y", [128, NC, TH], BF16)
            for hc in range(NC):
                b = hc % 2
                p.dma("qg", wq_sl[b][:], _wsl(io.wqT, hc),
                      reads=[X("wq", b, "free")], writes=[X("wq", b)])
                p.dma("qg", wv_sl[b][:], _wsl(io.wvT, hc),
                      reads=[X("wv", b, "free")], writes=[X("wv", b)])
                for nh in range(2):
                    p.mm(u.psC[:, nh * 512:nh * 512 + 512],
                         g.L0a_lam[0:1, hc * 128:hc * 128 + 128],
                         g.onesrow[:], reads=[("c", "L0a_lam"), ("c", "onesrow")],
                         writes=[("psC", nh)])
                p.op("dve", (lambda v, _o=lam_sb[:]: v.tensor_copy(_o, u.psC[:])),
                     reads=[("psC", 0), ("psC", 1)], writes=[X("lam")])
                for nh in range(2):
                    for fc in range(NC):
                        p.mm(u.psA[:, nh * 512:nh * 512 + 512],
                             wv_sl[b][:, fc, :],
                             a.XFb[:, fc, 1 + nh * 512:1 + nh * 512 + 512],
                             start=(fc == 0), stop=(fc == NC - 1),
                             reads=[X("wv", b), ("XF", fc)],
                             writes=[("psA", nh)])
                p.op("dve", (lambda v, _o=v_hc[b][:]: v.tensor_copy(_o, u.psA[:])),
                     reads=[("psA", 0), ("psA", 1)],
                     writes=[X("v", b), ("psA", 0), ("psA", 1)])
                for nh in range(2):
                    for fc in range(NC):
                        p.mm(u.psB[:, nh * 512:nh * 512 + 512],
                             wq_sl[b][:, fc, :],
                             a.XFb[:, fc, 1 + nh * 512:1 + nh * 512 + 512],
                             start=(fc == 0), stop=(fc == NC - 1),
                             reads=[X("wq", b), ("XF", fc)],
                             writes=[("psB", nh)])
                p.op("dve",
                     (lambda v, _o=st_hc[b][:], _l=lam_sb[:], _v=v_hc[b][:],
                      _c=a.carry[:, hc:hc + 1]: v.tensor_tensor_scan(
                         _o, _l, _v, _c, op0=ALU.mult, op1=ALU.add)),
                     reads=[X("lam"), X("v", b), ("carry",)],
                     writes=[X("st", b)], self_wait=True, hazard=True)
                p.op("dve",
                     (lambda v, _o=a.carry[:, hc:hc + 1], _i=st_hc[b][:, TH - 1:TH]:
                      v.tensor_copy(_o, _i)),
                     reads=[X("st", b)], writes=[("carry",)])
                p.op("dve",
                     (lambda v, _o=y_bf[:, hc, :], _s=st_hc[b][:]: v.tensor_mul(
                         _o, u.psB[:], _s)),
                     reads=[("psB", 0), ("psB", 1), X("st", b)],
                     writes=[X("y", hc), X("v", b, "free"), X("wq", b, "free"),
                             X("wv", b, "free"), ("psB", 0), ("psB", 1)])
            wg_sl = [sc.sb(f"wg{i}", [128, NC, 128], BF16) for i in range(2)]
            wo_sl = [sc.sb(f"wo{i}", [128, NC, 128], BF16) for i in range(2)]
            yln_bf = sc.sb("yln", [128, NC, 512], BF16)
            gate_bf = [sc.sb(f"gt{i}", [128, 512], BF16) for i in range(2)]
            yg_bf = sc.sb("yg", [128, NC, 512], BF16)
            atile = [sc.sb(f"atl{i}", [128, 512], F32) for i in range(2)]
            for tb in range(2):
                tsl = slice(tb * 512, tb * 512 + 512)
                emit_ln(
                    p, g, u, X("lny", h, tb),
                    get_src=lambda c, tsl=tsl: y_bf[:, c, tsl],
                    n=512,
                    out_bf16=lambda c: yln_bf[:, c, :],
                    src_reads=[X("y", c2) for c2 in range(NC)],
                    out_keys=lambda c: X("yln", c),
                    affine=(lambda c: g.L0a_lng[:, c:c + 1],
                            lambda c: g.L0a_lnb[:, c:c + 1]))
                for oc in range(NC):
                    b = oc % 2
                    p.dma("qg", wg_sl[b][:], _wsl(io.wgT, oc),
                          reads=[X("wg", b, "free")], writes=[X("wg", b)])
                    for fc in range(NC):
                        p.mm(u.psA[:, 0:512], wg_sl[b][:, fc, :],
                             a.XFb[:, fc, 1 + tb * 512:1 + tb * 512 + 512],
                             start=(fc == 0), stop=(fc == NC - 1),
                             reads=[X("wg", b), ("XF", fc)],
                             writes=[("psA", 0)])
                    p.op("act", (lambda sx, _o=gate_bf[b][:]: sx.activation(
                        _o, u.psA[:, 0:512], AF.Silu)),
                        reads=[("psA", 0)],
                        writes=[X("gate", b), ("psA", 0), X("wg", b, "free")])
                    p.op("dve", (lambda v, _o=yg_bf[:, oc, :], _a=yln_bf[:, oc, :],
                                 _g=gate_bf[b][:]: v.tensor_mul(_o, _a, _g)),
                        reads=[X("gate", b), X("yln", oc)],
                        writes=[X("yg", oc)])
                for mc in range(NC):
                    b = mc % 2
                    p.dma("qg", wo_sl[b][:], _wsl(io.woT, mc),
                          reads=[X("wo", b, "free")], writes=[X("wo", b)])
                    for kc in range(NC):
                        p.mm(u.psB[:, b * 512:b * 512 + 512],
                             wo_sl[b][:, kc, :], yg_bf[:, kc, :],
                             start=(kc == 0), stop=(kc == NC - 1),
                             reads=[X("wo", b), X("yg", kc)],
                             writes=[("psB", b)])
                    p.op("dve", (lambda v, _o=atile[b][:],
                                 _i=u.psB[:, b * 512:b * 512 + 512]: v.tensor_copy(
                        _o, _i)),
                        reads=[("psB", b)],
                        writes=[X("atile", b), ("psB", b), X("wo", b, "free")])
                    if not DEV.get("skip_amix"):
                        p.dma("qg",
                              io.x1a[s, mc][:, h0 + tb * 512:h0 + tb * 512 + 512],
                              atile[b][:],
                              reads=[X("atile", b), X("x1a", "init", h, tb)],
                              writes=[X("x1a", "acc"), X("atile", b)],
                              accum_op=ALU.add)

        for l in range(DEV.get("nlayers", 2)):
            emit_a_ffn_layer(p, g, io, u, a, s, h, l)

        if not DEV.get("skip_f"):
            p.dma("qg", io.x1a[s].rearrange("c p t -> p c t")[:, :, h0:h0 + TH],
                  a.XF[:, :, 1:TH + 1],
                  reads=[("XF", c) for c in range(NC)] + [X("x1a", "acc"),
                         X("x1a", "init", h, 0), X("x1a", "init", h, 1)],
                  writes=[X("x1a", "acc")], accum_op=ALU.add)
        if DEV.get("dump_carry") == (s, h):
            p.dma("qa", io.dbg[0][:, 0:NC], a.carry[:],
                  reads=[("carry",)], writes=[("dbgc",)])
        if DEV.get("dump_xf") == (s, h):
            p.dma("qa", io.dbg.rearrange("c p t -> p c t"), a.XF[:, :, 1:TH + 1],
                  reads=[("XF", c) for c in range(NC)], writes=[("dbg",)])


def emit_a_ffn_layer(p, g, io, u, a, s, h, l):
    nc = p.nc
    X = lambda *k: ("A", s, "f", l) + k
    XA = lambda *k: ("A", s) + k
    if h == 0:
        p.op("dve", lambda v: v.memset(a.XFb[:, :, 0:1], 0.0),
             reads=[("XF", c) for c in range(NC)],
             writes=[("XFb0",)])
    else:
        p.op("dve", lambda v: v.tensor_copy(a.XFb[:, :, 0:1],
                                            a.save_qkv[:, :, l:l + 1]),
             reads=[("save_qkv",)] + [("XF", c) for c in range(NC)],
             writes=[("XFb0",)])
    p.op("dve", lambda v: v.tensor_copy(a.save_qkv[:, :, l:l + 1],
                                        a.XFb[:, :, TH:TH + 1]),
         reads=[("XF", NC - 1)], writes=[("save_qkv",)])

    with _Scope(p) as lsc:
        wo_sb = lsc.sb("wo_sb", [128, 4, 1024], BF16)
        attn_bf = lsc.sb("attn", [128, 4, 512], BF16)
        negsh = lsc.sb("negsh", [128, 1], F32)
        rrow = lsc.sb("rrow", [1, 512], F32)
        bc_sb = lsc.sb("bc_sb", [64, 512], F32)
        dbgrow = lsc.sb("dbgrow", [1, 512], F32)
        p.op("pe", (lambda t, _i=g.L0f_qs[:, l:l + 1]: t.transpose(
            u.psS[0:1, 0:128], _i, g.ident_f[:])),
             reads=[("c", "L0f_qs"), ("c", "ident_f")], writes=[("psS",)])
        if DEV.get("dump_sh") == (s, h, l):
            p.op("dve", (lambda v, _o=dbgrow[0:1, 0:128]: v.tensor_copy(
                _o, u.psS[0:1, 0:128])),
                 reads=[("psS",)], writes=[X("dbgrow")])
        p.op("dve", (lambda v, _o=rrow[0:1, 0:1]: v.reduce_max(
            _o, u.psS[0:1, 0:128], axis=AX.X, apply_absolute_value=True)),
             reads=[("psS",)], writes=[X("rrow"), ("psS",)], hazard=True)
        p.op("pe", (lambda t, _i=g.L0f_ks[:, l:l + 1]: t.transpose(
            u.psS[0:1, 0:128], _i, g.ident_f[:])),
             reads=[("c", "L0f_ks"), ("c", "ident_f"), ("psS",)], writes=[("psS",)])
        p.op("dve", (lambda v, _o=rrow[0:1, 1:2]: v.reduce_max(
            _o, u.psS[0:1, 0:128], axis=AX.X, apply_absolute_value=True)),
             reads=[("psS",)], writes=[X("rrow"), ("psS",)], hazard=True)
        p.op("dve", (lambda v, _a=rrow[0:1, 0:1], _b=rrow[0:1, 1:2]:
                     v.tensor_mul(_a, _a, _b)),
             reads=[X("rrow")], writes=[X("rrow")])
        p.mm(u.psS[0:128, 0:1], g.ones1_f[:], rrow[0:1, 0:1],
             reads=[X("rrow")], writes=[("psS",)])
        p.op("dve", (lambda v, _o=negsh[:]: v.tensor_scalar_mul(_o, u.psS[0:128, 0:1], -8.0)),
             reads=[("psS",)], writes=[X("negsh"), ("psS",)], hazard=True)
        if DEV.get("dump_sh") == (s, h, l):
            p.op("dve", (lambda v, _o=dbgrow[0:1, 256:260], _i=rrow[0:1, 0:4]:
                         v.tensor_copy(_o, _i)),
                 reads=[X("rrow")], writes=[X("dbgrow")])
            p.dma("qa", io.dbg[6][0:1, 0:512], dbgrow[:],
                  reads=[X("dbgrow")], writes=[("dbgsh",)])
        p.dma("qg", wo_sb[:], io.wo[l].rearrange("(kc p) o -> p kc o", p=128),
              writes=[X("wo_sb")])

        with _Scope(p) as qsc:
            c1q = qsc.sb("c1q", [128, T], BF16)
            c2q = qsc.sb("c2q", [128, T], BF16)
            c1k = qsc.sb("c1k", [128, T], BF16)
            c2k = qsc.sb("c2k", [128, T], BF16)
            for nm, dst, sc_col in (("c1q", c1q, g.L0f_qs),
                                    ("c2q", c2q, g.L0f_qs_sw),
                                    ("c1k", c1k, g.L0f_ks),
                                    ("c2k", c2k, g.L0f_ks_sw)):
                p.dma("qs", dst[:], io.consts[nm + "_bf"], writes=[X("tbl", nm)])
                p.op("dve",
                     (lambda v, _d=dst[:], _s=sc_col[:, l:l + 1]:
                      v.tensor_scalar_mul(_d, _d, _s)),
                     reads=[X("tbl", nm), ("c", "L0f_qs")], writes=[X("tbl", nm)],
                     hazard=True)
            qf = qsc.sb("qf", [128, 4, TH], BF16)
            kf = qsc.sb("kf", [128, 4, TH], BF16)
            vf = qsc.sb("vf", [128, 8, 8, 65], BF16)
            wqkv_sl = [qsc.sb(f"wqk{i}", [128, NC, 128], BF16) for i in range(2)]
            wqkv_v = qsc.sb("wqkvv", [128, NC, 512], BF16)
            for oc in range(8):
                b = oc % 2
                p.dma("qg", wqkv_sl[b][:], _wsl(io.wqkv[l], oc),
                      reads=[X("wsl", b, "free")], writes=[X("wsl", b)])
                for nh in range(2):
                    for fc in range(NC):
                        p.mm(u.psA[:, nh * 512:nh * 512 + 512],
                             wqkv_sl[b][:, fc, :],
                             a.XFb[:, fc, _shiftoff(fc) + nh * 512:
                                   _shiftoff(fc) + nh * 512 + 512],
                             start=(fc == 0), stop=(fc == NC - 1),
                             reads=[X("wsl", b), ("XF", fc), ("XFb0",)],
                             writes=[("psA", nh)])
                dst = qf[:, oc, :] if oc < 4 else kf[:, oc - 4, :]
                key = X("qf", oc) if oc < 4 else X("kf", oc - 4)
                p.op("act", (lambda sx, dst=dst: sx.copy(dst, u.psA[:])),
                     reads=[("psA", 0), ("psA", 1)],
                     writes=[key, ("psA", 0), ("psA", 1), X("wsl", b, "free")])
            p.dma("qg", wqkv_v[:],
                  io.wqkv[l].rearrange("(kc p) o -> p kc o", p=128)[:, :, 1024:1536],
                  writes=[X("wqv")])
            p.op("dve", (lambda v, _o=vf[:, :, :, 64:65]: v.memset(_o, 1.0)),
                 writes=[X("vf_ones")])
            for tt in range(8):
                for fc in range(NC):
                    p.mm(u.psB[:, 0:512],
                         a.XFb[:, fc, _shiftoff(fc) + tt * 128:
                               _shiftoff(fc) + tt * 128 + 128],
                         wqkv_v[:, fc, :],
                         start=(fc == 0), stop=(fc == NC - 1),
                         reads=[X("wqv"), ("XF", fc), ("XFb0",)],
                         writes=[("psB", 0)])
                p.op("dve", (lambda v, _o=vf[:, tt, :, 0:64],
                             _i=u.psB[:, 0:512].rearrange("p (hh d) -> p hh d", hh=8):
                             v.tensor_copy(_o, _i)),
                    reads=[("psB", 0)],
                    writes=[X("vf", tt), ("psB", 0)])
            if DEV.get("dump_q") == (s, h, l, "qkv"):
                for qc in range(4):
                    p.dma("qg", io.dbg[qc], qf[:, qc, :], reads=[X("qf", qc)],
                          writes=[("dbgq", qc)])
                    p.dma("qg", io.dbg[4 + qc], kf[:, qc, :], reads=[X("kf", qc)],
                          writes=[("dbgk", qc)])
            if "a" in DEV.get("bar", ""):
                p.barrier()
            for side, buf, c1, c2, scol in (
                    ("q", qf, c1q, c2q, "q"), ("k", kf, c1k, c2k, "k")):
                for qc in range(4):
                    bkey = X("rot", side, qc)
                    sq = u.sq1024
                    p.op("act", (lambda sx, buf=buf, qc=qc: sx.activation(
                        sq[:], buf[:, qc, :], AF.Square)),
                        reads=[X(side + "f", qc)], writes=[("u", "sq1024")])
                    for nh in range(2):
                        p.mm(u.psS[0:1, 0:512], g.ones64a_bf[:],
                             sq[:, nh * 512:nh * 512 + 512],
                             reads=[("u", "sq1024")], writes=[("psS",)])
                        p.mm(u.psQ[0:1, 0:512], g.ones64b_bf[:],
                             sq[:, nh * 512:nh * 512 + 512],
                             reads=[("u", "sq1024")], writes=[("psQ",)])
                        for nm, pssrc, seg in (("nA", u.psS, 0), ("nB", u.psQ, 1)):
                            pskey = ("psS",) if nm == "nA" else ("psQ",)
                            rr = u.nrows[0:1, seg * 512:seg * 512 + 512]
                            p.op("act", (lambda sx, pssrc=pssrc, rr=rr:
                                         sx.activation(rr, pssrc[0:1, 0:512],
                                                       AF.Sqrt)),
                                 reads=[pskey],
                                 writes=[("u", "nrows", nm)])
                            p.op("dve", (lambda v, rr=rr: v.tensor_scalar_max(
                                rr, rr, 1e-12)),
                                reads=[("u", "nrows", nm)], writes=[("u", "nrows", nm)],
                                hazard=True)
                            p.op("dve", (lambda v, rr=rr: v.reciprocal(rr, rr)),
                                reads=[("u", "nrows", nm)],
                                writes=[("u", "nrows", nm), pskey], hazard=True)
                        p.mm(u.psC[:, nh * 512:nh * 512 + 512], g.selA_f[:],
                             u.nrows[0:1, 0:512],
                             start=True, stop=False,
                             reads=[("u", "nrows", "nA")], writes=[("psC", nh)])
                        p.mm(u.psC[:, nh * 512:nh * 512 + 512], g.selB_f[:],
                             u.nrows[0:1, 512:1024],
                             start=False, stop=True,
                             reads=[("u", "nrows", "nB")], writes=[("psC", nh)])
                    for nh in range(2):
                        p.mm(u.psB[:, nh * 512:nh * 512 + 512], g.rot_bf[:],
                             buf[:, qc, nh * 512:nh * 512 + 512],
                             reads=[X(side + "f", qc), ("c", "rot_bf")],
                             writes=[("psB", nh)])
                    tsl = slice(h * TH, h * TH + TH)
                    p.op("dve", (lambda v, buf=buf, qc=qc, c1=c1, tsl=tsl:
                                 v.tensor_mul(u.tmp_bf[:], buf[:, qc, :],
                                              c1[:, tsl])),
                         reads=[X(side + "f", qc), X("tbl", "c1" + scol)],
                         writes=[("u", "tmp_bf")])
                    p.op("dve", (lambda v, c2=c2, tsl=tsl: v.tensor_mul(
                        u.tmp_f[:], u.psB[:], c2[:, tsl])),
                        reads=[("psB", 0), ("psB", 1),
                               X("tbl", "c2" + scol)],
                        writes=[("u", "tmp_f"), ("psB", 0), ("psB", 1)])
                    p.op("dve", lambda v: v.tensor_add(u.tmp_f[:], u.tmp_f[:],
                                                       u.tmp_bf[:]),
                         reads=[("u", "tmp_bf"), ("u", "tmp_f")],
                         writes=[("u", "tmp_f")])
                    p.op("dve", (lambda v, buf=buf, qc=qc: v.tensor_mul(
                        buf[:, qc, :], u.tmp_f[:], u.psC[:])),
                        reads=[("u", "tmp_f"), ("psC", 0), ("psC", 1)],
                        writes=[X(side + "f", qc), ("psC", 0),
                                ("psC", 1)])
            if DEV.get("dump_rn") == (s, h, l):
                p.dma("qa", io.dbg[0][0:1, 0:1024], u.nrows[:],
                      reads=[("u", "nrows", "nA"), ("u", "nrows", "nB")],
                      writes=[("dbgr",)])
                p.dma("qg", io.dbg[1], c1q[:, h * TH:h * TH + TH],
                      reads=[X("tbl", "c1q")], writes=[("dbgt1",)])
                p.dma("qg", io.dbg[2], c2q[:, h * TH:h * TH + TH],
                      reads=[X("tbl", "c2q")], writes=[("dbgt2",)])
            if DEV.get("dump_q") == (s, h, l, "rot"):
                for qc in range(4):
                    p.dma("qg", io.dbg[qc], qf[:, qc, :], reads=[X("qf", qc)],
                          writes=[("dbgq", qc)])
                    p.dma("qg", io.dbg[4 + qc], kf[:, qc, :], reads=[X("kf", qc)],
                          writes=[("dbgk", qc)])
            if DEV.get("dump_v") == (s, h, l):
                for tt in range(8):
                    p.dma("qg", io.dbg[tt % 8][:, tt // 8 * 1040: tt // 8 * 1040 + 520]
                          if False else io.dbg[tt][:, 0:520],
                          vf[:, tt, :, :], reads=[X("vf", tt), X("vf_ones")],
                          writes=[("dbgv", tt)])
            if "b" in DEV.get("bar", ""):
                p.barrier()
            PT = [qsc.sb(f"pt{i}", [128, 512], BF16) for i in range(4)]
            for wb in range(2):
                W = 2 * h + wb
                kts = [kt for kt in range((W - 1) * 4, (W - 1) * 4 + 8) if kt >= 0]
                for hh in range(8):
                    pb = (hh % 2) * 64
                    ch = hh // 2
                    qrhs = qf[pb:pb + 64, ch, wb * 512:wb * 512 + 512]
                    nkt = len(kts)
                    for idx, kt in enumerate(kts):
                        bk = idx % 4
                        loc = kt - 8 * h
                        if loc >= 0:
                            klhs = kf[pb:pb + 64, ch, loc * 128:loc * 128 + 128]
                            vtile = vf[:, loc, hh, :]
                            kread = [X("kf", ch), X("vf", loc)]
                        else:
                            sloc = loc + 4
                            klhs = a.save_k[pb:pb + 64, ch, sloc * 128:
                                            sloc * 128 + 128, l]
                            vtile = a.save_v[:, sloc, hh * 65:hh * 65 + 65, l]
                            kread = [("save_k",), ("save_v",)]
                        o = kt * 128 - W * 512
                        msl = g.maskband_bf[:, 384 - o:384 - o + 512]
                        pst = u.psA if bk < 2 else u.psC
                        pk = ("psA", bk) if bk < 2 else ("psC", bk - 2)
                        hf = (bk % 2) * 512
                        p.mm(pst[:, hf:hf + 512], klhs, qrhs,
                             start=True, stop=False,
                             reads=kread + [X("qf", ch)],
                             writes=[pk])
                        p.mm(pst[:, hf:hf + 512], g.ident_bf[:],
                             msl, start=False, stop=True,
                             reads=[("c", "maskband_bf"), ("c", "ident_bf")],
                             writes=[pk])
                        p.op("act", (lambda sx, _o=PT[bk][:],
                            _i=pst[:, hf:hf + 512],
                            _b=negsh[:, 0:1]: sx.activation(
                            _o, _i, AF.Exp, bias=_b)),
                            reads=[pk, X("negsh")],
                            writes=[X("pt", bk), pk])
                        p.mm(u.psB[0:65, 0:512], vtile, PT[bk][:],
                             start=(idx == 0), stop=(idx == nkt - 1),
                             reads=kread + [X("pt", bk)],
                             writes=[("psB", 0), X("pt", bk)])
                    p.op("dve", (lambda v, _o=rrow[:]: v.tensor_scalar(
                        _o, u.psB[64:65, 0:512], 1e-30, None, op0=ALU.add)),
                        reads=[("psB", 0)], writes=[X("rrow")], hazard=True)
                    p.op("dve", (lambda v, _o=rrow[:]: v.reciprocal(_o, _o)),
                         reads=[X("rrow")], writes=[X("rrow")], hazard=True)
                    p.mm(u.psB[0:64, 512:1024], g.ones1_f[0:1, 0:64], rrow[:],
                         reads=[X("rrow")], writes=[("psB", 1)])
                    p.op("dve", (lambda v, _o=bc_sb[:]: v.tensor_copy(
                        _o, u.psB[0:64, 512:1024])),
                         reads=[("psB", 1)], writes=[X("bcs"), ("psB", 1)])
                    p.op("dve", (lambda v, _o=attn_bf[pb:pb + 64, ch, :],
                        _b=bc_sb[:]: v.tensor_mul(_o, u.psB[0:64, 0:512], _b)),
                        reads=[("psB", 0), X("bcs")],
                        writes=[X("at", hh), ("psB", 0)])
                for mc in range(NC):
                    bk = mc % 2
                    for kc in range(4):
                        p.mm(u.psB[:, bk * 512:bk * 512 + 512],
                             wo_sb[:, kc, mc * 128:mc * 128 + 128],
                             attn_bf[:, kc, :],
                             start=(kc == 0), stop=(kc == 3),
                             reads=[X("wo_sb"), X("at", 2 * kc), X("at", 2 * kc + 1)],
                             writes=[("psB", bk)])
                    xsl = a.XF[:, mc, 1 + wb * 512:1 + wb * 512 + 512]
                    p.op("dve", (lambda v, xsl=xsl, bk=bk: v.tensor_add(
                        xsl, u.psB[:, bk * 512:bk * 512 + 512], xsl)),
                        reads=[("psB", bk), ("XF", mc)],
                        writes=[("XF", mc), ("psB", bk)])
                    p.op("act", (lambda sx, mc=mc, wb=wb: sx.copy(
                        a.XFb[:, mc, 1 + wb * 512:1 + wb * 512 + 512],
                        a.XF[:, mc, 1 + wb * 512:1 + wb * 512 + 512])),
                        reads=[("XF", mc)], writes=[("XF", mc)])
            if "c" in DEV.get("bar", ""):
                p.barrier()
            if DEV.get("dump_attn") == (s, h, l):
                for kc in range(4):
                    p.dma("qg", io.dbg[kc][:, 0:512], attn_bf[:, kc, :],
                          reads=[X("at", 2 * kc), X("at", 2 * kc + 1)],
                          writes=[("dbga", kc)])
                p.dma("qa", io.dbg[7][:, 0:1], negsh[:],
                      reads=[X("negsh")], writes=[("dbgn",)],
                      allow_slow_non_contiguous=True)
            p.op("dve", (lambda v, _o=a.save_k[:, :, :, l],
                         _i=kf[:, :, 512:1024]: v.tensor_copy(_o, _i)),
                 reads=[X("kf", 0), X("kf", 1), X("kf", 2), X("kf", 3)],
                 writes=[("save_k",)])
            p.op("dve", (lambda v, _o=a.save_v[:, :, :, l],
                         _i=vf[:, 4:8, :, :]: v.tensor_copy(_o, _i)),
                 reads=[X("vf", tt) for tt in range(4, 8)] + [X("vf_ones")],
                 writes=[("save_v",)])

        if "d" in DEV.get("bar", ""):
            p.barrier()
        with _Scope(p) as fsc:
            hln = fsc.sb("hln", [128, NC, TH], BF16)
            if h == 0:
                p.op("dve", lambda v: v.memset(a.XF[:, :, 0:1], 0.0),
                     reads=[("XF", c) for c in range(NC)], writes=[("XFp0",)])
            else:
                p.op("dve", lambda v: v.tensor_copy(a.XF[:, :, 0:1],
                                                    a.save_ln_f[:, :, l:l + 1]),
                     reads=[("save_ln",)] + [("XF", c) for c in range(NC)],
                     writes=[("XFp0",)])
            p.op("act", lambda sx: sx.copy(a.XFb[:, :, 0:1], a.XF[:, :, 0:1]),
                 reads=[("XFp0",)], writes=[("XFb0",)])
            p.op("dve", lambda v: v.tensor_copy(a.save_ln_f[:, :, l:l + 1],
                                                a.XF[:, :, TH:TH + 1]),
                 reads=[("XF", NC - 1)], writes=[("save_ln",)])
            for tb in range(2):
                emit_ln(
                    p, g, u, X("lnf", h, tb),
                    get_src=lambda c, tb=tb: a.XFb[:, c, _shiftoff(c) + tb * 512:
                                                   _shiftoff(c) + tb * 512 + 512],
                    get_src_f=lambda c, tb=tb: a.XF[:, c, _shiftoff(c) + tb * 512:
                                                    _shiftoff(c) + tb * 512 + 512],
                    n=512,
                    out_bf16=lambda c, tb=tb: hln[:, c, tb * 512:tb * 512 + 512],
                    src_reads=[("XF", c2) for c2 in range(NC)] + [("XFb0",),
                               ("XFp0",)],
                    out_keys=lambda c: X("hln", c),
                    affine=(lambda c: g.L0f_lng[:, l, c:c + 1],
                            lambda c: g.L0f_lnb[:, l, c:c + 1]))
            if "e" in DEV.get("bar", ""):
                p.barrier()
            w1a_sl = [fsc.sb(f"w1a{i}", [128, NC, 128], BF16) for i in range(2)]
            w1g_sl = [fsc.sb(f"w1g{i}", [128, NC, 128], BF16) for i in range(2)]
            w2_sl = [fsc.sb("w2s", [128, NJ, 128], BF16)]
            t_all = fsc.sb("tall", [128, NJ, 2, 512], BF16)
            ag_bf = [fsc.sb(f"ag{i}", [128, 1024], BF16) for i in range(4)]
            for j in range(NJ):
                b = j % 2
                p.dma("qg", w1a_sl[b][:], _wsl(io.w1a[l], j),
                      reads=[X("w1a", b, "free")], writes=[X("w1a", b)])
                p.dma("qg", w1g_sl[b][:], _wsl(io.w1g[l], j),
                      reads=[X("w1g", b, "free")], writes=[X("w1g", b)])
                for tb in range(2):
                    tsl = slice(tb * 512, tb * 512 + 512)
                    pw = u.psA if tb == 0 else u.psC
                    pk0 = ("psA", 0) if tb == 0 else ("psC", 0)
                    pk1 = ("psA", 1) if tb == 0 else ("psC", 1)
                    ab = ag_bf[2 * tb + 0] if b == 0 else ag_bf[2 * tb + 1]
                    for fc in range(NC):
                        p.mm(pw[:, 0:512], w1a_sl[b][:, fc, :],
                             hln[:, fc, tsl], start=(fc == 0), stop=(fc == NC - 1),
                             reads=[X("w1a", b), X("hln", fc)],
                             writes=[pk0])
                    for fc in range(NC):
                        p.mm(pw[:, 512:1024], w1g_sl[b][:, fc, :],
                             hln[:, fc, tsl], start=(fc == 0), stop=(fc == NC - 1),
                             reads=[X("w1g", b), X("hln", fc)],
                             writes=[pk1])
                    p.op("act", (lambda sx, _o=ab[:, 0:512],
                        _i=pw[:, 0:512],
                        _b=g.L0f_b1a[:, l, j:j + 1]: sx.activation(
                        _o, _i, AF.Identity, bias=_b)),
                        reads=[pk0, ("c", "L0f_b1a")],
                        writes=[X("ag", tb, b, 0), pk0, X("w1a", b, "free")])
                    p.op("act", (lambda sx, _o=ab[:, 512:1024],
                        _i=pw[:, 512:1024],
                        _b=g.L0f_b1g[:, l, j:j + 1]: sx.activation(
                        _o, _i, AF.Gelu, bias=_b)),
                        reads=[pk1, ("c", "L0f_b1g")],
                        writes=[X("ag", tb, b, 1), pk1, X("w1g", b, "free")])
                    p.op("dve", (lambda v, _o=t_all[:, j, tb, :],
                        _a=ab[:, 0:512], _g=ab[:, 512:1024]:
                        v.tensor_mul(_o, _a, _g)),
                        reads=[X("ag", tb, b, 0), X("ag", tb, b, 1)],
                        writes=[X("tall", j, tb), X("ag", tb, b, 0),
                                X("ag", tb, b, 1)])
            for mc in range(NC):
                b = 0
                p.dma("qg", w2_sl[b][:],
                      io.w2[l].rearrange("(jc p) o -> p jc o", p=128)
                      [:, :, mc * 128:mc * 128 + 128],
                      reads=[X("w2", b, "free")], writes=[X("w2", b)])
                for tb in range(2):
                    for j in range(NJ):
                        p.mm(u.psB[:, tb * 512:tb * 512 + 512],
                             w2_sl[b][:, j, :], t_all[:, j, tb, :],
                             start=(j == 0), stop=(j == NJ - 1),
                             reads=[X("w2", b), X("tall", j, tb)],
                             writes=[("psB", tb)])
                for tb in range(2):
                    xsl = a.XF[:, mc, 1 + tb * 512:1 + tb * 512 + 512]
                    p.op("dve", (lambda v, xsl=xsl, tb=tb, mc=mc, l=l:
                                 v.scalar_tensor_tensor(
                                     xsl, u.psB[:, tb * 512:tb * 512 + 512],
                                     g.L0f_b2[:, l, mc:mc + 1], xsl,
                                     op0=ALU.add, op1=ALU.add)),
                         reads=[("psB", tb), ("XF", mc), ("c", "L0f_b2")],
                         writes=[("XF", mc), ("psB", tb), X("w2", b, "free")],
                         hazard=True)
                    p.op("act", (lambda sx, mc=mc, tb=tb: sx.copy(
                        a.XFb[:, mc, 1 + tb * 512:1 + tb * 512 + 512],
                        a.XF[:, mc, 1 + tb * 512:1 + tb * 512 + 512])),
                        reads=[("XF", mc)], writes=[("XF", mc)])


def emit_stage_b(p, g, io, u, bb):
    nc = p.nc
    with _Scope(p) as sc:
        WW = {"R": sc.sb("WR", [128, 4, 512], F32),
              "I": sc.sb("WI", [128, 4, 512], F32)}
        t1sb = sc.sb("t1sb", [32, 512], F32)
        tgsb = sc.sb("tgsb", [16, 512], F32)
        Ap = sc.sb("Ap", [128, 512], F32)
        Bp = sc.sb("Bp", [128, 512], F32)
        K = lambda *k: ("B", "w") + k
        for part in ("R", "I"):
            G1 = getattr(g, "L1f_G1" + part)
            G2 = getattr(g, "L1f_G2" + part)
            W = WW[part]
            p.op("dve", (lambda v, W=W: v.memset(W[:], 0.0)), writes=[K(part, "W")])
            for c in range(4):
                p.mm(u.psS[0:32, 0:512], G1[:, c, :], g.Yb_f[:],
                     reads=[("c", "L1f_G1" + part), ("c", "Yb_f")],
                     writes=[("psS",)])
                p.op("dve", lambda v: v.tensor_copy(t1sb[:], u.psS[0:32, 0:512]),
                     reads=[("psS",)], writes=[K("t1"), ("psS",)])
                p.mm(u.psQ[0:16, 0:512], G2[:, c, :], g.Ye_f[:],
                     reads=[("c", "L1f_G2" + part), ("c", "Ye_f")],
                     writes=[("psQ",)])
                p.op("dve", lambda v: v.tensor_copy(tgsb[:], u.psQ[0:16, 0:512]),
                     reads=[("psQ",)], writes=[K("tg"), ("psQ",)])
                for pc in range(4):
                    p.mm(u.psA[:, 0:512], g.Zd_f[:, pc, :], t1sb[:],
                         reads=[K("t1"), ("c", "Zd_f")], writes=[("psA", 0)])
                    p.op("dve", lambda v: v.tensor_copy(Ap[:], u.psA[:, 0:512]),
                         reads=[("psA", 0)], writes=[K("Ap"), ("psA", 0)])
                    p.mm(u.psB[:, 0:512], g.Zg_f[:, pc, :], tgsb[:],
                         reads=[K("tg"), ("c", "Zg_f")], writes=[("psB", 0)])
                    p.op("dve", lambda v: v.tensor_copy(Bp[:], u.psB[:, 0:512]),
                         reads=[("psB", 0)], writes=[K("Bp"), ("psB", 0)])
                    p.op("dve", lambda v: v.tensor_mul(Bp[:], Ap[:], Bp[:]),
                         reads=[K("Ap"), K("Bp")], writes=[K("Bp")])
                    p.op("dve", (lambda v, W=W, pc=pc: v.tensor_add(
                        W[:, pc, :], W[:, pc, :], Bp[:])),
                        reads=[K("Bp"), K(part, "W")], writes=[K(part, "W")])
        for ic in range(4):
            p.op("act", (lambda sx, ic=ic: sx.copy(bb.Wc[:, ic, 0:512],
                                                   WW["R"][:, ic, :])),
                 reads=[K("R", "W")], writes=[("Wc",)])
            p.op("act", (lambda sx, ic=ic: sx.copy(bb.Wc[:, ic, 512:1024],
                                                   WW["I"][:, ic, :])),
                 reads=[K("I", "W")], writes=[("Wc",)])
            p.op("act", (lambda sx, ic=ic: sx.activation(
                bb.Wc[:, 4 + ic, 0:512], WW["I"][:, ic, :], AF.Copy, scale=-1.0)),
                reads=[K("I", "W")], writes=[("Wc",)])
            p.op("act", (lambda sx, ic=ic: sx.copy(bb.Wc[:, 4 + ic, 512:1024],
                                                   WW["R"][:, ic, :])),
                 reads=[K("R", "W")], writes=[("Wc",)])

    for s in range(SEQS):
        X = lambda *k: ("B", s) + k
        with _Scope(p) as sc:
            xt = sc.sb("xt", [128, NC, 512], F32)
            xb = sc.sb("xb", [128, NC, 512], BF16)
            for tb in range(4):
                tsl = slice(tb * 512, tb * 512 + 512)
                p.dma("qs", xt[:], io.x1a[s].rearrange("c p t -> p c t")[:, :, tsl],
                      reads=[("A", s, "x1a", "acc")], writes=[X("xt")])
                p.dma("qa", io.x1b[s].rearrange("c p t -> p c t")[:, :, tsl],
                      xt[:], reads=[X("xt")], writes=[X("x1b", "init", tb)])
                p.op("act", (lambda sx, _o=xb[:], _i=xt[:]: sx.copy(_o, _i)),
                     reads=[X("xt")], writes=[X("xb")])
                emit_ln(
                    p, g, u, X("ln1", tb),
                    get_src=lambda c: xb[:, c, :],
                    get_src_f=lambda c: xt[:, c, :],
                    n=512,
                    out_bf16=lambda c, tb=tb: bb.XN1b[:, c, tsl],
                    src_reads=[X("xb"), X("xt")],
                    out_keys=lambda c: X("xn", c))
        with _Scope(p) as sc:
            f1 = [sc.sb(f"f1{i}", [128, 512], F32) for i in range(2)]
            for tb in range(4):
                tsl = slice(tb * 512, tb * 512 + 512)
                for mc in range(NC):
                    b = mc % 2
                    for ic in range(NC):
                        p.mm(u.psA[:, b * 512:b * 512 + 512],
                             bb.Wc[:, ic, mc * 128:mc * 128 + 128],
                             bb.XN1b[:, ic, tsl],
                             start=(ic == 0), stop=(ic == NC - 1),
                             reads=[("Wc",), X("xn", ic)], writes=[("psA", b)])
                    p.op("act", (lambda sx, b=b, mc=mc: sx.activation(
                        f1[b][:], u.psA[:, b * 512:b * 512 + 512], AF.Identity,
                        bias=g.L1f_bri[:, mc:mc + 1])),
                        reads=[("psA", b), ("c", "L1f_bri")],
                        writes=[X("f1", b), ("psA", b)])
                    p.dma("qg", io.x1b[s, mc][:, tsl], f1[b][:],
                          reads=[X("f1", b), X("x1b", "init", tb)],
                          writes=[X("x1b", "acc"), X("f1", b)],
                          accum_op=ALU.add)
        with _Scope(p) as sc:
            wsl = [sc.sb(f"wsl{i}", [128, NC, 128], BF16) for i in range(2)]
            stage = sc.sb("stage", [128, T], BF16)
            for oc in range(16):
                b = oc % 2
                p.dma("qg", wsl[b][:], _wsl(io.l1wqkv, oc),
                      reads=[X("wsl", b, "free")], writes=[X("wsl", b)])
                for tq in range(4):
                    bk = tq % 2
                    for fc in range(NC):
                        p.mm(u.psA[:, bk * 512:bk * 512 + 512],
                             wsl[b][:, fc, :],
                             bb.XN1b[:, fc, tq * 512:tq * 512 + 512],
                             start=(fc == 0), stop=(fc == NC - 1),
                             reads=[X("wsl", b), X("xn", fc)],
                             writes=[("psA", bk)])
                    p.op("act", (lambda sx, tq=tq, bk=bk: sx.copy(
                        stage[:, tq * 512:tq * 512 + 512],
                        u.psA[:, bk * 512:bk * 512 + 512])),
                        reads=[("psA", bk)],
                        writes=[X("stage", tq), ("psA", bk)])
                dst = io.qf[s, oc] if oc < 8 else io.kf[s, oc - 8]
                p.dma("qa", dst, stage[:],
                      reads=[X("stage", 0), X("stage", 1), X("stage", 2),
                             X("stage", 3)],
                      writes=[X("qk_out", oc), X("stage", 0), X("stage", 1),
                              X("stage", 2), X("stage", 3),
                              X("wsl", b, "free")])
            wqv = sc.sb("wqv", [128, NC, 1024], BF16)
            v_sb = sc.sb("v_sb", [128, 16, 65], BF16)
            p.dma("qg", wqv[:],
                  io.l1wqkv.rearrange("(kc p) o -> p kc o", p=128)[:, :, 2048:3072],
                  writes=[X("wqv")])
            p.op("dve", lambda v: v.memset(v_sb[:, :, 64:65], 1.0),
                 writes=[X("vones")])
            for tt in range(16):
                for vh in range(2):
                    for fc in range(NC):
                        p.mm(u.psB[:, vh * 512:vh * 512 + 512],
                             bb.XN1b[:, fc, tt * 128:tt * 128 + 128],
                             wqv[:, fc, vh * 512:vh * 512 + 512],
                             start=(fc == 0), stop=(fc == NC - 1),
                             reads=[X("wqv"), X("xn", fc)],
                             writes=[("psB", vh)])
                    p.op("dve", (lambda v, tt=tt, vh=vh: v.tensor_copy(
                        v_sb[:, vh * 8:vh * 8 + 8, 0:64],
                        u.psB[:, vh * 512:vh * 512 + 512].rearrange(
                            "p (hh d) -> p hh d", hh=8))),
                        reads=[("psB", vh)],
                        writes=[X("vsb", vh), ("psB", vh)])
                p.dma("qa", io.vf[s, tt], v_sb[:],
                      reads=[X("vsb", 0), X("vsb", 1), X("vones")],
                      writes=[X("v_out", tt), X("vsb", 0), X("vsb", 1)])


def emit_stage_c(p, g, io, u, s):
    nc = p.nc
    X = lambda *k: ("C", s) + k
    with _Scope(p) as sc:
        q_sb = sc.sb("q_sb", [128, NC, T], BF16)
        k_sb = sc.sb("k_sb", [128, NC, T], BF16)
        v_sb = sc.sb("v_sb", [128, 16, 1040], BF16)
        wproj = sc.sb("wproj", [128, NC, 1024], BF16)
        negsh = sc.sb("negsh", [128, 1], F32)
        rrow = sc.sb("rrow", [1, 512], F32)
        bc_sb = sc.sb("bc_sb", [64, 512], F32)
        phi_sb = [sc.sb("phi0", [128, 16, 512], BF16)]
        attn_bf = sc.sb("attn", [128, NC, 512], BF16)
        xt = [sc.sb("xt0", [128, NC, 512], F32)]
        out_sb = xt
        PT = [sc.sb(f"pt{i}", [128, 512], BF16) for i in range(4)]
        mrow = sc.sb("mrow", [1, 4], F32)

        bq = [("B", s, "qk_out", oc) for oc in range(16)]
        bv = [("B", s, "v_out", tt) for tt in range(16)]
        p.dma("qs", q_sb[:], io.qf[s].rearrange("c p t -> p c t"),
              reads=bq, writes=[X("q")])
        p.dma("qs", k_sb[:], io.kf[s].rearrange("c p t -> p c t"),
              reads=bq, writes=[X("k")])
        p.dma("qs", v_sb[:], io.vf[s].rearrange("tt p x -> p tt x"),
              reads=bv, writes=[X("v")])
        p.dma("qg", wproj[:], io.l1wproj.rearrange("(kc p) o -> p kc o", p=128),
              writes=[X("wp")])
        p.op("dve", lambda v: v.memset(mrow[:], 0.0), writes=[X("m0"), X("m1")])
        for side, buf, bkey, mi in ((0, q_sb, X("q"), 0), (1, k_sb, X("k"), 1)):
            for c in range(NC):
                for hf in range(2):
                    p.op("act", (lambda sx, buf=buf, c=c, hf=hf: sx.activation(
                        u.sq1024[:], buf[:, c, hf * 1024:hf * 1024 + 1024],
                        AF.Square)),
                        reads=[bkey], writes=[("u", "sq1024")])
                    for sub in range(2):
                        p.mm(u.psS[0:1, 0:512], g.ones_bf[:],
                             u.sq1024[:, sub * 512:sub * 512 + 512],
                             reads=[("u", "sq1024")], writes=[("psS",)])
                        p.op("dve", lambda v: v.reduce_max(
                            rrow[0:1, 0:1], u.psS[0:1, 0:512], axis=AX.X),
                            reads=[("psS",)], writes=[X("rr"), ("psS",)], hazard=True)
                        p.op("dve", (lambda v, mi=mi: v.tensor_max(
                            mrow[0:1, mi:mi + 1], mrow[0:1, mi:mi + 1],
                            rrow[0:1, 0:1])),
                            reads=[X("rr"), X("m%d" % mi)],
                            writes=[X("m%d" % mi)])
        p.op("act", lambda sx: sx.activation(mrow[0:1, 2:3], mrow[0:1, 0:1],
                                             AF.Sqrt),
             reads=[X("m0")], writes=[X("m2")])
        p.op("act", lambda sx: sx.activation(mrow[0:1, 3:4], mrow[0:1, 1:2],
                                             AF.Sqrt),
             reads=[X("m1")], writes=[X("m3")])
        p.op("dve", lambda v: v.tensor_mul(mrow[0:1, 2:3], mrow[0:1, 2:3],
                                           mrow[0:1, 3:4]),
             reads=[X("m2"), X("m3")], writes=[X("m2")])
        p.mm(u.psS[0:128, 0:1], g.ones1_f[:], mrow[0:1, 2:3],
             reads=[X("m2")], writes=[("psS",)])
        p.op("dve", lambda v: v.tensor_scalar_mul(negsh[:], u.psS[0:128, 0:1],
                                                  -0.125),
             reads=[("psS",)], writes=[X("negsh"), ("psS",)], hazard=True)
        for qt in range(4):
            bq_ = 0
            qsl = slice(qt * 512, qt * 512 + 512)
            p.dma("qs", phi_sb[bq_][:],
                  io.phi[qt].rearrange("kt p x -> p kt x"),
                  writes=[X("phi", bq_)])
            p.dma("qs", xt[bq_][:],
                  io.x1b[s].rearrange("c p t -> p c t")[:, :, qsl],
                  reads=[("B", s, "x1b", "acc")] +
                        [("B", s, "x1b", "init", tb) for tb in range(4)],
                  writes=[X("xt", bq_)] +
                         [X("osb", bq_, mc) for mc in range(NC)])
            for hh in range(16):
                pb = (hh % 2) * 64
                ch = hh // 2
                qrhs = q_sb[pb:pb + 64, ch, qsl]
                for kt in range(16):
                    bk = kt % 4
                    pst = u.psA if bk < 2 else u.psC
                    pk = ("psA", bk) if bk < 2 else ("psC", bk - 2)
                    hf = (bk % 2) * 512
                    klhs = k_sb[pb:pb + 64, ch, kt * 128:kt * 128 + 128]
                    p.mm(pst[:, hf:hf + 512], klhs, qrhs,
                         start=True, stop=False,
                         reads=[X("q"), X("k")], writes=[pk])
                    p.mm(pst[:, hf:hf + 512], g.ident_bf[:],
                         phi_sb[bq_][:, kt, :], start=False, stop=True,
                         reads=[X("phi", bq_), ("c", "ident_bf")],
                         writes=[pk])
                    p.op("act", (lambda sx, _o=PT[bk][:], _i=pst[:, hf:hf + 512],
                        _b=negsh[:, 0:1]: sx.activation(
                        _o, _i, AF.Exp, bias=_b, scale=0.125)),
                        reads=[pk, X("negsh")],
                        writes=[X("pt", bk), pk])
                    p.mm(u.psB[0:65, 0:512],
                         v_sb[:, kt, hh * 65:hh * 65 + 65], PT[bk][:],
                         start=(kt == 0), stop=(kt == 15),
                         reads=[X("v"), X("pt", bk)],
                         writes=[("psB", 0), X("pt", bk)])
                p.op("dve", lambda v: v.tensor_scalar(
                    rrow[:], u.psB[64:65, 0:512], 1e-30, None, op0=ALU.add),
                    reads=[("psB", 0)], writes=[X("rr")], hazard=True)
                p.op("dve", lambda v: v.reciprocal(rrow[:], rrow[:]),
                     reads=[X("rr")], writes=[X("rr")], hazard=True)
                p.mm(u.psB[0:64, 512:1024], g.ones1_f[0:1, 0:64], rrow[:],
                     reads=[X("rr")], writes=[("psB", 1)])
                p.op("dve", lambda v: v.tensor_copy(bc_sb[:],
                                                    u.psB[0:64, 512:1024]),
                     reads=[("psB", 1)], writes=[X("bcs"), ("psB", 1)])
                p.op("dve", (lambda v, pb=pb, ch=ch: v.tensor_mul(
                    attn_bf[pb:pb + 64, ch, :], u.psB[0:64, 0:512], bc_sb[:])),
                    reads=[("psB", 0), X("bcs")],
                    writes=[X("at", hh), ("psB", 0)])
            for mc in range(NC):
                bk = mc % 2
                for kc in range(NC):
                    p.mm(u.psC[:, bk * 512:bk * 512 + 512],
                         wproj[:, kc, mc * 128:mc * 128 + 128],
                         attn_bf[:, kc, :],
                         start=(kc == 0), stop=(kc == NC - 1),
                         reads=[X("wp"), X("at", 2 * kc), X("at", 2 * kc + 1)],
                         writes=[("psC", bk)])
                p.op("dve", (lambda v, bk=bk, mc=mc, bq_=bq_: v.scalar_tensor_tensor(
                    out_sb[bq_][:, mc, :], u.psC[:, bk * 512:bk * 512 + 512],
                    g.L1_bproj[:, mc:mc + 1], xt[bq_][:, mc, :],
                    op0=ALU.add, op1=ALU.add)),
                    reads=[("psC", bk), X("xt", bq_), ("c", "L1_bproj")],
                    writes=[X("osb", bq_, mc), ("psC", bk)], hazard=True)
            p.dma("qa", io.out[s].rearrange("c p t -> p c t")[:, :, qsl],
                  out_sb[bq_][:],
                  reads=[X("osb", bq_, mc) for mc in range(NC)],
                  writes=[X("out", qt)] + [X("osb", bq_, mc) for mc in range(NC)]
                  + [X("xt", bq_)])


def build_nc(stages, shared):
    nc = bass.Bass("TRN2", target_bir_lowering=False, debug=False)
    p = Prog(nc)
    io = H()
    declared = {}

    def din(name, shape, dtype):
        t = nc.dram_tensor(name, list(shape), dtype, kind="ExternalInput").ap()
        declared[name] = "in"
        return t

    def dout(name, shape, dtype):
        t = nc.dram_tensor(name, list(shape), dtype, kind="ExternalOutput").ap()
        declared[name] = "out"
        return t

    def dscratch(name, shape, dtype, produced, consumed):
        if produced and consumed:
            t = nc.dram_tensor(name, list(shape), dtype, kind="Internal").ap()
            declared[name] = "internal"
            return t
        if produced:
            return dout(name, shape, dtype)
        return din(name, shape, dtype)

    A = "A" in stages
    B = "B" in stages
    C = "C" in stages
    io.consts = {}
    cshapes = {}
    for k, v in shared.items():
        cshapes[k] = v
    const_names = ["ones_bf", "ones64a_bf", "ones64b_bf", "ones1_f", "selA_f",
                   "selB_f", "ident_bf", "ident_f", "rot_bf", "maskband_bf",
                   "L0a_lng", "L0a_lnb", "L0a_lam", "L0f_b1a", "L0f_b1g",
                   "L0f_b2", "L0f_lng", "L0f_lnb", "L0f_qs", "L0f_ks",
                   "L0f_qs_sw", "L0f_ks_sw", "L1_bproj", "L1f_bri",
                   "L1f_G1R", "L1f_G1I", "L1f_G2R", "L1f_G2I",
                   "Yb_f", "Ye_f", "Zd_f", "Zg_f",
                   "c1q_bf", "c2q_bf", "c1k_bf", "c2k_bf"]
    for nm in const_names:
        arr = shared[nm]
        dt = BF16 if arr.dtype == ml_dtypes.bfloat16 else F32
        io.consts[nm] = din("K_" + nm, arr.shape, dt)
    hd = io.consts

    if A and any(k in DEV for k in ("dump_xf", "dump_carry", "dump_q", "dump_v", "dump_attn", "dump_rn", "dump_sh")):
        io.dbg = dout("dbg", [NC, 128, TH], F32)
    if A:
        io.x = din("x", [SEQS, NC, 128, T], F32)
        io.wqT = din("L0a_WqT", [D, D], F32)
        io.wvT = din("L0a_WvT", [D, D], F32)
        io.wgT = din("L0a_WgT", [D, D], F32)
        io.woT = din("L0a_WoT", [D, D], F32)
        io.wqkv = din("L0f_wqkv", [2, D, 1536], F32)
        io.w1a = din("L0f_w1a", [2, D, HIDP], F32)
        io.w1g = din("L0f_w1g", [2, D, HIDP], F32)
        io.w2 = din("L0f_w2", [2, HIDP, D], F32)
        io.wo = din("L0f_wo", [2, 512, D], F32)
    if A or B:
        io.x1a = dscratch("x1a", [SEQS, NC, 128, T], F32, produced=A, consumed=B)
    if B:
        io.l1wqkv = din("L1_wqkv", [D, 3 * D], F32)
    if B or C:
        io.x1b = dscratch("x1b", [SEQS, NC, 128, T], F32, produced=B, consumed=C)
        io.qf = dscratch("qf", [SEQS, NC, 128, T], BF16, produced=B, consumed=C)
        io.kf = dscratch("kf", [SEQS, NC, 128, T], BF16, produced=B, consumed=C)
        io.vf = dscratch("vf", [SEQS, 16, 128, 16 * 65], BF16,
                         produced=B, consumed=C)
    if C:
        io.l1wproj = din("L1_wproj", [D, D], F32)
        io.phi = din("K_phi8_bf", [4, 16, 128, 512], BF16)
        io.out = dout("out", [SEQS, NC, 128, T], F32)
    u = H()
    u.psA = p.ps("psA", [128, 1024])
    u.psB = p.ps("psB", [128, 1024])
    u.psC = p.ps("psC", [128, 1024])
    u.psS = p.ps("psS", [128, 512])
    u.psQ = p.ps("psQ", [128, 512])
    u.sq = p.sb("u_sq", [128, 512], BF16)
    u.sq1024 = p.sb("u_sq1024", [128, 1024], BF16)
    u.tmp = p.sb("u_tmp", [128, 512], F32)
    u.tmp_f = p.sb("u_tmp_f", [128, 1024], F32)
    u.tmp_bf = p.sb("u_tmp_bf", [128, 1024], BF16)
    u.rows = p.sb("u_rows", [1, 1536], F32)
    u.nrows = p.sb("u_nrows", [1, 1024], F32)
    u.eps = p.sb("u_eps", [1, 1], F32)
    p.op("dve", lambda v: v.memset(u.eps[:], 1e-5), writes=[("u", "eps")])
    g = declare_consts(p, nc, io.consts)
    if A:
        with _Scope(p) as basesc:
            base = H()
            base.XF = basesc.sb("XF", [128, NC, TH + 1], F32)
            base.save_ln_f = basesc.sb("save_ln_f", [128, NC, 2], F32)
            base.XFb = basesc.sb("XFb", [128, NC, TH + 1], BF16)
            base.carry = basesc.sb("carry", [128, NC], F32)
            base.save_qkv = basesc.sb("save_qkv", [128, NC, 2], BF16)
            base.save_ln = basesc.sb("save_ln", [128, NC, 2], BF16)
            base.save_k = basesc.sb("save_k", [128, 4, 512, 2], BF16)
            base.save_v = basesc.sb("save_v", [128, 4, 520, 2], BF16)
            for s in range(DEV.get("nseq", SEQS)):
                emit_stage_a(p, g, io, u, base, s)
    if B:
        with _Scope(p) as bsc:
            bb = H()
            bb.Wc = bsc.sb("Wc", [128, NC, 1024], BF16)
            bb.XN1b = bsc.sb("XN1b", [128, NC, T], BF16)
            emit_stage_b(p, g, io, u, bb)
    if C:
        for s in range(SEQS):
            emit_stage_c(p, g, io, u, s)
    global LAST_PROG
    LAST_PROG = p
    p.finalize_and_emit()
    in_names = [k for k, v in declared.items() if v == "in"]
    out_names = [k for k, v in declared.items() if v == "out"]
    return nc, in_names, out_names


def _run(nc, in_names, maps):
    in_maps = [{k: m[k] for k in in_names} for m in maps]
    res = run_bass_kernel_spmd(nc, in_maps, core_ids=list(range(NCORES)))
    return res.results


def kernel(**inputs):
    shared, per_core = host_prep(inputs)
    smap = {}
    for k, v in shared.items():
        smap["K_" + k] = v
    smap["K_phi8_bf"] = shared["phi8_bf"]
    smap.update({
        "L0a_WqT": shared["L0a_WqT"], "L0a_WvT": shared["L0a_WvT"],
        "L0a_WgT": shared["L0a_WgT"], "L0a_WoT": shared["L0a_WoT"],
        "L0f_wqkv": shared["L0f_wqkv"], "L0f_w1a": shared["L0f_w1a"],
        "L0f_w1g": shared["L0f_w1g"], "L0f_w2": shared["L0f_w2"],
        "L0f_wo": shared["L0f_wo"], "L1_wqkv": shared["L1_wqkv"],
        "L1_wproj": shared["L1_wproj"],
    })
    if UNFUSED:
        stage_sets = [("A",), ("B",), ("C",)]
    else:
        stage_sets = [("A", "B", "C")]
    core_maps = [dict(smap, **pc) for pc in per_core]
    outs = None
    for stages in stage_sets:
        nc, in_names, out_names = build_nc(stages, shared)
        results = _run(nc, in_names, core_maps)
        for core in range(NCORES):
            for onm in out_names:
                core_maps[core][onm] = np.asarray(results[core][onm])
        outs = results
    out = np.empty((16, T, D), np.float32)
    for core in range(NCORES):
        o = np.asarray(outs[core]["out"], np.float32).reshape(SEQS, D, T)
        for s in range(SEQS):
            out[core * SEQS + s] = o[s].T
    return out


# version 05
# speedup vs baseline: 1.4140x; 1.2008x over previous
import math
from contextlib import ExitStack

import numpy as np
import ml_dtypes

import concourse.bass as bass
import concourse.mybir as mybir
from concourse.bass_utils import run_bass_kernel_spmd

F32 = mybir.dt.float32
BF16 = mybir.dt.bfloat16
AF = mybir.ActivationFunctionType
ALU = mybir.AluOpType
AX = mybir.AxisListType

NCORES = 8
SEQS = 2
T = 2048
D = 1024
NC = 8
TH = 1024
HID = 2730
HIDP = 2816
NJ = HIDP // 128
NEG = -1.0e30

UNFUSED = False
DEV = {}
LAST_PROG = None

COMPUTE_STREAMS = ("pe", "act", "dve", "pool", "sp")
QUEUE_ISSUER = {"qs": "sp", "qa": "act", "qg": "pool"}
QUEUE_LANES = {"qs": 8, "qa": 6, "qg": 12}
LANE_STREAMS = [(q, r) for q, R in QUEUE_LANES.items() for r in range(R)]
STREAM_ENGINE = {"pe": "tensor", "act": "scalar", "dve": "vector",
                 "pool": "gpsimd", "sp": "sync"}


def _is_lane(s):
    return isinstance(s, tuple)


class _Op:
    __slots__ = ("stream", "issuer", "emit", "waits", "count", "inc", "tag")

    def __init__(self, stream, issuer, emit, waits, count, tag=None):
        self.stream = stream
        self.issuer = issuer
        self.emit = emit
        self.waits = waits
        self.count = count
        self.inc = False
        self.tag = tag


class Prog:
    def __init__(self, nc):
        self.nc = nc
        self.es = ExitStack()
        self.counts = {s: 0 for s in COMPUTE_STREAMS}
        self.counts.update({ls: 0 for ls in LANE_STREAMS})
        self.qidx = {q: 0 for q in QUEUE_ISSUER}
        self.ops = {s: [] for s in COMPUTE_STREAMS}
        self.vc = {s: {} for s in COMPUTE_STREAMS}
        self.res = {}
        self.sems = {}
        self.hazards = {s: set() for s in COMPUTE_STREAMS}
        self.self_hw = {s: 0 for s in COMPUTE_STREAMS}

    def sb(self, name, shape, dtype):
        return self.es.enter_context(
            self.nc.sbuf_tensor(_uname(name), shape, dtype))

    def ps(self, name, shape, dtype=F32):
        return self.es.enter_context(
            self.nc.psum_tensor(_uname(name), shape, dtype))

    @staticmethod
    def _merge(dst, src):
        for k, v in src.items():
            if dst.get(k, 0) < v:
                dst[k] = v

    def op(self, stream, emit, reads=(), writes=(), queue=None, tag=None,
           self_wait=False, hazard=False):
        lane_prewait = None
        if queue is not None:
            issuer = QUEUE_ISSUER[queue]
            r = self.qidx[queue] % QUEUE_LANES[queue]
            self.qidx[queue] += 1
            cstream = (queue, r)
            step = 16
            if self.counts[cstream] > 0:
                lane_prewait = (cstream, self.counts[cstream])
        else:
            issuer = stream
            cstream = stream
            step = 1
        deps = []
        bar = self.res.pop(("__bar__", issuer), None)
        if bar is not None and bar[0] is not None:
            deps.append(bar[0])
        for r in reads:
            st = self.res.get(r)
            if st is not None and st[0] is not None:
                deps.append(st[0])
        for w in writes:
            st = self.res.get(w)
            if st is not None:
                if st[0] is not None:
                    deps.append(st[0])
                deps.extend(st[1])
        need = {}
        for st in deps:
            self._merge(need, st[2])
        self_tgt = 0
        if queue is None:
            for st in deps:
                if st[0] == issuer:
                    if issuer == "pe" and not self_wait:
                        continue
                    self_tgt = max(self_tgt, st[1])
        ivc = self.vc[issuer]
        waits = {}
        if lane_prewait is not None:
            s, n = lane_prewait
            if ivc.get(s, 0) < n:
                waits[s] = n
                self._merge(ivc, {s: n})
        for s, n in need.items():
            if s == issuer and queue is None:
                if self_tgt > 0 and self.self_hw[issuer] < self_tgt:
                    waits[s] = self_tgt
                    self.self_hw[issuer] = self_tgt
                continue
            if ivc.get(s, 0) < n:
                waits[s] = n
        self._merge(ivc, need)
        self.counts[cstream] += step
        c = self.counts[cstream]
        if queue is None:
            ivc[cstream] = c
        rec = _Op(cstream, issuer, emit, waits, c,
                  tag or (tuple(writes)[:1] or ("?",))[0])
        self.ops[issuer].append(rec)
        if hazard and queue is None:
            self.hazards[cstream].add(c)
        if queue is None:
            svc = dict(ivc)
        else:
            svc = {s: n for s, n in ivc.items() if s != issuer}
            svc[cstream] = c
        stamp = (cstream, c, svc)
        for w in writes:
            self.res[w] = (stamp, [])
        for r in reads:
            st = self.res.get(r)
            if st is None:
                self.res[r] = (None, [stamp])
            else:
                st[1].append(stamp)
        return stamp

    def dma(self, queue, out_ap, in_ap, reads=(), writes=(), **kw):
        return self.op(
            None,
            (lambda eng, o=out_ap, i=in_ap, kw=kw: eng.dma_start(out=o, in_=i, **kw)),
            reads=reads, writes=writes, queue=queue)

    def mm(self, out, lhsT, rhs, start=True, stop=True, reads=(), writes=()):
        return self.op(
            "pe",
            (lambda t, o=out, l=lhsT, r=rhs, st=start, sp=stop:
             t.matmul(o, l, r, start=st, stop=sp)),
            reads=reads, writes=writes)

    def barrier(self):
        cur = {s: c for s, c in self.counts.items() if c > 0 and s != "sp"}
        need = dict(cur)
        ivc = self.vc["sp"]
        waits = {s: n for s, n in need.items() if ivc.get(s, 0) < n}
        self._merge(ivc, need)
        self.counts["sp"] += 1
        c = self.counts["sp"]
        ivc["sp"] = c
        rec = _Op("sp", "sp", (lambda eng: eng.nop()), waits, c)
        self.ops["sp"].append(rec)
        bstamp = ("sp", c, dict(ivc))
        for s in COMPUTE_STREAMS:
            if s != "sp":
                self.res[("__bar__", s)] = (bstamp, [])
        return bstamp

    def finalize_and_emit(self):
        nc = self.nc
        finals = {s: c for s, c in self.counts.items() if c > 0 and s != "sp"}
        targets = {s: set() for s in self.counts}
        for lst in self.ops.values():
            for o in lst:
                for s, n in o.waits.items():
                    targets[s].add(n)
        for s, c in finals.items():
            targets[s].add(c)
        for ls in LANE_STREAMS:
            targets[ls] = set(range(16, self.counts[ls] + 1, 16))
        miles = {s: sorted(ts) for s, ts in targets.items()}
        import bisect

        def remap(s, n):
            return bisect.bisect_right(miles[s], n) * (16 if _is_lane(s) else 1)

        mileset = {s: set(ts) for s, ts in miles.items()}
        for lst in self.ops.values():
            for o in lst:
                if o.count in mileset[o.stream]:
                    o.inc = True
        for s, ts in miles.items():
            if ts:
                nm = "sem_" + ("_".join(map(str, s)) if _is_lane(s) else str(s))
                self.sems[s] = self.es.enter_context(nc.semaphore(nm))
        block = self.es.enter_context(nc.Block())

        def emit_stream(eng_stream):
            ops = self.ops[eng_stream]
            if not ops and eng_stream != "sp":
                return
            dec = getattr(block, STREAM_ENGINE[eng_stream])

            def body(eng):
                for o in ops:
                    for s, n in sorted(o.waits.items(), key=lambda x: str(x[0])):
                        eng.wait_ge(self.sems[s], remap(s, n))
                    instr = o.emit(eng)
                    if o.inc:
                        instr.then_inc(self.sems[o.stream],
                                       16 if _is_lane(o.stream) else 1)
                if eng_stream == "sp":
                    for s, c in sorted(finals.items(), key=lambda x: str(x[0])):
                        if c and self.vc["sp"].get(s, 0) < c:
                            eng.wait_ge(self.sems[s], remap(s, c))
            dec(body)

        for s in ("sp", "pe", "act", "dve", "pool"):
            emit_stream(s)


def _bf(x):
    return np.asarray(x, np.float32).astype(ml_dtypes.bfloat16)


def host_constants():
    C = {}
    C["ones_bf"] = _bf(np.ones((128, 1)))
    on64a = np.zeros((128, 1), np.float32); on64a[:64] = 1.0
    on64b = np.zeros((128, 1), np.float32); on64b[64:] = 1.0
    C["ones64a_bf"] = _bf(on64a)
    C["ones64b_bf"] = _bf(on64b)
    C["ones1_f"] = np.ones((1, 128), np.float32)
    sa = np.zeros((1, 128), np.float32); sa[0, :64] = 1.0
    sb_ = np.zeros((1, 128), np.float32); sb_[0, 64:] = 1.0
    C["selA_f"] = sa
    C["selB_f"] = sb_
    C["ident_bf"] = _bf(np.eye(128))
    C["ident_f"] = np.eye(128, dtype=np.float32)
    R = np.zeros((128, 128), np.float32)
    for base in (0, 64):
        for i in range(32):
            R[base + 32 + i, base + i] = -1.0
            R[base + i, base + 32 + i] = 1.0
    C["rot_bf"] = _bf(R)
    DH = 64
    WINDOW = 512
    n = T
    half_idx = np.arange(0, DH, 2, dtype=np.float64)
    inv_freq = 1.0 / 10000.0 ** (half_idx / DH)
    t = np.arange(n, dtype=np.float64)
    freqs = t[:, None] * inv_freq[None, :]
    freqs = np.concatenate([freqs, freqs], axis=-1)
    base = (half_idx + 0.4 * DH) / (1.4 * DH)
    power = (t - n // 2) / float(WINDOW // 2)
    scale = base[None, :] ** power[:, None]
    scale = np.concatenate([scale, scale], axis=-1)
    cos, sin = np.cos(freqs), np.sin(freqs)
    rep = lambda a: np.tile(np.ascontiguousarray(a.T), (2, 1))
    C["c1q_bf"] = _bf(rep(cos * scale * 8.0))
    C["c2q_bf"] = _bf(rep(sin * scale * 8.0))
    C["c1k_bf"] = _bf(rep(cos / scale))
    C["c2k_bf"] = _bf(rep(sin / scale))
    W = 1408
    jj = np.arange(128)[:, None]
    y = np.arange(W)[None, :] - 384
    band = (y - jj >= 0) & (y - jj <= 512)
    C["maskband_bf"] = _bf(np.where(band, 0.0, NEG))
    PHI = (1.0 + math.sqrt(5.0)) / 2.0
    pos = (np.arange(T, dtype=np.float32) * np.float32(PHI)) % np.float32(1.0)
    dist = np.abs(pos[:, None] - pos[None, :])
    phi = np.log(np.exp(-dist) + 1e-8).astype(np.float32) * 8.0
    phiT = np.ascontiguousarray(phi.T)
    C["phi8_bf"] = _bf(
        phiT.reshape(16, 128, 4, 512).transpose(2, 0, 1, 3))
    Yb = np.zeros((32, 512), np.float32)
    Ye = np.zeros((16, 512), np.float32)
    for idx in range(512):
        Yb[idx // 16, idx] = 1.0
        Ye[idx % 16, idx] = 1.0
    Zd = np.zeros((32, 4, 128), np.float32)
    Zg = np.zeros((16, 4, 128), np.float32)
    for pc in range(4):
        for col in range(128):
            idx = pc * 128 + col
            Zd[idx // 16, pc, col] = 1.0
            Zg[idx % 16, pc, col] = 1.0
    C["Yb_f"] = Yb
    C["Ye_f"] = Ye
    C["Zd_f"] = Zd
    C["Zg_f"] = Zg
    return C


def _pp(vec, nch):
    return np.ascontiguousarray(np.asarray(vec, np.float32).reshape(nch, 128).T)


def host_prep(inputs):
    ix = {k: np.asarray(v) for k, v in inputs.items()}
    shared = dict(host_constants())
    for nm in ("Wq", "Wv", "Wg", "Wo"):
        shared["L0a_" + nm + "T"] = np.ascontiguousarray(ix["L0_attn_" + nm].T)
    shared["L0a_lng"] = _pp(ix["L0_attn_ln_gamma"], 8)
    shared["L0a_lnb"] = _pp(ix["L0_attn_ln_beta"], 8)
    shared["L0a_lam"] = np.ascontiguousarray(
        np.repeat(ix["L0_attn_decay"].astype(np.float32), 64)[None, :])
    shared["L0f_wqkv"] = np.ascontiguousarray(ix["L0_ffn_wqkv"])
    w1 = ix["L0_ffn_w1"]
    w1a = np.zeros((2, 1024, HIDP), np.float32)
    w1g = np.zeros((2, 1024, HIDP), np.float32)
    w1a[:, :, :HID] = w1[:, :, :HID]
    w1g[:, :, :HID] = w1[:, :, HID:]
    shared["L0f_w1a"] = w1a
    shared["L0f_w1g"] = w1g
    w2 = ix["L0_ffn_w2"]
    w2p = np.zeros((2, HIDP, 1024), np.float32)
    w2p[:, :HID, :] = w2
    shared["L0f_w2"] = w2p
    shared["L0f_wo"] = np.ascontiguousarray(ix["L0_ffn_wo"])
    b1 = ix["L0_ffn_b1"]
    b1a = np.zeros((2, HIDP), np.float32); b1a[:, :HID] = b1[:, :HID]
    b1g = np.zeros((2, HIDP), np.float32); b1g[:, :HID] = b1[:, HID:]
    shared["L0f_b1a"] = np.stack([_pp(b1a[l], NJ) for l in range(2)], 1)
    shared["L0f_b1g"] = np.stack([_pp(b1g[l], NJ) for l in range(2)], 1)
    shared["L0f_b2"] = np.stack([_pp(ix["L0_ffn_b2"][l], 8) for l in range(2)], 1)
    shared["L0f_lng"] = np.stack([_pp(ix["L0_ffn_ln_g"][l], 8) for l in range(2)], 1)
    shared["L0f_lnb"] = np.stack([_pp(ix["L0_ffn_ln_b"][l], 8) for l in range(2)], 1)
    qs = ix["L0_ffn_q_scale"].astype(np.float32)
    ks = ix["L0_ffn_k_scale"].astype(np.float32)
    tile2 = lambda v: np.ascontiguousarray(np.tile(v, 2)[:, None])
    shared["L0f_qs"] = np.concatenate([tile2(qs[l]) for l in range(2)], 1)
    shared["L0f_ks"] = np.concatenate([tile2(ks[l]) for l in range(2)], 1)
    swap = lambda v: np.concatenate([v[32:], v[:32]])
    shared["L0f_qs_sw"] = np.concatenate([tile2(swap(qs[l])) for l in range(2)], 1)
    shared["L0f_ks_sw"] = np.concatenate([tile2(swap(ks[l])) for l in range(2)], 1)
    shared["L1_wqkv"] = np.ascontiguousarray(ix["L1_attn_w_qkv"])
    shared["L1_wproj"] = np.ascontiguousarray(ix["L1_attn_w_proj"])
    shared["L1_bproj"] = _pp(ix["L1_attn_b_proj"], 8)
    shared["L1f_G1R"] = np.ascontiguousarray(
        ix["L1_ffn_G1_R"].reshape(32, 4, 32).astype(np.float32))
    shared["L1f_G1I"] = np.ascontiguousarray(
        ix["L1_ffn_G1_I"].reshape(32, 4, 32).astype(np.float32))
    shared["L1f_G2R"] = np.ascontiguousarray(
        ix["L1_ffn_G2_R"].reshape(4, 16, 16).transpose(1, 0, 2))
    shared["L1f_G2I"] = np.ascontiguousarray(
        ix["L1_ffn_G2_I"].reshape(4, 16, 16).transpose(1, 0, 2))
    shared["L1f_bri"] = _pp(
        np.concatenate([ix["L1_ffn_bias_real"], ix["L1_ffn_bias_imag"]]), 8)
    x = np.asarray(ix["L0_attn_x"], np.float32)
    per_core = []
    for core in range(NCORES):
        xs = x[core * SEQS:(core + 1) * SEQS]
        xf = xs.transpose(0, 2, 1).reshape(SEQS, NC, 128, T)
        per_core.append({"x": np.ascontiguousarray(xf)})
    return shared, per_core


class H:
    pass


def declare_consts(p, nc, hd):
    g = H()
    ld = []

    def load(name, shape, dtype, src_name=None, cast=False):
        tsb = p.sb("c_" + name, shape, dtype)
        src = hd[src_name or name]
        q = "qg" if cast else "qs"
        p.dma(q, tsb[:], src, writes=[("c", name)])
        setattr(g, name, tsb)
        return tsb

    load("ones_bf", [128, 1], BF16)
    load("ones64a_bf", [128, 1], BF16)
    load("ones64b_bf", [128, 1], BF16)
    load("ones1_f", [1, 128], F32)
    load("selA_f", [1, 128], F32)
    load("selB_f", [1, 128], F32)
    load("ident_bf", [128, 128], BF16)
    load("ident_f", [128, 128], F32)
    load("rot_bf", [128, 128], BF16)
    load("maskband_bf", [128, 1408], BF16)
    load("L0a_lng", [128, 8], F32)
    load("L0a_lnb", [128, 8], F32)
    load("L0a_lam", [1, 1024], F32)
    load("L0f_b1a", [128, 2, NJ], F32)
    load("L0f_b1g", [128, 2, NJ], F32)
    load("L0f_b2", [128, 2, 8], F32)
    load("L0f_lng", [128, 2, 8], F32)
    load("L0f_lnb", [128, 2, 8], F32)
    load("L0f_qs", [128, 2], F32)
    load("L0f_ks", [128, 2], F32)
    load("L0f_qs_sw", [128, 2], F32)
    load("L0f_ks_sw", [128, 2], F32)
    load("L1_bproj", [128, 8], F32)
    load("L1f_bri", [128, 8], F32)
    load("L1f_G1R", [32, 4, 32], F32)
    load("L1f_G1I", [32, 4, 32], F32)
    load("L1f_G2R", [16, 4, 16], F32)
    load("L1f_G2I", [16, 4, 16], F32)
    load("Yb_f", [32, 512], F32)
    load("Ye_f", [16, 512], F32)
    load("Zd_f", [32, 4, 128], F32)
    load("Zg_f", [16, 4, 128], F32)
    onesrow = p.sb("c_onesrow", [1, 512], F32)
    p.op("dve", lambda v: v.memset(onesrow[:], 1.0), writes=[("c", "onesrow")])
    g.onesrow = onesrow
    return g


def emit_ln(p, g, u, key, get_src, n, out_bf16, src_reads, out_keys,
            out_fp32=None, affine=None, eps=1e-5, get_src_f=None):
    r0 = lambda sl: u.rows[0:1, 0 * 512 + 0:0 * 512 + n]
    r1 = lambda sl: u.rows[0:1, 1 * 512 + 0:1 * 512 + n]
    r2 = lambda sl: u.rows[0:1, 2 * 512 + 0:2 * 512 + n]
    K = lambda *a: (key,) + a
    for c in range(NC):
        src = get_src(c)
        p.op("act", (lambda s, src=src: s.activation(u.sq[:, :n], src, AF.Square)),
             reads=list(src_reads) + [("psQ",)], writes=[("u", "sq")])
        p.mm(u.psS[0:1, :n], g.ones_bf[:], src, start=(c == 0), stop=(c == NC - 1),
             reads=list(src_reads), writes=[("psS",)])
        p.mm(u.psQ[0:1, :n], g.ones_bf[:], u.sq[:, :n], start=(c == 0),
             stop=(c == NC - 1), reads=[("u", "sq")], writes=[("psQ",)])
    p.op("dve", lambda v: v.tensor_scalar_mul(r0(0), u.psS[0:1, :n], 1.0 / D),
         reads=[("psS",)], writes=[("u", "rows", 0)], hazard=True)
    p.op("dve", lambda v: v.tensor_scalar_mul(r1(0), u.psQ[0:1, :n], 1.0 / D),
         reads=[("psQ",)], writes=[("u", "rows", 1)], hazard=True)
    p.op("dve", lambda v: v.tensor_mul(r2(0), r0(0), r0(0)),
         reads=[("u", "rows", 0)], writes=[("u", "rows", 2)])
    p.op("dve", lambda v: v.tensor_sub(r1(0), r1(0), r2(0)),
         reads=[("u", "rows", 1), ("u", "rows", 2)], writes=[("u", "rows", 1)])
    p.op("act", lambda s: s.activation(r2(0), r1(0), AF.Sqrt,
                                       bias=u.eps[0:1, 0:1]),
         reads=[("u", "rows", 1), ("u", "eps")], writes=[("u", "rows", 2)])
    p.op("dve", lambda v: v.reciprocal(r1(0), r2(0)),
         reads=[("u", "rows", 2)], writes=[("u", "rows", 1)], hazard=True)
    p.op("dve", lambda v: v.tensor_mul(r2(0), r0(0), r1(0)),
         reads=[("u", "rows", 0), ("u", "rows", 1)], writes=[("u", "rows", 2)])
    p.mm(u.psC[:, 0:n], g.ones1_f[:], r1(0), reads=[("u", "rows", 1)], writes=[("psC", 0)])
    p.mm(u.psC[:, 512:512 + n], g.ones1_f[:], r2(0), reads=[("u", "rows", 2)], writes=[("psC", 1)])
    for c in range(NC):
        src = (get_src_f or get_src)(c)
        p.op("dve",
             (lambda v, src=src: v.tensor_mul(u.tmp[:, :n], src, u.psC[:, 0:n])),
             reads=list(src_reads) + [("psC", 0)], writes=[("u", "tmp")])
        if affine is not None:
            p.op("dve", lambda v: v.tensor_sub(u.tmp[:, :n], u.tmp[:, :n],
                                               u.psC[:, 512:512 + n]),
                 reads=[("u", "tmp"), ("psC", 1)], writes=[("u", "tmp")])
            gam, bet = affine
            dstb = out_bf16(c)
            p.op("act",
                 (lambda s, c=c, dstb=dstb: s.activation(
                     dstb, u.tmp[:, :n], AF.Identity,
                     scale=gam(c), bias=bet(c))),
                 reads=[("u", "tmp")], writes=[out_keys(c)])
        elif out_fp32 is not None:
            dst = out_fp32(c)
            p.op("dve", (lambda v, dst=dst: v.tensor_sub(dst, u.tmp[:, :n],
                                                         u.psC[:, 512:512 + n])),
                 reads=[("u", "tmp"), ("psC", 1)], writes=[out_keys(c)])
            if out_bf16 is not None:
                dstb = out_bf16(c)
                p.op("act", (lambda s, dst=dst, dstb=dstb: s.copy(dstb, dst)),
                     reads=[out_keys(c)], writes=[out_keys(c)])
        else:
            dstb = out_bf16(c)
            p.op("dve", (lambda v, dstb=dstb: v.tensor_sub(dstb, u.tmp[:, :n],
                                                           u.psC[:, 512:512 + n])),
                 reads=[("u", "tmp"), ("psC", 1)], writes=[out_keys(c)])


def _wsl(w2d, oc, ow=128):
    r = w2d.rearrange("(kc p) o -> p kc o", p=128)
    return r[:, :, oc * ow:(oc + 1) * ow]


_NAMECTR = [0]


def _uname(name):
    _NAMECTR[0] += 1
    return f"{name}_{_NAMECTR[0]}"


class _Scope:
    def __init__(self, p):
        self.p = p
        self.es = ExitStack()

    def __enter__(self):
        return self

    def sb(self, name, shape, dtype):
        return self.es.enter_context(
            self.p.nc.sbuf_tensor(_uname(name), shape, dtype))

    def __exit__(self, *a):
        self.p.barrier()
        self.es.close()
        return False


def _shiftoff(fc):
    return 0 if fc >= 4 else 1


def emit_stage_a(p, g, io, u, base, s):
    nc = p.nc
    a = base
    X = lambda *k: ("A", s) + k
    p.barrier()
    p.op("dve", lambda v: v.memset(a.carry[:], float(DEV.get('carry0', 0.0))), writes=[("carry",)])

    for h in range(2):
        h0 = h * TH
        with _Scope(p) as sc:
            xt = sc.sb("xt", [128, NC, 512], F32)
            xb = sc.sb("xb", [128, NC, 512], BF16)
            for tb in range(2):
                tsl = slice(h0 + tb * 512, h0 + tb * 512 + 512)
                src = io.x[s].rearrange("c p t -> p c t")[:, :, tsl]
                p.dma("qs", xt[:], src, writes=[X("xt")])
                p.dma("qa", io.x1a[s].rearrange("c p t -> p c t")[:, :, tsl],
                      xt[:], reads=[X("xt")], writes=[X("x1a", "init", h, tb)])
                p.op("act", (lambda sx, _o=xb[:], _i=xt[:]: sx.copy(_o, _i)),
                     reads=[X("xt")], writes=[X("xb")])
                if "0" in DEV.get("bar", ""):
                    p.barrier()
                emit_ln(
                    p, g, u, X("ln0", h, tb),
                    get_src=lambda c: xb[:, c, :],
                    get_src_f=lambda c: xt[:, c, :],
                    n=512,
                    out_fp32=lambda c, tb=tb: a.XF[:, c, 1 + tb * 512:1 + tb * 512 + 512],
                    out_bf16=lambda c, tb=tb: a.XFb[:, c, 1 + tb * 512:1 + tb * 512 + 512],
                    src_reads=[X("xb"), X("xt")],
                    out_keys=lambda c: ("XF", c))
                if "0" in DEV.get("bar", ""):
                    p.barrier()

        if DEV.get("skip_mixer"):
            pass
        else:
         with _Scope(p) as sc:
            wq_sl = [sc.sb(f"wq{i}", [128, NC, 128], BF16) for i in range(2)]
            wv_sl = [sc.sb(f"wv{i}", [128, NC, 128], BF16) for i in range(2)]
            lam_sb = sc.sb("lam", [128, TH], F32)
            v_hc = [sc.sb(f"vhc{i}", [128, TH], F32) for i in range(2)]
            st_hc = [sc.sb(f"sthc{i}", [128, TH], F32) for i in range(2)]
            y_bf = sc.sb("y", [128, NC, TH], BF16)
            for hc in range(NC):
                b = hc % 2
                p.dma("qg", wq_sl[b][:], _wsl(io.wqT, hc),
                      reads=[X("wq", b, "free")], writes=[X("wq", b)])
                p.dma("qg", wv_sl[b][:], _wsl(io.wvT, hc),
                      reads=[X("wv", b, "free")], writes=[X("wv", b)])
                for nh in range(2):
                    p.mm(u.psC[:, nh * 512:nh * 512 + 512],
                         g.L0a_lam[0:1, hc * 128:hc * 128 + 128],
                         g.onesrow[:], reads=[("c", "L0a_lam"), ("c", "onesrow")],
                         writes=[("psC", nh)])
                p.op("dve", (lambda v, _o=lam_sb[:]: v.tensor_copy(_o, u.psC[:])),
                     reads=[("psC", 0), ("psC", 1)], writes=[X("lam")])
                for nh in range(2):
                    for fc in range(NC):
                        p.mm(u.psA[:, nh * 512:nh * 512 + 512],
                             wv_sl[b][:, fc, :],
                             a.XFb[:, fc, 1 + nh * 512:1 + nh * 512 + 512],
                             start=(fc == 0), stop=(fc == NC - 1),
                             reads=[X("wv", b), ("XF", fc)],
                             writes=[("psA", nh)])
                p.op("dve", (lambda v, _o=v_hc[b][:]: v.tensor_copy(_o, u.psA[:])),
                     reads=[("psA", 0), ("psA", 1)],
                     writes=[X("v", b), ("psA", 0), ("psA", 1)])
                for nh in range(2):
                    for fc in range(NC):
                        p.mm(u.psB[:, nh * 512:nh * 512 + 512],
                             wq_sl[b][:, fc, :],
                             a.XFb[:, fc, 1 + nh * 512:1 + nh * 512 + 512],
                             start=(fc == 0), stop=(fc == NC - 1),
                             reads=[X("wq", b), ("XF", fc)],
                             writes=[("psB", nh)])
                p.op("dve",
                     (lambda v, _o=st_hc[b][:], _l=lam_sb[:], _v=v_hc[b][:],
                      _c=a.carry[:, hc:hc + 1]: v.tensor_tensor_scan(
                         _o, _l, _v, _c, op0=ALU.mult, op1=ALU.add)),
                     reads=[X("lam"), X("v", b), ("carry",)],
                     writes=[X("st", b)], self_wait=True, hazard=True)
                p.op("dve",
                     (lambda v, _o=a.carry[:, hc:hc + 1], _i=st_hc[b][:, TH - 1:TH]:
                      v.tensor_copy(_o, _i)),
                     reads=[X("st", b)], writes=[("carry",)])
                p.op("dve",
                     (lambda v, _o=y_bf[:, hc, :], _s=st_hc[b][:]: v.tensor_mul(
                         _o, u.psB[:], _s)),
                     reads=[("psB", 0), ("psB", 1), X("st", b)],
                     writes=[X("y", hc), X("v", b, "free"), X("wq", b, "free"),
                             X("wv", b, "free"), ("psB", 0), ("psB", 1)])
            wg_sl = [sc.sb(f"wg{i}", [128, NC, 128], BF16) for i in range(2)]
            wo_sl = [sc.sb(f"wo{i}", [128, NC, 128], BF16) for i in range(2)]
            yln_bf = sc.sb("yln", [128, NC, 512], BF16)
            gate_bf = [sc.sb(f"gt{i}", [128, 512], BF16) for i in range(2)]
            yg_bf = sc.sb("yg", [128, NC, 512], BF16)
            atile = [sc.sb(f"atl{i}", [128, 512], F32) for i in range(2)]
            for tb in range(2):
                tsl = slice(tb * 512, tb * 512 + 512)
                emit_ln(
                    p, g, u, X("lny", h, tb),
                    get_src=lambda c, tsl=tsl: y_bf[:, c, tsl],
                    n=512,
                    out_bf16=lambda c: yln_bf[:, c, :],
                    src_reads=[X("y", c2) for c2 in range(NC)],
                    out_keys=lambda c: X("yln", c),
                    affine=(lambda c: g.L0a_lng[:, c:c + 1],
                            lambda c: g.L0a_lnb[:, c:c + 1]))
                for oc in range(NC):
                    b = oc % 2
                    p.dma("qg", wg_sl[b][:], _wsl(io.wgT, oc),
                          reads=[X("wg", b, "free")], writes=[X("wg", b)])
                    for fc in range(NC):
                        p.mm(u.psA[:, 0:512], wg_sl[b][:, fc, :],
                             a.XFb[:, fc, 1 + tb * 512:1 + tb * 512 + 512],
                             start=(fc == 0), stop=(fc == NC - 1),
                             reads=[X("wg", b), ("XF", fc)],
                             writes=[("psA", 0)])
                    p.op("act", (lambda sx, _o=gate_bf[b][:]: sx.activation(
                        _o, u.psA[:, 0:512], AF.Silu)),
                        reads=[("psA", 0)],
                        writes=[X("gate", b), ("psA", 0), X("wg", b, "free")])
                    p.op("dve", (lambda v, _o=yg_bf[:, oc, :], _a=yln_bf[:, oc, :],
                                 _g=gate_bf[b][:]: v.tensor_mul(_o, _a, _g)),
                        reads=[X("gate", b), X("yln", oc)],
                        writes=[X("yg", oc)])
                for mc in range(NC):
                    b = mc % 2
                    p.dma("qg", wo_sl[b][:], _wsl(io.woT, mc),
                          reads=[X("wo", b, "free")], writes=[X("wo", b)])
                    for kc in range(NC):
                        p.mm(u.psB[:, b * 512:b * 512 + 512],
                             wo_sl[b][:, kc, :], yg_bf[:, kc, :],
                             start=(kc == 0), stop=(kc == NC - 1),
                             reads=[X("wo", b), X("yg", kc)],
                             writes=[("psB", b)])
                    p.op("dve", (lambda v, _o=atile[b][:],
                                 _i=u.psB[:, b * 512:b * 512 + 512]: v.tensor_copy(
                        _o, _i)),
                        reads=[("psB", b)],
                        writes=[X("atile", b), ("psB", b), X("wo", b, "free")])
                    if not DEV.get("skip_amix"):
                        p.dma("qg",
                              io.x1a[s, mc][:, h0 + tb * 512:h0 + tb * 512 + 512],
                              atile[b][:],
                              reads=[X("atile", b), X("x1a", "init", h, tb)],
                              writes=[X("x1a", "acc"), X("atile", b)],
                              accum_op=ALU.add)

        for l in range(DEV.get("nlayers", 2)):
            emit_a_ffn_layer(p, g, io, u, a, s, h, l)

        if not DEV.get("skip_f"):
            p.dma("qg", io.x1a[s].rearrange("c p t -> p c t")[:, :, h0:h0 + TH],
                  a.XF[:, :, 1:TH + 1],
                  reads=[("XF", c) for c in range(NC)] + [X("x1a", "acc"),
                         X("x1a", "init", h, 0), X("x1a", "init", h, 1)],
                  writes=[X("x1a", "acc")], accum_op=ALU.add)
        if DEV.get("dump_carry") == (s, h):
            p.dma("qa", io.dbg[0][:, 0:NC], a.carry[:],
                  reads=[("carry",)], writes=[("dbgc",)])
        if DEV.get("dump_xf") == (s, h):
            p.dma("qa", io.dbg.rearrange("c p t -> p c t"), a.XF[:, :, 1:TH + 1],
                  reads=[("XF", c) for c in range(NC)], writes=[("dbg",)])


def emit_a_ffn_layer(p, g, io, u, a, s, h, l):
    nc = p.nc
    X = lambda *k: ("A", s, "f", l) + k
    XA = lambda *k: ("A", s) + k
    if h == 0:
        p.op("dve", lambda v: v.memset(a.XFb[:, :, 0:1], 0.0),
             reads=[("XF", c) for c in range(NC)],
             writes=[("XFb0",)])
    else:
        p.op("dve", lambda v: v.tensor_copy(a.XFb[:, :, 0:1],
                                            a.save_qkv[:, :, l:l + 1]),
             reads=[("save_qkv",)] + [("XF", c) for c in range(NC)],
             writes=[("XFb0",)])
    p.op("dve", lambda v: v.tensor_copy(a.save_qkv[:, :, l:l + 1],
                                        a.XFb[:, :, TH:TH + 1]),
         reads=[("XF", NC - 1)], writes=[("save_qkv",)])

    with _Scope(p) as lsc:
        wo_sb = lsc.sb("wo_sb", [128, 4, 1024], BF16)
        attn_bf = lsc.sb("attn", [128, 4, 512], BF16)
        negsh = lsc.sb("negsh", [128, 1], F32)
        rrow = lsc.sb("rrow", [1, 512], F32)
        bc_sb = lsc.sb("bc_sb", [64, 512], F32)
        dbgrow = lsc.sb("dbgrow", [1, 512], F32)
        p.op("pe", (lambda t, _i=g.L0f_qs[:, l:l + 1]: t.transpose(
            u.psS[0:1, 0:128], _i, g.ident_f[:])),
             reads=[("c", "L0f_qs"), ("c", "ident_f")], writes=[("psS",)])
        if DEV.get("dump_sh") == (s, h, l):
            p.op("dve", (lambda v, _o=dbgrow[0:1, 0:128]: v.tensor_copy(
                _o, u.psS[0:1, 0:128])),
                 reads=[("psS",)], writes=[X("dbgrow")])
        p.op("dve", (lambda v, _o=rrow[0:1, 0:1]: v.reduce_max(
            _o, u.psS[0:1, 0:128], axis=AX.X, apply_absolute_value=True)),
             reads=[("psS",)], writes=[X("rrow"), ("psS",)], hazard=True)
        p.op("pe", (lambda t, _i=g.L0f_ks[:, l:l + 1]: t.transpose(
            u.psS[0:1, 0:128], _i, g.ident_f[:])),
             reads=[("c", "L0f_ks"), ("c", "ident_f"), ("psS",)], writes=[("psS",)])
        p.op("dve", (lambda v, _o=rrow[0:1, 1:2]: v.reduce_max(
            _o, u.psS[0:1, 0:128], axis=AX.X, apply_absolute_value=True)),
             reads=[("psS",)], writes=[X("rrow"), ("psS",)], hazard=True)
        p.op("dve", (lambda v, _a=rrow[0:1, 0:1], _b=rrow[0:1, 1:2]:
                     v.tensor_mul(_a, _a, _b)),
             reads=[X("rrow")], writes=[X("rrow")])
        p.mm(u.psS[0:128, 0:1], g.ones1_f[:], rrow[0:1, 0:1],
             reads=[X("rrow")], writes=[("psS",)])
        p.op("dve", (lambda v, _o=negsh[:]: v.tensor_scalar_mul(_o, u.psS[0:128, 0:1], -8.0)),
             reads=[("psS",)], writes=[X("negsh"), ("psS",)], hazard=True)
        if DEV.get("dump_sh") == (s, h, l):
            p.op("dve", (lambda v, _o=dbgrow[0:1, 256:260], _i=rrow[0:1, 0:4]:
                         v.tensor_copy(_o, _i)),
                 reads=[X("rrow")], writes=[X("dbgrow")])
            p.dma("qa", io.dbg[6][0:1, 0:512], dbgrow[:],
                  reads=[X("dbgrow")], writes=[("dbgsh",)])
        p.dma("qg", wo_sb[:], io.wo[l].rearrange("(kc p) o -> p kc o", p=128),
              writes=[X("wo_sb")])

        with _Scope(p) as qsc:
            c1q = qsc.sb("c1q", [128, T], BF16)
            c2q = qsc.sb("c2q", [128, T], BF16)
            c1k = qsc.sb("c1k", [128, T], BF16)
            c2k = qsc.sb("c2k", [128, T], BF16)
            for nm, dst, sc_col in (("c1q", c1q, g.L0f_qs),
                                    ("c2q", c2q, g.L0f_qs_sw),
                                    ("c1k", c1k, g.L0f_ks),
                                    ("c2k", c2k, g.L0f_ks_sw)):
                p.dma("qs", dst[:], io.consts[nm + "_bf"], writes=[X("tbl", nm)])
                p.op("dve",
                     (lambda v, _d=dst[:], _s=sc_col[:, l:l + 1]:
                      v.tensor_scalar_mul(_d, _d, _s)),
                     reads=[X("tbl", nm), ("c", "L0f_qs")], writes=[X("tbl", nm)],
                     hazard=True)
            qf = qsc.sb("qf", [128, 4, TH], BF16)
            kf = qsc.sb("kf", [128, 4, TH], BF16)
            vf = qsc.sb("vf", [128, 8, 8, 65], BF16)
            wqkv_sl = [qsc.sb(f"wqk{i}", [128, NC, 128], BF16) for i in range(2)]
            wqkv_v = qsc.sb("wqkvv", [128, NC, 512], BF16)
            for oc in range(8):
                b = oc % 2
                p.dma("qg", wqkv_sl[b][:], _wsl(io.wqkv[l], oc),
                      reads=[X("wsl", b, "free")], writes=[X("wsl", b)])
                for nh in range(2):
                    for fc in range(NC):
                        p.mm(u.psA[:, nh * 512:nh * 512 + 512],
                             wqkv_sl[b][:, fc, :],
                             a.XFb[:, fc, _shiftoff(fc) + nh * 512:
                                   _shiftoff(fc) + nh * 512 + 512],
                             start=(fc == 0), stop=(fc == NC - 1),
                             reads=[X("wsl", b), ("XF", fc), ("XFb0",)],
                             writes=[("psA", nh)])
                dst = qf[:, oc, :] if oc < 4 else kf[:, oc - 4, :]
                key = X("qf", oc) if oc < 4 else X("kf", oc - 4)
                p.op("act", (lambda sx, dst=dst: sx.copy(dst, u.psA[:])),
                     reads=[("psA", 0), ("psA", 1)],
                     writes=[key, ("psA", 0), ("psA", 1), X("wsl", b, "free")])
            p.dma("qg", wqkv_v[:],
                  io.wqkv[l].rearrange("(kc p) o -> p kc o", p=128)[:, :, 1024:1536],
                  writes=[X("wqv")])
            p.op("dve", (lambda v, _o=vf[:, :, :, 64:65]: v.memset(_o, 1.0)),
                 writes=[X("vf_ones")])
            for tt in range(8):
                for fc in range(NC):
                    p.mm(u.psB[:, 0:512],
                         a.XFb[:, fc, _shiftoff(fc) + tt * 128:
                               _shiftoff(fc) + tt * 128 + 128],
                         wqkv_v[:, fc, :],
                         start=(fc == 0), stop=(fc == NC - 1),
                         reads=[X("wqv"), ("XF", fc), ("XFb0",)],
                         writes=[("psB", 0)])
                p.op("dve", (lambda v, _o=vf[:, tt, :, 0:64],
                             _i=u.psB[:, 0:512].rearrange("p (hh d) -> p hh d", hh=8):
                             v.tensor_copy(_o, _i)),
                    reads=[("psB", 0)],
                    writes=[X("vf", tt), ("psB", 0)])
            if DEV.get("dump_q") == (s, h, l, "qkv"):
                for qc in range(4):
                    p.dma("qg", io.dbg[qc], qf[:, qc, :], reads=[X("qf", qc)],
                          writes=[("dbgq", qc)])
                    p.dma("qg", io.dbg[4 + qc], kf[:, qc, :], reads=[X("kf", qc)],
                          writes=[("dbgk", qc)])
            if "a" in DEV.get("bar", ""):
                p.barrier()
            for side, buf, c1, c2, scol in (
                    ("q", qf, c1q, c2q, "q"), ("k", kf, c1k, c2k, "k")):
                for qc in range(4):
                    bkey = X("rot", side, qc)
                    sq = u.sq1024
                    p.op("act", (lambda sx, buf=buf, qc=qc: sx.activation(
                        sq[:], buf[:, qc, :], AF.Square)),
                        reads=[X(side + "f", qc)], writes=[("u", "sq1024")])
                    for nh in range(2):
                        p.mm(u.psS[0:1, 0:512], g.ones64a_bf[:],
                             sq[:, nh * 512:nh * 512 + 512],
                             reads=[("u", "sq1024")], writes=[("psS",)])
                        p.mm(u.psQ[0:1, 0:512], g.ones64b_bf[:],
                             sq[:, nh * 512:nh * 512 + 512],
                             reads=[("u", "sq1024")], writes=[("psQ",)])
                        for nm, pssrc, seg in (("nA", u.psS, 0), ("nB", u.psQ, 1)):
                            pskey = ("psS",) if nm == "nA" else ("psQ",)
                            rr = u.nrows[0:1, seg * 512:seg * 512 + 512]
                            p.op("act", (lambda sx, pssrc=pssrc, rr=rr:
                                         sx.activation(rr, pssrc[0:1, 0:512],
                                                       AF.Sqrt)),
                                 reads=[pskey],
                                 writes=[("u", "nrows", nm)])
                            p.op("dve", (lambda v, rr=rr: v.tensor_scalar_max(
                                rr, rr, 1e-12)),
                                reads=[("u", "nrows", nm)], writes=[("u", "nrows", nm)],
                                hazard=True)
                            p.op("dve", (lambda v, rr=rr: v.reciprocal(rr, rr)),
                                reads=[("u", "nrows", nm)],
                                writes=[("u", "nrows", nm), pskey], hazard=True)
                        p.mm(u.psC[:, nh * 512:nh * 512 + 512], g.selA_f[:],
                             u.nrows[0:1, 0:512],
                             start=True, stop=False,
                             reads=[("u", "nrows", "nA")], writes=[("psC", nh)])
                        p.mm(u.psC[:, nh * 512:nh * 512 + 512], g.selB_f[:],
                             u.nrows[0:1, 512:1024],
                             start=False, stop=True,
                             reads=[("u", "nrows", "nB")], writes=[("psC", nh)])
                    for nh in range(2):
                        p.mm(u.psB[:, nh * 512:nh * 512 + 512], g.rot_bf[:],
                             buf[:, qc, nh * 512:nh * 512 + 512],
                             reads=[X(side + "f", qc), ("c", "rot_bf")],
                             writes=[("psB", nh)])
                    tsl = slice(h * TH, h * TH + TH)
                    p.op("dve", (lambda v, buf=buf, qc=qc, c1=c1, tsl=tsl:
                                 v.tensor_mul(u.tmp_bf[:], buf[:, qc, :],
                                              c1[:, tsl])),
                         reads=[X(side + "f", qc), X("tbl", "c1" + scol)],
                         writes=[("u", "tmp_bf")])
                    p.op("dve", (lambda v, c2=c2, tsl=tsl: v.tensor_mul(
                        u.tmp_f[:], u.psB[:], c2[:, tsl])),
                        reads=[("psB", 0), ("psB", 1),
                               X("tbl", "c2" + scol)],
                        writes=[("u", "tmp_f"), ("psB", 0), ("psB", 1)])
                    p.op("dve", lambda v: v.tensor_add(u.tmp_f[:], u.tmp_f[:],
                                                       u.tmp_bf[:]),
                         reads=[("u", "tmp_bf"), ("u", "tmp_f")],
                         writes=[("u", "tmp_f")])
                    p.op("dve", (lambda v, buf=buf, qc=qc: v.tensor_mul(
                        buf[:, qc, :], u.tmp_f[:], u.psC[:])),
                        reads=[("u", "tmp_f"), ("psC", 0), ("psC", 1)],
                        writes=[X(side + "f", qc), ("psC", 0),
                                ("psC", 1)])
            if DEV.get("dump_rn") == (s, h, l):
                p.dma("qa", io.dbg[0][0:1, 0:1024], u.nrows[:],
                      reads=[("u", "nrows", "nA"), ("u", "nrows", "nB")],
                      writes=[("dbgr",)])
                p.dma("qg", io.dbg[1], c1q[:, h * TH:h * TH + TH],
                      reads=[X("tbl", "c1q")], writes=[("dbgt1",)])
                p.dma("qg", io.dbg[2], c2q[:, h * TH:h * TH + TH],
                      reads=[X("tbl", "c2q")], writes=[("dbgt2",)])
            if DEV.get("dump_q") == (s, h, l, "rot"):
                for qc in range(4):
                    p.dma("qg", io.dbg[qc], qf[:, qc, :], reads=[X("qf", qc)],
                          writes=[("dbgq", qc)])
                    p.dma("qg", io.dbg[4 + qc], kf[:, qc, :], reads=[X("kf", qc)],
                          writes=[("dbgk", qc)])
            if DEV.get("dump_v") == (s, h, l):
                for tt in range(8):
                    p.dma("qg", io.dbg[tt % 8][:, tt // 8 * 1040: tt // 8 * 1040 + 520]
                          if False else io.dbg[tt][:, 0:520],
                          vf[:, tt, :, :], reads=[X("vf", tt), X("vf_ones")],
                          writes=[("dbgv", tt)])
            if "b" in DEV.get("bar", ""):
                p.barrier()
            PT = [qsc.sb(f"pt{i}", [128, 512], BF16) for i in range(4)]
            for wb in range(2):
                W = 2 * h + wb
                kts = [kt for kt in range((W - 1) * 4, (W - 1) * 4 + 8) if kt >= 0]
                for hh in range(8):
                    pb = (hh % 2) * 64
                    ch = hh // 2
                    qrhs = qf[pb:pb + 64, ch, wb * 512:wb * 512 + 512]
                    nkt = len(kts)
                    LAG = 2

                    def tile_info(idx):
                        kt = kts[idx]
                        loc = kt - 8 * h
                        if loc >= 0:
                            klhs = kf[pb:pb + 64, ch, loc * 128:loc * 128 + 128]
                            vtile = vf[:, loc, hh, :]
                            kread = [X("kf", ch), X("vf", loc)]
                        else:
                            sloc = loc + 4
                            klhs = a.save_k[pb:pb + 64, ch, sloc * 128:
                                            sloc * 128 + 128, l]
                            vtile = a.save_v[:, sloc, hh * 65:hh * 65 + 65, l]
                            kread = [("save_k",), ("save_v",)]
                        return kt, klhs, vtile, kread

                    for i in range(nkt + LAG):
                        if i < nkt:
                            idx = i
                            bk = idx % 4
                            kt, klhs, vtile, kread = tile_info(idx)
                            o = kt * 128 - W * 512
                            msl = g.maskband_bf[:, 384 - o:384 - o + 512]
                            pst = u.psA if bk < 2 else u.psC
                            pk = ("psA", bk) if bk < 2 else ("psC", bk - 2)
                            hf = (bk % 2) * 512
                            p.mm(pst[:, hf:hf + 512], klhs, qrhs,
                                 start=True, stop=False,
                                 reads=kread + [X("qf", ch)],
                                 writes=[pk])
                            p.mm(pst[:, hf:hf + 512], g.ident_bf[:],
                                 msl, start=False, stop=True,
                                 reads=[("c", "maskband_bf"), ("c", "ident_bf")],
                                 writes=[pk])
                            p.op("act", (lambda sx, _o=PT[bk][:],
                                _i=pst[:, hf:hf + 512],
                                _b=negsh[:, 0:1]: sx.activation(
                                _o, _i, AF.Exp, bias=_b)),
                                reads=[pk, X("negsh")],
                                writes=[X("pt", bk), pk])
                        if i >= LAG:
                            idx = i - LAG
                            bk = idx % 4
                            kt, klhs, vtile, kread = tile_info(idx)
                            p.mm(u.psB[0:65, 0:512], vtile, PT[bk][:],
                                 start=(idx == 0), stop=(idx == nkt - 1),
                                 reads=kread + [X("pt", bk)],
                                 writes=[("psB", 0), X("pt", bk)])
                    p.op("dve", (lambda v, _o=rrow[:]: v.tensor_scalar(
                        _o, u.psB[64:65, 0:512], 1e-30, None, op0=ALU.add)),
                        reads=[("psB", 0)], writes=[X("rrow")], hazard=True)
                    p.op("dve", (lambda v, _o=rrow[:]: v.reciprocal(_o, _o)),
                         reads=[X("rrow")], writes=[X("rrow")], hazard=True)
                    p.mm(u.psB[0:64, 512:1024], g.ones1_f[0:1, 0:64], rrow[:],
                         reads=[X("rrow")], writes=[("psB", 1)])
                    p.op("dve", (lambda v, _o=bc_sb[:]: v.tensor_copy(
                        _o, u.psB[0:64, 512:1024])),
                         reads=[("psB", 1)], writes=[X("bcs"), ("psB", 1)])
                    p.op("dve", (lambda v, _o=attn_bf[pb:pb + 64, ch, :],
                        _b=bc_sb[:]: v.tensor_mul(_o, u.psB[0:64, 0:512], _b)),
                        reads=[("psB", 0), X("bcs")],
                        writes=[X("at", hh), ("psB", 0)])
                for mc in range(NC):
                    bk = mc % 2
                    for kc in range(4):
                        p.mm(u.psB[:, bk * 512:bk * 512 + 512],
                             wo_sb[:, kc, mc * 128:mc * 128 + 128],
                             attn_bf[:, kc, :],
                             start=(kc == 0), stop=(kc == 3),
                             reads=[X("wo_sb"), X("at", 2 * kc), X("at", 2 * kc + 1)],
                             writes=[("psB", bk)])
                    xsl = a.XF[:, mc, 1 + wb * 512:1 + wb * 512 + 512]
                    p.op("dve", (lambda v, xsl=xsl, bk=bk: v.tensor_add(
                        xsl, u.psB[:, bk * 512:bk * 512 + 512], xsl)),
                        reads=[("psB", bk), ("XF", mc)],
                        writes=[("XF", mc), ("psB", bk)])
                    p.op("act", (lambda sx, mc=mc, wb=wb: sx.copy(
                        a.XFb[:, mc, 1 + wb * 512:1 + wb * 512 + 512],
                        a.XF[:, mc, 1 + wb * 512:1 + wb * 512 + 512])),
                        reads=[("XF", mc)], writes=[("XF", mc)])
            if "c" in DEV.get("bar", ""):
                p.barrier()
            if DEV.get("dump_attn") == (s, h, l):
                for kc in range(4):
                    p.dma("qg", io.dbg[kc][:, 0:512], attn_bf[:, kc, :],
                          reads=[X("at", 2 * kc), X("at", 2 * kc + 1)],
                          writes=[("dbga", kc)])
                p.dma("qa", io.dbg[7][:, 0:1], negsh[:],
                      reads=[X("negsh")], writes=[("dbgn",)],
                      allow_slow_non_contiguous=True)
            p.op("dve", (lambda v, _o=a.save_k[:, :, :, l],
                         _i=kf[:, :, 512:1024]: v.tensor_copy(_o, _i)),
                 reads=[X("kf", 0), X("kf", 1), X("kf", 2), X("kf", 3)],
                 writes=[("save_k",)])
            p.op("dve", (lambda v, _o=a.save_v[:, :, :, l],
                         _i=vf[:, 4:8, :, :]: v.tensor_copy(_o, _i)),
                 reads=[X("vf", tt) for tt in range(4, 8)] + [X("vf_ones")],
                 writes=[("save_v",)])

        if "d" in DEV.get("bar", ""):
            p.barrier()
        with _Scope(p) as fsc:
            hln = fsc.sb("hln", [128, NC, TH], BF16)
            if h == 0:
                p.op("dve", lambda v: v.memset(a.XF[:, :, 0:1], 0.0),
                     reads=[("XF", c) for c in range(NC)], writes=[("XFp0",)])
            else:
                p.op("dve", lambda v: v.tensor_copy(a.XF[:, :, 0:1],
                                                    a.save_ln_f[:, :, l:l + 1]),
                     reads=[("save_ln",)] + [("XF", c) for c in range(NC)],
                     writes=[("XFp0",)])
            p.op("act", lambda sx: sx.copy(a.XFb[:, :, 0:1], a.XF[:, :, 0:1]),
                 reads=[("XFp0",)], writes=[("XFb0",)])
            p.op("dve", lambda v: v.tensor_copy(a.save_ln_f[:, :, l:l + 1],
                                                a.XF[:, :, TH:TH + 1]),
                 reads=[("XF", NC - 1)], writes=[("save_ln",)])
            for tb in range(2):
                emit_ln(
                    p, g, u, X("lnf", h, tb),
                    get_src=lambda c, tb=tb: a.XFb[:, c, _shiftoff(c) + tb * 512:
                                                   _shiftoff(c) + tb * 512 + 512],
                    get_src_f=lambda c, tb=tb: a.XF[:, c, _shiftoff(c) + tb * 512:
                                                    _shiftoff(c) + tb * 512 + 512],
                    n=512,
                    out_bf16=lambda c, tb=tb: hln[:, c, tb * 512:tb * 512 + 512],
                    src_reads=[("XF", c2) for c2 in range(NC)] + [("XFb0",),
                               ("XFp0",)],
                    out_keys=lambda c: X("hln", c),
                    affine=(lambda c: g.L0f_lng[:, l, c:c + 1],
                            lambda c: g.L0f_lnb[:, l, c:c + 1]))
            if "e" in DEV.get("bar", ""):
                p.barrier()
            w1a_sl = [fsc.sb(f"w1a{i}", [128, NC, 128], BF16) for i in range(2)]
            w1g_sl = [fsc.sb(f"w1g{i}", [128, NC, 128], BF16) for i in range(2)]
            w2_sl = [fsc.sb("w2s", [128, NJ, 128], BF16)]
            t_all = fsc.sb("tall", [128, NJ, 2, 512], BF16)
            ag_bf = [fsc.sb(f"ag{i}", [128, 1024], BF16) for i in range(4)]
            for j in range(NJ):
                b = j % 2
                p.dma("qg", w1a_sl[b][:], _wsl(io.w1a[l], j),
                      reads=[X("w1a", b, "free")], writes=[X("w1a", b)])
                p.dma("qg", w1g_sl[b][:], _wsl(io.w1g[l], j),
                      reads=[X("w1g", b, "free")], writes=[X("w1g", b)])
                for tb in range(2):
                    tsl = slice(tb * 512, tb * 512 + 512)
                    pw = u.psA if tb == 0 else u.psC
                    pk0 = ("psA", 0) if tb == 0 else ("psC", 0)
                    pk1 = ("psA", 1) if tb == 0 else ("psC", 1)
                    ab = ag_bf[2 * tb + 0] if b == 0 else ag_bf[2 * tb + 1]
                    for fc in range(NC):
                        p.mm(pw[:, 0:512], w1a_sl[b][:, fc, :],
                             hln[:, fc, tsl], start=(fc == 0), stop=(fc == NC - 1),
                             reads=[X("w1a", b), X("hln", fc)],
                             writes=[pk0])
                    for fc in range(NC):
                        p.mm(pw[:, 512:1024], w1g_sl[b][:, fc, :],
                             hln[:, fc, tsl], start=(fc == 0), stop=(fc == NC - 1),
                             reads=[X("w1g", b), X("hln", fc)],
                             writes=[pk1])
                    p.op("act", (lambda sx, _o=ab[:, 0:512],
                        _i=pw[:, 0:512],
                        _b=g.L0f_b1a[:, l, j:j + 1]: sx.activation(
                        _o, _i, AF.Identity, bias=_b)),
                        reads=[pk0, ("c", "L0f_b1a")],
                        writes=[X("ag", tb, b, 0), pk0, X("w1a", b, "free")])
                    p.op("act", (lambda sx, _o=ab[:, 512:1024],
                        _i=pw[:, 512:1024],
                        _b=g.L0f_b1g[:, l, j:j + 1]: sx.activation(
                        _o, _i, AF.Gelu, bias=_b)),
                        reads=[pk1, ("c", "L0f_b1g")],
                        writes=[X("ag", tb, b, 1), pk1, X("w1g", b, "free")])
                    p.op("dve", (lambda v, _o=t_all[:, j, tb, :],
                        _a=ab[:, 0:512], _g=ab[:, 512:1024]:
                        v.tensor_mul(_o, _a, _g)),
                        reads=[X("ag", tb, b, 0), X("ag", tb, b, 1)],
                        writes=[X("tall", j, tb), X("ag", tb, b, 0),
                                X("ag", tb, b, 1)])
            for mc in range(NC):
                b = 0
                p.dma("qg", w2_sl[b][:],
                      io.w2[l].rearrange("(jc p) o -> p jc o", p=128)
                      [:, :, mc * 128:mc * 128 + 128],
                      reads=[X("w2", b, "free")], writes=[X("w2", b)])
                for tb in range(2):
                    for j in range(NJ):
                        p.mm(u.psB[:, tb * 512:tb * 512 + 512],
                             w2_sl[b][:, j, :], t_all[:, j, tb, :],
                             start=(j == 0), stop=(j == NJ - 1),
                             reads=[X("w2", b), X("tall", j, tb)],
                             writes=[("psB", tb)])
                for tb in range(2):
                    xsl = a.XF[:, mc, 1 + tb * 512:1 + tb * 512 + 512]
                    p.op("dve", (lambda v, xsl=xsl, tb=tb, mc=mc, l=l:
                                 v.scalar_tensor_tensor(
                                     xsl, u.psB[:, tb * 512:tb * 512 + 512],
                                     g.L0f_b2[:, l, mc:mc + 1], xsl,
                                     op0=ALU.add, op1=ALU.add)),
                         reads=[("psB", tb), ("XF", mc), ("c", "L0f_b2")],
                         writes=[("XF", mc), ("psB", tb), X("w2", b, "free")],
                         hazard=True)
                    p.op("act", (lambda sx, mc=mc, tb=tb: sx.copy(
                        a.XFb[:, mc, 1 + tb * 512:1 + tb * 512 + 512],
                        a.XF[:, mc, 1 + tb * 512:1 + tb * 512 + 512])),
                        reads=[("XF", mc)], writes=[("XF", mc)])


def emit_stage_b(p, g, io, u, bb):
    nc = p.nc
    with _Scope(p) as sc:
        WW = {"R": sc.sb("WR", [128, 4, 512], F32),
              "I": sc.sb("WI", [128, 4, 512], F32)}
        t1sb = sc.sb("t1sb", [32, 512], F32)
        tgsb = sc.sb("tgsb", [16, 512], F32)
        Ap = sc.sb("Ap", [128, 512], F32)
        Bp = sc.sb("Bp", [128, 512], F32)
        K = lambda *k: ("B", "w") + k
        for part in ("R", "I"):
            G1 = getattr(g, "L1f_G1" + part)
            G2 = getattr(g, "L1f_G2" + part)
            W = WW[part]
            p.op("dve", (lambda v, W=W: v.memset(W[:], 0.0)), writes=[K(part, "W")])
            for c in range(4):
                p.mm(u.psS[0:32, 0:512], G1[:, c, :], g.Yb_f[:],
                     reads=[("c", "L1f_G1" + part), ("c", "Yb_f")],
                     writes=[("psS",)])
                p.op("dve", lambda v: v.tensor_copy(t1sb[:], u.psS[0:32, 0:512]),
                     reads=[("psS",)], writes=[K("t1"), ("psS",)])
                p.mm(u.psQ[0:16, 0:512], G2[:, c, :], g.Ye_f[:],
                     reads=[("c", "L1f_G2" + part), ("c", "Ye_f")],
                     writes=[("psQ",)])
                p.op("dve", lambda v: v.tensor_copy(tgsb[:], u.psQ[0:16, 0:512]),
                     reads=[("psQ",)], writes=[K("tg"), ("psQ",)])
                for pc in range(4):
                    p.mm(u.psA[:, 0:512], g.Zd_f[:, pc, :], t1sb[:],
                         reads=[K("t1"), ("c", "Zd_f")], writes=[("psA", 0)])
                    p.op("dve", lambda v: v.tensor_copy(Ap[:], u.psA[:, 0:512]),
                         reads=[("psA", 0)], writes=[K("Ap"), ("psA", 0)])
                    p.mm(u.psB[:, 0:512], g.Zg_f[:, pc, :], tgsb[:],
                         reads=[K("tg"), ("c", "Zg_f")], writes=[("psB", 0)])
                    p.op("dve", lambda v: v.tensor_copy(Bp[:], u.psB[:, 0:512]),
                         reads=[("psB", 0)], writes=[K("Bp"), ("psB", 0)])
                    p.op("dve", lambda v: v.tensor_mul(Bp[:], Ap[:], Bp[:]),
                         reads=[K("Ap"), K("Bp")], writes=[K("Bp")])
                    p.op("dve", (lambda v, W=W, pc=pc: v.tensor_add(
                        W[:, pc, :], W[:, pc, :], Bp[:])),
                        reads=[K("Bp"), K(part, "W")], writes=[K(part, "W")])
        for ic in range(4):
            p.op("act", (lambda sx, ic=ic: sx.copy(bb.Wc[:, ic, 0:512],
                                                   WW["R"][:, ic, :])),
                 reads=[K("R", "W")], writes=[("Wc",)])
            p.op("act", (lambda sx, ic=ic: sx.copy(bb.Wc[:, ic, 512:1024],
                                                   WW["I"][:, ic, :])),
                 reads=[K("I", "W")], writes=[("Wc",)])
            p.op("act", (lambda sx, ic=ic: sx.activation(
                bb.Wc[:, 4 + ic, 0:512], WW["I"][:, ic, :], AF.Copy, scale=-1.0)),
                reads=[K("I", "W")], writes=[("Wc",)])
            p.op("act", (lambda sx, ic=ic: sx.copy(bb.Wc[:, 4 + ic, 512:1024],
                                                   WW["R"][:, ic, :])),
                 reads=[K("R", "W")], writes=[("Wc",)])

    for s in range(SEQS):
        X = lambda *k: ("B", s) + k
        with _Scope(p) as sc:
            xt = sc.sb("xt", [128, NC, 512], F32)
            xb = sc.sb("xb", [128, NC, 512], BF16)
            for tb in range(4):
                tsl = slice(tb * 512, tb * 512 + 512)
                p.dma("qs", xt[:], io.x1a[s].rearrange("c p t -> p c t")[:, :, tsl],
                      reads=[("A", s, "x1a", "acc")], writes=[X("xt")])
                p.dma("qa", io.x1b[s].rearrange("c p t -> p c t")[:, :, tsl],
                      xt[:], reads=[X("xt")], writes=[X("x1b", "init", tb)])
                p.op("act", (lambda sx, _o=xb[:], _i=xt[:]: sx.copy(_o, _i)),
                     reads=[X("xt")], writes=[X("xb")])
                emit_ln(
                    p, g, u, X("ln1", tb),
                    get_src=lambda c: xb[:, c, :],
                    get_src_f=lambda c: xt[:, c, :],
                    n=512,
                    out_bf16=lambda c, tb=tb: bb.XN1b[:, c, tsl],
                    src_reads=[X("xb"), X("xt")],
                    out_keys=lambda c: X("xn", c))
        with _Scope(p) as sc:
            f1 = [sc.sb(f"f1{i}", [128, 512], F32) for i in range(2)]
            for tb in range(4):
                tsl = slice(tb * 512, tb * 512 + 512)
                for mc in range(NC):
                    b = mc % 2
                    for ic in range(NC):
                        p.mm(u.psA[:, b * 512:b * 512 + 512],
                             bb.Wc[:, ic, mc * 128:mc * 128 + 128],
                             bb.XN1b[:, ic, tsl],
                             start=(ic == 0), stop=(ic == NC - 1),
                             reads=[("Wc",), X("xn", ic)], writes=[("psA", b)])
                    p.op("act", (lambda sx, b=b, mc=mc: sx.activation(
                        f1[b][:], u.psA[:, b * 512:b * 512 + 512], AF.Identity,
                        bias=g.L1f_bri[:, mc:mc + 1])),
                        reads=[("psA", b), ("c", "L1f_bri")],
                        writes=[X("f1", b), ("psA", b)])
                    p.dma("qg", io.x1b[s, mc][:, tsl], f1[b][:],
                          reads=[X("f1", b), X("x1b", "init", tb)],
                          writes=[X("x1b", "acc"), X("f1", b)],
                          accum_op=ALU.add)
        with _Scope(p) as sc:
            wsl = [sc.sb(f"wsl{i}", [128, NC, 128], BF16) for i in range(2)]
            stage = sc.sb("stage", [128, T], BF16)
            for oc in range(16):
                b = oc % 2
                p.dma("qg", wsl[b][:], _wsl(io.l1wqkv, oc),
                      reads=[X("wsl", b, "free")], writes=[X("wsl", b)])
                for tq in range(4):
                    bk = tq % 2
                    for fc in range(NC):
                        p.mm(u.psA[:, bk * 512:bk * 512 + 512],
                             wsl[b][:, fc, :],
                             bb.XN1b[:, fc, tq * 512:tq * 512 + 512],
                             start=(fc == 0), stop=(fc == NC - 1),
                             reads=[X("wsl", b), X("xn", fc)],
                             writes=[("psA", bk)])
                    p.op("act", (lambda sx, tq=tq, bk=bk: sx.copy(
                        stage[:, tq * 512:tq * 512 + 512],
                        u.psA[:, bk * 512:bk * 512 + 512])),
                        reads=[("psA", bk)],
                        writes=[X("stage", tq), ("psA", bk)])
                dst = io.qf[s, oc] if oc < 8 else io.kf[s, oc - 8]
                p.dma("qa", dst, stage[:],
                      reads=[X("stage", 0), X("stage", 1), X("stage", 2),
                             X("stage", 3)],
                      writes=[X("qk_out", oc), X("stage", 0), X("stage", 1),
                              X("stage", 2), X("stage", 3),
                              X("wsl", b, "free")])
            wqv = sc.sb("wqv", [128, NC, 1024], BF16)
            v_sb = sc.sb("v_sb", [128, 16, 65], BF16)
            p.dma("qg", wqv[:],
                  io.l1wqkv.rearrange("(kc p) o -> p kc o", p=128)[:, :, 2048:3072],
                  writes=[X("wqv")])
            p.op("dve", lambda v: v.memset(v_sb[:, :, 64:65], 1.0),
                 writes=[X("vones")])
            for tt in range(16):
                for vh in range(2):
                    for fc in range(NC):
                        p.mm(u.psB[:, vh * 512:vh * 512 + 512],
                             bb.XN1b[:, fc, tt * 128:tt * 128 + 128],
                             wqv[:, fc, vh * 512:vh * 512 + 512],
                             start=(fc == 0), stop=(fc == NC - 1),
                             reads=[X("wqv"), X("xn", fc)],
                             writes=[("psB", vh)])
                    p.op("dve", (lambda v, tt=tt, vh=vh: v.tensor_copy(
                        v_sb[:, vh * 8:vh * 8 + 8, 0:64],
                        u.psB[:, vh * 512:vh * 512 + 512].rearrange(
                            "p (hh d) -> p hh d", hh=8))),
                        reads=[("psB", vh)],
                        writes=[X("vsb", vh), ("psB", vh)])
                p.dma("qa", io.vf[s, tt], v_sb[:],
                      reads=[X("vsb", 0), X("vsb", 1), X("vones")],
                      writes=[X("v_out", tt), X("vsb", 0), X("vsb", 1)])


def emit_stage_c(p, g, io, u, s):
    nc = p.nc
    X = lambda *k: ("C", s) + k
    with _Scope(p) as sc:
        q_sb = sc.sb("q_sb", [128, NC, T], BF16)
        k_sb = sc.sb("k_sb", [128, NC, T], BF16)
        v_sb = sc.sb("v_sb", [128, 16, 1040], BF16)
        wproj = sc.sb("wproj", [128, NC, 1024], BF16)
        negsh = sc.sb("negsh", [128, 1], F32)
        rrow = sc.sb("rrow", [1, 512], F32)
        bc_sb = sc.sb("bc_sb", [64, 512], F32)
        phi_sb = [sc.sb("phi0", [128, 16, 512], BF16)]
        attn_bf = sc.sb("attn", [128, NC, 512], BF16)
        xt = [sc.sb("xt0", [128, NC, 512], F32)]
        out_sb = xt
        PT = [sc.sb(f"pt{i}", [128, 512], BF16) for i in range(4)]
        mrow = sc.sb("mrow", [1, 4], F32)

        bq = [("B", s, "qk_out", oc) for oc in range(16)]
        bv = [("B", s, "v_out", tt) for tt in range(16)]
        p.dma("qs", q_sb[:], io.qf[s].rearrange("c p t -> p c t"),
              reads=bq, writes=[X("q")])
        p.dma("qs", k_sb[:], io.kf[s].rearrange("c p t -> p c t"),
              reads=bq, writes=[X("k")])
        p.dma("qs", v_sb[:], io.vf[s].rearrange("tt p x -> p tt x"),
              reads=bv, writes=[X("v")])
        p.dma("qg", wproj[:], io.l1wproj.rearrange("(kc p) o -> p kc o", p=128),
              writes=[X("wp")])
        p.op("dve", lambda v: v.memset(mrow[:], 0.0), writes=[X("m0"), X("m1")])
        for side, buf, bkey, mi in ((0, q_sb, X("q"), 0), (1, k_sb, X("k"), 1)):
            for c in range(NC):
                for hf in range(2):
                    p.op("act", (lambda sx, buf=buf, c=c, hf=hf: sx.activation(
                        u.sq1024[:], buf[:, c, hf * 1024:hf * 1024 + 1024],
                        AF.Square)),
                        reads=[bkey], writes=[("u", "sq1024")])
                    for sub in range(2):
                        p.mm(u.psS[0:1, 0:512], g.ones_bf[:],
                             u.sq1024[:, sub * 512:sub * 512 + 512],
                             reads=[("u", "sq1024")], writes=[("psS",)])
                        p.op("dve", lambda v: v.reduce_max(
                            rrow[0:1, 0:1], u.psS[0:1, 0:512], axis=AX.X),
                            reads=[("psS",)], writes=[X("rr"), ("psS",)], hazard=True)
                        p.op("dve", (lambda v, mi=mi: v.tensor_max(
                            mrow[0:1, mi:mi + 1], mrow[0:1, mi:mi + 1],
                            rrow[0:1, 0:1])),
                            reads=[X("rr"), X("m%d" % mi)],
                            writes=[X("m%d" % mi)])
        p.op("act", lambda sx: sx.activation(mrow[0:1, 2:3], mrow[0:1, 0:1],
                                             AF.Sqrt),
             reads=[X("m0")], writes=[X("m2")])
        p.op("act", lambda sx: sx.activation(mrow[0:1, 3:4], mrow[0:1, 1:2],
                                             AF.Sqrt),
             reads=[X("m1")], writes=[X("m3")])
        p.op("dve", lambda v: v.tensor_mul(mrow[0:1, 2:3], mrow[0:1, 2:3],
                                           mrow[0:1, 3:4]),
             reads=[X("m2"), X("m3")], writes=[X("m2")])
        p.mm(u.psS[0:128, 0:1], g.ones1_f[:], mrow[0:1, 2:3],
             reads=[X("m2")], writes=[("psS",)])
        p.op("dve", lambda v: v.tensor_scalar_mul(negsh[:], u.psS[0:128, 0:1],
                                                  -0.125),
             reads=[("psS",)], writes=[X("negsh"), ("psS",)], hazard=True)
        for qt in range(4):
            bq_ = 0
            qsl = slice(qt * 512, qt * 512 + 512)
            p.dma("qs", phi_sb[bq_][:],
                  io.phi[qt].rearrange("kt p x -> p kt x"),
                  writes=[X("phi", bq_)])
            p.dma("qs", xt[bq_][:],
                  io.x1b[s].rearrange("c p t -> p c t")[:, :, qsl],
                  reads=[("B", s, "x1b", "acc")] +
                        [("B", s, "x1b", "init", tb) for tb in range(4)],
                  writes=[X("xt", bq_)] +
                         [X("osb", bq_, mc) for mc in range(NC)])
            for hh in range(16):
                pb = (hh % 2) * 64
                ch = hh // 2
                qrhs = q_sb[pb:pb + 64, ch, qsl]
                LAG = 2
                for i in range(16 + LAG):
                    if i < 16:
                        kt = i
                        bk = kt % 4
                        pst = u.psA if bk < 2 else u.psC
                        pk = ("psA", bk) if bk < 2 else ("psC", bk - 2)
                        hf = (bk % 2) * 512
                        klhs = k_sb[pb:pb + 64, ch, kt * 128:kt * 128 + 128]
                        p.mm(pst[:, hf:hf + 512], klhs, qrhs,
                             start=True, stop=False,
                             reads=[X("q"), X("k")], writes=[pk])
                        p.mm(pst[:, hf:hf + 512], g.ident_bf[:],
                             phi_sb[bq_][:, kt, :], start=False, stop=True,
                             reads=[X("phi", bq_), ("c", "ident_bf")],
                             writes=[pk])
                        p.op("act", (lambda sx, _o=PT[bk][:],
                            _i=pst[:, hf:hf + 512],
                            _b=negsh[:, 0:1]: sx.activation(
                            _o, _i, AF.Exp, bias=_b, scale=0.125)),
                            reads=[pk, X("negsh")],
                            writes=[X("pt", bk), pk])
                    if i >= LAG:
                        kt = i - LAG
                        bk = kt % 4
                        p.mm(u.psB[0:65, 0:512],
                             v_sb[:, kt, hh * 65:hh * 65 + 65], PT[bk][:],
                             start=(kt == 0), stop=(kt == 15),
                             reads=[X("v"), X("pt", bk)],
                             writes=[("psB", 0), X("pt", bk)])
                p.op("dve", lambda v: v.tensor_scalar(
                    rrow[:], u.psB[64:65, 0:512], 1e-30, None, op0=ALU.add),
                    reads=[("psB", 0)], writes=[X("rr")], hazard=True)
                p.op("dve", lambda v: v.reciprocal(rrow[:], rrow[:]),
                     reads=[X("rr")], writes=[X("rr")], hazard=True)
                p.mm(u.psB[0:64, 512:1024], g.ones1_f[0:1, 0:64], rrow[:],
                     reads=[X("rr")], writes=[("psB", 1)])
                p.op("dve", lambda v: v.tensor_copy(bc_sb[:],
                                                    u.psB[0:64, 512:1024]),
                     reads=[("psB", 1)], writes=[X("bcs"), ("psB", 1)])
                p.op("dve", (lambda v, pb=pb, ch=ch: v.tensor_mul(
                    attn_bf[pb:pb + 64, ch, :], u.psB[0:64, 0:512], bc_sb[:])),
                    reads=[("psB", 0), X("bcs")],
                    writes=[X("at", hh), ("psB", 0)])
            for mc in range(NC):
                bk = mc % 2
                for kc in range(NC):
                    p.mm(u.psC[:, bk * 512:bk * 512 + 512],
                         wproj[:, kc, mc * 128:mc * 128 + 128],
                         attn_bf[:, kc, :],
                         start=(kc == 0), stop=(kc == NC - 1),
                         reads=[X("wp"), X("at", 2 * kc), X("at", 2 * kc + 1)],
                         writes=[("psC", bk)])
                p.op("dve", (lambda v, bk=bk, mc=mc, bq_=bq_: v.scalar_tensor_tensor(
                    out_sb[bq_][:, mc, :], u.psC[:, bk * 512:bk * 512 + 512],
                    g.L1_bproj[:, mc:mc + 1], xt[bq_][:, mc, :],
                    op0=ALU.add, op1=ALU.add)),
                    reads=[("psC", bk), X("xt", bq_), ("c", "L1_bproj")],
                    writes=[X("osb", bq_, mc), ("psC", bk)], hazard=True)
            p.dma("qa", io.out[s].rearrange("c p t -> p c t")[:, :, qsl],
                  out_sb[bq_][:],
                  reads=[X("osb", bq_, mc) for mc in range(NC)],
                  writes=[X("out", qt)] + [X("osb", bq_, mc) for mc in range(NC)]
                  + [X("xt", bq_)])


def build_nc(stages, shared):
    nc = bass.Bass("TRN2", target_bir_lowering=False, debug=False)
    p = Prog(nc)
    io = H()
    declared = {}

    def din(name, shape, dtype):
        t = nc.dram_tensor(name, list(shape), dtype, kind="ExternalInput").ap()
        declared[name] = "in"
        return t

    def dout(name, shape, dtype):
        t = nc.dram_tensor(name, list(shape), dtype, kind="ExternalOutput").ap()
        declared[name] = "out"
        return t

    def dscratch(name, shape, dtype, produced, consumed):
        if produced and consumed:
            t = nc.dram_tensor(name, list(shape), dtype, kind="Internal").ap()
            declared[name] = "internal"
            return t
        if produced:
            return dout(name, shape, dtype)
        return din(name, shape, dtype)

    A = "A" in stages
    B = "B" in stages
    C = "C" in stages
    io.consts = {}
    cshapes = {}
    for k, v in shared.items():
        cshapes[k] = v
    const_names = ["ones_bf", "ones64a_bf", "ones64b_bf", "ones1_f", "selA_f",
                   "selB_f", "ident_bf", "ident_f", "rot_bf", "maskband_bf",
                   "L0a_lng", "L0a_lnb", "L0a_lam", "L0f_b1a", "L0f_b1g",
                   "L0f_b2", "L0f_lng", "L0f_lnb", "L0f_qs", "L0f_ks",
                   "L0f_qs_sw", "L0f_ks_sw", "L1_bproj", "L1f_bri",
                   "L1f_G1R", "L1f_G1I", "L1f_G2R", "L1f_G2I",
                   "Yb_f", "Ye_f", "Zd_f", "Zg_f",
                   "c1q_bf", "c2q_bf", "c1k_bf", "c2k_bf"]
    for nm in const_names:
        arr = shared[nm]
        dt = BF16 if arr.dtype == ml_dtypes.bfloat16 else F32
        io.consts[nm] = din("K_" + nm, arr.shape, dt)
    hd = io.consts

    if A and any(k in DEV for k in ("dump_xf", "dump_carry", "dump_q", "dump_v", "dump_attn", "dump_rn", "dump_sh")):
        io.dbg = dout("dbg", [NC, 128, TH], F32)
    if A:
        io.x = din("x", [SEQS, NC, 128, T], F32)
        io.wqT = din("L0a_WqT", [D, D], F32)
        io.wvT = din("L0a_WvT", [D, D], F32)
        io.wgT = din("L0a_WgT", [D, D], F32)
        io.woT = din("L0a_WoT", [D, D], F32)
        io.wqkv = din("L0f_wqkv", [2, D, 1536], F32)
        io.w1a = din("L0f_w1a", [2, D, HIDP], F32)
        io.w1g = din("L0f_w1g", [2, D, HIDP], F32)
        io.w2 = din("L0f_w2", [2, HIDP, D], F32)
        io.wo = din("L0f_wo", [2, 512, D], F32)
    if A or B:
        io.x1a = dscratch("x1a", [SEQS, NC, 128, T], F32, produced=A, consumed=B)
    if B:
        io.l1wqkv = din("L1_wqkv", [D, 3 * D], F32)
    if B or C:
        io.x1b = dscratch("x1b", [SEQS, NC, 128, T], F32, produced=B, consumed=C)
        io.qf = dscratch("qf", [SEQS, NC, 128, T], BF16, produced=B, consumed=C)
        io.kf = dscratch("kf", [SEQS, NC, 128, T], BF16, produced=B, consumed=C)
        io.vf = dscratch("vf", [SEQS, 16, 128, 16 * 65], BF16,
                         produced=B, consumed=C)
    if C:
        io.l1wproj = din("L1_wproj", [D, D], F32)
        io.phi = din("K_phi8_bf", [4, 16, 128, 512], BF16)
        io.out = dout("out", [SEQS, NC, 128, T], F32)
    u = H()
    u.psA = p.ps("psA", [128, 1024])
    u.psB = p.ps("psB", [128, 1024])
    u.psC = p.ps("psC", [128, 1024])
    u.psS = p.ps("psS", [128, 512])
    u.psQ = p.ps("psQ", [128, 512])
    u.sq = p.sb("u_sq", [128, 512], BF16)
    u.sq1024 = p.sb("u_sq1024", [128, 1024], BF16)
    u.tmp = p.sb("u_tmp", [128, 512], F32)
    u.tmp_f = p.sb("u_tmp_f", [128, 1024], F32)
    u.tmp_bf = p.sb("u_tmp_bf", [128, 1024], BF16)
    u.rows = p.sb("u_rows", [1, 1536], F32)
    u.nrows = p.sb("u_nrows", [1, 1024], F32)
    u.eps = p.sb("u_eps", [1, 1], F32)
    p.op("dve", lambda v: v.memset(u.eps[:], 1e-5), writes=[("u", "eps")])
    g = declare_consts(p, nc, io.consts)
    if A:
        with _Scope(p) as basesc:
            base = H()
            base.XF = basesc.sb("XF", [128, NC, TH + 1], F32)
            base.save_ln_f = basesc.sb("save_ln_f", [128, NC, 2], F32)
            base.XFb = basesc.sb("XFb", [128, NC, TH + 1], BF16)
            base.carry = basesc.sb("carry", [128, NC], F32)
            base.save_qkv = basesc.sb("save_qkv", [128, NC, 2], BF16)
            base.save_ln = basesc.sb("save_ln", [128, NC, 2], BF16)
            base.save_k = basesc.sb("save_k", [128, 4, 512, 2], BF16)
            base.save_v = basesc.sb("save_v", [128, 4, 520, 2], BF16)
            for s in range(DEV.get("nseq", SEQS)):
                emit_stage_a(p, g, io, u, base, s)
    if B:
        with _Scope(p) as bsc:
            bb = H()
            bb.Wc = bsc.sb("Wc", [128, NC, 1024], BF16)
            bb.XN1b = bsc.sb("XN1b", [128, NC, T], BF16)
            emit_stage_b(p, g, io, u, bb)
    if C:
        for s in range(SEQS):
            emit_stage_c(p, g, io, u, s)
    global LAST_PROG
    LAST_PROG = p
    p.finalize_and_emit()
    in_names = [k for k, v in declared.items() if v == "in"]
    out_names = [k for k, v in declared.items() if v == "out"]
    return nc, in_names, out_names


def _run(nc, in_names, maps):
    in_maps = [{k: m[k] for k in in_names} for m in maps]
    res = run_bass_kernel_spmd(nc, in_maps, core_ids=list(range(NCORES)))
    return res.results


def kernel(**inputs):
    shared, per_core = host_prep(inputs)
    smap = {}
    for k, v in shared.items():
        smap["K_" + k] = v
    smap["K_phi8_bf"] = shared["phi8_bf"]
    smap.update({
        "L0a_WqT": shared["L0a_WqT"], "L0a_WvT": shared["L0a_WvT"],
        "L0a_WgT": shared["L0a_WgT"], "L0a_WoT": shared["L0a_WoT"],
        "L0f_wqkv": shared["L0f_wqkv"], "L0f_w1a": shared["L0f_w1a"],
        "L0f_w1g": shared["L0f_w1g"], "L0f_w2": shared["L0f_w2"],
        "L0f_wo": shared["L0f_wo"], "L1_wqkv": shared["L1_wqkv"],
        "L1_wproj": shared["L1_wproj"],
    })
    if UNFUSED:
        stage_sets = [("A",), ("B",), ("C",)]
    else:
        stage_sets = [("A", "B", "C")]
    core_maps = [dict(smap, **pc) for pc in per_core]
    outs = None
    for stages in stage_sets:
        nc, in_names, out_names = build_nc(stages, shared)
        results = _run(nc, in_names, core_maps)
        for core in range(NCORES):
            for onm in out_names:
                core_maps[core][onm] = np.asarray(results[core][onm])
        outs = results
    out = np.empty((16, T, D), np.float32)
    for core in range(NCORES):
        o = np.asarray(outs[core]["out"], np.float32).reshape(SEQS, D, T)
        for s in range(SEQS):
            out[core * SEQS + s] = o[s].T
    return out


# version 06 (final)
# speedup vs baseline: 1.4260x; 1.0085x over previous
import math
from contextlib import ExitStack

import numpy as np
import ml_dtypes

import concourse.bass as bass
import concourse.mybir as mybir
from concourse.bass_utils import run_bass_kernel_spmd

F32 = mybir.dt.float32
BF16 = mybir.dt.bfloat16
AF = mybir.ActivationFunctionType
ALU = mybir.AluOpType
AX = mybir.AxisListType

NCORES = 8
SEQS = 2
T = 2048
D = 1024
NC = 8
TH = 1024
HID = 2730
HIDP = 2816
NJ = HIDP // 128
NEG = -1.0e30

UNFUSED = False
DEV = {}
LAST_PROG = None

COMPUTE_STREAMS = ("pe", "act", "dve", "pool", "sp")
QUEUE_ISSUER = {"qs": "sp", "qa": "act", "qg": "pool"}
QUEUE_LANES = {"qs": 8, "qa": 6, "qg": 12}
LANE_STREAMS = [(q, r) for q, R in QUEUE_LANES.items() for r in range(R)]
STREAM_ENGINE = {"pe": "tensor", "act": "scalar", "dve": "vector",
                 "pool": "gpsimd", "sp": "sync"}


def _is_lane(s):
    return isinstance(s, tuple)


class _Op:
    __slots__ = ("stream", "issuer", "emit", "waits", "count", "inc", "tag")

    def __init__(self, stream, issuer, emit, waits, count, tag=None):
        self.stream = stream
        self.issuer = issuer
        self.emit = emit
        self.waits = waits
        self.count = count
        self.inc = False
        self.tag = tag


class Prog:
    def __init__(self, nc):
        self.nc = nc
        self.es = ExitStack()
        self.counts = {s: 0 for s in COMPUTE_STREAMS}
        self.counts.update({ls: 0 for ls in LANE_STREAMS})
        self.qidx = {q: 0 for q in QUEUE_ISSUER}
        self.ops = {s: [] for s in COMPUTE_STREAMS}
        self.vc = {s: {} for s in COMPUTE_STREAMS}
        self.res = {}
        self.sems = {}
        self.hazards = {s: set() for s in COMPUTE_STREAMS}
        self.self_hw = {s: 0 for s in COMPUTE_STREAMS}

    def sb(self, name, shape, dtype):
        return self.es.enter_context(
            self.nc.sbuf_tensor(_uname(name), shape, dtype))

    def ps(self, name, shape, dtype=F32):
        return self.es.enter_context(
            self.nc.psum_tensor(_uname(name), shape, dtype))

    @staticmethod
    def _merge(dst, src):
        for k, v in src.items():
            if dst.get(k, 0) < v:
                dst[k] = v

    def op(self, stream, emit, reads=(), writes=(), queue=None, tag=None,
           self_wait=False, hazard=False):
        lane_prewait = None
        if queue is not None:
            issuer = QUEUE_ISSUER[queue]
            r = self.qidx[queue] % QUEUE_LANES[queue]
            self.qidx[queue] += 1
            cstream = (queue, r)
            step = 16
            if self.counts[cstream] > 0:
                lane_prewait = (cstream, self.counts[cstream])
        else:
            issuer = stream
            cstream = stream
            step = 1
        deps = []
        bar = self.res.pop(("__bar__", issuer), None)
        if bar is not None and bar[0] is not None:
            deps.append(bar[0])
        for r in reads:
            st = self.res.get(r)
            if st is not None and st[0] is not None:
                deps.append(st[0])
        for w in writes:
            st = self.res.get(w)
            if st is not None:
                if st[0] is not None:
                    deps.append(st[0])
                deps.extend(st[1])
        need = {}
        for st in deps:
            self._merge(need, st[2])
        self_tgt = 0
        if queue is None:
            for st in deps:
                if st[0] == issuer:
                    if issuer == "pe" and not self_wait:
                        continue
                    self_tgt = max(self_tgt, st[1])
        ivc = self.vc[issuer]
        waits = {}
        if lane_prewait is not None:
            s, n = lane_prewait
            if ivc.get(s, 0) < n:
                waits[s] = n
                self._merge(ivc, {s: n})
        for s, n in need.items():
            if s == issuer and queue is None:
                if self_tgt > 0 and self.self_hw[issuer] < self_tgt:
                    waits[s] = self_tgt
                    self.self_hw[issuer] = self_tgt
                continue
            if ivc.get(s, 0) < n:
                waits[s] = n
        self._merge(ivc, need)
        self.counts[cstream] += step
        c = self.counts[cstream]
        if queue is None:
            ivc[cstream] = c
        rec = _Op(cstream, issuer, emit, waits, c,
                  tag or (tuple(writes)[:1] or ("?",))[0])
        self.ops[issuer].append(rec)
        if hazard and queue is None:
            self.hazards[cstream].add(c)
        if queue is None:
            svc = dict(ivc)
        else:
            svc = {s: n for s, n in ivc.items() if s != issuer}
            svc[cstream] = c
        stamp = (cstream, c, svc)
        for w in writes:
            self.res[w] = (stamp, [])
        for r in reads:
            st = self.res.get(r)
            if st is None:
                self.res[r] = (None, [stamp])
            else:
                st[1].append(stamp)
        return stamp

    def dma(self, queue, out_ap, in_ap, reads=(), writes=(), **kw):
        return self.op(
            None,
            (lambda eng, o=out_ap, i=in_ap, kw=kw: eng.dma_start(out=o, in_=i, **kw)),
            reads=reads, writes=writes, queue=queue)

    def mm(self, out, lhsT, rhs, start=True, stop=True, reads=(), writes=()):
        return self.op(
            "pe",
            (lambda t, o=out, l=lhsT, r=rhs, st=start, sp=stop:
             t.matmul(o, l, r, start=st, stop=sp)),
            reads=reads, writes=writes)

    def barrier(self):
        cur = {s: c for s, c in self.counts.items() if c > 0 and s != "sp"}
        need = dict(cur)
        ivc = self.vc["sp"]
        waits = {s: n for s, n in need.items() if ivc.get(s, 0) < n}
        self._merge(ivc, need)
        self.counts["sp"] += 1
        c = self.counts["sp"]
        ivc["sp"] = c
        rec = _Op("sp", "sp", (lambda eng: eng.nop()), waits, c)
        self.ops["sp"].append(rec)
        bstamp = ("sp", c, dict(ivc))
        for s in COMPUTE_STREAMS:
            if s != "sp":
                self.res[("__bar__", s)] = (bstamp, [])
        return bstamp

    def finalize_and_emit(self):
        nc = self.nc
        finals = {s: c for s, c in self.counts.items() if c > 0 and s != "sp"}
        targets = {s: set() for s in self.counts}
        for lst in self.ops.values():
            for o in lst:
                for s, n in o.waits.items():
                    targets[s].add(n)
        for s, c in finals.items():
            targets[s].add(c)
        for ls in LANE_STREAMS:
            targets[ls] = set(range(16, self.counts[ls] + 1, 16))
        miles = {s: sorted(ts) for s, ts in targets.items()}
        import bisect

        def remap(s, n):
            return bisect.bisect_right(miles[s], n) * (16 if _is_lane(s) else 1)

        mileset = {s: set(ts) for s, ts in miles.items()}
        for lst in self.ops.values():
            for o in lst:
                if o.count in mileset[o.stream]:
                    o.inc = True
        for s, ts in miles.items():
            if ts:
                nm = "sem_" + ("_".join(map(str, s)) if _is_lane(s) else str(s))
                self.sems[s] = self.es.enter_context(nc.semaphore(nm))
        block = self.es.enter_context(nc.Block())

        def emit_stream(eng_stream):
            ops = self.ops[eng_stream]
            if not ops and eng_stream != "sp":
                return
            dec = getattr(block, STREAM_ENGINE[eng_stream])

            def body(eng):
                for o in ops:
                    for s, n in sorted(o.waits.items(), key=lambda x: str(x[0])):
                        eng.wait_ge(self.sems[s], remap(s, n))
                    instr = o.emit(eng)
                    if o.inc:
                        instr.then_inc(self.sems[o.stream],
                                       16 if _is_lane(o.stream) else 1)
                if eng_stream == "sp":
                    for s, c in sorted(finals.items(), key=lambda x: str(x[0])):
                        if c and self.vc["sp"].get(s, 0) < c:
                            eng.wait_ge(self.sems[s], remap(s, c))
            dec(body)

        for s in ("sp", "pe", "act", "dve", "pool"):
            emit_stream(s)


def _bf(x):
    return np.asarray(x, np.float32).astype(ml_dtypes.bfloat16)


def host_constants():
    C = {}
    C["ones_bf"] = _bf(np.ones((128, 1)))
    on64a = np.zeros((128, 1), np.float32); on64a[:64] = 1.0
    on64b = np.zeros((128, 1), np.float32); on64b[64:] = 1.0
    C["ones64a_bf"] = _bf(on64a)
    C["ones64b_bf"] = _bf(on64b)
    C["ones1_f"] = np.ones((1, 128), np.float32)
    sa = np.zeros((1, 128), np.float32); sa[0, :64] = 1.0
    sb_ = np.zeros((1, 128), np.float32); sb_[0, 64:] = 1.0
    C["selA_f"] = sa
    C["selB_f"] = sb_
    C["ident_bf"] = _bf(np.eye(128))
    C["ident_f"] = np.eye(128, dtype=np.float32)
    R = np.zeros((128, 128), np.float32)
    for base in (0, 64):
        for i in range(32):
            R[base + 32 + i, base + i] = -1.0
            R[base + i, base + 32 + i] = 1.0
    C["rot_bf"] = _bf(R)
    DH = 64
    WINDOW = 512
    n = T
    half_idx = np.arange(0, DH, 2, dtype=np.float64)
    inv_freq = 1.0 / 10000.0 ** (half_idx / DH)
    t = np.arange(n, dtype=np.float64)
    freqs = t[:, None] * inv_freq[None, :]
    freqs = np.concatenate([freqs, freqs], axis=-1)
    base = (half_idx + 0.4 * DH) / (1.4 * DH)
    power = (t - n // 2) / float(WINDOW // 2)
    scale = base[None, :] ** power[:, None]
    scale = np.concatenate([scale, scale], axis=-1)
    cos, sin = np.cos(freqs), np.sin(freqs)
    rep = lambda a: np.tile(np.ascontiguousarray(a.T), (2, 1))
    C["c1q_bf"] = _bf(rep(cos * scale * 8.0))
    C["c2q_bf"] = _bf(rep(sin * scale * 8.0))
    C["c1k_bf"] = _bf(rep(cos / scale))
    C["c2k_bf"] = _bf(rep(sin / scale))
    W = 1408
    jj = np.arange(128)[:, None]
    y = np.arange(W)[None, :] - 384
    band = (y - jj >= 0) & (y - jj <= 512)
    C["maskband_bf"] = _bf(np.where(band, 0.0, NEG))
    PHI = (1.0 + math.sqrt(5.0)) / 2.0
    pos = (np.arange(T, dtype=np.float32) * np.float32(PHI)) % np.float32(1.0)
    dist = np.abs(pos[:, None] - pos[None, :])
    phi = np.log(np.exp(-dist) + 1e-8).astype(np.float32) * 8.0
    phiT = np.ascontiguousarray(phi.T)
    C["phi8_bf"] = _bf(
        phiT.reshape(16, 128, 4, 512).transpose(2, 0, 1, 3))
    Yb = np.zeros((32, 512), np.float32)
    Ye = np.zeros((16, 512), np.float32)
    for idx in range(512):
        Yb[idx // 16, idx] = 1.0
        Ye[idx % 16, idx] = 1.0
    Zd = np.zeros((32, 4, 128), np.float32)
    Zg = np.zeros((16, 4, 128), np.float32)
    for pc in range(4):
        for col in range(128):
            idx = pc * 128 + col
            Zd[idx // 16, pc, col] = 1.0
            Zg[idx % 16, pc, col] = 1.0
    C["Yb_f"] = Yb
    C["Ye_f"] = Ye
    C["Zd_f"] = Zd
    C["Zg_f"] = Zg
    return C


def _pp(vec, nch):
    return np.ascontiguousarray(np.asarray(vec, np.float32).reshape(nch, 128).T)


def host_prep(inputs):
    ix = {k: np.asarray(v) for k, v in inputs.items()}
    shared = dict(host_constants())
    for nm in ("Wq", "Wv", "Wg", "Wo"):
        shared["L0a_" + nm + "T"] = np.ascontiguousarray(ix["L0_attn_" + nm].T)
    shared["L0a_lng"] = _pp(ix["L0_attn_ln_gamma"], 8)
    shared["L0a_lnb"] = _pp(ix["L0_attn_ln_beta"], 8)
    shared["L0a_lam"] = np.ascontiguousarray(
        np.repeat(ix["L0_attn_decay"].astype(np.float32), 64)[None, :])
    shared["L0f_wqkv"] = np.ascontiguousarray(ix["L0_ffn_wqkv"])
    w1 = ix["L0_ffn_w1"]
    w1a = np.zeros((2, 1024, HIDP), np.float32)
    w1g = np.zeros((2, 1024, HIDP), np.float32)
    w1a[:, :, :HID] = w1[:, :, :HID]
    w1g[:, :, :HID] = w1[:, :, HID:]
    shared["L0f_w1a"] = w1a
    shared["L0f_w1g"] = w1g
    w2 = ix["L0_ffn_w2"]
    w2p = np.zeros((2, HIDP, 1024), np.float32)
    w2p[:, :HID, :] = w2
    shared["L0f_w2"] = w2p
    shared["L0f_wo"] = np.ascontiguousarray(ix["L0_ffn_wo"])
    b1 = ix["L0_ffn_b1"]
    b1a = np.zeros((2, HIDP), np.float32); b1a[:, :HID] = b1[:, :HID]
    b1g = np.zeros((2, HIDP), np.float32); b1g[:, :HID] = b1[:, HID:]
    shared["L0f_b1a"] = np.stack([_pp(b1a[l], NJ) for l in range(2)], 1)
    shared["L0f_b1g"] = np.stack([_pp(b1g[l], NJ) for l in range(2)], 1)
    shared["L0f_b2"] = np.stack([_pp(ix["L0_ffn_b2"][l], 8) for l in range(2)], 1)
    shared["L0f_lng"] = np.stack([_pp(ix["L0_ffn_ln_g"][l], 8) for l in range(2)], 1)
    shared["L0f_lnb"] = np.stack([_pp(ix["L0_ffn_ln_b"][l], 8) for l in range(2)], 1)
    qs = ix["L0_ffn_q_scale"].astype(np.float32)
    ks = ix["L0_ffn_k_scale"].astype(np.float32)
    tile2 = lambda v: np.ascontiguousarray(np.tile(v, 2)[:, None])
    shared["L0f_qs"] = np.concatenate([tile2(qs[l]) for l in range(2)], 1)
    shared["L0f_ks"] = np.concatenate([tile2(ks[l]) for l in range(2)], 1)
    swap = lambda v: np.concatenate([v[32:], v[:32]])
    shared["L0f_qs_sw"] = np.concatenate([tile2(swap(qs[l])) for l in range(2)], 1)
    shared["L0f_ks_sw"] = np.concatenate([tile2(swap(ks[l])) for l in range(2)], 1)
    shared["L1_wqkv"] = np.ascontiguousarray(ix["L1_attn_w_qkv"])
    shared["L1_wproj"] = np.ascontiguousarray(ix["L1_attn_w_proj"])
    shared["L1_bproj"] = _pp(ix["L1_attn_b_proj"], 8)
    shared["L1f_G1R"] = np.ascontiguousarray(
        ix["L1_ffn_G1_R"].reshape(32, 4, 32).astype(np.float32))
    shared["L1f_G1I"] = np.ascontiguousarray(
        ix["L1_ffn_G1_I"].reshape(32, 4, 32).astype(np.float32))
    shared["L1f_G2R"] = np.ascontiguousarray(
        ix["L1_ffn_G2_R"].reshape(4, 16, 16).transpose(1, 0, 2))
    shared["L1f_G2I"] = np.ascontiguousarray(
        ix["L1_ffn_G2_I"].reshape(4, 16, 16).transpose(1, 0, 2))
    shared["L1f_bri"] = _pp(
        np.concatenate([ix["L1_ffn_bias_real"], ix["L1_ffn_bias_imag"]]), 8)
    x = np.asarray(ix["L0_attn_x"], np.float32)
    per_core = []
    for core in range(NCORES):
        xs = x[core * SEQS:(core + 1) * SEQS]
        xf = xs.transpose(0, 2, 1).reshape(SEQS, NC, 128, T)
        per_core.append({"x": np.ascontiguousarray(xf)})
    return shared, per_core


class H:
    pass


def declare_consts(p, nc, hd):
    g = H()
    ld = []

    def load(name, shape, dtype, src_name=None, cast=False):
        tsb = p.sb("c_" + name, shape, dtype)
        src = hd[src_name or name]
        q = "qg" if cast else "qs"
        p.dma(q, tsb[:], src, writes=[("c", name)])
        setattr(g, name, tsb)
        return tsb

    load("ones_bf", [128, 1], BF16)
    load("ones64a_bf", [128, 1], BF16)
    load("ones64b_bf", [128, 1], BF16)
    load("ones1_f", [1, 128], F32)
    load("selA_f", [1, 128], F32)
    load("selB_f", [1, 128], F32)
    load("ident_bf", [128, 128], BF16)
    load("ident_f", [128, 128], F32)
    load("rot_bf", [128, 128], BF16)
    load("maskband_bf", [128, 1408], BF16)
    load("L0a_lng", [128, 8], F32)
    load("L0a_lnb", [128, 8], F32)
    load("L0a_lam", [1, 1024], F32)
    load("L0f_b1a", [128, 2, NJ], F32)
    load("L0f_b1g", [128, 2, NJ], F32)
    load("L0f_b2", [128, 2, 8], F32)
    load("L0f_lng", [128, 2, 8], F32)
    load("L0f_lnb", [128, 2, 8], F32)
    load("L0f_qs", [128, 2], F32)
    load("L0f_ks", [128, 2], F32)
    load("L0f_qs_sw", [128, 2], F32)
    load("L0f_ks_sw", [128, 2], F32)
    load("L1_bproj", [128, 8], F32)
    load("L1f_bri", [128, 8], F32)
    load("L1f_G1R", [32, 4, 32], F32)
    load("L1f_G1I", [32, 4, 32], F32)
    load("L1f_G2R", [16, 4, 16], F32)
    load("L1f_G2I", [16, 4, 16], F32)
    load("Yb_f", [32, 512], F32)
    load("Ye_f", [16, 512], F32)
    load("Zd_f", [32, 4, 128], F32)
    load("Zg_f", [16, 4, 128], F32)
    onesrow = p.sb("c_onesrow", [1, 512], F32)
    p.op("dve", lambda v: v.memset(onesrow[:], 1.0), writes=[("c", "onesrow")])
    g.onesrow = onesrow
    return g


def emit_ln(p, g, u, key, get_src, n, out_bf16, src_reads, out_keys,
            out_fp32=None, affine=None, eps=1e-5, get_src_f=None):
    r0 = lambda sl: u.rows[0:1, 0 * 512 + 0:0 * 512 + n]
    r1 = lambda sl: u.rows[0:1, 1 * 512 + 0:1 * 512 + n]
    r2 = lambda sl: u.rows[0:1, 2 * 512 + 0:2 * 512 + n]
    K = lambda *a: (key,) + a
    for c in range(NC):
        src = get_src(c)
        p.op("act", (lambda s, src=src: s.activation(u.sq[:, :n], src, AF.Square)),
             reads=list(src_reads) + [("psQ",)], writes=[("u", "sq")])
        p.mm(u.psS[0:1, :n], g.ones_bf[:], src, start=(c == 0), stop=(c == NC - 1),
             reads=list(src_reads), writes=[("psS",)])
        p.mm(u.psQ[0:1, :n], g.ones_bf[:], u.sq[:, :n], start=(c == 0),
             stop=(c == NC - 1), reads=[("u", "sq")], writes=[("psQ",)])
    p.op("dve", lambda v: v.tensor_scalar_mul(r0(0), u.psS[0:1, :n], 1.0 / D),
         reads=[("psS",)], writes=[("u", "rows", 0)], hazard=True)
    p.op("dve", lambda v: v.tensor_scalar_mul(r1(0), u.psQ[0:1, :n], 1.0 / D),
         reads=[("psQ",)], writes=[("u", "rows", 1)], hazard=True)
    p.op("dve", lambda v: v.tensor_mul(r2(0), r0(0), r0(0)),
         reads=[("u", "rows", 0)], writes=[("u", "rows", 2)])
    p.op("dve", lambda v: v.tensor_sub(r1(0), r1(0), r2(0)),
         reads=[("u", "rows", 1), ("u", "rows", 2)], writes=[("u", "rows", 1)])
    p.op("act", lambda s: s.activation(r2(0), r1(0), AF.Sqrt,
                                       bias=u.eps[0:1, 0:1]),
         reads=[("u", "rows", 1), ("u", "eps")], writes=[("u", "rows", 2)])
    p.op("dve", lambda v: v.reciprocal(r1(0), r2(0)),
         reads=[("u", "rows", 2)], writes=[("u", "rows", 1)], hazard=True)
    p.op("dve", lambda v: v.tensor_mul(r2(0), r0(0), r1(0)),
         reads=[("u", "rows", 0), ("u", "rows", 1)], writes=[("u", "rows", 2)])
    p.mm(u.psC[:, 0:n], g.ones1_f[:], r1(0), reads=[("u", "rows", 1)], writes=[("psC", 0)])
    p.mm(u.psC[:, 512:512 + n], g.ones1_f[:], r2(0), reads=[("u", "rows", 2)], writes=[("psC", 1)])
    for c in range(NC):
        src = (get_src_f or get_src)(c)
        p.op("dve",
             (lambda v, src=src: v.tensor_mul(u.tmp[:, :n], src, u.psC[:, 0:n])),
             reads=list(src_reads) + [("psC", 0)], writes=[("u", "tmp")])
        if affine is not None:
            p.op("dve", lambda v: v.tensor_sub(u.tmp[:, :n], u.tmp[:, :n],
                                               u.psC[:, 512:512 + n]),
                 reads=[("u", "tmp"), ("psC", 1)], writes=[("u", "tmp")])
            gam, bet = affine
            dstb = out_bf16(c)
            p.op("act",
                 (lambda s, c=c, dstb=dstb: s.activation(
                     dstb, u.tmp[:, :n], AF.Identity,
                     scale=gam(c), bias=bet(c))),
                 reads=[("u", "tmp")], writes=[out_keys(c)])
        elif out_fp32 is not None:
            dst = out_fp32(c)
            p.op("dve", (lambda v, dst=dst: v.tensor_sub(dst, u.tmp[:, :n],
                                                         u.psC[:, 512:512 + n])),
                 reads=[("u", "tmp"), ("psC", 1)], writes=[out_keys(c)])
            if out_bf16 is not None:
                dstb = out_bf16(c)
                p.op("act", (lambda s, dst=dst, dstb=dstb: s.copy(dstb, dst)),
                     reads=[out_keys(c)], writes=[out_keys(c)])
        else:
            dstb = out_bf16(c)
            p.op("dve", (lambda v, dstb=dstb: v.tensor_sub(dstb, u.tmp[:, :n],
                                                           u.psC[:, 512:512 + n])),
                 reads=[("u", "tmp"), ("psC", 1)], writes=[out_keys(c)])


def _wsl(w2d, oc, ow=128):
    r = w2d.rearrange("(kc p) o -> p kc o", p=128)
    return r[:, :, oc * ow:(oc + 1) * ow]


_NAMECTR = [0]


def _uname(name):
    _NAMECTR[0] += 1
    return f"{name}_{_NAMECTR[0]}"


class _Scope:
    def __init__(self, p):
        self.p = p
        self.es = ExitStack()

    def __enter__(self):
        return self

    def sb(self, name, shape, dtype):
        return self.es.enter_context(
            self.p.nc.sbuf_tensor(_uname(name), shape, dtype))

    def __exit__(self, *a):
        self.p.barrier()
        self.es.close()
        return False


def _shiftoff(fc):
    return 0 if fc >= 4 else 1


def emit_stage_a(p, g, io, u, base, s):
    nc = p.nc
    a = base
    X = lambda *k: ("A", s) + k
    p.barrier()
    p.op("dve", lambda v: v.memset(a.carry[:], float(DEV.get('carry0', 0.0))), writes=[("carry",)])

    for h in range(2):
        h0 = h * TH
        with _Scope(p) as sc:
            xt = sc.sb("xt", [128, NC, 512], F32)
            xb = sc.sb("xb", [128, NC, 512], BF16)
            for tb in range(2):
                tsl = slice(h0 + tb * 512, h0 + tb * 512 + 512)
                src = io.x[s].rearrange("c p t -> p c t")[:, :, tsl]
                p.dma("qs", xt[:], src, writes=[X("xt")])
                p.dma("qa", io.x1a[s].rearrange("c p t -> p c t")[:, :, tsl],
                      xt[:], reads=[X("xt")], writes=[X("x1a", "init", h, tb)])
                p.op("act", (lambda sx, _o=xb[:], _i=xt[:]: sx.copy(_o, _i)),
                     reads=[X("xt")], writes=[X("xb")])
                if "0" in DEV.get("bar", ""):
                    p.barrier()
                emit_ln(
                    p, g, u, X("ln0", h, tb),
                    get_src=lambda c: xb[:, c, :],
                    get_src_f=lambda c: xt[:, c, :],
                    n=512,
                    out_fp32=lambda c, tb=tb: a.XF[:, c, 1 + tb * 512:1 + tb * 512 + 512],
                    out_bf16=lambda c, tb=tb: a.XFb[:, c, 1 + tb * 512:1 + tb * 512 + 512],
                    src_reads=[X("xb"), X("xt")],
                    out_keys=lambda c: ("XF", c))
                if "0" in DEV.get("bar", ""):
                    p.barrier()

        if DEV.get("skip_mixer"):
            pass
        else:
         with _Scope(p) as sc:
            wq_sl = [sc.sb(f"wq{i}", [128, NC, 128], BF16) for i in range(2)]
            wv_sl = [sc.sb(f"wv{i}", [128, NC, 128], BF16) for i in range(2)]
            lam_sb = sc.sb("lam", [128, TH], F32)
            v_hc = [sc.sb(f"vhc{i}", [128, TH], F32) for i in range(2)]
            st_hc = [sc.sb(f"sthc{i}", [128, TH], F32) for i in range(2)]
            y_bf = sc.sb("y", [128, NC, TH], BF16)
            for hc in range(NC):
                b = hc % 2
                p.dma("qg", wq_sl[b][:], _wsl(io.wqT, hc),
                      reads=[X("wq", b, "free")], writes=[X("wq", b)])
                p.dma("qg", wv_sl[b][:], _wsl(io.wvT, hc),
                      reads=[X("wv", b, "free")], writes=[X("wv", b)])
                for nh in range(2):
                    p.mm(u.psC[:, nh * 512:nh * 512 + 512],
                         g.L0a_lam[0:1, hc * 128:hc * 128 + 128],
                         g.onesrow[:], reads=[("c", "L0a_lam"), ("c", "onesrow")],
                         writes=[("psC", nh)])
                p.op("dve", (lambda v, _o=lam_sb[:]: v.tensor_copy(_o, u.psC[:])),
                     reads=[("psC", 0), ("psC", 1)], writes=[X("lam")])
                for nh in range(2):
                    for fc in range(NC):
                        p.mm(u.psA[:, nh * 512:nh * 512 + 512],
                             wv_sl[b][:, fc, :],
                             a.XFb[:, fc, 1 + nh * 512:1 + nh * 512 + 512],
                             start=(fc == 0), stop=(fc == NC - 1),
                             reads=[X("wv", b), ("XF", fc)],
                             writes=[("psA", nh)])
                p.op("dve", (lambda v, _o=v_hc[b][:]: v.tensor_copy(_o, u.psA[:])),
                     reads=[("psA", 0), ("psA", 1)],
                     writes=[X("v", b), ("psA", 0), ("psA", 1)])
                for nh in range(2):
                    for fc in range(NC):
                        p.mm(u.psB[:, nh * 512:nh * 512 + 512],
                             wq_sl[b][:, fc, :],
                             a.XFb[:, fc, 1 + nh * 512:1 + nh * 512 + 512],
                             start=(fc == 0), stop=(fc == NC - 1),
                             reads=[X("wq", b), ("XF", fc)],
                             writes=[("psB", nh)])
                p.op("dve",
                     (lambda v, _o=st_hc[b][:], _l=lam_sb[:], _v=v_hc[b][:],
                      _c=a.carry[:, hc:hc + 1]: v.tensor_tensor_scan(
                         _o, _l, _v, _c, op0=ALU.mult, op1=ALU.add)),
                     reads=[X("lam"), X("v", b), ("carry",)],
                     writes=[X("st", b)], self_wait=True, hazard=True)
                p.op("dve",
                     (lambda v, _o=a.carry[:, hc:hc + 1], _i=st_hc[b][:, TH - 1:TH]:
                      v.tensor_copy(_o, _i)),
                     reads=[X("st", b)], writes=[("carry",)])
                p.op("dve",
                     (lambda v, _o=y_bf[:, hc, :], _s=st_hc[b][:]: v.tensor_mul(
                         _o, u.psB[:], _s)),
                     reads=[("psB", 0), ("psB", 1), X("st", b)],
                     writes=[X("y", hc), X("v", b, "free"), X("wq", b, "free"),
                             X("wv", b, "free"), ("psB", 0), ("psB", 1)])
            wg_sl = [sc.sb(f"wg{i}", [128, NC, 128], BF16) for i in range(2)]
            wo_sl = [sc.sb(f"wo{i}", [128, NC, 128], BF16) for i in range(2)]
            yln_bf = sc.sb("yln", [128, NC, 512], BF16)
            gate_bf = [sc.sb(f"gt{i}", [128, 512], BF16) for i in range(2)]
            yg_bf = sc.sb("yg", [128, NC, 512], BF16)
            atile = [sc.sb(f"atl{i}", [128, 512], F32) for i in range(2)]
            for tb in range(2):
                tsl = slice(tb * 512, tb * 512 + 512)
                emit_ln(
                    p, g, u, X("lny", h, tb),
                    get_src=lambda c, tsl=tsl: y_bf[:, c, tsl],
                    n=512,
                    out_bf16=lambda c: yln_bf[:, c, :],
                    src_reads=[X("y", c2) for c2 in range(NC)],
                    out_keys=lambda c: X("yln", c),
                    affine=(lambda c: g.L0a_lng[:, c:c + 1],
                            lambda c: g.L0a_lnb[:, c:c + 1]))
                for oc in range(NC):
                    b = oc % 2
                    p.dma("qg", wg_sl[b][:], _wsl(io.wgT, oc),
                          reads=[X("wg", b, "free")], writes=[X("wg", b)])
                    for fc in range(NC):
                        p.mm(u.psA[:, 0:512], wg_sl[b][:, fc, :],
                             a.XFb[:, fc, 1 + tb * 512:1 + tb * 512 + 512],
                             start=(fc == 0), stop=(fc == NC - 1),
                             reads=[X("wg", b), ("XF", fc)],
                             writes=[("psA", 0)])
                    p.op("act", (lambda sx, _o=gate_bf[b][:]: sx.activation(
                        _o, u.psA[:, 0:512], AF.Silu)),
                        reads=[("psA", 0)],
                        writes=[X("gate", b), ("psA", 0), X("wg", b, "free")])
                    p.op("dve", (lambda v, _o=yg_bf[:, oc, :], _a=yln_bf[:, oc, :],
                                 _g=gate_bf[b][:]: v.tensor_mul(_o, _a, _g)),
                        reads=[X("gate", b), X("yln", oc)],
                        writes=[X("yg", oc)])
                for mc in range(NC):
                    b = mc % 2
                    p.dma("qg", wo_sl[b][:], _wsl(io.woT, mc),
                          reads=[X("wo", b, "free")], writes=[X("wo", b)])
                    for kc in range(NC):
                        p.mm(u.psB[:, b * 512:b * 512 + 512],
                             wo_sl[b][:, kc, :], yg_bf[:, kc, :],
                             start=(kc == 0), stop=(kc == NC - 1),
                             reads=[X("wo", b), X("yg", kc)],
                             writes=[("psB", b)])
                    p.op("dve", (lambda v, _o=atile[b][:],
                                 _i=u.psB[:, b * 512:b * 512 + 512]: v.tensor_copy(
                        _o, _i)),
                        reads=[("psB", b)],
                        writes=[X("atile", b), ("psB", b), X("wo", b, "free")])
                    if not DEV.get("skip_amix"):
                        p.dma("qg",
                              io.x1a[s, mc][:, h0 + tb * 512:h0 + tb * 512 + 512],
                              atile[b][:],
                              reads=[X("atile", b), X("x1a", "init", h, tb)],
                              writes=[X("x1a", "acc"), X("atile", b)],
                              accum_op=ALU.add)

        for l in range(DEV.get("nlayers", 2)):
            emit_a_ffn_layer(p, g, io, u, a, s, h, l)

        if not DEV.get("skip_f"):
            p.dma("qg", io.x1a[s].rearrange("c p t -> p c t")[:, :, h0:h0 + TH],
                  a.XF[:, :, 1:TH + 1],
                  reads=[("XF", c) for c in range(NC)] + [X("x1a", "acc"),
                         X("x1a", "init", h, 0), X("x1a", "init", h, 1)],
                  writes=[X("x1a", "acc")], accum_op=ALU.add)
        if DEV.get("dump_carry") == (s, h):
            p.dma("qa", io.dbg[0][:, 0:NC], a.carry[:],
                  reads=[("carry",)], writes=[("dbgc",)])
        if DEV.get("dump_xf") == (s, h):
            p.dma("qa", io.dbg.rearrange("c p t -> p c t"), a.XF[:, :, 1:TH + 1],
                  reads=[("XF", c) for c in range(NC)], writes=[("dbg",)])


def emit_a_ffn_layer(p, g, io, u, a, s, h, l):
    nc = p.nc
    X = lambda *k: ("A", s, "f", l) + k
    XA = lambda *k: ("A", s) + k
    if h == 0:
        p.op("dve", lambda v: v.memset(a.XFb[:, :, 0:1], 0.0),
             reads=[("XF", c) for c in range(NC)],
             writes=[("XFb0",)])
    else:
        p.op("dve", lambda v: v.tensor_copy(a.XFb[:, :, 0:1],
                                            a.save_qkv[:, :, l:l + 1]),
             reads=[("save_qkv",)] + [("XF", c) for c in range(NC)],
             writes=[("XFb0",)])
    p.op("dve", lambda v: v.tensor_copy(a.save_qkv[:, :, l:l + 1],
                                        a.XFb[:, :, TH:TH + 1]),
         reads=[("XF", NC - 1)], writes=[("save_qkv",)])

    with _Scope(p) as lsc:
        wo_sb = lsc.sb("wo_sb", [128, 4, 1024], BF16)
        attn_bf = lsc.sb("attn", [128, 4, 512], BF16)
        negsh = lsc.sb("negsh", [128, 1], F32)
        rrow = lsc.sb("rrow", [1, 512], F32)
        bc_sb = lsc.sb("bc_sb", [64, 512], F32)
        dbgrow = lsc.sb("dbgrow", [1, 512], F32)
        p.op("pe", (lambda t, _i=g.L0f_qs[:, l:l + 1]: t.transpose(
            u.psS[0:1, 0:128], _i, g.ident_f[:])),
             reads=[("c", "L0f_qs"), ("c", "ident_f")], writes=[("psS",)])
        if DEV.get("dump_sh") == (s, h, l):
            p.op("dve", (lambda v, _o=dbgrow[0:1, 0:128]: v.tensor_copy(
                _o, u.psS[0:1, 0:128])),
                 reads=[("psS",)], writes=[X("dbgrow")])
        p.op("dve", (lambda v, _o=rrow[0:1, 0:1]: v.reduce_max(
            _o, u.psS[0:1, 0:128], axis=AX.X, apply_absolute_value=True)),
             reads=[("psS",)], writes=[X("rrow"), ("psS",)], hazard=True)
        p.op("pe", (lambda t, _i=g.L0f_ks[:, l:l + 1]: t.transpose(
            u.psS[0:1, 0:128], _i, g.ident_f[:])),
             reads=[("c", "L0f_ks"), ("c", "ident_f"), ("psS",)], writes=[("psS",)])
        p.op("dve", (lambda v, _o=rrow[0:1, 1:2]: v.reduce_max(
            _o, u.psS[0:1, 0:128], axis=AX.X, apply_absolute_value=True)),
             reads=[("psS",)], writes=[X("rrow"), ("psS",)], hazard=True)
        p.op("dve", (lambda v, _a=rrow[0:1, 0:1], _b=rrow[0:1, 1:2]:
                     v.tensor_mul(_a, _a, _b)),
             reads=[X("rrow")], writes=[X("rrow")])
        p.mm(u.psS[0:128, 0:1], g.ones1_f[:], rrow[0:1, 0:1],
             reads=[X("rrow")], writes=[("psS",)])
        p.op("dve", (lambda v, _o=negsh[:]: v.tensor_scalar_mul(_o, u.psS[0:128, 0:1], -8.0)),
             reads=[("psS",)], writes=[X("negsh"), ("psS",)], hazard=True)
        if DEV.get("dump_sh") == (s, h, l):
            p.op("dve", (lambda v, _o=dbgrow[0:1, 256:260], _i=rrow[0:1, 0:4]:
                         v.tensor_copy(_o, _i)),
                 reads=[X("rrow")], writes=[X("dbgrow")])
            p.dma("qa", io.dbg[6][0:1, 0:512], dbgrow[:],
                  reads=[X("dbgrow")], writes=[("dbgsh",)])
        p.dma("qg", wo_sb[:], io.wo[l].rearrange("(kc p) o -> p kc o", p=128),
              writes=[X("wo_sb")])

        with _Scope(p) as qsc:
            c1q = qsc.sb("c1q", [128, T], BF16)
            c2q = qsc.sb("c2q", [128, T], BF16)
            c1k = qsc.sb("c1k", [128, T], BF16)
            c2k = qsc.sb("c2k", [128, T], BF16)
            for nm, dst, sc_col in (("c1q", c1q, g.L0f_qs),
                                    ("c2q", c2q, g.L0f_qs_sw),
                                    ("c1k", c1k, g.L0f_ks),
                                    ("c2k", c2k, g.L0f_ks_sw)):
                p.dma("qs", dst[:], io.consts[nm + "_bf"], writes=[X("tbl", nm)])
                p.op("dve",
                     (lambda v, _d=dst[:], _s=sc_col[:, l:l + 1]:
                      v.tensor_scalar_mul(_d, _d, _s)),
                     reads=[X("tbl", nm), ("c", "L0f_qs")], writes=[X("tbl", nm)],
                     hazard=True)
            qf = qsc.sb("qf", [128, 4, TH], BF16)
            kf = qsc.sb("kf", [128, 4, TH], BF16)
            vf = qsc.sb("vf", [128, 8, 8, 65], BF16)
            wqkv_sl = [qsc.sb(f"wqk{i}", [128, NC, 128], BF16) for i in range(2)]
            wqkv_v = qsc.sb("wqkvv", [128, NC, 512], BF16)
            for oc in range(8):
                b = oc % 2
                p.dma("qg", wqkv_sl[b][:], _wsl(io.wqkv[l], oc),
                      reads=[X("wsl", b, "free")], writes=[X("wsl", b)])
                for nh in range(2):
                    for fc in range(NC):
                        p.mm(u.psA[:, nh * 512:nh * 512 + 512],
                             wqkv_sl[b][:, fc, :],
                             a.XFb[:, fc, _shiftoff(fc) + nh * 512:
                                   _shiftoff(fc) + nh * 512 + 512],
                             start=(fc == 0), stop=(fc == NC - 1),
                             reads=[X("wsl", b), ("XF", fc), ("XFb0",)],
                             writes=[("psA", nh)])
                dst = qf[:, oc, :] if oc < 4 else kf[:, oc - 4, :]
                key = X("qf", oc) if oc < 4 else X("kf", oc - 4)
                p.op("act", (lambda sx, dst=dst: sx.copy(dst, u.psA[:])),
                     reads=[("psA", 0), ("psA", 1)],
                     writes=[key, ("psA", 0), ("psA", 1), X("wsl", b, "free")])
            p.dma("qg", wqkv_v[:],
                  io.wqkv[l].rearrange("(kc p) o -> p kc o", p=128)[:, :, 1024:1536],
                  writes=[X("wqv")])
            p.op("dve", (lambda v, _o=vf[:, :, :, 64:65]: v.memset(_o, 1.0)),
                 writes=[X("vf_ones")])
            for tt in range(8):
                for fc in range(NC):
                    p.mm(u.psB[:, 0:512],
                         a.XFb[:, fc, _shiftoff(fc) + tt * 128:
                               _shiftoff(fc) + tt * 128 + 128],
                         wqkv_v[:, fc, :],
                         start=(fc == 0), stop=(fc == NC - 1),
                         reads=[X("wqv"), ("XF", fc), ("XFb0",)],
                         writes=[("psB", 0)])
                p.op("dve", (lambda v, _o=vf[:, tt, :, 0:64],
                             _i=u.psB[:, 0:512].rearrange("p (hh d) -> p hh d", hh=8):
                             v.tensor_copy(_o, _i)),
                    reads=[("psB", 0)],
                    writes=[X("vf", tt), ("psB", 0)])
            if DEV.get("dump_q") == (s, h, l, "qkv"):
                for qc in range(4):
                    p.dma("qg", io.dbg[qc], qf[:, qc, :], reads=[X("qf", qc)],
                          writes=[("dbgq", qc)])
                    p.dma("qg", io.dbg[4 + qc], kf[:, qc, :], reads=[X("kf", qc)],
                          writes=[("dbgk", qc)])
            if "a" in DEV.get("bar", ""):
                p.barrier()
            for side, buf, c1, c2, scol in (
                    ("q", qf, c1q, c2q, "q"), ("k", kf, c1k, c2k, "k")):
                for qc in range(4):
                    bkey = X("rot", side, qc)
                    sq = u.sq1024
                    p.op("act", (lambda sx, buf=buf, qc=qc: sx.activation(
                        sq[:], buf[:, qc, :], AF.Square)),
                        reads=[X(side + "f", qc)], writes=[("u", "sq1024")])
                    for nh in range(2):
                        p.mm(u.psS[0:1, 0:512], g.ones64a_bf[:],
                             sq[:, nh * 512:nh * 512 + 512],
                             reads=[("u", "sq1024")], writes=[("psS",)])
                        p.mm(u.psQ[0:1, 0:512], g.ones64b_bf[:],
                             sq[:, nh * 512:nh * 512 + 512],
                             reads=[("u", "sq1024")], writes=[("psQ",)])
                        for nm, pssrc, seg in (("nA", u.psS, 0), ("nB", u.psQ, 1)):
                            pskey = ("psS",) if nm == "nA" else ("psQ",)
                            rr = u.nrows[0:1, seg * 512:seg * 512 + 512]
                            p.op("act", (lambda sx, pssrc=pssrc, rr=rr:
                                         sx.activation(rr, pssrc[0:1, 0:512],
                                                       AF.Sqrt)),
                                 reads=[pskey],
                                 writes=[("u", "nrows", nm)])
                            p.op("dve", (lambda v, rr=rr: v.tensor_scalar_max(
                                rr, rr, 1e-12)),
                                reads=[("u", "nrows", nm)], writes=[("u", "nrows", nm)],
                                hazard=True)
                            p.op("dve", (lambda v, rr=rr: v.reciprocal(rr, rr)),
                                reads=[("u", "nrows", nm)],
                                writes=[("u", "nrows", nm), pskey], hazard=True)
                        p.mm(u.psC[:, nh * 512:nh * 512 + 512], g.selA_f[:],
                             u.nrows[0:1, 0:512],
                             start=True, stop=False,
                             reads=[("u", "nrows", "nA")], writes=[("psC", nh)])
                        p.mm(u.psC[:, nh * 512:nh * 512 + 512], g.selB_f[:],
                             u.nrows[0:1, 512:1024],
                             start=False, stop=True,
                             reads=[("u", "nrows", "nB")], writes=[("psC", nh)])
                    for nh in range(2):
                        p.mm(u.psB[:, nh * 512:nh * 512 + 512], g.rot_bf[:],
                             buf[:, qc, nh * 512:nh * 512 + 512],
                             reads=[X(side + "f", qc), ("c", "rot_bf")],
                             writes=[("psB", nh)])
                    tsl = slice(h * TH, h * TH + TH)
                    p.op("dve", (lambda v, buf=buf, qc=qc, c1=c1, tsl=tsl:
                                 v.tensor_mul(u.tmp_bf[:], buf[:, qc, :],
                                              c1[:, tsl])),
                         reads=[X(side + "f", qc), X("tbl", "c1" + scol)],
                         writes=[("u", "tmp_bf")])
                    p.op("dve", (lambda v, c2=c2, tsl=tsl: v.tensor_mul(
                        u.tmp_f[:], u.psB[:], c2[:, tsl])),
                        reads=[("psB", 0), ("psB", 1),
                               X("tbl", "c2" + scol)],
                        writes=[("u", "tmp_f"), ("psB", 0), ("psB", 1)])
                    p.op("dve", lambda v: v.tensor_add(u.tmp_f[:], u.tmp_f[:],
                                                       u.tmp_bf[:]),
                         reads=[("u", "tmp_bf"), ("u", "tmp_f")],
                         writes=[("u", "tmp_f")])
                    p.op("dve", (lambda v, buf=buf, qc=qc: v.tensor_mul(
                        buf[:, qc, :], u.tmp_f[:], u.psC[:])),
                        reads=[("u", "tmp_f"), ("psC", 0), ("psC", 1)],
                        writes=[X(side + "f", qc), ("psC", 0),
                                ("psC", 1)])
            if DEV.get("dump_rn") == (s, h, l):
                p.dma("qa", io.dbg[0][0:1, 0:1024], u.nrows[:],
                      reads=[("u", "nrows", "nA"), ("u", "nrows", "nB")],
                      writes=[("dbgr",)])
                p.dma("qg", io.dbg[1], c1q[:, h * TH:h * TH + TH],
                      reads=[X("tbl", "c1q")], writes=[("dbgt1",)])
                p.dma("qg", io.dbg[2], c2q[:, h * TH:h * TH + TH],
                      reads=[X("tbl", "c2q")], writes=[("dbgt2",)])
            if DEV.get("dump_q") == (s, h, l, "rot"):
                for qc in range(4):
                    p.dma("qg", io.dbg[qc], qf[:, qc, :], reads=[X("qf", qc)],
                          writes=[("dbgq", qc)])
                    p.dma("qg", io.dbg[4 + qc], kf[:, qc, :], reads=[X("kf", qc)],
                          writes=[("dbgk", qc)])
            if DEV.get("dump_v") == (s, h, l):
                for tt in range(8):
                    p.dma("qg", io.dbg[tt % 8][:, tt // 8 * 1040: tt // 8 * 1040 + 520]
                          if False else io.dbg[tt][:, 0:520],
                          vf[:, tt, :, :], reads=[X("vf", tt), X("vf_ones")],
                          writes=[("dbgv", tt)])
            if "b" in DEV.get("bar", ""):
                p.barrier()
            PT = [qsc.sb(f"pt{i}", [128, 512], BF16) for i in range(4)]
            for wb in range(2):
                W = 2 * h + wb
                kts = [kt for kt in range((W - 1) * 4, (W - 1) * 4 + 8) if kt >= 0]
                for hh in range(8):
                    pb = (hh % 2) * 64
                    ch = hh // 2
                    av = hh % 2
                    qrhs = qf[pb:pb + 64, ch, wb * 512:wb * 512 + 512]
                    nkt = len(kts)
                    LAG = 2

                    def tile_info(idx):
                        kt = kts[idx]
                        loc = kt - 8 * h
                        if loc >= 0:
                            klhs = kf[pb:pb + 64, ch, loc * 128:loc * 128 + 128]
                            vtile = vf[:, loc, hh, :]
                            kread = [X("kf", ch), X("vf", loc)]
                        else:
                            sloc = loc + 4
                            klhs = a.save_k[pb:pb + 64, ch, sloc * 128:
                                            sloc * 128 + 128, l]
                            vtile = a.save_v[:, sloc, hh * 65:hh * 65 + 65, l]
                            kread = [("save_k",), ("save_v",)]
                        return kt, klhs, vtile, kread

                    for i in range(nkt + LAG):
                        if i < nkt:
                            idx = i
                            bk = idx % 4
                            kt, klhs, vtile, kread = tile_info(idx)
                            o = kt * 128 - W * 512
                            msl = g.maskband_bf[:, 384 - o:384 - o + 512]
                            pst = u.psA if bk < 2 else u.psC
                            pk = ("psA", bk) if bk < 2 else ("psC", bk - 2)
                            hf = (bk % 2) * 512
                            p.mm(pst[:, hf:hf + 512], klhs, qrhs,
                                 start=True, stop=False,
                                 reads=kread + [X("qf", ch)],
                                 writes=[pk])
                            p.mm(pst[:, hf:hf + 512], g.ident_bf[:],
                                 msl, start=False, stop=True,
                                 reads=[("c", "maskband_bf"), ("c", "ident_bf")],
                                 writes=[pk])
                            p.op("act", (lambda sx, _o=PT[bk][:],
                                _i=pst[:, hf:hf + 512],
                                _b=negsh[:, 0:1]: sx.activation(
                                _o, _i, AF.Exp, bias=_b)),
                                reads=[pk, X("negsh")],
                                writes=[X("pt", bk), pk])
                        if i >= LAG:
                            idx = i - LAG
                            bk = idx % 4
                            kt, klhs, vtile, kread = tile_info(idx)
                            p.mm(u.psB[0:65, av * 512:av * 512 + 512],
                                 vtile, PT[bk][:],
                                 start=(idx == 0), stop=(idx == nkt - 1),
                                 reads=kread + [X("pt", bk)],
                                 writes=[("psB", av), X("pt", bk)])
                    p.op("dve", (lambda v, _o=rrow[:],
                        _i=u.psB[64:65, av * 512:av * 512 + 512]: v.tensor_scalar(
                        _o, _i, 1e-30, None, op0=ALU.add)),
                        reads=[("psB", av)], writes=[X("rrow")], hazard=True)
                    p.op("dve", (lambda v, _o=rrow[:]: v.reciprocal(_o, _o)),
                         reads=[X("rrow")], writes=[X("rrow")], hazard=True)
                    p.mm(u.psS[0:64, 0:512], g.ones1_f[0:1, 0:64], rrow[:],
                         reads=[X("rrow")], writes=[("psS",)])
                    p.op("dve", (lambda v, _o=bc_sb[:]: v.tensor_copy(
                        _o, u.psS[0:64, 0:512])),
                         reads=[("psS",)], writes=[X("bcs"), ("psS",)])
                    p.op("dve", (lambda v, _o=attn_bf[pb:pb + 64, ch, :],
                        _i=u.psB[0:64, av * 512:av * 512 + 512],
                        _b=bc_sb[:]: v.tensor_mul(_o, _i, _b)),
                        reads=[("psB", av), X("bcs")],
                        writes=[X("at", hh), ("psB", av)])
                for mc in range(NC):
                    bk = mc % 2
                    for kc in range(4):
                        p.mm(u.psB[:, bk * 512:bk * 512 + 512],
                             wo_sb[:, kc, mc * 128:mc * 128 + 128],
                             attn_bf[:, kc, :],
                             start=(kc == 0), stop=(kc == 3),
                             reads=[X("wo_sb"), X("at", 2 * kc), X("at", 2 * kc + 1)],
                             writes=[("psB", bk)])
                    xsl = a.XF[:, mc, 1 + wb * 512:1 + wb * 512 + 512]
                    p.op("dve", (lambda v, xsl=xsl, bk=bk: v.tensor_add(
                        xsl, u.psB[:, bk * 512:bk * 512 + 512], xsl)),
                        reads=[("psB", bk), ("XF", mc)],
                        writes=[("XF", mc), ("psB", bk)])
                    p.op("act", (lambda sx, mc=mc, wb=wb: sx.copy(
                        a.XFb[:, mc, 1 + wb * 512:1 + wb * 512 + 512],
                        a.XF[:, mc, 1 + wb * 512:1 + wb * 512 + 512])),
                        reads=[("XF", mc)], writes=[("XF", mc)])
            if "c" in DEV.get("bar", ""):
                p.barrier()
            if DEV.get("dump_attn") == (s, h, l):
                for kc in range(4):
                    p.dma("qg", io.dbg[kc][:, 0:512], attn_bf[:, kc, :],
                          reads=[X("at", 2 * kc), X("at", 2 * kc + 1)],
                          writes=[("dbga", kc)])
                p.dma("qa", io.dbg[7][:, 0:1], negsh[:],
                      reads=[X("negsh")], writes=[("dbgn",)],
                      allow_slow_non_contiguous=True)
            p.op("dve", (lambda v, _o=a.save_k[:, :, :, l],
                         _i=kf[:, :, 512:1024]: v.tensor_copy(_o, _i)),
                 reads=[X("kf", 0), X("kf", 1), X("kf", 2), X("kf", 3)],
                 writes=[("save_k",)])
            p.op("dve", (lambda v, _o=a.save_v[:, :, :, l],
                         _i=vf[:, 4:8, :, :]: v.tensor_copy(_o, _i)),
                 reads=[X("vf", tt) for tt in range(4, 8)] + [X("vf_ones")],
                 writes=[("save_v",)])

        if "d" in DEV.get("bar", ""):
            p.barrier()
        with _Scope(p) as fsc:
            hln = fsc.sb("hln", [128, NC, TH], BF16)
            if h == 0:
                p.op("dve", lambda v: v.memset(a.XF[:, :, 0:1], 0.0),
                     reads=[("XF", c) for c in range(NC)], writes=[("XFp0",)])
            else:
                p.op("dve", lambda v: v.tensor_copy(a.XF[:, :, 0:1],
                                                    a.save_ln_f[:, :, l:l + 1]),
                     reads=[("save_ln",)] + [("XF", c) for c in range(NC)],
                     writes=[("XFp0",)])
            p.op("act", lambda sx: sx.copy(a.XFb[:, :, 0:1], a.XF[:, :, 0:1]),
                 reads=[("XFp0",)], writes=[("XFb0",)])
            p.op("dve", lambda v: v.tensor_copy(a.save_ln_f[:, :, l:l + 1],
                                                a.XF[:, :, TH:TH + 1]),
                 reads=[("XF", NC - 1)], writes=[("save_ln",)])
            for tb in range(2):
                emit_ln(
                    p, g, u, X("lnf", h, tb),
                    get_src=lambda c, tb=tb: a.XFb[:, c, _shiftoff(c) + tb * 512:
                                                   _shiftoff(c) + tb * 512 + 512],
                    get_src_f=lambda c, tb=tb: a.XF[:, c, _shiftoff(c) + tb * 512:
                                                    _shiftoff(c) + tb * 512 + 512],
                    n=512,
                    out_bf16=lambda c, tb=tb: hln[:, c, tb * 512:tb * 512 + 512],
                    src_reads=[("XF", c2) for c2 in range(NC)] + [("XFb0",),
                               ("XFp0",)],
                    out_keys=lambda c: X("hln", c),
                    affine=(lambda c: g.L0f_lng[:, l, c:c + 1],
                            lambda c: g.L0f_lnb[:, l, c:c + 1]))
            if "e" in DEV.get("bar", ""):
                p.barrier()
            w1a_sl = [fsc.sb(f"w1a{i}", [128, NC, 128], BF16) for i in range(2)]
            w1g_sl = [fsc.sb(f"w1g{i}", [128, NC, 128], BF16) for i in range(2)]
            w2_sl = [fsc.sb("w2s", [128, NJ, 128], BF16)]
            t_all = fsc.sb("tall", [128, NJ, 2, 512], BF16)
            ag_bf = [fsc.sb(f"ag{i}", [128, 1024], BF16) for i in range(4)]
            for j in range(NJ):
                b = j % 2
                p.dma("qg", w1a_sl[b][:], _wsl(io.w1a[l], j),
                      reads=[X("w1a", b, "free")], writes=[X("w1a", b)])
                p.dma("qg", w1g_sl[b][:], _wsl(io.w1g[l], j),
                      reads=[X("w1g", b, "free")], writes=[X("w1g", b)])
                for tb in range(2):
                    tsl = slice(tb * 512, tb * 512 + 512)
                    pw = u.psA if tb == 0 else u.psC
                    pk0 = ("psA", 0) if tb == 0 else ("psC", 0)
                    pk1 = ("psA", 1) if tb == 0 else ("psC", 1)
                    ab = ag_bf[2 * tb + 0] if b == 0 else ag_bf[2 * tb + 1]
                    for fc in range(NC):
                        p.mm(pw[:, 0:512], w1a_sl[b][:, fc, :],
                             hln[:, fc, tsl], start=(fc == 0), stop=(fc == NC - 1),
                             reads=[X("w1a", b), X("hln", fc)],
                             writes=[pk0])
                    for fc in range(NC):
                        p.mm(pw[:, 512:1024], w1g_sl[b][:, fc, :],
                             hln[:, fc, tsl], start=(fc == 0), stop=(fc == NC - 1),
                             reads=[X("w1g", b), X("hln", fc)],
                             writes=[pk1])
                    p.op("act", (lambda sx, _o=ab[:, 0:512],
                        _i=pw[:, 0:512],
                        _b=g.L0f_b1a[:, l, j:j + 1]: sx.activation(
                        _o, _i, AF.Identity, bias=_b)),
                        reads=[pk0, ("c", "L0f_b1a")],
                        writes=[X("ag", tb, b, 0), pk0, X("w1a", b, "free")])
                    p.op("act", (lambda sx, _o=ab[:, 512:1024],
                        _i=pw[:, 512:1024],
                        _b=g.L0f_b1g[:, l, j:j + 1]: sx.activation(
                        _o, _i, AF.Gelu, bias=_b)),
                        reads=[pk1, ("c", "L0f_b1g")],
                        writes=[X("ag", tb, b, 1), pk1, X("w1g", b, "free")])
                    p.op("dve", (lambda v, _o=t_all[:, j, tb, :],
                        _a=ab[:, 0:512], _g=ab[:, 512:1024]:
                        v.tensor_mul(_o, _a, _g)),
                        reads=[X("ag", tb, b, 0), X("ag", tb, b, 1)],
                        writes=[X("tall", j, tb), X("ag", tb, b, 0),
                                X("ag", tb, b, 1)])
            for mc in range(NC):
                b = 0
                p.dma("qg", w2_sl[b][:],
                      io.w2[l].rearrange("(jc p) o -> p jc o", p=128)
                      [:, :, mc * 128:mc * 128 + 128],
                      reads=[X("w2", b, "free")], writes=[X("w2", b)])
                for tb in range(2):
                    for j in range(NJ):
                        p.mm(u.psB[:, tb * 512:tb * 512 + 512],
                             w2_sl[b][:, j, :], t_all[:, j, tb, :],
                             start=(j == 0), stop=(j == NJ - 1),
                             reads=[X("w2", b), X("tall", j, tb)],
                             writes=[("psB", tb)])
                for tb in range(2):
                    xsl = a.XF[:, mc, 1 + tb * 512:1 + tb * 512 + 512]
                    p.op("dve", (lambda v, xsl=xsl, tb=tb, mc=mc, l=l:
                                 v.scalar_tensor_tensor(
                                     xsl, u.psB[:, tb * 512:tb * 512 + 512],
                                     g.L0f_b2[:, l, mc:mc + 1], xsl,
                                     op0=ALU.add, op1=ALU.add)),
                         reads=[("psB", tb), ("XF", mc), ("c", "L0f_b2")],
                         writes=[("XF", mc), ("psB", tb), X("w2", b, "free")],
                         hazard=True)
                    p.op("act", (lambda sx, mc=mc, tb=tb: sx.copy(
                        a.XFb[:, mc, 1 + tb * 512:1 + tb * 512 + 512],
                        a.XF[:, mc, 1 + tb * 512:1 + tb * 512 + 512])),
                        reads=[("XF", mc)], writes=[("XF", mc)])


def emit_stage_b(p, g, io, u, bb):
    nc = p.nc
    with _Scope(p) as sc:
        WW = {"R": sc.sb("WR", [128, 4, 512], F32),
              "I": sc.sb("WI", [128, 4, 512], F32)}
        t1sb = sc.sb("t1sb", [32, 512], F32)
        tgsb = sc.sb("tgsb", [16, 512], F32)
        Ap = sc.sb("Ap", [128, 512], F32)
        Bp = sc.sb("Bp", [128, 512], F32)
        K = lambda *k: ("B", "w") + k
        for part in ("R", "I"):
            G1 = getattr(g, "L1f_G1" + part)
            G2 = getattr(g, "L1f_G2" + part)
            W = WW[part]
            p.op("dve", (lambda v, W=W: v.memset(W[:], 0.0)), writes=[K(part, "W")])
            for c in range(4):
                p.mm(u.psS[0:32, 0:512], G1[:, c, :], g.Yb_f[:],
                     reads=[("c", "L1f_G1" + part), ("c", "Yb_f")],
                     writes=[("psS",)])
                p.op("dve", lambda v: v.tensor_copy(t1sb[:], u.psS[0:32, 0:512]),
                     reads=[("psS",)], writes=[K("t1"), ("psS",)])
                p.mm(u.psQ[0:16, 0:512], G2[:, c, :], g.Ye_f[:],
                     reads=[("c", "L1f_G2" + part), ("c", "Ye_f")],
                     writes=[("psQ",)])
                p.op("dve", lambda v: v.tensor_copy(tgsb[:], u.psQ[0:16, 0:512]),
                     reads=[("psQ",)], writes=[K("tg"), ("psQ",)])
                for pc in range(4):
                    p.mm(u.psA[:, 0:512], g.Zd_f[:, pc, :], t1sb[:],
                         reads=[K("t1"), ("c", "Zd_f")], writes=[("psA", 0)])
                    p.op("dve", lambda v: v.tensor_copy(Ap[:], u.psA[:, 0:512]),
                         reads=[("psA", 0)], writes=[K("Ap"), ("psA", 0)])
                    p.mm(u.psB[:, 0:512], g.Zg_f[:, pc, :], tgsb[:],
                         reads=[K("tg"), ("c", "Zg_f")], writes=[("psB", 0)])
                    p.op("dve", lambda v: v.tensor_copy(Bp[:], u.psB[:, 0:512]),
                         reads=[("psB", 0)], writes=[K("Bp"), ("psB", 0)])
                    p.op("dve", lambda v: v.tensor_mul(Bp[:], Ap[:], Bp[:]),
                         reads=[K("Ap"), K("Bp")], writes=[K("Bp")])
                    p.op("dve", (lambda v, W=W, pc=pc: v.tensor_add(
                        W[:, pc, :], W[:, pc, :], Bp[:])),
                        reads=[K("Bp"), K(part, "W")], writes=[K(part, "W")])
        for ic in range(4):
            p.op("act", (lambda sx, ic=ic: sx.copy(bb.Wc[:, ic, 0:512],
                                                   WW["R"][:, ic, :])),
                 reads=[K("R", "W")], writes=[("Wc",)])
            p.op("act", (lambda sx, ic=ic: sx.copy(bb.Wc[:, ic, 512:1024],
                                                   WW["I"][:, ic, :])),
                 reads=[K("I", "W")], writes=[("Wc",)])
            p.op("act", (lambda sx, ic=ic: sx.activation(
                bb.Wc[:, 4 + ic, 0:512], WW["I"][:, ic, :], AF.Copy, scale=-1.0)),
                reads=[K("I", "W")], writes=[("Wc",)])
            p.op("act", (lambda sx, ic=ic: sx.copy(bb.Wc[:, 4 + ic, 512:1024],
                                                   WW["R"][:, ic, :])),
                 reads=[K("R", "W")], writes=[("Wc",)])

    for s in range(SEQS):
        X = lambda *k: ("B", s) + k
        with _Scope(p) as sc:
            xt = sc.sb("xt", [128, NC, 512], F32)
            xb = sc.sb("xb", [128, NC, 512], BF16)
            for tb in range(4):
                tsl = slice(tb * 512, tb * 512 + 512)
                p.dma("qs", xt[:], io.x1a[s].rearrange("c p t -> p c t")[:, :, tsl],
                      reads=[("A", s, "x1a", "acc")], writes=[X("xt")])
                p.dma("qa", io.x1b[s].rearrange("c p t -> p c t")[:, :, tsl],
                      xt[:], reads=[X("xt")], writes=[X("x1b", "init", tb)])
                p.op("act", (lambda sx, _o=xb[:], _i=xt[:]: sx.copy(_o, _i)),
                     reads=[X("xt")], writes=[X("xb")])
                emit_ln(
                    p, g, u, X("ln1", tb),
                    get_src=lambda c: xb[:, c, :],
                    get_src_f=lambda c: xt[:, c, :],
                    n=512,
                    out_bf16=lambda c, tb=tb: bb.XN1b[:, c, tsl],
                    src_reads=[X("xb"), X("xt")],
                    out_keys=lambda c: X("xn", c))
        with _Scope(p) as sc:
            f1 = [sc.sb(f"f1{i}", [128, 512], F32) for i in range(2)]
            for tb in range(4):
                tsl = slice(tb * 512, tb * 512 + 512)
                for mc in range(NC):
                    b = mc % 2
                    for ic in range(NC):
                        p.mm(u.psA[:, b * 512:b * 512 + 512],
                             bb.Wc[:, ic, mc * 128:mc * 128 + 128],
                             bb.XN1b[:, ic, tsl],
                             start=(ic == 0), stop=(ic == NC - 1),
                             reads=[("Wc",), X("xn", ic)], writes=[("psA", b)])
                    p.op("act", (lambda sx, b=b, mc=mc: sx.activation(
                        f1[b][:], u.psA[:, b * 512:b * 512 + 512], AF.Identity,
                        bias=g.L1f_bri[:, mc:mc + 1])),
                        reads=[("psA", b), ("c", "L1f_bri")],
                        writes=[X("f1", b), ("psA", b)])
                    p.dma("qg", io.x1b[s, mc][:, tsl], f1[b][:],
                          reads=[X("f1", b), X("x1b", "init", tb)],
                          writes=[X("x1b", "acc"), X("f1", b)],
                          accum_op=ALU.add)
        with _Scope(p) as sc:
            wsl = [sc.sb(f"wsl{i}", [128, NC, 128], BF16) for i in range(2)]
            stage = sc.sb("stage", [128, T], BF16)
            for oc in range(16):
                b = oc % 2
                p.dma("qg", wsl[b][:], _wsl(io.l1wqkv, oc),
                      reads=[X("wsl", b, "free")], writes=[X("wsl", b)])
                for tq in range(4):
                    bk = tq % 2
                    for fc in range(NC):
                        p.mm(u.psA[:, bk * 512:bk * 512 + 512],
                             wsl[b][:, fc, :],
                             bb.XN1b[:, fc, tq * 512:tq * 512 + 512],
                             start=(fc == 0), stop=(fc == NC - 1),
                             reads=[X("wsl", b), X("xn", fc)],
                             writes=[("psA", bk)])
                    p.op("act", (lambda sx, tq=tq, bk=bk: sx.copy(
                        stage[:, tq * 512:tq * 512 + 512],
                        u.psA[:, bk * 512:bk * 512 + 512])),
                        reads=[("psA", bk)],
                        writes=[X("stage", tq), ("psA", bk)])
                dst = io.qf[s, oc] if oc < 8 else io.kf[s, oc - 8]
                p.dma("qa", dst, stage[:],
                      reads=[X("stage", 0), X("stage", 1), X("stage", 2),
                             X("stage", 3)],
                      writes=[X("qk_out", oc), X("stage", 0), X("stage", 1),
                              X("stage", 2), X("stage", 3),
                              X("wsl", b, "free")])
            wqv = sc.sb("wqv", [128, NC, 1024], BF16)
            v_sb = sc.sb("v_sb", [128, 16, 65], BF16)
            p.dma("qg", wqv[:],
                  io.l1wqkv.rearrange("(kc p) o -> p kc o", p=128)[:, :, 2048:3072],
                  writes=[X("wqv")])
            p.op("dve", lambda v: v.memset(v_sb[:, :, 64:65], 1.0),
                 writes=[X("vones")])
            for tt in range(16):
                for vh in range(2):
                    for fc in range(NC):
                        p.mm(u.psB[:, vh * 512:vh * 512 + 512],
                             bb.XN1b[:, fc, tt * 128:tt * 128 + 128],
                             wqv[:, fc, vh * 512:vh * 512 + 512],
                             start=(fc == 0), stop=(fc == NC - 1),
                             reads=[X("wqv"), X("xn", fc)],
                             writes=[("psB", vh)])
                    p.op("dve", (lambda v, tt=tt, vh=vh: v.tensor_copy(
                        v_sb[:, vh * 8:vh * 8 + 8, 0:64],
                        u.psB[:, vh * 512:vh * 512 + 512].rearrange(
                            "p (hh d) -> p hh d", hh=8))),
                        reads=[("psB", vh)],
                        writes=[X("vsb", vh), ("psB", vh)])
                p.dma("qa", io.vf[s, tt], v_sb[:],
                      reads=[X("vsb", 0), X("vsb", 1), X("vones")],
                      writes=[X("v_out", tt), X("vsb", 0), X("vsb", 1)])


def emit_stage_c(p, g, io, u, s):
    nc = p.nc
    X = lambda *k: ("C", s) + k
    with _Scope(p) as sc:
        q_sb = sc.sb("q_sb", [128, NC, T], BF16)
        k_sb = sc.sb("k_sb", [128, NC, T], BF16)
        v_sb = sc.sb("v_sb", [128, 16, 1040], BF16)
        wproj = sc.sb("wproj", [128, NC, 1024], BF16)
        negsh = sc.sb("negsh", [128, 1], F32)
        rrow = sc.sb("rrow", [1, 512], F32)
        bc_sb = sc.sb("bc_sb", [64, 512], F32)
        phi_sb = [sc.sb("phi0", [128, 16, 512], BF16)]
        attn_bf = sc.sb("attn", [128, NC, 512], BF16)
        xt = [sc.sb("xt0", [128, NC, 512], F32)]
        out_sb = xt
        PT = [sc.sb(f"pt{i}", [128, 512], BF16) for i in range(4)]
        mrow = sc.sb("mrow", [1, 4], F32)

        bq = [("B", s, "qk_out", oc) for oc in range(16)]
        bv = [("B", s, "v_out", tt) for tt in range(16)]
        p.dma("qs", q_sb[:], io.qf[s].rearrange("c p t -> p c t"),
              reads=bq, writes=[X("q")])
        p.dma("qs", k_sb[:], io.kf[s].rearrange("c p t -> p c t"),
              reads=bq, writes=[X("k")])
        p.dma("qs", v_sb[:], io.vf[s].rearrange("tt p x -> p tt x"),
              reads=bv, writes=[X("v")])
        p.dma("qg", wproj[:], io.l1wproj.rearrange("(kc p) o -> p kc o", p=128),
              writes=[X("wp")])
        p.op("dve", lambda v: v.memset(mrow[:], 0.0), writes=[X("m0"), X("m1")])
        for side, buf, bkey, mi in ((0, q_sb, X("q"), 0), (1, k_sb, X("k"), 1)):
            for c in range(NC):
                for hf in range(2):
                    p.op("act", (lambda sx, buf=buf, c=c, hf=hf: sx.activation(
                        u.sq1024[:], buf[:, c, hf * 1024:hf * 1024 + 1024],
                        AF.Square)),
                        reads=[bkey], writes=[("u", "sq1024")])
                    for sub in range(2):
                        p.mm(u.psS[0:1, 0:512], g.ones_bf[:],
                             u.sq1024[:, sub * 512:sub * 512 + 512],
                             reads=[("u", "sq1024")], writes=[("psS",)])
                        p.op("dve", lambda v: v.reduce_max(
                            rrow[0:1, 0:1], u.psS[0:1, 0:512], axis=AX.X),
                            reads=[("psS",)], writes=[X("rr"), ("psS",)], hazard=True)
                        p.op("dve", (lambda v, mi=mi: v.tensor_max(
                            mrow[0:1, mi:mi + 1], mrow[0:1, mi:mi + 1],
                            rrow[0:1, 0:1])),
                            reads=[X("rr"), X("m%d" % mi)],
                            writes=[X("m%d" % mi)])
        p.op("act", lambda sx: sx.activation(mrow[0:1, 2:3], mrow[0:1, 0:1],
                                             AF.Sqrt),
             reads=[X("m0")], writes=[X("m2")])
        p.op("act", lambda sx: sx.activation(mrow[0:1, 3:4], mrow[0:1, 1:2],
                                             AF.Sqrt),
             reads=[X("m1")], writes=[X("m3")])
        p.op("dve", lambda v: v.tensor_mul(mrow[0:1, 2:3], mrow[0:1, 2:3],
                                           mrow[0:1, 3:4]),
             reads=[X("m2"), X("m3")], writes=[X("m2")])
        p.mm(u.psS[0:128, 0:1], g.ones1_f[:], mrow[0:1, 2:3],
             reads=[X("m2")], writes=[("psS",)])
        p.op("dve", lambda v: v.tensor_scalar_mul(negsh[:], u.psS[0:128, 0:1],
                                                  -0.125),
             reads=[("psS",)], writes=[X("negsh"), ("psS",)], hazard=True)
        for qt in range(4):
            bq_ = 0
            qsl = slice(qt * 512, qt * 512 + 512)
            p.dma("qs", phi_sb[bq_][:],
                  io.phi[qt].rearrange("kt p x -> p kt x"),
                  writes=[X("phi", bq_)])
            p.dma("qs", xt[bq_][:],
                  io.x1b[s].rearrange("c p t -> p c t")[:, :, qsl],
                  reads=[("B", s, "x1b", "acc")] +
                        [("B", s, "x1b", "init", tb) for tb in range(4)],
                  writes=[X("xt", bq_)] +
                         [X("osb", bq_, mc) for mc in range(NC)])
            for hh in range(16):
                pb = (hh % 2) * 64
                ch = hh // 2
                av = hh % 2
                qrhs = q_sb[pb:pb + 64, ch, qsl]
                LAG = 2
                for i in range(16 + LAG):
                    if i < 16:
                        kt = i
                        bk = kt % 4
                        pst = u.psA if bk < 2 else u.psC
                        pk = ("psA", bk) if bk < 2 else ("psC", bk - 2)
                        hf = (bk % 2) * 512
                        klhs = k_sb[pb:pb + 64, ch, kt * 128:kt * 128 + 128]
                        p.mm(pst[:, hf:hf + 512], klhs, qrhs,
                             start=True, stop=False,
                             reads=[X("q"), X("k")], writes=[pk])
                        p.mm(pst[:, hf:hf + 512], g.ident_bf[:],
                             phi_sb[bq_][:, kt, :], start=False, stop=True,
                             reads=[X("phi", bq_), ("c", "ident_bf")],
                             writes=[pk])
                        p.op("act", (lambda sx, _o=PT[bk][:],
                            _i=pst[:, hf:hf + 512],
                            _b=negsh[:, 0:1]: sx.activation(
                            _o, _i, AF.Exp, bias=_b, scale=0.125)),
                            reads=[pk, X("negsh")],
                            writes=[X("pt", bk), pk])
                    if i >= LAG:
                        kt = i - LAG
                        bk = kt % 4
                        p.mm(u.psB[0:65, av * 512:av * 512 + 512],
                             v_sb[:, kt, hh * 65:hh * 65 + 65], PT[bk][:],
                             start=(kt == 0), stop=(kt == 15),
                             reads=[X("v"), X("pt", bk)],
                             writes=[("psB", av), X("pt", bk)])
                p.op("dve", (lambda v, _o=rrow[:],
                    _i=u.psB[64:65, av * 512:av * 512 + 512]: v.tensor_scalar(
                    _o, _i, 1e-30, None, op0=ALU.add)),
                    reads=[("psB", av)], writes=[X("rr")], hazard=True)
                p.op("dve", (lambda v, _o=rrow[:]: v.reciprocal(_o, _o)),
                     reads=[X("rr")], writes=[X("rr")], hazard=True)
                p.mm(u.psS[0:64, 0:512], g.ones1_f[0:1, 0:64], rrow[:],
                     reads=[X("rr")], writes=[("psS",)])
                p.op("dve", (lambda v, _o=bc_sb[:]: v.tensor_copy(_o,
                                                    u.psS[0:64, 0:512])),
                     reads=[("psS",)], writes=[X("bcs"), ("psS",)])
                p.op("dve", (lambda v, _o=attn_bf[pb:pb + 64, ch, :],
                    _i=u.psB[0:64, av * 512:av * 512 + 512],
                    _b=bc_sb[:]: v.tensor_mul(_o, _i, _b)),
                    reads=[("psB", av), X("bcs")],
                    writes=[X("at", hh), ("psB", av)])
            for mc in range(NC):
                bk = mc % 2
                for kc in range(NC):
                    p.mm(u.psC[:, bk * 512:bk * 512 + 512],
                         wproj[:, kc, mc * 128:mc * 128 + 128],
                         attn_bf[:, kc, :],
                         start=(kc == 0), stop=(kc == NC - 1),
                         reads=[X("wp"), X("at", 2 * kc), X("at", 2 * kc + 1)],
                         writes=[("psC", bk)])
                p.op("dve", (lambda v, bk=bk, mc=mc, bq_=bq_: v.scalar_tensor_tensor(
                    out_sb[bq_][:, mc, :], u.psC[:, bk * 512:bk * 512 + 512],
                    g.L1_bproj[:, mc:mc + 1], xt[bq_][:, mc, :],
                    op0=ALU.add, op1=ALU.add)),
                    reads=[("psC", bk), X("xt", bq_), ("c", "L1_bproj")],
                    writes=[X("osb", bq_, mc), ("psC", bk)], hazard=True)
            p.dma("qa", io.out[s].rearrange("c p t -> p c t")[:, :, qsl],
                  out_sb[bq_][:],
                  reads=[X("osb", bq_, mc) for mc in range(NC)],
                  writes=[X("out", qt)] + [X("osb", bq_, mc) for mc in range(NC)]
                  + [X("xt", bq_)])


def build_nc(stages, shared):
    nc = bass.Bass("TRN2", target_bir_lowering=False, debug=False)
    p = Prog(nc)
    io = H()
    declared = {}

    def din(name, shape, dtype):
        t = nc.dram_tensor(name, list(shape), dtype, kind="ExternalInput").ap()
        declared[name] = "in"
        return t

    def dout(name, shape, dtype):
        t = nc.dram_tensor(name, list(shape), dtype, kind="ExternalOutput").ap()
        declared[name] = "out"
        return t

    def dscratch(name, shape, dtype, produced, consumed):
        if produced and consumed:
            t = nc.dram_tensor(name, list(shape), dtype, kind="Internal").ap()
            declared[name] = "internal"
            return t
        if produced:
            return dout(name, shape, dtype)
        return din(name, shape, dtype)

    A = "A" in stages
    B = "B" in stages
    C = "C" in stages
    io.consts = {}
    cshapes = {}
    for k, v in shared.items():
        cshapes[k] = v
    const_names = ["ones_bf", "ones64a_bf", "ones64b_bf", "ones1_f", "selA_f",
                   "selB_f", "ident_bf", "ident_f", "rot_bf", "maskband_bf",
                   "L0a_lng", "L0a_lnb", "L0a_lam", "L0f_b1a", "L0f_b1g",
                   "L0f_b2", "L0f_lng", "L0f_lnb", "L0f_qs", "L0f_ks",
                   "L0f_qs_sw", "L0f_ks_sw", "L1_bproj", "L1f_bri",
                   "L1f_G1R", "L1f_G1I", "L1f_G2R", "L1f_G2I",
                   "Yb_f", "Ye_f", "Zd_f", "Zg_f",
                   "c1q_bf", "c2q_bf", "c1k_bf", "c2k_bf"]
    for nm in const_names:
        arr = shared[nm]
        dt = BF16 if arr.dtype == ml_dtypes.bfloat16 else F32
        io.consts[nm] = din("K_" + nm, arr.shape, dt)
    hd = io.consts

    if A and any(k in DEV for k in ("dump_xf", "dump_carry", "dump_q", "dump_v", "dump_attn", "dump_rn", "dump_sh")):
        io.dbg = dout("dbg", [NC, 128, TH], F32)
    if A:
        io.x = din("x", [SEQS, NC, 128, T], F32)
        io.wqT = din("L0a_WqT", [D, D], F32)
        io.wvT = din("L0a_WvT", [D, D], F32)
        io.wgT = din("L0a_WgT", [D, D], F32)
        io.woT = din("L0a_WoT", [D, D], F32)
        io.wqkv = din("L0f_wqkv", [2, D, 1536], F32)
        io.w1a = din("L0f_w1a", [2, D, HIDP], F32)
        io.w1g = din("L0f_w1g", [2, D, HIDP], F32)
        io.w2 = din("L0f_w2", [2, HIDP, D], F32)
        io.wo = din("L0f_wo", [2, 512, D], F32)
    if A or B:
        io.x1a = dscratch("x1a", [SEQS, NC, 128, T], F32, produced=A, consumed=B)
    if B:
        io.l1wqkv = din("L1_wqkv", [D, 3 * D], F32)
    if B or C:
        io.x1b = dscratch("x1b", [SEQS, NC, 128, T], F32, produced=B, consumed=C)
        io.qf = dscratch("qf", [SEQS, NC, 128, T], BF16, produced=B, consumed=C)
        io.kf = dscratch("kf", [SEQS, NC, 128, T], BF16, produced=B, consumed=C)
        io.vf = dscratch("vf", [SEQS, 16, 128, 16 * 65], BF16,
                         produced=B, consumed=C)
    if C:
        io.l1wproj = din("L1_wproj", [D, D], F32)
        io.phi = din("K_phi8_bf", [4, 16, 128, 512], BF16)
        io.out = dout("out", [SEQS, NC, 128, T], F32)
    u = H()
    u.psA = p.ps("psA", [128, 1024])
    u.psB = p.ps("psB", [128, 1024])
    u.psC = p.ps("psC", [128, 1024])
    u.psS = p.ps("psS", [128, 512])
    u.psQ = p.ps("psQ", [128, 512])
    u.sq = p.sb("u_sq", [128, 512], BF16)
    u.sq1024 = p.sb("u_sq1024", [128, 1024], BF16)
    u.tmp = p.sb("u_tmp", [128, 512], F32)
    u.tmp_f = p.sb("u_tmp_f", [128, 1024], F32)
    u.tmp_bf = p.sb("u_tmp_bf", [128, 1024], BF16)
    u.rows = p.sb("u_rows", [1, 1536], F32)
    u.nrows = p.sb("u_nrows", [1, 1024], F32)
    u.eps = p.sb("u_eps", [1, 1], F32)
    p.op("dve", lambda v: v.memset(u.eps[:], 1e-5), writes=[("u", "eps")])
    g = declare_consts(p, nc, io.consts)
    if A:
        with _Scope(p) as basesc:
            base = H()
            base.XF = basesc.sb("XF", [128, NC, TH + 1], F32)
            base.save_ln_f = basesc.sb("save_ln_f", [128, NC, 2], F32)
            base.XFb = basesc.sb("XFb", [128, NC, TH + 1], BF16)
            base.carry = basesc.sb("carry", [128, NC], F32)
            base.save_qkv = basesc.sb("save_qkv", [128, NC, 2], BF16)
            base.save_ln = basesc.sb("save_ln", [128, NC, 2], BF16)
            base.save_k = basesc.sb("save_k", [128, 4, 512, 2], BF16)
            base.save_v = basesc.sb("save_v", [128, 4, 520, 2], BF16)
            for s in range(DEV.get("nseq", SEQS)):
                emit_stage_a(p, g, io, u, base, s)
    if B:
        with _Scope(p) as bsc:
            bb = H()
            bb.Wc = bsc.sb("Wc", [128, NC, 1024], BF16)
            bb.XN1b = bsc.sb("XN1b", [128, NC, T], BF16)
            emit_stage_b(p, g, io, u, bb)
    if C:
        for s in range(SEQS):
            emit_stage_c(p, g, io, u, s)
    global LAST_PROG
    LAST_PROG = p
    p.finalize_and_emit()
    in_names = [k for k, v in declared.items() if v == "in"]
    out_names = [k for k, v in declared.items() if v == "out"]
    return nc, in_names, out_names


def _run(nc, in_names, maps):
    in_maps = [{k: m[k] for k in in_names} for m in maps]
    res = run_bass_kernel_spmd(nc, in_maps, core_ids=list(range(NCORES)))
    return res.results


def kernel(**inputs):
    shared, per_core = host_prep(inputs)
    smap = {}
    for k, v in shared.items():
        smap["K_" + k] = v
    smap["K_phi8_bf"] = shared["phi8_bf"]
    smap.update({
        "L0a_WqT": shared["L0a_WqT"], "L0a_WvT": shared["L0a_WvT"],
        "L0a_WgT": shared["L0a_WgT"], "L0a_WoT": shared["L0a_WoT"],
        "L0f_wqkv": shared["L0f_wqkv"], "L0f_w1a": shared["L0f_w1a"],
        "L0f_w1g": shared["L0f_w1g"], "L0f_w2": shared["L0f_w2"],
        "L0f_wo": shared["L0f_wo"], "L1_wqkv": shared["L1_wqkv"],
        "L1_wproj": shared["L1_wproj"],
    })
    if UNFUSED:
        stage_sets = [("A",), ("B",), ("C",)]
    else:
        stage_sets = [("A", "B", "C")]
    core_maps = [dict(smap, **pc) for pc in per_core]
    outs = None
    for stages in stage_sets:
        nc, in_names, out_names = build_nc(stages, shared)
        results = _run(nc, in_names, core_maps)
        for core in range(NCORES):
            for onm in out_names:
                core_maps[core][onm] = np.asarray(results[core][onm])
        outs = results
    out = np.empty((16, T, D), np.float32)
    for core in range(NCORES):
        o = np.asarray(outs[core]["out"], np.float32).reshape(SEQS, D, T)
        for s in range(SEQS):
            out[core * SEQS + s] = o[s].T
    return out


# version 08
# speedup vs baseline: 1.4939x; 1.0476x over previous
import math
from contextlib import ExitStack

import numpy as np
import ml_dtypes

import concourse.bass as bass
import concourse.mybir as mybir
from concourse.bass_utils import run_bass_kernel_spmd

F32 = mybir.dt.float32
BF16 = mybir.dt.bfloat16
AF = mybir.ActivationFunctionType
ALU = mybir.AluOpType
AX = mybir.AxisListType

NCORES = 8
SEQS = 2
T = 2048
D = 1024
NC = 8
TH = 1024
HID = 2730
HIDP = 2816
NJ = HIDP // 128
NEG = -1.0e30

UNFUSED = False
DEV = {}
LAST_PROG = None

COMPUTE_STREAMS = ("pe", "act", "dve", "pool", "sp")
QUEUE_ISSUER = {"qs": "sp", "qa": "act", "qg": "pool"}
QUEUE_LANES = {"qs": 8, "qa": 6, "qg": 12}
LANE_STREAMS = [(q, r) for q, R in QUEUE_LANES.items() for r in range(R)]
STREAM_ENGINE = {"pe": "tensor", "act": "scalar", "dve": "vector",
                 "pool": "gpsimd", "sp": "sync"}


def _is_lane(s):
    return isinstance(s, tuple)


class _Op:
    __slots__ = ("stream", "issuer", "emit", "waits", "count", "inc", "tag")

    def __init__(self, stream, issuer, emit, waits, count, tag=None):
        self.stream = stream
        self.issuer = issuer
        self.emit = emit
        self.waits = waits
        self.count = count
        self.inc = False
        self.tag = tag


class Prog:
    def __init__(self, nc):
        self.nc = nc
        self.es = ExitStack()
        self.counts = {s: 0 for s in COMPUTE_STREAMS}
        self.counts.update({ls: 0 for ls in LANE_STREAMS})
        self.qidx = {q: 0 for q in QUEUE_ISSUER}
        self.ops = {s: [] for s in COMPUTE_STREAMS}
        self.vc = {s: {} for s in COMPUTE_STREAMS}
        self.res = {}
        self.sems = {}
        self.hazards = {s: set() for s in COMPUTE_STREAMS}
        self.self_hw = {s: 0 for s in COMPUTE_STREAMS}

    def sb(self, name, shape, dtype):
        return self.es.enter_context(
            self.nc.sbuf_tensor(_uname(name), shape, dtype))

    def ps(self, name, shape, dtype=F32):
        return self.es.enter_context(
            self.nc.psum_tensor(_uname(name), shape, dtype))

    @staticmethod
    def _merge(dst, src):
        for k, v in src.items():
            if dst.get(k, 0) < v:
                dst[k] = v

    def op(self, stream, emit, reads=(), writes=(), queue=None, tag=None,
           self_wait=False, hazard=False):
        lane_prewait = None
        if queue is not None:
            issuer = QUEUE_ISSUER[queue]
            r = self.qidx[queue] % QUEUE_LANES[queue]
            self.qidx[queue] += 1
            cstream = (queue, r)
            step = 16
            if self.counts[cstream] > 0:
                lane_prewait = (cstream, self.counts[cstream])
        else:
            issuer = stream
            cstream = stream
            step = 1
        deps = []
        bar = self.res.pop(("__bar__", issuer), None)
        if bar is not None and bar[0] is not None:
            deps.append(bar[0])
        for r in reads:
            st = self.res.get(r)
            if st is not None and st[0] is not None:
                deps.append(st[0])
        for w in writes:
            st = self.res.get(w)
            if st is not None:
                if st[0] is not None:
                    deps.append(st[0])
                deps.extend(st[1])
        need = {}
        for st in deps:
            self._merge(need, st[2])
        self_tgt = 0
        if queue is None:
            for st in deps:
                if st[0] == issuer:
                    if issuer == "pe" and not self_wait:
                        continue
                    self_tgt = max(self_tgt, st[1])
        ivc = self.vc[issuer]
        waits = {}
        if lane_prewait is not None:
            s, n = lane_prewait
            if ivc.get(s, 0) < n:
                waits[s] = n
                self._merge(ivc, {s: n})
        for s, n in need.items():
            if s == issuer and queue is None:
                if self_tgt > 0 and self.self_hw[issuer] < self_tgt:
                    waits[s] = self_tgt
                    self.self_hw[issuer] = self_tgt
                continue
            if ivc.get(s, 0) < n:
                waits[s] = n
        self._merge(ivc, need)
        self.counts[cstream] += step
        c = self.counts[cstream]
        if queue is None:
            ivc[cstream] = c
        rec = _Op(cstream, issuer, emit, waits, c,
                  tag or (tuple(writes)[:1] or ("?",))[0])
        self.ops[issuer].append(rec)
        if hazard and queue is None:
            self.hazards[cstream].add(c)
        if queue is None:
            svc = dict(ivc)
        else:
            svc = {s: n for s, n in ivc.items() if s != issuer}
            svc[cstream] = c
        stamp = (cstream, c, svc)
        for w in writes:
            self.res[w] = (stamp, [])
        for r in reads:
            st = self.res.get(r)
            if st is None:
                self.res[r] = (None, [stamp])
            else:
                st[1].append(stamp)
        return stamp

    def dma(self, queue, out_ap, in_ap, reads=(), writes=(), **kw):
        return self.op(
            None,
            (lambda eng, o=out_ap, i=in_ap, kw=kw: eng.dma_start(out=o, in_=i, **kw)),
            reads=reads, writes=writes, queue=queue)

    def mm(self, out, lhsT, rhs, start=True, stop=True, reads=(), writes=()):
        return self.op(
            "pe",
            (lambda t, o=out, l=lhsT, r=rhs, st=start, sp=stop:
             t.matmul(o, l, r, start=st, stop=sp)),
            reads=reads, writes=writes)

    def barrier(self):
        cur = {s: c for s, c in self.counts.items() if c > 0 and s != "sp"}
        need = dict(cur)
        ivc = self.vc["sp"]
        waits = {s: n for s, n in need.items() if ivc.get(s, 0) < n}
        self._merge(ivc, need)
        self.counts["sp"] += 1
        c = self.counts["sp"]
        ivc["sp"] = c
        rec = _Op("sp", "sp", (lambda eng: eng.nop()), waits, c)
        self.ops["sp"].append(rec)
        bstamp = ("sp", c, dict(ivc))
        for s in COMPUTE_STREAMS:
            if s != "sp":
                self.res[("__bar__", s)] = (bstamp, [])
        return bstamp

    def finalize_and_emit(self):
        nc = self.nc
        finals = {s: c for s, c in self.counts.items() if c > 0 and s != "sp"}
        targets = {s: set() for s in self.counts}
        for lst in self.ops.values():
            for o in lst:
                for s, n in o.waits.items():
                    targets[s].add(n)
        for s, c in finals.items():
            targets[s].add(c)
        for ls in LANE_STREAMS:
            targets[ls] = set(range(16, self.counts[ls] + 1, 16))
        miles = {s: sorted(ts) for s, ts in targets.items()}
        import bisect

        def remap(s, n):
            return bisect.bisect_right(miles[s], n) * (16 if _is_lane(s) else 1)

        mileset = {s: set(ts) for s, ts in miles.items()}
        for lst in self.ops.values():
            for o in lst:
                if o.count in mileset[o.stream]:
                    o.inc = True
        for s, ts in miles.items():
            if ts:
                nm = "sem_" + ("_".join(map(str, s)) if _is_lane(s) else str(s))
                self.sems[s] = self.es.enter_context(nc.semaphore(nm))
        block = self.es.enter_context(nc.Block())

        def emit_stream(eng_stream):
            ops = self.ops[eng_stream]
            if not ops and eng_stream != "sp":
                return
            dec = getattr(block, STREAM_ENGINE[eng_stream])

            def body(eng):
                for o in ops:
                    for s, n in sorted(o.waits.items(), key=lambda x: str(x[0])):
                        eng.wait_ge(self.sems[s], remap(s, n))
                    instr = o.emit(eng)
                    if o.inc:
                        instr.then_inc(self.sems[o.stream],
                                       16 if _is_lane(o.stream) else 1)
                if eng_stream == "sp":
                    for s, c in sorted(finals.items(), key=lambda x: str(x[0])):
                        if c and self.vc["sp"].get(s, 0) < c:
                            eng.wait_ge(self.sems[s], remap(s, c))
            dec(body)

        for s in ("sp", "pe", "act", "dve", "pool"):
            emit_stream(s)


def _bf(x):
    return np.asarray(x, np.float32).astype(ml_dtypes.bfloat16)


def host_constants():
    C = {}
    C["ones_bf"] = _bf(np.ones((128, 1)))
    on64a = np.zeros((128, 1), np.float32); on64a[:64] = 1.0
    on64b = np.zeros((128, 1), np.float32); on64b[64:] = 1.0
    C["ones64a_bf"] = _bf(on64a)
    C["ones64b_bf"] = _bf(on64b)
    C["ones1_f"] = np.ones((1, 128), np.float32)
    sa = np.zeros((1, 128), np.float32); sa[0, :64] = 1.0
    sb_ = np.zeros((1, 128), np.float32); sb_[0, 64:] = 1.0
    C["selA_f"] = sa
    C["selB_f"] = sb_
    C["ident_bf"] = _bf(np.eye(128))
    C["ident_f"] = np.eye(128, dtype=np.float32)
    R = np.zeros((128, 128), np.float32)
    for base in (0, 64):
        for i in range(32):
            R[base + 32 + i, base + i] = -1.0
            R[base + i, base + 32 + i] = 1.0
    C["rot_bf"] = _bf(R)
    DH = 64
    WINDOW = 512
    n = T
    half_idx = np.arange(0, DH, 2, dtype=np.float64)
    inv_freq = 1.0 / 10000.0 ** (half_idx / DH)
    t = np.arange(n, dtype=np.float64)
    freqs = t[:, None] * inv_freq[None, :]
    freqs = np.concatenate([freqs, freqs], axis=-1)
    base = (half_idx + 0.4 * DH) / (1.4 * DH)
    power = (t - n // 2) / float(WINDOW // 2)
    scale = base[None, :] ** power[:, None]
    scale = np.concatenate([scale, scale], axis=-1)
    cos, sin = np.cos(freqs), np.sin(freqs)
    rep = lambda a: np.tile(np.ascontiguousarray(a.T), (2, 1))
    C["c1q_bf"] = _bf(rep(cos * scale * 8.0))
    C["c2q_bf"] = _bf(rep(sin * scale * 8.0))
    C["c1k_bf"] = _bf(rep(cos / scale))
    C["c2k_bf"] = _bf(rep(sin / scale))
    W = 1408
    jj = np.arange(128)[:, None]
    y = np.arange(W)[None, :] - 384
    band = (y - jj >= 0) & (y - jj <= 512)
    C["maskband_bf"] = _bf(np.where(band, 0.0, NEG))
    PHI = (1.0 + math.sqrt(5.0)) / 2.0
    pos = (np.arange(T, dtype=np.float32) * np.float32(PHI)) % np.float32(1.0)
    dist = np.abs(pos[:, None] - pos[None, :])
    phi = np.log(np.exp(-dist) + 1e-8).astype(np.float32) * 8.0
    phiT = np.ascontiguousarray(phi.T)
    C["phi8_bf"] = _bf(
        phiT.reshape(16, 128, 4, 512).transpose(2, 0, 1, 3))
    Yb = np.zeros((32, 512), np.float32)
    Ye = np.zeros((16, 512), np.float32)
    for idx in range(512):
        Yb[idx // 16, idx] = 1.0
        Ye[idx % 16, idx] = 1.0
    Zd = np.zeros((32, 4, 128), np.float32)
    Zg = np.zeros((16, 4, 128), np.float32)
    for pc in range(4):
        for col in range(128):
            idx = pc * 128 + col
            Zd[idx // 16, pc, col] = 1.0
            Zg[idx % 16, pc, col] = 1.0
    C["Yb_f"] = Yb
    C["Ye_f"] = Ye
    C["Zd_f"] = Zd
    C["Zg_f"] = Zg
    return C


def _pp(vec, nch):
    return np.ascontiguousarray(np.asarray(vec, np.float32).reshape(nch, 128).T)


def host_prep(inputs):
    ix = {k: np.asarray(v) for k, v in inputs.items()}
    shared = dict(host_constants())
    for nm in ("Wq", "Wv", "Wg", "Wo"):
        shared["L0a_" + nm + "T"] = np.ascontiguousarray(ix["L0_attn_" + nm].T)
    shared["L0a_lng"] = _pp(ix["L0_attn_ln_gamma"], 8)
    shared["L0a_lnb"] = _pp(ix["L0_attn_ln_beta"], 8)
    shared["L0a_lam"] = np.ascontiguousarray(
        np.repeat(ix["L0_attn_decay"].astype(np.float32), 64)[None, :])
    shared["L0f_wqkv"] = np.ascontiguousarray(ix["L0_ffn_wqkv"])
    w1 = ix["L0_ffn_w1"]
    w1a = np.zeros((2, 1024, HIDP), np.float32)
    w1g = np.zeros((2, 1024, HIDP), np.float32)
    w1a[:, :, :HID] = w1[:, :, :HID]
    w1g[:, :, :HID] = w1[:, :, HID:]
    shared["L0f_w1a"] = w1a
    shared["L0f_w1g"] = w1g
    w2 = ix["L0_ffn_w2"]
    w2p = np.zeros((2, HIDP, 1024), np.float32)
    w2p[:, :HID, :] = w2
    shared["L0f_w2"] = w2p
    shared["L0f_wo"] = np.ascontiguousarray(ix["L0_ffn_wo"])
    b1 = ix["L0_ffn_b1"]
    b1a = np.zeros((2, HIDP), np.float32); b1a[:, :HID] = b1[:, :HID]
    b1g = np.zeros((2, HIDP), np.float32); b1g[:, :HID] = b1[:, HID:]
    shared["L0f_b1a"] = np.stack([_pp(b1a[l], NJ) for l in range(2)], 1)
    shared["L0f_b1g"] = np.stack([_pp(b1g[l], NJ) for l in range(2)], 1)
    shared["L0f_b2"] = np.stack([_pp(ix["L0_ffn_b2"][l], 8) for l in range(2)], 1)
    shared["L0f_lng"] = np.stack([_pp(ix["L0_ffn_ln_g"][l], 8) for l in range(2)], 1)
    shared["L0f_lnb"] = np.stack([_pp(ix["L0_ffn_ln_b"][l], 8) for l in range(2)], 1)
    qs = ix["L0_ffn_q_scale"].astype(np.float32)
    ks = ix["L0_ffn_k_scale"].astype(np.float32)
    tile2 = lambda v: np.ascontiguousarray(np.tile(v, 2)[:, None])
    shared["L0f_qs"] = np.concatenate([tile2(qs[l]) for l in range(2)], 1)
    shared["L0f_ks"] = np.concatenate([tile2(ks[l]) for l in range(2)], 1)
    swap = lambda v: np.concatenate([v[32:], v[:32]])
    shared["L0f_qs_sw"] = np.concatenate([tile2(swap(qs[l])) for l in range(2)], 1)
    shared["L0f_ks_sw"] = np.concatenate([tile2(swap(ks[l])) for l in range(2)], 1)
    shared["L1_wqkv"] = np.ascontiguousarray(ix["L1_attn_w_qkv"])
    shared["L1_wproj"] = np.ascontiguousarray(ix["L1_attn_w_proj"])
    shared["L1_bproj"] = _pp(ix["L1_attn_b_proj"], 8)
    shared["L1f_G1R"] = np.ascontiguousarray(
        ix["L1_ffn_G1_R"].reshape(32, 4, 32).astype(np.float32))
    shared["L1f_G1I"] = np.ascontiguousarray(
        ix["L1_ffn_G1_I"].reshape(32, 4, 32).astype(np.float32))
    shared["L1f_G2R"] = np.ascontiguousarray(
        ix["L1_ffn_G2_R"].reshape(4, 16, 16).transpose(1, 0, 2))
    shared["L1f_G2I"] = np.ascontiguousarray(
        ix["L1_ffn_G2_I"].reshape(4, 16, 16).transpose(1, 0, 2))
    shared["L1f_bri"] = _pp(
        np.concatenate([ix["L1_ffn_bias_real"], ix["L1_ffn_bias_imag"]]), 8)
    x = np.asarray(ix["L0_attn_x"], np.float32)
    per_core = []
    for core in range(NCORES):
        xs = x[core * SEQS:(core + 1) * SEQS]
        xf = xs.transpose(0, 2, 1).reshape(SEQS, NC, 128, T)
        per_core.append({"x": np.ascontiguousarray(xf)})
    return shared, per_core


class H:
    pass


def declare_consts(p, nc, hd):
    g = H()
    ld = []

    def load(name, shape, dtype, src_name=None, cast=False):
        tsb = p.sb("c_" + name, shape, dtype)
        src = hd[src_name or name]
        q = "qg" if cast else "qs"
        p.dma(q, tsb[:], src, writes=[("c", name)])
        setattr(g, name, tsb)
        return tsb

    load("ones_bf", [128, 1], BF16)
    load("ones64a_bf", [128, 1], BF16)
    load("ones64b_bf", [128, 1], BF16)
    load("ones1_f", [1, 128], F32)
    load("selA_f", [1, 128], F32)
    load("selB_f", [1, 128], F32)
    load("ident_bf", [128, 128], BF16)
    load("ident_f", [128, 128], F32)
    load("rot_bf", [128, 128], BF16)
    load("maskband_bf", [128, 1408], BF16)
    load("L0a_lng", [128, 8], F32)
    load("L0a_lnb", [128, 8], F32)
    load("L0a_lam", [1, 1024], F32)
    load("L0f_b1a", [128, 2, NJ], F32)
    load("L0f_b1g", [128, 2, NJ], F32)
    load("L0f_b2", [128, 2, 8], F32)
    load("L0f_lng", [128, 2, 8], F32)
    load("L0f_lnb", [128, 2, 8], F32)
    load("L0f_qs", [128, 2], F32)
    load("L0f_ks", [128, 2], F32)
    load("L0f_qs_sw", [128, 2], F32)
    load("L0f_ks_sw", [128, 2], F32)
    load("L1_bproj", [128, 8], F32)
    load("L1f_bri", [128, 8], F32)
    load("L1f_G1R", [32, 4, 32], F32)
    load("L1f_G1I", [32, 4, 32], F32)
    load("L1f_G2R", [16, 4, 16], F32)
    load("L1f_G2I", [16, 4, 16], F32)
    load("Yb_f", [32, 512], F32)
    load("Ye_f", [16, 512], F32)
    load("Zd_f", [32, 4, 128], F32)
    load("Zg_f", [16, 4, 128], F32)
    onesrow = p.sb("c_onesrow", [1, 512], F32)
    p.op("dve", lambda v: v.memset(onesrow[:], 1.0), writes=[("c", "onesrow")])
    g.onesrow = onesrow
    return g


def emit_ln(p, g, u, key, get_src, n, out_bf16, src_reads, out_keys,
            out_fp32=None, affine=None, eps=1e-5, get_src_f=None):
    r0 = lambda sl: u.rows[0:1, 0 * 512 + 0:0 * 512 + n]
    r1 = lambda sl: u.rows[0:1, 1 * 512 + 0:1 * 512 + n]
    r2 = lambda sl: u.rows[0:1, 2 * 512 + 0:2 * 512 + n]
    K = lambda *a: (key,) + a
    for c in range(NC):
        src = get_src(c)
        p.op("act", (lambda s, src=src: s.activation(u.sq[:, :n], src, AF.Square)),
             reads=list(src_reads) + [("psQ",)], writes=[("u", "sq")])
        p.mm(u.psS[0:1, :n], g.ones_bf[:], src, start=(c == 0), stop=(c == NC - 1),
             reads=list(src_reads), writes=[("psS",)])
        p.mm(u.psQ[0:1, :n], g.ones_bf[:], u.sq[:, :n], start=(c == 0),
             stop=(c == NC - 1), reads=[("u", "sq")], writes=[("psQ",)])
    p.op("dve", lambda v: v.tensor_scalar_mul(r0(0), u.psS[0:1, :n], 1.0 / D),
         reads=[("psS",)], writes=[("u", "rows", 0)], hazard=True)
    p.op("dve", lambda v: v.tensor_scalar_mul(r1(0), u.psQ[0:1, :n], 1.0 / D),
         reads=[("psQ",)], writes=[("u", "rows", 1)], hazard=True)
    p.op("dve", lambda v: v.tensor_mul(r2(0), r0(0), r0(0)),
         reads=[("u", "rows", 0)], writes=[("u", "rows", 2)])
    p.op("dve", lambda v: v.tensor_sub(r1(0), r1(0), r2(0)),
         reads=[("u", "rows", 1), ("u", "rows", 2)], writes=[("u", "rows", 1)])
    p.op("act", lambda s: s.activation(r2(0), r1(0), AF.Sqrt,
                                       bias=u.eps[0:1, 0:1]),
         reads=[("u", "rows", 1), ("u", "eps")], writes=[("u", "rows", 2)])
    p.op("dve", lambda v: v.reciprocal(r1(0), r2(0)),
         reads=[("u", "rows", 2)], writes=[("u", "rows", 1)], hazard=True)
    p.op("dve", lambda v: v.tensor_mul(r2(0), r0(0), r1(0)),
         reads=[("u", "rows", 0), ("u", "rows", 1)], writes=[("u", "rows", 2)])
    p.mm(u.psC[:, 0:n], g.ones1_f[:], r1(0), reads=[("u", "rows", 1)], writes=[("psC", 0)])
    p.mm(u.psC[:, 512:512 + n], g.ones1_f[:], r2(0), reads=[("u", "rows", 2)], writes=[("psC", 1)])
    for c in range(NC):
        src = (get_src_f or get_src)(c)
        p.op("dve",
             (lambda v, src=src: v.tensor_mul(u.tmp[:, :n], src, u.psC[:, 0:n])),
             reads=list(src_reads) + [("psC", 0)], writes=[("u", "tmp")])
        if affine is not None:
            p.op("dve", lambda v: v.tensor_sub(u.tmp[:, :n], u.tmp[:, :n],
                                               u.psC[:, 512:512 + n]),
                 reads=[("u", "tmp"), ("psC", 1)], writes=[("u", "tmp")])
            gam, bet = affine
            dstb = out_bf16(c)
            p.op("act",
                 (lambda s, c=c, dstb=dstb: s.activation(
                     dstb, u.tmp[:, :n], AF.Identity,
                     scale=gam(c), bias=bet(c))),
                 reads=[("u", "tmp")], writes=[out_keys(c)])
        elif out_fp32 is not None:
            dst = out_fp32(c)
            p.op("dve", (lambda v, dst=dst: v.tensor_sub(dst, u.tmp[:, :n],
                                                         u.psC[:, 512:512 + n])),
                 reads=[("u", "tmp"), ("psC", 1)], writes=[out_keys(c)])
            if out_bf16 is not None:
                dstb = out_bf16(c)
                p.op("act", (lambda s, dst=dst, dstb=dstb: s.copy(dstb, dst)),
                     reads=[out_keys(c)], writes=[out_keys(c)])
        else:
            dstb = out_bf16(c)
            p.op("dve", (lambda v, dstb=dstb: v.tensor_sub(dstb, u.tmp[:, :n],
                                                           u.psC[:, 512:512 + n])),
                 reads=[("u", "tmp"), ("psC", 1)], writes=[out_keys(c)])


def _wsl(w2d, oc, ow=128):
    r = w2d.rearrange("(kc p) o -> p kc o", p=128)
    return r[:, :, oc * ow:(oc + 1) * ow]


_NAMECTR = [0]


def _uname(name):
    _NAMECTR[0] += 1
    return f"{name}_{_NAMECTR[0]}"


class _Scope:
    def __init__(self, p):
        self.p = p
        self.es = ExitStack()

    def __enter__(self):
        return self

    def sb(self, name, shape, dtype):
        return self.es.enter_context(
            self.p.nc.sbuf_tensor(_uname(name), shape, dtype))

    def __exit__(self, *a):
        self.p.barrier()
        self.es.close()
        return False


def _shiftoff(fc):
    return 0 if fc >= 4 else 1


def emit_stage_a(p, g, io, u, base, s):
    nc = p.nc
    a = base
    X = lambda *k: ("A", s) + k
    p.barrier()
    p.op("dve", lambda v: v.memset(a.carry[:], float(DEV.get('carry0', 0.0))), writes=[("carry",)])

    for h in range(2):
        h0 = h * TH
        with _Scope(p) as sc:
            xt = sc.sb("xt", [128, NC, 512], F32)
            xb = sc.sb("xb", [128, NC, 512], BF16)
            for tb in range(2):
                tsl = slice(h0 + tb * 512, h0 + tb * 512 + 512)
                src = io.x[s].rearrange("c p t -> p c t")[:, :, tsl]
                p.dma("qs", xt[:], src, writes=[X("xt")])
                p.dma("qa", io.x1a[s].rearrange("c p t -> p c t")[:, :, tsl],
                      xt[:], reads=[X("xt")], writes=[X("x1a", "init", h, tb)])
                p.op("act", (lambda sx, _o=xb[:], _i=xt[:]: sx.copy(_o, _i)),
                     reads=[X("xt")], writes=[X("xb")])
                if "0" in DEV.get("bar", ""):
                    p.barrier()
                emit_ln(
                    p, g, u, X("ln0", h, tb),
                    get_src=lambda c: xb[:, c, :],
                    get_src_f=lambda c: xt[:, c, :],
                    n=512,
                    out_fp32=lambda c, tb=tb: a.XF[:, c, 1 + tb * 512:1 + tb * 512 + 512],
                    out_bf16=lambda c, tb=tb: a.XFb[:, c, 1 + tb * 512:1 + tb * 512 + 512],
                    src_reads=[X("xb"), X("xt")],
                    out_keys=lambda c: ("XF", c))
                if "0" in DEV.get("bar", ""):
                    p.barrier()

        if DEV.get("skip_mixer"):
            pass
        else:
         with _Scope(p) as sc:
            wq_sl = [sc.sb(f"wq{i}", [128, NC, 128], BF16) for i in range(2)]
            wv_sl = [sc.sb(f"wv{i}", [128, NC, 128], BF16) for i in range(2)]
            lam_sb = sc.sb("lam", [128, TH], F32)
            v_hc = [sc.sb(f"vhc{i}", [128, TH], F32) for i in range(2)]
            st_hc = [sc.sb(f"sthc{i}", [128, TH], F32) for i in range(2)]
            y_bf = sc.sb("y", [128, NC, TH], BF16)
            for hc in range(NC):
                b = hc % 2
                p.dma("qg", wq_sl[b][:], _wsl(io.wqT, hc),
                      reads=[X("wq", b, "free")], writes=[X("wq", b)])
                p.dma("qg", wv_sl[b][:], _wsl(io.wvT, hc),
                      reads=[X("wv", b, "free")], writes=[X("wv", b)])
                for nh in range(2):
                    p.mm(u.psC[:, nh * 512:nh * 512 + 512],
                         g.L0a_lam[0:1, hc * 128:hc * 128 + 128],
                         g.onesrow[:], reads=[("c", "L0a_lam"), ("c", "onesrow")],
                         writes=[("psC", nh)])
                p.op("dve", (lambda v, _o=lam_sb[:]: v.tensor_copy(_o, u.psC[:])),
                     reads=[("psC", 0), ("psC", 1)], writes=[X("lam")])
                for nh in range(2):
                    for fc in range(NC):
                        p.mm(u.psA[:, nh * 512:nh * 512 + 512],
                             wv_sl[b][:, fc, :],
                             a.XFb[:, fc, 1 + nh * 512:1 + nh * 512 + 512],
                             start=(fc == 0), stop=(fc == NC - 1),
                             reads=[X("wv", b), ("XF", fc)],
                             writes=[("psA", nh)])
                p.op("dve", (lambda v, _o=v_hc[b][:]: v.tensor_copy(_o, u.psA[:])),
                     reads=[("psA", 0), ("psA", 1)],
                     writes=[X("v", b), ("psA", 0), ("psA", 1)])
                for nh in range(2):
                    for fc in range(NC):
                        p.mm(u.psB[:, nh * 512:nh * 512 + 512],
                             wq_sl[b][:, fc, :],
                             a.XFb[:, fc, 1 + nh * 512:1 + nh * 512 + 512],
                             start=(fc == 0), stop=(fc == NC - 1),
                             reads=[X("wq", b), ("XF", fc)],
                             writes=[("psB", nh)])
                p.op("dve",
                     (lambda v, _o=st_hc[b][:], _l=lam_sb[:], _v=v_hc[b][:],
                      _c=a.carry[:, hc:hc + 1]: v.tensor_tensor_scan(
                         _o, _l, _v, _c, op0=ALU.mult, op1=ALU.add)),
                     reads=[X("lam"), X("v", b), ("carry",)],
                     writes=[X("st", b)], self_wait=True, hazard=True)
                p.op("dve",
                     (lambda v, _o=a.carry[:, hc:hc + 1], _i=st_hc[b][:, TH - 1:TH]:
                      v.tensor_copy(_o, _i)),
                     reads=[X("st", b)], writes=[("carry",)])
                p.op("dve",
                     (lambda v, _o=y_bf[:, hc, :], _s=st_hc[b][:]: v.tensor_mul(
                         _o, u.psB[:], _s)),
                     reads=[("psB", 0), ("psB", 1), X("st", b)],
                     writes=[X("y", hc), X("v", b, "free"), X("wq", b, "free"),
                             X("wv", b, "free"), ("psB", 0), ("psB", 1)])
            wg_sl = [sc.sb(f"wg{i}", [128, NC, 128], BF16) for i in range(2)]
            wo_sl = [sc.sb(f"wo{i}", [128, NC, 128], BF16) for i in range(2)]
            yln_bf = sc.sb("yln", [128, NC, 512], BF16)
            gate_bf = [sc.sb(f"gt{i}", [128, 512], BF16) for i in range(2)]
            yg_bf = sc.sb("yg", [128, NC, 512], BF16)
            atile = [sc.sb(f"atl{i}", [128, 512], F32) for i in range(2)]
            for tb in range(2):
                tsl = slice(tb * 512, tb * 512 + 512)
                emit_ln(
                    p, g, u, X("lny", h, tb),
                    get_src=lambda c, tsl=tsl: y_bf[:, c, tsl],
                    n=512,
                    out_bf16=lambda c: yln_bf[:, c, :],
                    src_reads=[X("y", c2) for c2 in range(NC)],
                    out_keys=lambda c: X("yln", c),
                    affine=(lambda c: g.L0a_lng[:, c:c + 1],
                            lambda c: g.L0a_lnb[:, c:c + 1]))
                for oc in range(NC):
                    b = oc % 2
                    p.dma("qg", wg_sl[b][:], _wsl(io.wgT, oc),
                          reads=[X("wg", b, "free")], writes=[X("wg", b)])
                    for fc in range(NC):
                        p.mm(u.psA[:, 0:512], wg_sl[b][:, fc, :],
                             a.XFb[:, fc, 1 + tb * 512:1 + tb * 512 + 512],
                             start=(fc == 0), stop=(fc == NC - 1),
                             reads=[X("wg", b), ("XF", fc)],
                             writes=[("psA", 0)])
                    p.op("act", (lambda sx, _o=gate_bf[b][:]: sx.activation(
                        _o, u.psA[:, 0:512], AF.Silu)),
                        reads=[("psA", 0)],
                        writes=[X("gate", b), ("psA", 0), X("wg", b, "free")])
                    p.op("dve", (lambda v, _o=yg_bf[:, oc, :], _a=yln_bf[:, oc, :],
                                 _g=gate_bf[b][:]: v.tensor_mul(_o, _a, _g)),
                        reads=[X("gate", b), X("yln", oc)],
                        writes=[X("yg", oc)])
                for mc in range(NC):
                    b = mc % 2
                    p.dma("qg", wo_sl[b][:], _wsl(io.woT, mc),
                          reads=[X("wo", b, "free")], writes=[X("wo", b)])
                    for kc in range(NC):
                        p.mm(u.psB[:, b * 512:b * 512 + 512],
                             wo_sl[b][:, kc, :], yg_bf[:, kc, :],
                             start=(kc == 0), stop=(kc == NC - 1),
                             reads=[X("wo", b), X("yg", kc)],
                             writes=[("psB", b)])
                    p.op("dve", (lambda v, _o=atile[b][:],
                                 _i=u.psB[:, b * 512:b * 512 + 512]: v.tensor_copy(
                        _o, _i)),
                        reads=[("psB", b)],
                        writes=[X("atile", b), ("psB", b), X("wo", b, "free")])
                    if not DEV.get("skip_amix"):
                        p.dma("qg",
                              io.x1a[s, mc][:, h0 + tb * 512:h0 + tb * 512 + 512],
                              atile[b][:],
                              reads=[X("atile", b), X("x1a", "init", h, tb)],
                              writes=[X("x1a", "acc"), X("atile", b)],
                              accum_op=ALU.add)

        for l in range(DEV.get("nlayers", 2)):
            emit_a_ffn_layer(p, g, io, u, a, s, h, l)

        if not DEV.get("skip_f"):
            p.dma("qg", io.x1a[s].rearrange("c p t -> p c t")[:, :, h0:h0 + TH],
                  a.XF[:, :, 1:TH + 1],
                  reads=[("XF", c) for c in range(NC)] + [X("x1a", "acc"),
                         X("x1a", "init", h, 0), X("x1a", "init", h, 1)],
                  writes=[X("x1a", "acc")], accum_op=ALU.add)
        if DEV.get("dump_carry") == (s, h):
            p.dma("qa", io.dbg[0][:, 0:NC], a.carry[:],
                  reads=[("carry",)], writes=[("dbgc",)])
        if DEV.get("dump_xf") == (s, h):
            p.dma("qa", io.dbg.rearrange("c p t -> p c t"), a.XF[:, :, 1:TH + 1],
                  reads=[("XF", c) for c in range(NC)], writes=[("dbg",)])


def emit_a_ffn_layer(p, g, io, u, a, s, h, l):
    nc = p.nc
    X = lambda *k: ("A", s, "f", l) + k
    XA = lambda *k: ("A", s) + k
    if h == 0:
        p.op("dve", lambda v: v.memset(a.XFb[:, :, 0:1], 0.0),
             reads=[("XF", c) for c in range(NC)],
             writes=[("XFb0",)])
    else:
        p.op("dve", lambda v: v.tensor_copy(a.XFb[:, :, 0:1],
                                            a.save_qkv[:, :, l:l + 1]),
             reads=[("save_qkv",)] + [("XF", c) for c in range(NC)],
             writes=[("XFb0",)])
    p.op("dve", lambda v: v.tensor_copy(a.save_qkv[:, :, l:l + 1],
                                        a.XFb[:, :, TH:TH + 1]),
         reads=[("XF", NC - 1)], writes=[("save_qkv",)])

    with _Scope(p) as lsc:
        wo_sb = lsc.sb("wo_sb", [128, 4, 1024], BF16)
        attn_bf = lsc.sb("attn", [128, 4, 512], BF16)
        negsh = lsc.sb("negsh", [128, 1], F32)
        rrow = lsc.sb("rrow", [1, 512], F32)
        bc_sb = lsc.sb("bc_sb", [64, 512], F32)
        dbgrow = lsc.sb("dbgrow", [1, 512], F32)
        p.op("pe", (lambda t, _i=g.L0f_qs[:, l:l + 1]: t.transpose(
            u.psS[0:1, 0:128], _i, g.ident_f[:])),
             reads=[("c", "L0f_qs"), ("c", "ident_f")], writes=[("psS",)])
        if DEV.get("dump_sh") == (s, h, l):
            p.op("dve", (lambda v, _o=dbgrow[0:1, 0:128]: v.tensor_copy(
                _o, u.psS[0:1, 0:128])),
                 reads=[("psS",)], writes=[X("dbgrow")])
        p.op("dve", (lambda v, _o=rrow[0:1, 0:1]: v.reduce_max(
            _o, u.psS[0:1, 0:128], axis=AX.X, apply_absolute_value=True)),
             reads=[("psS",)], writes=[X("rrow"), ("psS",)], hazard=True)
        p.op("pe", (lambda t, _i=g.L0f_ks[:, l:l + 1]: t.transpose(
            u.psS[0:1, 0:128], _i, g.ident_f[:])),
             reads=[("c", "L0f_ks"), ("c", "ident_f"), ("psS",)], writes=[("psS",)])
        p.op("dve", (lambda v, _o=rrow[0:1, 1:2]: v.reduce_max(
            _o, u.psS[0:1, 0:128], axis=AX.X, apply_absolute_value=True)),
             reads=[("psS",)], writes=[X("rrow"), ("psS",)], hazard=True)
        p.op("dve", (lambda v, _a=rrow[0:1, 0:1], _b=rrow[0:1, 1:2]:
                     v.tensor_mul(_a, _a, _b)),
             reads=[X("rrow")], writes=[X("rrow")])
        p.mm(u.psS[0:128, 0:1], g.ones1_f[:], rrow[0:1, 0:1],
             reads=[X("rrow")], writes=[("psS",)])
        p.op("dve", (lambda v, _o=negsh[:]: v.tensor_scalar_mul(_o, u.psS[0:128, 0:1], -8.0)),
             reads=[("psS",)], writes=[X("negsh"), ("psS",)], hazard=True)
        if DEV.get("dump_sh") == (s, h, l):
            p.op("dve", (lambda v, _o=dbgrow[0:1, 256:260], _i=rrow[0:1, 0:4]:
                         v.tensor_copy(_o, _i)),
                 reads=[X("rrow")], writes=[X("dbgrow")])
            p.dma("qa", io.dbg[6][0:1, 0:512], dbgrow[:],
                  reads=[X("dbgrow")], writes=[("dbgsh",)])
        p.dma("qg", wo_sb[:], io.wo[l].rearrange("(kc p) o -> p kc o", p=128),
              writes=[X("wo_sb")])

        with _Scope(p) as qsc:
            c1q = qsc.sb("c1q", [128, T], BF16)
            c2q = qsc.sb("c2q", [128, T], BF16)
            c1k = qsc.sb("c1k", [128, T], BF16)
            c2k = qsc.sb("c2k", [128, T], BF16)
            for nm, dst, sc_col in (("c1q", c1q, g.L0f_qs),
                                    ("c2q", c2q, g.L0f_qs_sw),
                                    ("c1k", c1k, g.L0f_ks),
                                    ("c2k", c2k, g.L0f_ks_sw)):
                p.dma("qs", dst[:], io.consts[nm + "_bf"], writes=[X("tbl", nm)])
                p.op("dve",
                     (lambda v, _d=dst[:], _s=sc_col[:, l:l + 1]:
                      v.tensor_scalar_mul(_d, _d, _s)),
                     reads=[X("tbl", nm), ("c", "L0f_qs")], writes=[X("tbl", nm)],
                     hazard=True)
            qf = qsc.sb("qf", [128, 4, TH], BF16)
            kf = qsc.sb("kf", [128, 4, TH], BF16)
            vf = qsc.sb("vf", [128, 8, 8, 65], BF16)
            wqkv_sl = [qsc.sb(f"wqk{i}", [128, NC, 128], BF16) for i in range(2)]
            wqkv_v = qsc.sb("wqkvv", [128, NC, 512], BF16)
            for oc in range(8):
                b = oc % 2
                p.dma("qg", wqkv_sl[b][:], _wsl(io.wqkv[l], oc),
                      reads=[X("wsl", b, "free")], writes=[X("wsl", b)])
                for nh in range(2):
                    for fc in range(NC):
                        p.mm(u.psA[:, nh * 512:nh * 512 + 512],
                             wqkv_sl[b][:, fc, :],
                             a.XFb[:, fc, _shiftoff(fc) + nh * 512:
                                   _shiftoff(fc) + nh * 512 + 512],
                             start=(fc == 0), stop=(fc == NC - 1),
                             reads=[X("wsl", b), ("XF", fc), ("XFb0",)],
                             writes=[("psA", nh)])
                dst = qf[:, oc, :] if oc < 4 else kf[:, oc - 4, :]
                key = X("qf", oc) if oc < 4 else X("kf", oc - 4)
                p.op("act", (lambda sx, dst=dst: sx.copy(dst, u.psA[:])),
                     reads=[("psA", 0), ("psA", 1)],
                     writes=[key, ("psA", 0), ("psA", 1), X("wsl", b, "free")])
            p.dma("qg", wqkv_v[:],
                  io.wqkv[l].rearrange("(kc p) o -> p kc o", p=128)[:, :, 1024:1536],
                  writes=[X("wqv")])
            p.op("dve", (lambda v, _o=vf[:, :, :, 64:65]: v.memset(_o, 1.0)),
                 writes=[X("vf_ones")])
            for tt in range(8):
                for fc in range(NC):
                    p.mm(u.psB[:, 0:512],
                         a.XFb[:, fc, _shiftoff(fc) + tt * 128:
                               _shiftoff(fc) + tt * 128 + 128],
                         wqkv_v[:, fc, :],
                         start=(fc == 0), stop=(fc == NC - 1),
                         reads=[X("wqv"), ("XF", fc), ("XFb0",)],
                         writes=[("psB", 0)])
                p.op("dve", (lambda v, _o=vf[:, tt, :, 0:64],
                             _i=u.psB[:, 0:512].rearrange("p (hh d) -> p hh d", hh=8):
                             v.tensor_copy(_o, _i)),
                    reads=[("psB", 0)],
                    writes=[X("vf", tt), ("psB", 0)])
            if DEV.get("dump_q") == (s, h, l, "qkv"):
                for qc in range(4):
                    p.dma("qg", io.dbg[qc], qf[:, qc, :], reads=[X("qf", qc)],
                          writes=[("dbgq", qc)])
                    p.dma("qg", io.dbg[4 + qc], kf[:, qc, :], reads=[X("kf", qc)],
                          writes=[("dbgk", qc)])
            if "a" in DEV.get("bar", ""):
                p.barrier()
            for side, buf, c1, c2, scol in (
                    ("q", qf, c1q, c2q, "q"), ("k", kf, c1k, c2k, "k")):
                for qc in range(4):
                    bkey = X("rot", side, qc)
                    sq = u.sq1024
                    p.op("act", (lambda sx, buf=buf, qc=qc: sx.activation(
                        sq[:], buf[:, qc, :], AF.Square)),
                        reads=[X(side + "f", qc)], writes=[("u", "sq1024")])
                    for nh in range(2):
                        p.mm(u.psS[0:1, 0:512], g.ones64a_bf[:],
                             sq[:, nh * 512:nh * 512 + 512],
                             reads=[("u", "sq1024")], writes=[("psS",)])
                        p.mm(u.psQ[0:1, 0:512], g.ones64b_bf[:],
                             sq[:, nh * 512:nh * 512 + 512],
                             reads=[("u", "sq1024")], writes=[("psQ",)])
                        for nm, pssrc, seg in (("nA", u.psS, 0), ("nB", u.psQ, 1)):
                            pskey = ("psS",) if nm == "nA" else ("psQ",)
                            rr = u.nrows[0:1, seg * 512:seg * 512 + 512]
                            p.op("act", (lambda sx, pssrc=pssrc, rr=rr:
                                         sx.activation(rr, pssrc[0:1, 0:512],
                                                       AF.Sqrt)),
                                 reads=[pskey],
                                 writes=[("u", "nrows", nm)])
                            p.op("dve", (lambda v, rr=rr: v.tensor_scalar_max(
                                rr, rr, 1e-12)),
                                reads=[("u", "nrows", nm)], writes=[("u", "nrows", nm)],
                                hazard=True)
                            p.op("dve", (lambda v, rr=rr: v.reciprocal(rr, rr)),
                                reads=[("u", "nrows", nm)],
                                writes=[("u", "nrows", nm), pskey], hazard=True)
                        p.mm(u.psC[:, nh * 512:nh * 512 + 512], g.selA_f[:],
                             u.nrows[0:1, 0:512],
                             start=True, stop=False,
                             reads=[("u", "nrows", "nA")], writes=[("psC", nh)])
                        p.mm(u.psC[:, nh * 512:nh * 512 + 512], g.selB_f[:],
                             u.nrows[0:1, 512:1024],
                             start=False, stop=True,
                             reads=[("u", "nrows", "nB")], writes=[("psC", nh)])
                    for nh in range(2):
                        p.mm(u.psB[:, nh * 512:nh * 512 + 512], g.rot_bf[:],
                             buf[:, qc, nh * 512:nh * 512 + 512],
                             reads=[X(side + "f", qc), ("c", "rot_bf")],
                             writes=[("psB", nh)])
                    tsl = slice(h * TH, h * TH + TH)
                    p.op("dve", (lambda v, buf=buf, qc=qc, c1=c1, tsl=tsl:
                                 v.tensor_mul(u.tmp_bf[:], buf[:, qc, :],
                                              c1[:, tsl])),
                         reads=[X(side + "f", qc), X("tbl", "c1" + scol)],
                         writes=[("u", "tmp_bf")])
                    p.op("dve", (lambda v, c2=c2, tsl=tsl: v.tensor_mul(
                        u.tmp_f[:], u.psB[:], c2[:, tsl])),
                        reads=[("psB", 0), ("psB", 1),
                               X("tbl", "c2" + scol)],
                        writes=[("u", "tmp_f"), ("psB", 0), ("psB", 1)])
                    p.op("dve", lambda v: v.tensor_add(u.tmp_f[:], u.tmp_f[:],
                                                       u.tmp_bf[:]),
                         reads=[("u", "tmp_bf"), ("u", "tmp_f")],
                         writes=[("u", "tmp_f")])
                    p.op("dve", (lambda v, buf=buf, qc=qc: v.tensor_mul(
                        buf[:, qc, :], u.tmp_f[:], u.psC[:])),
                        reads=[("u", "tmp_f"), ("psC", 0), ("psC", 1)],
                        writes=[X(side + "f", qc), ("psC", 0),
                                ("psC", 1)])
            if DEV.get("dump_rn") == (s, h, l):
                p.dma("qa", io.dbg[0][0:1, 0:1024], u.nrows[:],
                      reads=[("u", "nrows", "nA"), ("u", "nrows", "nB")],
                      writes=[("dbgr",)])
                p.dma("qg", io.dbg[1], c1q[:, h * TH:h * TH + TH],
                      reads=[X("tbl", "c1q")], writes=[("dbgt1",)])
                p.dma("qg", io.dbg[2], c2q[:, h * TH:h * TH + TH],
                      reads=[X("tbl", "c2q")], writes=[("dbgt2",)])
            if DEV.get("dump_q") == (s, h, l, "rot"):
                for qc in range(4):
                    p.dma("qg", io.dbg[qc], qf[:, qc, :], reads=[X("qf", qc)],
                          writes=[("dbgq", qc)])
                    p.dma("qg", io.dbg[4 + qc], kf[:, qc, :], reads=[X("kf", qc)],
                          writes=[("dbgk", qc)])
            if DEV.get("dump_v") == (s, h, l):
                for tt in range(8):
                    p.dma("qg", io.dbg[tt % 8][:, tt // 8 * 1040: tt // 8 * 1040 + 520]
                          if False else io.dbg[tt][:, 0:520],
                          vf[:, tt, :, :], reads=[X("vf", tt), X("vf_ones")],
                          writes=[("dbgv", tt)])
            if "b" in DEV.get("bar", ""):
                p.barrier()
            PT = [qsc.sb(f"pt{i}", [128, 512], BF16) for i in range(4)]
            for wb in range(2):
                W = 2 * h + wb
                kts = [kt for kt in range((W - 1) * 4, (W - 1) * 4 + 8) if kt >= 0]
                for hh in range(8):
                    pb = (hh % 2) * 64
                    ch = hh // 2
                    av = hh % 2
                    qrhs = qf[pb:pb + 64, ch, wb * 512:wb * 512 + 512]
                    nkt = len(kts)
                    LAG = 2

                    def tile_info(idx):
                        kt = kts[idx]
                        loc = kt - 8 * h
                        if loc >= 0:
                            klhs = kf[pb:pb + 64, ch, loc * 128:loc * 128 + 128]
                            vtile = vf[:, loc, hh, :]
                            kread = [X("kf", ch), X("vf", loc)]
                        else:
                            sloc = loc + 4
                            klhs = a.save_k[pb:pb + 64, ch, sloc * 128:
                                            sloc * 128 + 128, l]
                            vtile = a.save_v[:, sloc, hh * 65:hh * 65 + 65, l]
                            kread = [("save_k",), ("save_v",)]
                        return kt, klhs, vtile, kread

                    for i in range(nkt + LAG):
                        if i < nkt:
                            idx = i
                            bk = idx % 4
                            kt, klhs, vtile, kread = tile_info(idx)
                            o = kt * 128 - W * 512
                            msl = g.maskband_bf[:, 384 - o:384 - o + 512]
                            pst = u.psA if bk < 2 else u.psC
                            pk = ("psA", bk) if bk < 2 else ("psC", bk - 2)
                            hf = (bk % 2) * 512
                            p.mm(pst[:, hf:hf + 512], klhs, qrhs,
                                 start=True, stop=False,
                                 reads=kread + [X("qf", ch)],
                                 writes=[pk])
                            p.mm(pst[:, hf:hf + 512], g.ident_bf[:],
                                 msl, start=False, stop=True,
                                 reads=[("c", "maskband_bf"), ("c", "ident_bf")],
                                 writes=[pk])
                            p.op("act", (lambda sx, _o=PT[bk][:],
                                _i=pst[:, hf:hf + 512],
                                _b=negsh[:, 0:1]: sx.activation(
                                _o, _i, AF.Exp, bias=_b)),
                                reads=[pk, X("negsh")],
                                writes=[X("pt", bk), pk])
                        if i >= LAG:
                            idx = i - LAG
                            bk = idx % 4
                            kt, klhs, vtile, kread = tile_info(idx)
                            p.mm(u.psB[0:65, av * 512:av * 512 + 512],
                                 vtile, PT[bk][:],
                                 start=(idx == 0), stop=(idx == nkt - 1),
                                 reads=kread + [X("pt", bk)],
                                 writes=[("psB", av), X("pt", bk)])
                    p.op("dve", (lambda v, _o=rrow[:],
                        _i=u.psB[64:65, av * 512:av * 512 + 512]: v.tensor_scalar(
                        _o, _i, 1e-30, None, op0=ALU.add)),
                        reads=[("psB", av)], writes=[X("rrow")], hazard=True)
                    p.op("dve", (lambda v, _o=rrow[:]: v.reciprocal(_o, _o)),
                         reads=[X("rrow")], writes=[X("rrow")], hazard=True)
                    p.mm(u.psS[0:64, 0:512], g.ones1_f[0:1, 0:64], rrow[:],
                         reads=[X("rrow")], writes=[("psS",)])
                    p.op("dve", (lambda v, _o=bc_sb[:]: v.tensor_copy(
                        _o, u.psS[0:64, 0:512])),
                         reads=[("psS",)], writes=[X("bcs"), ("psS",)])
                    p.op("dve", (lambda v, _o=attn_bf[pb:pb + 64, ch, :],
                        _i=u.psB[0:64, av * 512:av * 512 + 512],
                        _b=bc_sb[:]: v.tensor_mul(_o, _i, _b)),
                        reads=[("psB", av), X("bcs")],
                        writes=[X("at", hh), ("psB", av)])
                for mc in range(NC):
                    bk = mc % 2
                    for kc in range(4):
                        p.mm(u.psB[:, bk * 512:bk * 512 + 512],
                             wo_sb[:, kc, mc * 128:mc * 128 + 128],
                             attn_bf[:, kc, :],
                             start=(kc == 0), stop=(kc == 3),
                             reads=[X("wo_sb"), X("at", 2 * kc), X("at", 2 * kc + 1)],
                             writes=[("psB", bk)])
                    xsl = a.XF[:, mc, 1 + wb * 512:1 + wb * 512 + 512]
                    p.op("dve", (lambda v, xsl=xsl, bk=bk: v.tensor_add(
                        xsl, u.psB[:, bk * 512:bk * 512 + 512], xsl)),
                        reads=[("psB", bk), ("XF", mc)],
                        writes=[("XF", mc), ("psB", bk)])
                    p.op("act", (lambda sx, mc=mc, wb=wb: sx.copy(
                        a.XFb[:, mc, 1 + wb * 512:1 + wb * 512 + 512],
                        a.XF[:, mc, 1 + wb * 512:1 + wb * 512 + 512])),
                        reads=[("XF", mc)], writes=[("XF", mc)])
            if "c" in DEV.get("bar", ""):
                p.barrier()
            if DEV.get("dump_attn") == (s, h, l):
                for kc in range(4):
                    p.dma("qg", io.dbg[kc][:, 0:512], attn_bf[:, kc, :],
                          reads=[X("at", 2 * kc), X("at", 2 * kc + 1)],
                          writes=[("dbga", kc)])
                p.dma("qa", io.dbg[7][:, 0:1], negsh[:],
                      reads=[X("negsh")], writes=[("dbgn",)],
                      allow_slow_non_contiguous=True)
            p.op("dve", (lambda v, _o=a.save_k[:, :, :, l],
                         _i=kf[:, :, 512:1024]: v.tensor_copy(_o, _i)),
                 reads=[X("kf", 0), X("kf", 1), X("kf", 2), X("kf", 3)],
                 writes=[("save_k",)])
            p.op("dve", (lambda v, _o=a.save_v[:, :, :, l],
                         _i=vf[:, 4:8, :, :]: v.tensor_copy(_o, _i)),
                 reads=[X("vf", tt) for tt in range(4, 8)] + [X("vf_ones")],
                 writes=[("save_v",)])

        if "d" in DEV.get("bar", ""):
            p.barrier()
        with _Scope(p) as fsc:
            hln = fsc.sb("hln", [128, NC, TH], BF16)
            if h == 0:
                p.op("dve", lambda v: v.memset(a.XF[:, :, 0:1], 0.0),
                     reads=[("XF", c) for c in range(NC)], writes=[("XFp0",)])
            else:
                p.op("dve", lambda v: v.tensor_copy(a.XF[:, :, 0:1],
                                                    a.save_ln_f[:, :, l:l + 1]),
                     reads=[("save_ln",)] + [("XF", c) for c in range(NC)],
                     writes=[("XFp0",)])
            p.op("act", lambda sx: sx.copy(a.XFb[:, :, 0:1], a.XF[:, :, 0:1]),
                 reads=[("XFp0",)], writes=[("XFb0",)])
            p.op("dve", lambda v: v.tensor_copy(a.save_ln_f[:, :, l:l + 1],
                                                a.XF[:, :, TH:TH + 1]),
                 reads=[("XF", NC - 1)], writes=[("save_ln",)])
            for tb in range(2):
                emit_ln(
                    p, g, u, X("lnf", h, tb),
                    get_src=lambda c, tb=tb: a.XFb[:, c, _shiftoff(c) + tb * 512:
                                                   _shiftoff(c) + tb * 512 + 512],
                    get_src_f=lambda c, tb=tb: a.XF[:, c, _shiftoff(c) + tb * 512:
                                                    _shiftoff(c) + tb * 512 + 512],
                    n=512,
                    out_bf16=lambda c, tb=tb: hln[:, c, tb * 512:tb * 512 + 512],
                    src_reads=[("XF", c2) for c2 in range(NC)] + [("XFb0",),
                               ("XFp0",)],
                    out_keys=lambda c: X("hln", c),
                    affine=(lambda c: g.L0f_lng[:, l, c:c + 1],
                            lambda c: g.L0f_lnb[:, l, c:c + 1]))
            if "e" in DEV.get("bar", ""):
                p.barrier()
            w1a_sl = [fsc.sb(f"w1a{i}", [128, NC, 128], BF16) for i in range(2)]
            w1g_sl = [fsc.sb(f"w1g{i}", [128, NC, 128], BF16) for i in range(2)]
            w2_sl = [fsc.sb("w2s", [128, NJ, 128], BF16)]
            t_all = fsc.sb("tall", [128, NJ, 2, 512], BF16)
            ag_bf = [fsc.sb(f"ag{i}", [128, 1024], BF16) for i in range(4)]
            for j in range(NJ):
                b = j % 2
                p.dma("qg", w1a_sl[b][:], _wsl(io.w1a[l], j),
                      reads=[X("w1a", b, "free")], writes=[X("w1a", b)])
                p.dma("qg", w1g_sl[b][:], _wsl(io.w1g[l], j),
                      reads=[X("w1g", b, "free")], writes=[X("w1g", b)])
                for tb in range(2):
                    tsl = slice(tb * 512, tb * 512 + 512)
                    pw = u.psA if tb == 0 else u.psC
                    pk0 = ("psA", 0) if tb == 0 else ("psC", 0)
                    pk1 = ("psA", 1) if tb == 0 else ("psC", 1)
                    ab = ag_bf[2 * tb + 0] if b == 0 else ag_bf[2 * tb + 1]
                    for fc in range(NC):
                        p.mm(pw[:, 0:512], w1a_sl[b][:, fc, :],
                             hln[:, fc, tsl], start=(fc == 0), stop=(fc == NC - 1),
                             reads=[X("w1a", b), X("hln", fc)],
                             writes=[pk0])
                    for fc in range(NC):
                        p.mm(pw[:, 512:1024], w1g_sl[b][:, fc, :],
                             hln[:, fc, tsl], start=(fc == 0), stop=(fc == NC - 1),
                             reads=[X("w1g", b), X("hln", fc)],
                             writes=[pk1])
                    p.op("act", (lambda sx, _o=ab[:, 0:512],
                        _i=pw[:, 0:512],
                        _b=g.L0f_b1a[:, l, j:j + 1]: sx.activation(
                        _o, _i, AF.Identity, bias=_b)),
                        reads=[pk0, ("c", "L0f_b1a")],
                        writes=[X("ag", tb, b, 0), pk0, X("w1a", b, "free")])
                    p.op("act", (lambda sx, _o=ab[:, 512:1024],
                        _i=pw[:, 512:1024],
                        _b=g.L0f_b1g[:, l, j:j + 1]: sx.activation(
                        _o, _i, AF.Gelu, bias=_b)),
                        reads=[pk1, ("c", "L0f_b1g")],
                        writes=[X("ag", tb, b, 1), pk1, X("w1g", b, "free")])
                    p.op("dve", (lambda v, _o=t_all[:, j, tb, :],
                        _a=ab[:, 0:512], _g=ab[:, 512:1024]:
                        v.tensor_mul(_o, _a, _g)),
                        reads=[X("ag", tb, b, 0), X("ag", tb, b, 1)],
                        writes=[X("tall", j, tb), X("ag", tb, b, 0),
                                X("ag", tb, b, 1)])
            for mc in range(NC):
                b = 0
                p.dma("qg", w2_sl[b][:],
                      io.w2[l].rearrange("(jc p) o -> p jc o", p=128)
                      [:, :, mc * 128:mc * 128 + 128],
                      reads=[X("w2", b, "free")], writes=[X("w2", b)])
                for tb in range(2):
                    for j in range(NJ):
                        p.mm(u.psB[:, tb * 512:tb * 512 + 512],
                             w2_sl[b][:, j, :], t_all[:, j, tb, :],
                             start=(j == 0), stop=(j == NJ - 1),
                             reads=[X("w2", b), X("tall", j, tb)],
                             writes=[("psB", tb)])
                for tb in range(2):
                    xsl = a.XF[:, mc, 1 + tb * 512:1 + tb * 512 + 512]
                    p.op("dve", (lambda v, xsl=xsl, tb=tb, mc=mc, l=l:
                                 v.scalar_tensor_tensor(
                                     xsl, u.psB[:, tb * 512:tb * 512 + 512],
                                     g.L0f_b2[:, l, mc:mc + 1], xsl,
                                     op0=ALU.add, op1=ALU.add)),
                         reads=[("psB", tb), ("XF", mc), ("c", "L0f_b2")],
                         writes=[("XF", mc), ("psB", tb), X("w2", b, "free")],
                         hazard=True)
                    p.op("act", (lambda sx, mc=mc, tb=tb: sx.copy(
                        a.XFb[:, mc, 1 + tb * 512:1 + tb * 512 + 512],
                        a.XF[:, mc, 1 + tb * 512:1 + tb * 512 + 512])),
                        reads=[("XF", mc)], writes=[("XF", mc)])


def emit_stage_b(p, g, io, u, bb):
    nc = p.nc
    with _Scope(p) as sc:
        WW = {"R": sc.sb("WR", [128, 4, 512], F32),
              "I": sc.sb("WI", [128, 4, 512], F32)}
        t1sb = sc.sb("t1sb", [32, 512], F32)
        tgsb = sc.sb("tgsb", [16, 512], F32)
        Ap = sc.sb("Ap", [128, 512], F32)
        Bp = sc.sb("Bp", [128, 512], F32)
        K = lambda *k: ("B", "w") + k
        for part in ("R", "I"):
            G1 = getattr(g, "L1f_G1" + part)
            G2 = getattr(g, "L1f_G2" + part)
            W = WW[part]
            p.op("dve", (lambda v, W=W: v.memset(W[:], 0.0)), writes=[K(part, "W")])
            for c in range(4):
                p.mm(u.psS[0:32, 0:512], G1[:, c, :], g.Yb_f[:],
                     reads=[("c", "L1f_G1" + part), ("c", "Yb_f")],
                     writes=[("psS",)])
                p.op("dve", lambda v: v.tensor_copy(t1sb[:], u.psS[0:32, 0:512]),
                     reads=[("psS",)], writes=[K("t1"), ("psS",)])
                p.mm(u.psQ[0:16, 0:512], G2[:, c, :], g.Ye_f[:],
                     reads=[("c", "L1f_G2" + part), ("c", "Ye_f")],
                     writes=[("psQ",)])
                p.op("dve", lambda v: v.tensor_copy(tgsb[:], u.psQ[0:16, 0:512]),
                     reads=[("psQ",)], writes=[K("tg"), ("psQ",)])
                for pc in range(4):
                    p.mm(u.psA[:, 0:512], g.Zd_f[:, pc, :], t1sb[:],
                         reads=[K("t1"), ("c", "Zd_f")], writes=[("psA", 0)])
                    p.op("dve", lambda v: v.tensor_copy(Ap[:], u.psA[:, 0:512]),
                         reads=[("psA", 0)], writes=[K("Ap"), ("psA", 0)])
                    p.mm(u.psB[:, 0:512], g.Zg_f[:, pc, :], tgsb[:],
                         reads=[K("tg"), ("c", "Zg_f")], writes=[("psB", 0)])
                    p.op("dve", lambda v: v.tensor_copy(Bp[:], u.psB[:, 0:512]),
                         reads=[("psB", 0)], writes=[K("Bp"), ("psB", 0)])
                    p.op("dve", lambda v: v.tensor_mul(Bp[:], Ap[:], Bp[:]),
                         reads=[K("Ap"), K("Bp")], writes=[K("Bp")])
                    p.op("dve", (lambda v, W=W, pc=pc: v.tensor_add(
                        W[:, pc, :], W[:, pc, :], Bp[:])),
                        reads=[K("Bp"), K(part, "W")], writes=[K(part, "W")])
        for ic in range(4):
            p.op("act", (lambda sx, ic=ic: sx.copy(bb.Wc[:, ic, 0:512],
                                                   WW["R"][:, ic, :])),
                 reads=[K("R", "W")], writes=[("Wc",)])
            p.op("act", (lambda sx, ic=ic: sx.copy(bb.Wc[:, ic, 512:1024],
                                                   WW["I"][:, ic, :])),
                 reads=[K("I", "W")], writes=[("Wc",)])
            p.op("act", (lambda sx, ic=ic: sx.activation(
                bb.Wc[:, 4 + ic, 0:512], WW["I"][:, ic, :], AF.Copy, scale=-1.0)),
                reads=[K("I", "W")], writes=[("Wc",)])
            p.op("act", (lambda sx, ic=ic: sx.copy(bb.Wc[:, 4 + ic, 512:1024],
                                                   WW["R"][:, ic, :])),
                 reads=[K("R", "W")], writes=[("Wc",)])

    for s in range(SEQS):
        X = lambda *k: ("B", s) + k
        with _Scope(p) as sc:
            xt = sc.sb("xt", [128, NC, 512], F32)
            xb = sc.sb("xb", [128, NC, 512], BF16)
            for tb in range(4):
                tsl = slice(tb * 512, tb * 512 + 512)
                p.dma("qs", xt[:], io.x1a[s].rearrange("c p t -> p c t")[:, :, tsl],
                      reads=[("A", s, "x1a", "acc")], writes=[X("xt")])
                p.dma("qa", io.x1b[s].rearrange("c p t -> p c t")[:, :, tsl],
                      xt[:], reads=[X("xt")], writes=[X("x1b", "init", tb)])
                p.op("act", (lambda sx, _o=xb[:], _i=xt[:]: sx.copy(_o, _i)),
                     reads=[X("xt")], writes=[X("xb")])
                emit_ln(
                    p, g, u, X("ln1", tb),
                    get_src=lambda c: xb[:, c, :],
                    get_src_f=lambda c: xt[:, c, :],
                    n=512,
                    out_bf16=lambda c, tb=tb: bb.XN1b[:, c, tsl],
                    src_reads=[X("xb"), X("xt")],
                    out_keys=lambda c: X("xn", c))
        with _Scope(p) as sc:
            f1 = [sc.sb(f"f1{i}", [128, 512], F32) for i in range(2)]
            for tb in range(4):
                tsl = slice(tb * 512, tb * 512 + 512)
                for mc in range(NC):
                    b = mc % 2
                    for ic in range(NC):
                        p.mm(u.psA[:, b * 512:b * 512 + 512],
                             bb.Wc[:, ic, mc * 128:mc * 128 + 128],
                             bb.XN1b[:, ic, tsl],
                             start=(ic == 0), stop=(ic == NC - 1),
                             reads=[("Wc",), X("xn", ic)], writes=[("psA", b)])
                    p.op("act", (lambda sx, b=b, mc=mc: sx.activation(
                        f1[b][:], u.psA[:, b * 512:b * 512 + 512], AF.Identity,
                        bias=g.L1f_bri[:, mc:mc + 1])),
                        reads=[("psA", b), ("c", "L1f_bri")],
                        writes=[X("f1", b), ("psA", b)])
                    p.dma("qg", io.x1b[s, mc][:, tsl], f1[b][:],
                          reads=[X("f1", b), X("x1b", "init", tb)],
                          writes=[X("x1b", "acc"), X("f1", b)],
                          accum_op=ALU.add)
        with _Scope(p) as sc:
            wsl = [sc.sb(f"wsl{i}", [128, NC, 128], BF16) for i in range(2)]
            stage = sc.sb("stage", [128, T], BF16)
            for oc in range(16):
                b = oc % 2
                p.dma("qg", wsl[b][:], _wsl(io.l1wqkv, oc),
                      reads=[X("wsl", b, "free")], writes=[X("wsl", b)])
                for tq in range(4):
                    bk = tq % 2
                    for fc in range(NC):
                        p.mm(u.psA[:, bk * 512:bk * 512 + 512],
                             wsl[b][:, fc, :],
                             bb.XN1b[:, fc, tq * 512:tq * 512 + 512],
                             start=(fc == 0), stop=(fc == NC - 1),
                             reads=[X("wsl", b), X("xn", fc)],
                             writes=[("psA", bk)])
                    p.op("act", (lambda sx, tq=tq, bk=bk: sx.copy(
                        stage[:, tq * 512:tq * 512 + 512],
                        u.psA[:, bk * 512:bk * 512 + 512])),
                        reads=[("psA", bk)],
                        writes=[X("stage", tq), ("psA", bk)])
                dst = io.qf[s, oc] if oc < 8 else io.kf[s, oc - 8]
                p.dma("qa", dst, stage[:],
                      reads=[X("stage", 0), X("stage", 1), X("stage", 2),
                             X("stage", 3)],
                      writes=[X("qk_out", oc), X("stage", 0), X("stage", 1),
                              X("stage", 2), X("stage", 3),
                              X("wsl", b, "free")])
            wqv = sc.sb("wqv", [128, NC, 1024], BF16)
            v_sb = sc.sb("v_sb", [128, 16, 65], BF16)
            p.dma("qg", wqv[:],
                  io.l1wqkv.rearrange("(kc p) o -> p kc o", p=128)[:, :, 2048:3072],
                  writes=[X("wqv")])
            p.op("dve", lambda v: v.memset(v_sb[:, :, 64:65], 1.0),
                 writes=[X("vones")])
            for tt in range(16):
                for vh in range(2):
                    for fc in range(NC):
                        p.mm(u.psB[:, vh * 512:vh * 512 + 512],
                             bb.XN1b[:, fc, tt * 128:tt * 128 + 128],
                             wqv[:, fc, vh * 512:vh * 512 + 512],
                             start=(fc == 0), stop=(fc == NC - 1),
                             reads=[X("wqv"), X("xn", fc)],
                             writes=[("psB", vh)])
                    p.op("dve", (lambda v, tt=tt, vh=vh: v.tensor_copy(
                        v_sb[:, vh * 8:vh * 8 + 8, 0:64],
                        u.psB[:, vh * 512:vh * 512 + 512].rearrange(
                            "p (hh d) -> p hh d", hh=8))),
                        reads=[("psB", vh)],
                        writes=[X("vsb", vh), ("psB", vh)])
                p.dma("qa", io.vf[s, tt], v_sb[:],
                      reads=[X("vsb", 0), X("vsb", 1), X("vones")],
                      writes=[X("v_out", tt), X("vsb", 0), X("vsb", 1)])


def emit_stage_c(p, g, io, u, s):
    nc = p.nc
    X = lambda *k: ("C", s) + k
    with _Scope(p) as sc:
        q_sb = sc.sb("q_sb", [128, NC, T], BF16)
        k_sb = sc.sb("k_sb", [128, NC, T], BF16)
        v_sb = sc.sb("v_sb", [128, 16, 1040], BF16)
        wproj = sc.sb("wproj", [128, NC, 1024], BF16)
        negsh = sc.sb("negsh", [128, 1], F32)
        rrow = sc.sb("rrow", [1, 512], F32)
        bc_sb = sc.sb("bc_sb", [64, 512], F32)
        phi_sb = [sc.sb("phi0", [128, 16, 512], BF16)]
        attn_bf = sc.sb("attn", [128, NC, 512], BF16)
        xt = [sc.sb("xt0", [128, NC, 512], F32)]
        out_sb = xt
        PT = [sc.sb(f"pt{i}", [128, 512], BF16) for i in range(4)]
        mrow = sc.sb("mrow", [1, 4], F32)

        bq = [("B", s, "qk_out", oc) for oc in range(16)]
        bv = [("B", s, "v_out", tt) for tt in range(16)]
        p.dma("qs", q_sb[:], io.qf[s].rearrange("c p t -> p c t"),
              reads=bq, writes=[X("q")])
        p.dma("qs", k_sb[:], io.kf[s].rearrange("c p t -> p c t"),
              reads=bq, writes=[X("k")])
        p.dma("qs", v_sb[:], io.vf[s].rearrange("tt p x -> p tt x"),
              reads=bv, writes=[X("v")])
        p.dma("qg", wproj[:], io.l1wproj.rearrange("(kc p) o -> p kc o", p=128),
              writes=[X("wp")])
        p.op("dve", lambda v: v.memset(mrow[:], 0.0), writes=[X("m0"), X("m1")])
        for side, buf, bkey, mi in ((0, q_sb, X("q"), 0), (1, k_sb, X("k"), 1)):
            for c in range(NC):
                for hf in range(2):
                    p.op("act", (lambda sx, buf=buf, c=c, hf=hf: sx.activation(
                        u.sq1024[:], buf[:, c, hf * 1024:hf * 1024 + 1024],
                        AF.Square)),
                        reads=[bkey], writes=[("u", "sq1024")])
                    for sub in range(2):
                        p.mm(u.psS[0:1, 0:512], g.ones_bf[:],
                             u.sq1024[:, sub * 512:sub * 512 + 512],
                             reads=[("u", "sq1024")], writes=[("psS",)])
                        p.op("dve", lambda v: v.reduce_max(
                            rrow[0:1, 0:1], u.psS[0:1, 0:512], axis=AX.X),
                            reads=[("psS",)], writes=[X("rr"), ("psS",)], hazard=True)
                        p.op("dve", (lambda v, mi=mi: v.tensor_max(
                            mrow[0:1, mi:mi + 1], mrow[0:1, mi:mi + 1],
                            rrow[0:1, 0:1])),
                            reads=[X("rr"), X("m%d" % mi)],
                            writes=[X("m%d" % mi)])
        p.op("act", lambda sx: sx.activation(mrow[0:1, 2:3], mrow[0:1, 0:1],
                                             AF.Sqrt),
             reads=[X("m0")], writes=[X("m2")])
        p.op("act", lambda sx: sx.activation(mrow[0:1, 3:4], mrow[0:1, 1:2],
                                             AF.Sqrt),
             reads=[X("m1")], writes=[X("m3")])
        p.op("dve", lambda v: v.tensor_mul(mrow[0:1, 2:3], mrow[0:1, 2:3],
                                           mrow[0:1, 3:4]),
             reads=[X("m2"), X("m3")], writes=[X("m2")])
        p.mm(u.psS[0:128, 0:1], g.ones1_f[:], mrow[0:1, 2:3],
             reads=[X("m2")], writes=[("psS",)])
        p.op("dve", lambda v: v.tensor_scalar_mul(negsh[:], u.psS[0:128, 0:1],
                                                  -0.125),
             reads=[("psS",)], writes=[X("negsh"), ("psS",)], hazard=True)
        for qt in range(4):
            bq_ = 0
            qsl = slice(qt * 512, qt * 512 + 512)
            p.dma("qs", phi_sb[bq_][:],
                  io.phi[qt].rearrange("kt p x -> p kt x"),
                  writes=[X("phi", bq_)])
            p.dma("qs", xt[bq_][:],
                  io.x1b[s].rearrange("c p t -> p c t")[:, :, qsl],
                  reads=[("B", s, "x1b", "acc")] +
                        [("B", s, "x1b", "init", tb) for tb in range(4)],
                  writes=[X("xt", bq_)] +
                         [X("osb", bq_, mc) for mc in range(NC)])
            def c_norm(hh):
                pb = (hh % 2) * 64
                ch = hh // 2
                av = hh % 2
                p.op("dve", (lambda v, _o=rrow[:],
                    _i=u.psB[64:65, av * 512:av * 512 + 512]: v.tensor_scalar(
                    _o, _i, 1e-30, None, op0=ALU.add)),
                    reads=[("psB", av)], writes=[X("rr")], hazard=True)
                p.op("dve", (lambda v, _o=rrow[:]: v.reciprocal(_o, _o)),
                     reads=[X("rr")], writes=[X("rr")], hazard=True)
                p.mm(u.psS[0:64, 0:512], g.ones1_f[0:1, 0:64], rrow[:],
                     reads=[X("rr")], writes=[("psS",)])
                p.op("dve", (lambda v, _o=bc_sb[:]: v.tensor_copy(_o,
                                                    u.psS[0:64, 0:512])),
                     reads=[("psS",)], writes=[X("bcs"), ("psS",)])
                p.op("dve", (lambda v, _o=attn_bf[pb:pb + 64, ch, :],
                    _i=u.psB[0:64, av * 512:av * 512 + 512],
                    _b=bc_sb[:]: v.tensor_mul(_o, _i, _b)),
                    reads=[("psB", av), X("bcs")],
                    writes=[X("at", hh), ("psB", av)])

            for hh in range(16):
                pb = (hh % 2) * 64
                ch = hh // 2
                av = hh % 2
                qrhs = q_sb[pb:pb + 64, ch, qsl]
                LAG = 2
                for i in range(16 + LAG):
                    if i < 16:
                        kt = i
                        bk = kt % 4
                        pst = u.psA if bk < 2 else u.psC
                        pk = ("psA", bk) if bk < 2 else ("psC", bk - 2)
                        hf = (bk % 2) * 512
                        klhs = k_sb[pb:pb + 64, ch, kt * 128:kt * 128 + 128]
                        p.mm(pst[:, hf:hf + 512], klhs, qrhs,
                             start=True, stop=False,
                             reads=[X("q"), X("k")], writes=[pk])
                        p.mm(pst[:, hf:hf + 512], g.ident_bf[:],
                             phi_sb[bq_][:, kt, :], start=False, stop=True,
                             reads=[X("phi", bq_), ("c", "ident_bf")],
                             writes=[pk])
                        p.op("act", (lambda sx, _o=PT[bk][:],
                            _i=pst[:, hf:hf + 512],
                            _b=negsh[:, 0:1]: sx.activation(
                            _o, _i, AF.Exp, bias=_b, scale=0.125)),
                            reads=[pk, X("negsh")],
                            writes=[X("pt", bk), pk])
                    if i >= LAG:
                        kt = i - LAG
                        bk = kt % 4
                        p.mm(u.psB[0:65, av * 512:av * 512 + 512],
                             v_sb[:, kt, hh * 65:hh * 65 + 65], PT[bk][:],
                             start=(kt == 0), stop=(kt == 15),
                             reads=[X("v"), X("pt", bk)],
                             writes=[("psB", av), X("pt", bk)])
                if hh >= 1:
                    c_norm(hh - 1)
            c_norm(15)
            for mc in range(NC):
                bk = mc % 2
                for kc in range(NC):
                    p.mm(u.psC[:, bk * 512:bk * 512 + 512],
                         wproj[:, kc, mc * 128:mc * 128 + 128],
                         attn_bf[:, kc, :],
                         start=(kc == 0), stop=(kc == NC - 1),
                         reads=[X("wp"), X("at", 2 * kc), X("at", 2 * kc + 1)],
                         writes=[("psC", bk)])
                p.op("dve", (lambda v, bk=bk, mc=mc, bq_=bq_: v.scalar_tensor_tensor(
                    out_sb[bq_][:, mc, :], u.psC[:, bk * 512:bk * 512 + 512],
                    g.L1_bproj[:, mc:mc + 1], xt[bq_][:, mc, :],
                    op0=ALU.add, op1=ALU.add)),
                    reads=[("psC", bk), X("xt", bq_), ("c", "L1_bproj")],
                    writes=[X("osb", bq_, mc), ("psC", bk)], hazard=True)
            p.dma("qa", io.out[s].rearrange("c p t -> p c t")[:, :, qsl],
                  out_sb[bq_][:],
                  reads=[X("osb", bq_, mc) for mc in range(NC)],
                  writes=[X("out", qt)] + [X("osb", bq_, mc) for mc in range(NC)]
                  + [X("xt", bq_)])


def build_nc(stages, shared):
    nc = bass.Bass("TRN2", target_bir_lowering=False, debug=False)
    p = Prog(nc)
    io = H()
    declared = {}

    def din(name, shape, dtype):
        t = nc.dram_tensor(name, list(shape), dtype, kind="ExternalInput").ap()
        declared[name] = "in"
        return t

    def dout(name, shape, dtype):
        t = nc.dram_tensor(name, list(shape), dtype, kind="ExternalOutput").ap()
        declared[name] = "out"
        return t

    def dscratch(name, shape, dtype, produced, consumed):
        if produced and consumed:
            t = nc.dram_tensor(name, list(shape), dtype, kind="Internal").ap()
            declared[name] = "internal"
            return t
        if produced:
            return dout(name, shape, dtype)
        return din(name, shape, dtype)

    A = "A" in stages
    B = "B" in stages
    C = "C" in stages
    io.consts = {}
    cshapes = {}
    for k, v in shared.items():
        cshapes[k] = v
    const_names = ["ones_bf", "ones64a_bf", "ones64b_bf", "ones1_f", "selA_f",
                   "selB_f", "ident_bf", "ident_f", "rot_bf", "maskband_bf",
                   "L0a_lng", "L0a_lnb", "L0a_lam", "L0f_b1a", "L0f_b1g",
                   "L0f_b2", "L0f_lng", "L0f_lnb", "L0f_qs", "L0f_ks",
                   "L0f_qs_sw", "L0f_ks_sw", "L1_bproj", "L1f_bri",
                   "L1f_G1R", "L1f_G1I", "L1f_G2R", "L1f_G2I",
                   "Yb_f", "Ye_f", "Zd_f", "Zg_f",
                   "c1q_bf", "c2q_bf", "c1k_bf", "c2k_bf"]
    for nm in const_names:
        arr = shared[nm]
        dt = BF16 if arr.dtype == ml_dtypes.bfloat16 else F32
        io.consts[nm] = din("K_" + nm, arr.shape, dt)
    hd = io.consts

    if A and any(k in DEV for k in ("dump_xf", "dump_carry", "dump_q", "dump_v", "dump_attn", "dump_rn", "dump_sh")):
        io.dbg = dout("dbg", [NC, 128, TH], F32)
    if A:
        io.x = din("x", [SEQS, NC, 128, T], F32)
        io.wqT = din("L0a_WqT", [D, D], F32)
        io.wvT = din("L0a_WvT", [D, D], F32)
        io.wgT = din("L0a_WgT", [D, D], F32)
        io.woT = din("L0a_WoT", [D, D], F32)
        io.wqkv = din("L0f_wqkv", [2, D, 1536], F32)
        io.w1a = din("L0f_w1a", [2, D, HIDP], F32)
        io.w1g = din("L0f_w1g", [2, D, HIDP], F32)
        io.w2 = din("L0f_w2", [2, HIDP, D], F32)
        io.wo = din("L0f_wo", [2, 512, D], F32)
    if A or B:
        io.x1a = dscratch("x1a", [SEQS, NC, 128, T], F32, produced=A, consumed=B)
    if B:
        io.l1wqkv = din("L1_wqkv", [D, 3 * D], F32)
    if B or C:
        io.x1b = dscratch("x1b", [SEQS, NC, 128, T], F32, produced=B, consumed=C)
        io.qf = dscratch("qf", [SEQS, NC, 128, T], BF16, produced=B, consumed=C)
        io.kf = dscratch("kf", [SEQS, NC, 128, T], BF16, produced=B, consumed=C)
        io.vf = dscratch("vf", [SEQS, 16, 128, 16 * 65], BF16,
                         produced=B, consumed=C)
    if C:
        io.l1wproj = din("L1_wproj", [D, D], F32)
        io.phi = din("K_phi8_bf", [4, 16, 128, 512], BF16)
        io.out = dout("out", [SEQS, NC, 128, T], F32)
    u = H()
    u.psA = p.ps("psA", [128, 1024])
    u.psB = p.ps("psB", [128, 1024])
    u.psC = p.ps("psC", [128, 1024])
    u.psS = p.ps("psS", [128, 512])
    u.psQ = p.ps("psQ", [128, 512])
    u.sq = p.sb("u_sq", [128, 512], BF16)
    u.sq1024 = p.sb("u_sq1024", [128, 1024], BF16)
    u.tmp = p.sb("u_tmp", [128, 512], F32)
    u.tmp_f = p.sb("u_tmp_f", [128, 1024], F32)
    u.tmp_bf = p.sb("u_tmp_bf", [128, 1024], BF16)
    u.rows = p.sb("u_rows", [1, 1536], F32)
    u.nrows = p.sb("u_nrows", [1, 1024], F32)
    u.eps = p.sb("u_eps", [1, 1], F32)
    p.op("dve", lambda v: v.memset(u.eps[:], 1e-5), writes=[("u", "eps")])
    g = declare_consts(p, nc, io.consts)
    if A:
        with _Scope(p) as basesc:
            base = H()
            base.XF = basesc.sb("XF", [128, NC, TH + 1], F32)
            base.save_ln_f = basesc.sb("save_ln_f", [128, NC, 2], F32)
            base.XFb = basesc.sb("XFb", [128, NC, TH + 1], BF16)
            base.carry = basesc.sb("carry", [128, NC], F32)
            base.save_qkv = basesc.sb("save_qkv", [128, NC, 2], BF16)
            base.save_ln = basesc.sb("save_ln", [128, NC, 2], BF16)
            base.save_k = basesc.sb("save_k", [128, 4, 512, 2], BF16)
            base.save_v = basesc.sb("save_v", [128, 4, 520, 2], BF16)
            for s in range(DEV.get("nseq", SEQS)):
                emit_stage_a(p, g, io, u, base, s)
    if B:
        with _Scope(p) as bsc:
            bb = H()
            bb.Wc = bsc.sb("Wc", [128, NC, 1024], BF16)
            bb.XN1b = bsc.sb("XN1b", [128, NC, T], BF16)
            emit_stage_b(p, g, io, u, bb)
    if C:
        for s in range(SEQS):
            emit_stage_c(p, g, io, u, s)
    global LAST_PROG
    LAST_PROG = p
    p.finalize_and_emit()
    in_names = [k for k, v in declared.items() if v == "in"]
    out_names = [k for k, v in declared.items() if v == "out"]
    return nc, in_names, out_names


def _run(nc, in_names, maps):
    in_maps = [{k: m[k] for k in in_names} for m in maps]
    res = run_bass_kernel_spmd(nc, in_maps, core_ids=list(range(NCORES)))
    return res.results


def kernel(**inputs):
    shared, per_core = host_prep(inputs)
    smap = {}
    for k, v in shared.items():
        smap["K_" + k] = v
    smap["K_phi8_bf"] = shared["phi8_bf"]
    smap.update({
        "L0a_WqT": shared["L0a_WqT"], "L0a_WvT": shared["L0a_WvT"],
        "L0a_WgT": shared["L0a_WgT"], "L0a_WoT": shared["L0a_WoT"],
        "L0f_wqkv": shared["L0f_wqkv"], "L0f_w1a": shared["L0f_w1a"],
        "L0f_w1g": shared["L0f_w1g"], "L0f_w2": shared["L0f_w2"],
        "L0f_wo": shared["L0f_wo"], "L1_wqkv": shared["L1_wqkv"],
        "L1_wproj": shared["L1_wproj"],
    })
    if UNFUSED:
        stage_sets = [("A",), ("B",), ("C",)]
    else:
        stage_sets = [("A", "B", "C")]
    core_maps = [dict(smap, **pc) for pc in per_core]
    outs = None
    for stages in stage_sets:
        nc, in_names, out_names = build_nc(stages, shared)
        results = _run(nc, in_names, core_maps)
        for core in range(NCORES):
            for onm in out_names:
                core_maps[core][onm] = np.asarray(results[core][onm])
        outs = results
    out = np.empty((16, T, D), np.float32)
    for core in range(NCORES):
        o = np.asarray(outs[core]["out"], np.float32).reshape(SEQS, D, T)
        for s in range(SEQS):
            out[core * SEQS + s] = o[s].T
    return out
